# version 15
# baseline (speedup 1.0000x reference)
import contextlib
import numpy as np
import ml_dtypes

import concourse.bass as bass
import concourse.mybir as mybir
from concourse.alu_op_type import AluOpType as OP
from concourse.bass_utils import run_bass_kernel_spmd

F32 = mybir.dt.float32
BF16 = mybir.dt.bfloat16
AF = mybir.ActivationFunctionType
AX = mybir.AxisListType
BF = ml_dtypes.bfloat16

ENG = ("sync", "tensor", "vector", "scalar", "gpsimd")
AUTO_SELF = {"vector", "scalar", "gpsimd"}

T = 2048
D = 1024
NT = T // 128
H = 8
DK = 64
DV = 128
LNR8 = -2.0794415416798357


class Prog:

    def __init__(self):
        self.nc = bass.Bass("TRN2", target_bir_lowering=False)
        self.es = contextlib.ExitStack()
        self.es_wob = self.es.enter_context(contextlib.ExitStack())
        self.sched = {e: [] for e in ENG}
        self.ctr = {e: 0 for e in ENG}
        self.chctr = {}
        self.waited = {c: {} for c in ENG}
        self._n = 0

    def _nm(self, pfx):
        self._n += 1
        return f"{pfx}{self._n}"

    def sb(self, shape, dt=F32, name=None, es=None):
        return (es or self.es).enter_context(
            self.nc.sbuf_tensor(name or self._nm("sb"), shape, dt))

    def ps(self, shape, dt=F32, name=None, es=None):
        return (es or self.es).enter_context(
            self.nc.psum_tensor(name or self._nm("ps"), shape, dt))

    def din(self, name, shape, dt=F32):
        return self.nc.dram_tensor(name, shape, dt, kind="ExternalInput").ap()

    def dout(self, name, shape, dt=F32):
        return self.nc.dram_tensor(name, shape, dt, kind="ExternalOutput").ap()

    def dscr(self, name, shape, dt=F32):
        return self.nc.dram_tensor(name, shape, dt, kind="Internal").ap()

    def _wait(self, eng, sem, v):
        if self.waited[eng].get(sem, 0) < v:
            self.sched[eng].append(("w", sem, v))
            self.waited[eng][sem] = v

    def op(self, eng, fn, after=(), inc=True, ch=None):
        sch = self.sched[eng]
        if eng in AUTO_SELF and self.ctr[eng] > self.waited[eng].get(eng, 0):
            self._wait(eng, eng, self.ctr[eng])
        for tok in after:
            if tok is None:
                continue
            self._wait(eng, tok[0], tok[1])
        if ch is not None:
            self.chctr[ch] = self.chctr.get(ch, 0) + 16
            sch.append(("op", fn, ("c", ch, self.chctr[ch])))
            return (ch, self.chctr[ch])
        if inc:
            self.ctr[eng] += 1
            sch.append(("op", fn, ("e", eng, self.ctr[eng])))
            return (eng, self.ctr[eng])
        sch.append(("op", fn, None))
        return None

    def tok(self, eng):
        return (eng, self.ctr[eng])

    def barrier(self):
        toks = [(e, self.ctr[e]) for e in ENG]
        toks += [(c, v) for c, v in self.chctr.items()]
        for e in ENG:
            for pe, v in toks:
                if v > 0:
                    self._wait(e, pe, v)

    def finalize(self):
        self.barrier()
        sems = {e: self.es.enter_context(self.nc.semaphore("S" + e)) for e in ENG}
        for c in self.chctr:
            sems[c] = self.es.enter_context(self.nc.semaphore("C" + c))
        block = self.es.enter_context(self.nc.Block())

        def emit2(eng_name):
            def body(engine):
                for item in self.sched[eng_name]:
                    if item[0] == "w":
                        engine.wait_ge(sems[item[1]], item[2])
                    else:
                        inst = item[1](engine)
                        if item[2] is not None:
                            kind, nm, val = item[2]
                            inst.then_inc(sems[nm], 16 if kind == "c" else 1)
            return body

        block.sync(emit2("sync"))
        block.tensor(emit2("tensor"))
        block.vector(emit2("vector"))
        block.scalar(emit2("scalar"))
        block.gpsimd(emit2("gpsimd"))
        self.es.close()
        return self.nc

    def dma(self, eng, out, in_, after=(), ch="c0"):
        return self.op(eng, lambda e, o=out, i=in_: e.dma_start(out=o, in_=i),
                       after=after, ch=ch)

    def mm(self, out, lhsT, rhs, start, stop, after=(), inc=True):
        return self.op("tensor",
                       lambda e, o=out, l=lhsT, r=rhs, s1=start, s2=stop:
                       e.matmul(o, l, r, start=s1, stop=s2),
                       after=after, inc=inc)

    def tr(self, out, in_, ident, after=(), inc=True):
        return self.op("tensor",
                       lambda e, o=out, i=in_, d=ident: e.transpose(o, i, d),
                       after=after, inc=inc)

    def act(self, out, in_, func, bias=None, scale=1.0, accum_out=None, after=()):
        def fn(e, o=out, i=in_, f=func, b=bias, s=scale, a=accum_out):
            kw = dict(scale=s)
            if b is not None:
                kw["bias"] = b
            if a is not None:
                kw["accum_out"] = a
            return e.activation(o, i, f, **kw)
        return self.op("scalar", fn, after=after)

    def scopy(self, out, in_, after=()):
        return self.op("scalar", lambda e, o=out, i=in_: e.copy(o, i), after=after)

    def tt(self, out, in0, in1, op, after=(), eng="vector"):
        return self.op(eng, lambda e, o=out, a=in0, b=in1, p=op:
                       e.tensor_tensor(o, a, b, p), after=after)

    def ts(self, out, in0, s1, s2, op0, op1=None, after=(), accum_out=None, eng="vector"):
        def fn(e, o=out, i=in0, a=s1, b=s2, p0=op0, p1=op1, ac=accum_out):
            kw = {}
            if p1 is not None:
                kw["op1"] = p1
            if ac is not None:
                kw["accum_out"] = ac
            return e.tensor_scalar(o, i, a, b, p0, **kw)
        return self.op(eng, fn, after=after)

    def stt(self, out, in0, scalar, in1, op0, op1, after=()):
        return self.op("vector", lambda e, o=out, i0=in0, s=scalar, i1=in1, p0=op0, p1=op1:
                       e.scalar_tensor_tensor(o, i0, s, i1, p0, p1), after=after)

    def vcopy(self, out, in_, after=(), eng="vector"):
        return self.op(eng, lambda e, o=out, i=in_: e.tensor_copy(o, i), after=after)

    def red(self, out, in_, op, negate=False, after=()):
        return self.op("vector", lambda e, o=out, i=in_, p=op, n=negate:
                       e.tensor_reduce(o, i, axis=AX.X, op=p, negate=n), after=after)

    def recip(self, out, in_, after=()):
        return self.op("vector", lambda e, o=out, i=in_: e.reciprocal(o, i), after=after)

    def memset(self, ap, val, after=(), eng="gpsimd"):
        return self.op(eng, lambda e, a=ap, v=val: e.memset(a, v), after=after)



def host_prep(inputs, core):
    f32 = np.float32
    bf = BF
    d = {}
    d["x"] = np.ascontiguousarray(inputs["L0_attn_x"][core], f32)
    d["wq"] = np.ascontiguousarray(inputs["L0_attn_Wq"], f32)
    d["wk"] = np.ascontiguousarray(inputs["L0_attn_Wk"], f32)
    d["wv"] = np.ascontiguousarray(inputs["L0_attn_Wv"], f32)
    d["wg"] = np.ascontiguousarray(inputs["L0_attn_Wg"], f32)
    d["wgk1"] = np.ascontiguousarray(inputs["L0_attn_Wgk1"], f32)
    d["wgk2"] = np.ascontiguousarray(inputs["L0_attn_Wgk2"], f32)
    d["bgk2r"] = np.ascontiguousarray(inputs["L0_attn_bgk2"].reshape(1, 512), f32)
    d["nwc"] = np.ascontiguousarray(inputs["L0_attn_norm_w"].reshape(128, 1), f32)
    d["wo"] = np.ascontiguousarray(inputs["L0_attn_Wo"], f32)
    d["gwT"] = np.ascontiguousarray(inputs["L0_ffn_gate_w"].T, f32)
    wupT = np.ascontiguousarray(inputs["L0_ffn_w_up"].transpose(0, 2, 1), f32)
    d["wupT_own"] = np.ascontiguousarray(wupT[core])
    d["wdn_own"] = np.ascontiguousarray(inputs["L0_ffn_w_down"][core], f32)
    d["gam_c"] = np.ascontiguousarray(inputs["L1_attn_gn_gamma"].reshape(8, 128).T, f32)
    d["bet_c"] = np.ascontiguousarray(inputs["L1_attn_gn_beta"].reshape(8, 128).T, f32)
    d["wqkv"] = np.ascontiguousarray(inputs["L1_attn_W_qkv"], f32)
    d["bqkvr"] = np.ascontiguousarray(inputs["L1_attn_b_qkv"].reshape(1, 3072), f32)
    d["wo1"] = np.ascontiguousarray(inputs["L1_attn_W_o"], f32)
    d["bo1r"] = np.ascontiguousarray(inputs["L1_attn_b_o"].reshape(1, 1024), f32)
    d["wg1"] = np.ascontiguousarray(inputs["L1_ffn_wg1"], f32)
    d["bg1c"] = np.ascontiguousarray(inputs["L1_ffn_bg1"].reshape(2, 128).T, f32)
    d["wg2"] = np.ascontiguousarray(inputs["L1_ffn_wg2"], f32)
    d["win"] = np.ascontiguousarray(inputs["L1_ffn_w_in"], f32)
    d["wout"] = np.ascontiguousarray(inputs["L1_ffn_w_out"], f32)
    tri = np.triu(np.ones((64, 64), f32))
    u2 = np.zeros((128, 128), f32)
    u2[:64, :64] = tri / 16.0
    u2[64:, 64:] = tri / 16.0
    d["U2b"] = u2.astype(bf)
    m2 = np.zeros((128, 128), f32)
    m2[:64, :64] = tri
    m2[64:, 64:] = tri
    d["M2f"] = m2
    d["idb"] = np.eye(128, dtype=f32).astype(bf)
    d["idf"] = np.eye(128, dtype=f32)
    d["onesr"] = np.ones((1, 128), f32).astype(bf)
    d["onesc"] = np.ones((128, 1), f32).astype(bf)
    cc = np.zeros((128, 6), f32)
    cc[:, 0] = 1.0
    cc[:, 1] = 1e-5
    cc[:, 2] = 1e-6
    cc[:, 3] = LNR8
    cc[:, 4] = 1e-30
    d["cc"] = cc
    d["onesr512"] = np.ones((1, 512), f32).astype(bf)
    e4 = np.zeros((128, 4), f32)
    for j in range(4):
        e4[32 * j:32 * (j + 1), j] = 1.0
    d["E4"] = e4
    d["E4T"] = np.ascontiguousarray(e4.T)
    return d


def declare_io(p):
    io = {}
    io["x"] = p.din("x", [T, D])
    io["wq"] = p.din("wq", [D, 512]); io["wk"] = p.din("wk", [D, 512])
    io["wv"] = p.din("wv", [D, D]); io["wg"] = p.din("wg", [D, D])
    io["wgk1"] = p.din("wgk1", [D, 16]); io["wgk2"] = p.din("wgk2", [16, 512])
    io["bgk2r"] = p.din("bgk2r", [1, 512]); io["nwc"] = p.din("nwc", [128, 1])
    io["wo"] = p.din("wo", [D, D]); io["gwT"] = p.din("gwT", [D, 8])
    io["wupT_own"] = p.din("wupT_own", [4096, D])
    io["wdn_own"] = p.din("wdn_own", [4096, D])
    io["gam_c"] = p.din("gam_c", [128, 8]); io["bet_c"] = p.din("bet_c", [128, 8])
    io["wqkv"] = p.din("wqkv", [D, 3072]); io["bqkvr"] = p.din("bqkvr", [1, 3072])
    io["wo1"] = p.din("wo1", [D, D]); io["bo1r"] = p.din("bo1r", [1, D])
    io["wg1"] = p.din("wg1", [D, 256]); io["bg1c"] = p.din("bg1c", [128, 2])
    io["wg2"] = p.din("wg2", [256, 8])
    io["win"] = p.din("win", [8, D, 2048]); io["wout"] = p.din("wout", [8, 2048, D])
    io["U2b"] = p.din("U2b", [128, 128], BF16)
    io["M2f"] = p.din("M2f", [128, 128])
    io["idb"] = p.din("idb", [128, 128], BF16)
    io["idf"] = p.din("idf", [128, 128])
    io["onesr"] = p.din("onesr", [1, 128], BF16)
    io["onesc"] = p.din("onesc", [128, 1], BF16)
    io["cc"] = p.din("cc", [128, 6])
    io["onesr512"] = p.din("onesr512", [1, 512], BF16)
    io["E4"] = p.din("E4", [128, 4])
    io["E4T"] = p.din("E4T", [4, 128])
    io["out"] = p.dout("out", [T, D])
    return io



def load_weight_bf16(p, src, rows, cols, es, scale_col=None, prev=None):
    nch = rows // 128
    chp = p._nm("lw")
    wbf = p.sb([128, nch * cols], BF16, es=es)
    stg = [p.sb([128, cols], F32, es=es), p.sb([128, cols], F32, es=es)]
    stok = [None, None]
    for c in range(nch):
        s = c % 2
        tl = p.dma("sync", stg[s][:], src[c * 128:(c + 1) * 128, :],
                   after=[stok[s], prev], ch=chp + str(s))
        if scale_col is None:
            tv = p.vcopy(wbf[:, c * cols:(c + 1) * cols], stg[s][:], after=[tl],
                         eng="gpsimd")
        else:
            tv = p.ts(wbf[:, c * cols:(c + 1) * cols], stg[s][:], scale_col, None,
                      OP.mult, after=[tl], eng="gpsimd")
        stok[s] = tv
    return wbf, stok[0], stok[1]


def load_row_bf16(p, src, cols, es):
    stg = p.sb([1, cols], F32, es=es)
    row = p.sb([1, cols], BF16, es=es)
    tl = p.dma("sync", stg[:], src, ch=p._nm("lr"))
    tv = p.vcopy(row[:], stg[:], after=[tl])
    return row, tv



def emit_ln(p, src_dram, xnfm, cc, idb, eps_col_idx=1, xn_tm_dram=None):
    with contextlib.ExitStack() as es:
        xt = [p.sb([128, D], F32, es=es) for _ in range(2)]
        junk = p.sb([128, D], BF16, es=es)
        xnb = [p.sb([128, D], BF16, es=es) for _ in range(2)]
        colp = [p.sb([128, 8], F32, es=es) for _ in range(2)]
        ptb = [p.ps([128, 128], BF16, es=es) for _ in range(2)]
        tx = [None, None]
        txn = [None, None]
        tpt = [None, None]
        tj = None
        for t in range(NT):
            s = t % 2
            ctl = colp[s]
            tl = p.dma("sync", xt[s][:], src_dram[t * 128:(t + 1) * 128, :],
                       after=[tx[s]], ch="lnx%d" % s)
            tssq = p.act(junk[:], xt[s][:], AF.Square, accum_out=ctl[:, 1:2],
                         after=[tl, tj, tx[s]])
            tj = tssq
            tsum = p.red(ctl[:, 0:1], xt[s][:], OP.add, after=[tl, tx[s]])
            tm = p.ts(ctl[:, 2:3], ctl[:, 0:1], 1.0 / D, None, OP.mult, after=[tsum])
            tq = p.ts(ctl[:, 3:4], ctl[:, 1:2], 1.0 / D, None, OP.mult, after=[tssq])
            tmm = p.tt(ctl[:, 4:5], ctl[:, 2:3], ctl[:, 2:3], OP.mult, after=[tm])
            tvar = p.tt(ctl[:, 4:5], ctl[:, 3:4], ctl[:, 4:5], OP.subtract,
                        after=[tq, tmm])
            tln = p.act(ctl[:, 5:6], ctl[:, 4:5], AF.Ln,
                        bias=cc[:, eps_col_idx:eps_col_idx + 1], after=[tvar])
            trr = p.act(ctl[:, 5:6], ctl[:, 5:6], AF.Exp, scale=-0.5, after=[tln])
            tmr = p.tt(ctl[:, 6:7], ctl[:, 2:3], ctl[:, 5:6], OP.mult, after=[trr])
            txnb = p.ts(xnb[s][:], xt[s][:], ctl[:, 5:6], ctl[:, 6:7],
                        OP.mult, OP.subtract, after=[tmr, txn[s]])
            tx[s] = txnb
            if xn_tm_dram is not None:
                txn_d = p.dma("gpsimd", xn_tm_dram[t * 128:(t + 1) * 128, :],
                              xnb[s][:], after=[txnb], ch="lnxo%d" % s)
            else:
                txn_d = None
            last = txnb
            for c in range(8):
                ss = c % 2
                ttr = p.tr(ptb[ss][:], xnb[s][:, c * 128:(c + 1) * 128], idb,
                           after=[txnb, tpt[ss]])
                tcp = p.vcopy(xnfm[:, c * T + t * 128: c * T + (t + 1) * 128],
                              ptb[ss][:], after=[ttr])
                tpt[ss] = tcp
                last = tcp
            txn[s] = last if txn_d is None else txn_d
        p.barrier()


def emit_moe_mown(p, io, scr, cst):
    with contextlib.ExitStack() as es:
        wdnb = p.sb([128, 32 * 1024], BF16, es=es)
        wupb = [p.sb([128, 4096], BF16, es=es) for _ in range(2)]
        stgw = [p.sb([128, 1024], F32, es=es) for _ in range(2)]
        stgu = [p.sb([128, 2048], F32, es=es) for _ in range(2)]
        mstg = p.sb([128, 1024], BF16, es=es)
        pM = [p.ps([128, 1024], F32, es=es) for _ in range(2)]
        wstg = [None, None]
        stu_tok = [None, None]
        wdn_ready = None
        for c in range(32):
            s = c % 2
            tl = p.dma("scalar", stgw[s][:],
                       io["wdn_own"][c * 128:(c + 1) * 128, :], after=[wstg[s]],
                       ch="wdn%d" % s)
            wdn_ready = p.vcopy(wdnb[:, 1024 * c:1024 * (c + 1)], stgw[s][:],
                                after=[tl], eng="gpsimd")
            wstg[s] = wdn_ready
        upcv = [None, None]
        upcons = [None, None]

        def load_up(mi):
            bm = mi % 2
            wsrc = io["wupT_own"].rearrange("(k p) c -> p k c", p=128)
            wsrc = wsrc[:, :, 128 * mi:128 * (mi + 1)]
            tcv = None
            for hh in range(2):
                sl = wsrc[:, 16 * hh:16 * (hh + 1), :]
                tl = p.dma("scalar", stgu[hh][:], sl,
                           after=[stu_tok[hh], upcons[bm]], ch="wup%d" % hh)
                tcv = p.vcopy(wupb[bm][:, 2048 * hh:2048 * (hh + 1)],
                              stgu[hh][:], after=[tl, upcons[bm]], eng="gpsimd")
                stu_tok[hh] = tcv
            upcv[bm] = tcv

        load_up(0)
        wpM = [None, None]
        wms = None
        for m in range(8):
            sm = m % 2
            if m < 7:
                load_up(m + 1)
            tk = None
            for k in range(32):
                for n in range(2):
                    tk = p.mm(pM[sm][:, 512 * n:512 * (n + 1)],
                              wupb[m % 2][:, 128 * k:128 * (k + 1)],
                              wdnb[:, 1024 * k + 512 * n: 1024 * k + 512 * (n + 1)],
                              start=(k == 0), stop=(k == 31),
                              after=[upcv[m % 2], wdn_ready, wpM[sm]] if k == 0 else (),
                              inc=(k == 31 and n == 1))
            upcons[m % 2] = tk
            tev = p.act(mstg[:], pM[sm][:], AF.Copy, after=[tk, wms])
            wpM[sm] = tev
            wms = p.dma("gpsimd", scr["mown"][m * 128:(m + 1) * 128, :], mstg[:],
                        after=[tev], ch="mow")
        p.barrier()
        tcc = p.op("gpsimd",
                   lambda e: e.collective_compute(
                       "AllGather", mybir.AluOpType.bypass,
                       replica_groups=[list(range(8))],
                       ins=[scr["mown"].opt()],
                       outs=[scr["mall"].opt()]),
                   after=[wms])
        p.mall_tok = tcc


def emit_l0_proj(p, io, scr, cst, xnfm):
    cc, onesr = cst["cc"], cst["onesr"]
    with contextlib.ExitStack() as es:
        wqb, _, _ = load_weight_bf16(p, io["wq"], D, 512, es)
        wkb, _, _ = load_weight_bf16(p, io["wk"], D, 512, es)
        wvb, _, _ = load_weight_bf16(p, io["wv"], D, D, es)
        wgb, _, _ = load_weight_bf16(p, io["wg"], D, D, es)
        nwcs = p.sb([128, 1], F32, es=es)
        tnw = p.dma("sync", nwcs[:], io["nwc"], ch="nwc")
        wob = p.wob
        stgo = [p.sb([128, D], F32, es=es) for _ in range(2)]
        stok = [None, None]
        for c in range(8):
            s = c % 2
            tl = p.dma("sync", stgo[s][:], io["wo"][c * 128:(c + 1) * 128, :],
                       after=[stok[s]], ch="wo%d" % s)
            tv = p.ts(wob[:, c * D:(c + 1) * D], stgo[s][:], nwcs[:], None,
                      OP.mult, after=[tl, tnw], eng="gpsimd")
            stok[s] = tv
        wgk1b, _, _ = load_weight_bf16(p, io["wgk1"], D, 16, es)
        wgk2b = p.sb([16, 512], BF16, es=es)
        stg2 = p.sb([16, 512], F32, es=es)
        t2 = p.dma("sync", stg2[:], io["wgk2"], ch="wgk2")
        p.vcopy(wgk2b[:], stg2[:], after=[t2])
        bgk2b, tbg = load_row_bf16(p, io["bgk2r"], 512, es)
        p.barrier()

        psA = [p.ps([128, 512], F32, es=es) for _ in range(2)]
        psB = [p.ps([128, 1024], F32, es=es) for _ in range(2)]
        stgb = [p.sb([128, 1024], BF16, es=es) for _ in range(2)]
        ptoks = [None, None]
        etoksA = [None, None]
        etoksB = [None, None]
        for (wb, dst) in ((wqb, scr["qfm"]), (wkb, scr["kfm"])):
            for m in range(4):
                for g in range(4):
                    s = (m * 4 + g) % 2
                    tk = None
                    for c in range(8):
                        tk = p.mm(psA[s][:],
                                  wb[:, 512 * c + 128 * m: 512 * c + 128 * (m + 1)],
                                  xnfm[:, c * T + 512 * g: c * T + 512 * (g + 1)],
                                  start=(c == 0), stop=(c == 7),
                                  after=[etoksA[s]] if c == 0 else (), inc=(c == 7))
                    te = p.act(stgb[s][:, :512], psA[s][:], AF.Copy,
                               after=[tk, ptoks[s]])
                    td = p.dma("gpsimd",
                               dst[128 * m:128 * (m + 1), 512 * g:512 * (g + 1)],
                               stgb[s][:, :512], after=[te], ch="pev%d" % s)
                    ptoks[s] = td
                    etoksA[s] = te
        for (wb, ncols, dst) in ((wkb, 512, scr["ktm"]), (wvb, D, scr["vtm"]),
                                 (wgb, D, scr["gtm"])):
            ngr = ncols // 512
            for t in range(NT):
                s = t % 2
                ps = psB[s]
                tk = None
                for n in range(ngr):
                    for c in range(8):
                        tk = p.mm(ps[:, 512 * n:512 * (n + 1)],
                                  xnfm[:, c * T + 128 * t: c * T + 128 * (t + 1)],
                                  wb[:, ncols * c + 512 * n: ncols * c + 512 * (n + 1)],
                                  start=(c == 0), stop=(c == 7),
                                  after=[etoksB[s]] if (c == 0 and n == 0) else (),
                                  inc=(c == 7 and n == ngr - 1))
                te = p.act(stgb[s][:, :ncols], ps[:, :ncols], AF.Copy,
                           after=[tk, ptoks[s]])
                td = p.dma("gpsimd", dst[128 * t:128 * (t + 1), :],
                           stgb[s][:, :ncols], after=[te], ch="pev%d" % s)
                ptoks[s] = td
                etoksB[s] = te
        ufmb = p.sb([16, T], BF16, es=es)
        psU = p.ps([16, 512], F32, es=es)
        tprev = None
        for g in range(4):
            tk = None
            for c in range(8):
                tk = p.mm(psU[:], wgk1b[:, 16 * c:16 * (c + 1)],
                          xnfm[:, c * T + 512 * g: c * T + 512 * (g + 1)],
                          start=(c == 0), stop=(c == 7),
                          after=[tprev] if c == 0 else (), inc=(c == 7))
            tprev = p.act(ufmb[:, 512 * g:512 * (g + 1)], psU[:], AF.Copy, after=[tk])
        zcol = [p.sb([128, 512], F32, es=es) for _ in range(2)]
        zc2 = [p.sb([128, 512], F32, es=es) for _ in range(2)]
        lsb = [p.sb([128, 512], BF16, es=es) for _ in range(2)]
        dtoks = [None, None]
        wzc = [None, None]
        for t in range(NT):
            s = t % 2
            ps = psA[s]
            p.mm(ps[:], ufmb[:, 128 * t:128 * (t + 1)], wgk2b[:],
                 start=True, stop=False, after=[tprev, etoksA[s]], inc=False)
            tk = p.mm(ps[:], onesr, bgk2b[:], start=False, stop=True, after=[tbg])
            tabs = p.act(zcol[s][:], ps[:], AF.Abs, after=[tk, wzc[s]])
            texp = p.act(zcol[s][:], zcol[s][:], AF.Exp, scale=-1.0, after=[tabs])
            tl1p = p.act(zcol[s][:], zcol[s][:], AF.Ln, bias=cc[:, 0:1], after=[texp])
            tmin = p.ts(zc2[s][:], ps[:], 0.0, None, OP.min, after=[tk, tabs])
            etoksA[s] = tmin
            tls = p.tt(lsb[s][:], zc2[s][:], zcol[s][:], OP.subtract,
                       after=[tmin, tl1p, dtoks[s]])
            wzc[s] = tls
            td = p.dma("gpsimd", scr["lsz"][128 * t:(t + 1) * 128, :], lsb[s][:],
                       after=[tls], ch="lszo%d" % s)
            dtoks[s] = td
        p.barrier()


def emit_l0_scan(p, io, scr, cst):
    cc, idb, U2b, M2f = cst["cc"], cst["idb"], cst["U2b"], cst["M2f"]
    wob = p.wob
    with contextlib.ExitStack() as es:
        lszt = [p.sb([128, 512], BF16, es=es) for _ in range(2)]
        ktt = [p.sb([128, 512], BF16, es=es) for _ in range(2)]
        vtt = [p.sb([128, 1024], BF16, es=es) for _ in range(2)]
        gtt = [p.sb([128, 1024], BF16, es=es) for _ in range(2)]
        qfh = [[p.sb([64, 128], BF16, es=es) for _ in range(H)] for _ in range(2)]
        kfh = [[p.sb([64, 128], BF16, es=es) for _ in range(H)] for _ in range(2)]
        QE0 = p.sb([128, 128], BF16, es=es)
        QE1 = p.sb([128, 128], BF16, es=es)
        KE0 = p.sb([128, 128], BF16, es=es)
        KE1 = p.sb([128, 128], BF16, es=es)
        KT0 = p.sb([128, 64], BF16, es=es)
        KT1 = p.sb([128, 64], BF16, es=es)
        ebf = p.sb([64, 128], F32, es=es)
        enf = p.sb([64, 128], F32, es=es)
        ent = p.sb([128, 64], F32, es=es)
        e0c = p.sb([64, 1], F32, es=es)
        e1c = p.sb([64, 1], F32, es=es)
        A2m = p.sb([128, 128], BF16, es=es)
        S_A = p.sb([64, 8 * 128], F32, es=es)
        S_B = p.sb([64, 8 * 128], F32, es=es)
        S_Ab = p.sb([128, 8 * 128], BF16, es=es)
        S_Bb = p.sb([128, 8 * 128], BF16, es=es)
        tz = []
        tz.append(p.memset(QE0[:], 0.0))
        tz.append(p.memset(QE1[:], 0.0))
        tz.append(p.memset(KE0[:], 0.0))
        tz.append(p.memset(KE1[:], 0.0))
        tz.append(p.memset(KT0[:], 0.0))
        tz.append(p.memset(KT1[:], 0.0))
        tz.append(p.memset(S_A[:], 0.0))
        tz.append(p.memset(S_B[:], 0.0))
        tz.append(p.memset(S_Ab[:], 0.0))
        tz.append(p.memset(S_Bb[:], 0.0))
        p.barrier()
        pbtm = p.ps([128, 64], F32, es=es)
        pbfm = p.ps([64, 128], F32, es=es)
        pAT = p.ps([128, 128], F32, es=es)
        pkv0 = p.ps([64, 128], F32, es=es)
        pkv1 = pkv0
        po = p.ps([128, 128], F32, es=es)
        ot = p.sb([128, 1024], F32, es=es)
        sqt = p.sb([128, 1024], F32, es=es)
        sgt = p.sb([128, 1024], F32, es=es)
        rs8 = p.sb([128, 8], F32, es=es)
        ogb = p.sb([128, 1024], BF16, es=es)
        ogf = p.sb([128, 1024], BF16, es=es)
        ptb = [p.ps([128, 128], BF16, es=es) for _ in range(2)]
        ps0 = p.ps([128, 512], F32, es=es)
        s0t = [p.sb([128, 1024], F32, es=es) for _ in range(2)]

        tld = [dict(lsz=None, k=None, v=None, g=None, q=[None] * H, kf=None) for _ in range(2)]
        w = dict(pbtm=None, pbfm=None, pAT=None, pkv0=None, pkv1=None, po=None,
                 ebf=None, enf=None, ent=None, e0=None, e1=None, A2m=None,
                 QE0=None, QE1=None, KE0=None, KE1=None, KT0=None, KT1=None,
                 SA=None, SB=None, SAb=None, SBb=None, ot=None, ogb=None,
                 ogf=None, ps0=None, s0t=[None, None], sqt=None, sgt=None,
                 rs8=None)
        prevdma = [None, None]
        for t in range(NT):
            s = t % 2
            pd = prevdma[s]
            chn = "scan%d" % s
            p.dma("sync", lszt[s][:], scr["lsz"][t * 128:(t + 1) * 128, :], after=[pd], ch=chn)
            p.dma("sync", ktt[s][:], scr["ktm"][t * 128:(t + 1) * 128, :], after=[pd], ch=chn)
            p.dma("sync", vtt[s][:], scr["vtm"][t * 128:(t + 1) * 128, :], after=[pd], ch=chn)
            p.dma("sync", gtt[s][:], scr["gtm"][t * 128:(t + 1) * 128, :], after=[pd], ch=chn)
            for h in range(H):
                p.dma("sync", qfh[s][h][:],
                      scr["qfm"][64 * h:64 * (h + 1), t * 128:(t + 1) * 128],
                      after=[pd], ch=chn)
                t_ld = p.dma("sync", kfh[s][h][:],
                             scr["kfm"][64 * h:64 * (h + 1), t * 128:(t + 1) * 128],
                             after=[pd], ch=chn)
            t_lsz = t_k = t_v = t_g = t_ld
            t_q = [t_ld] * H
            t_kf = [t_ld] * H
            for h in range(H):
                hs = slice(64 * h, 64 * (h + 1))
                vs = slice(128 * h, 128 * (h + 1))
                tb1 = p.mm(pbtm[:], U2b, lszt[s][:, hs], True, True,
                           after=[t_lsz, w["pbtm"]])
                tb2 = p.mm(pbfm[:], lszt[s][:, hs], U2b, True, True,
                           after=[t_lsz, w["pbfm"]])
                tent = p.act(ent[:], pbtm[:], AF.Exp, scale=-1.0,
                             after=[tb1, w["ent"]])
                w["pbtm"] = tent
                tebf = p.act(ebf[:], pbfm[:], AF.Exp, bias=cc[0:64, 3:4],
                             after=[tb2, w["ebf"]])
                tenf = p.act(enf[:], pbfm[:], AF.Exp, scale=-1.0,
                             after=[tb2, w["enf"]])
                te0 = p.act(e0c[:], pbfm[:, 63:64], AF.Exp, after=[tb2, w["e0"]])
                te1 = p.act(e1c[:], pbfm[:, 127:128], AF.Exp, after=[tb2, w["e1"]])
                w["pbfm"] = te1
                tQ0 = p.tt(QE0[0:64, 0:64], qfh[s][h][:, 0:64], ebf[:, 0:64],
                           OP.mult, after=[t_q[h], tebf, w["QE0"]])
                tQ1 = p.tt(QE1[0:64, 64:128], qfh[s][h][:, 64:128], ebf[:, 64:128],
                           OP.mult, after=[t_q[h], tebf, w["QE1"]])
                tK0 = p.tt(KE0[0:64, 0:64], kfh[s][h][:, 0:64], enf[:, 0:64],
                           OP.mult, after=[t_kf[h], tenf, w["KE0"]])
                tK1 = p.tt(KE1[0:64, 64:128], kfh[s][h][:, 64:128], enf[:, 64:128],
                           OP.mult, after=[t_kf[h], tenf, w["KE1"]])
                w["ebf"] = tQ1
                w["enf"] = tK1
                tT0 = p.tt(KT0[0:64, :], ktt[s][0:64, hs], ent[0:64, :], OP.mult,
                           after=[t_k, tent, w["KT0"]])
                tT1 = p.tt(KT1[64:128, :], ktt[s][64:128, hs], ent[64:128, :],
                           OP.mult, after=[t_k, tent, w["KT1"]])
                w["ent"] = tT1
                p.mm(pAT[:], KE0[:], QE0[:], True, False,
                     after=[tK0, tQ0, w["pAT"]], inc=False)
                tat = p.mm(pAT[:], KE1[:], QE1[:], False, True, after=[tK1, tQ1])
                ta2 = p.tt(A2m[:], pAT[:], M2f, OP.mult, after=[tat, w["A2m"]])
                w["pAT"] = ta2
                tkv0 = p.mm(pkv0[:], KT0[:], vtt[s][:, vs], True, True,
                            after=[tT0, t_v, w["pkv0"]])
                tSB = p.tt(S_B[:, vs], S_A[:, vs], pkv0[:], OP.add,
                           after=[tkv0, w["SB"], w["SA"]])
                w["pkv0"] = tSB
                tSB2 = p.ts(S_B[:, vs], S_B[:, vs], e0c[:], None, OP.mult,
                            after=[te0, tSB])
                tSBb = p.vcopy(S_Bb[0:64, vs], S_B[:, vs], after=[tSB2, w["SBb"]])
                tkv1 = p.mm(pkv1[:], KT1[:], vtt[s][:, vs], True, True,
                            after=[tT1, t_v, tSB])
                p.mm(po[:], A2m[:], vtt[s][:, vs], True, False,
                     after=[ta2, t_v, w["po"]], inc=False)
                tom2 = p.mm(po[:], QE0[:], S_Ab[:, vs], False, False,
                            after=[tQ0, w["SAb"]])
                tom = p.mm(po[:], QE1[:], S_Bb[:, vs], False, True,
                           after=[tQ1, tSBb])
                tSA = p.tt(S_A[:, vs], S_B[:, vs], pkv1[:], OP.add,
                           after=[tkv1, tSB2, tom])
                w["pkv0"] = tSA
                w["SB"] = tSA
                tSA2 = p.ts(S_A[:, vs], S_A[:, vs], e1c[:], None, OP.mult,
                            after=[te1, tSA])
                w["SA"] = tSA2
                tSAb = p.vcopy(S_Ab[0:64, vs], S_A[:, vs], after=[tSA2, tom2])
                w["SAb"] = tSAb
                w["SBb"] = tom
                w["e0"] = tSB2
                w["e1"] = tSA2
                w["QE0"] = tom2
                w["QE1"] = tom
                w["KE0"] = tat
                w["KE1"] = tat
                w["KT0"] = tkv0
                w["KT1"] = tkv1
                tev = p.scopy(ot[:, vs], po[:], after=[tom, w["ot"]])
                w["po"] = tev
                if h == H - 1:
                    w["ot"] = None
                oevac_last = tev
            tsq = p.act(sqt[:], ot[:], AF.Square, after=[oevac_last, w["sqt"]])
            trs = p.red(rs8[:], sqt[:].rearrange("p (h c) -> p h c", c=128),
                        OP.add, after=[tsq, w["rs8"]])
            w["sqt"] = trs
            tms = p.ts(rs8[:], rs8[:], 1.0 / 128.0, None, OP.mult, after=[trs])
            tln = p.act(rs8[:], rs8[:], AF.Ln, bias=cc[:, 1:2], after=[tms])
            trr = p.act(rs8[:], rs8[:], AF.Exp, scale=-0.5, after=[tln])
            w["rs8"] = trr
            tsg = p.act(sgt[:], gtt[s][:], AF.Silu, after=[t_g, w["sgt"]])
            ogtoks = []
            for h in range(H):
                vs = slice(128 * h, 128 * (h + 1))
                tog = p.stt(ogb[:, vs], ot[:, vs], rs8[:, h:h + 1], sgt[:, vs],
                            OP.mult, OP.mult, after=[trr, tsg, w["ogb"]])
                ogtoks.append(tog)
            w["ogb"] = ogtoks[-1]
            w["ot"] = ogtoks[-1]
            w["sgt"] = ogtoks[-1]
            tcpl = None
            for c in range(8):
                ss = c % 2
                ttr = p.tr(ptb[ss][:], ogb[:, c * 128:(c + 1) * 128], idb,
                           after=[ogtoks[c], w.get("ptb%d" % ss)])
                tcp = p.vcopy(ogf[:, c * 128:(c + 1) * 128], ptb[ss][:],
                              after=[ttr, w["ogf"]])
                w["ptb%d" % ss] = tcp
                tcpl = tcp
            for n in range(2):
                tk = None
                for c in range(8):
                    tk = p.mm(ps0[:],
                              ogf[:, 128 * c:128 * (c + 1)],
                              wob[:, D * c + 512 * n: D * c + 512 * (n + 1)],
                              start=(c == 0), stop=(c == 7),
                              after=[tcpl, w["ps0"]] if c == 0 else (),
                              inc=(c == 7))
                tev = p.scopy(s0t[s][:, 512 * n:512 * (n + 1)], ps0[:],
                              after=[tk, w["s0t"][s]])
                w["ps0"] = tev
                w["ogf"] = tk
            td = p.dma("gpsimd", scr["s0"][t * 128:(t + 1) * 128, :], s0t[s][:],
                       after=[tev], ch="s0o%d" % s)
            w["s0t"][s] = td
            prevdma[s] = ogtoks[-1]
        p.barrier()


def emit_l0_moe(p, io, scr, cst, xnfm):
    cc = cst["cc"]
    with contextlib.ExitStack() as es:
        masks = p.sb([128, NT * 8], F32, es=es)
        es_gate = contextlib.ExitStack()
        gwTb = p.sb([128, 64], BF16, es=es_gate)
        stg = p.sb([128, 8], F32, es=es_gate)
        tgw = None
        for c in range(8):
            tl = p.dma("sync", stg[:], io["gwT"][c * 128:(c + 1) * 128, :], after=[tgw], ch="gwt")
            tgw = p.vcopy(gwTb[:, 8 * c:8 * (c + 1)], stg[:], after=[tl])
        pl = p.ps([128, 8], F32, es=es_gate)
        lmt = p.sb([128, 8], F32, es=es_gate)
        ismx = p.sb([128, 8], F32, es=es_gate)
        l2t = p.sb([128, 8], F32, es=es_gate)
        mxc = p.sb([128, 2], F32, es=es_gate)
        wpl = None
        for t in range(NT):
            tk = None
            for c in range(8):
                tk = p.mm(pl[:], xnfm[:, c * T + 128 * t: c * T + 128 * (t + 1)],
                          gwTb[:, 8 * c:8 * (c + 1)], start=(c == 0), stop=(c == 7),
                          after=[tgw, wpl] if c == 0 else (), inc=(c == 7))
            t1 = p.red(mxc[:, 0:1], pl[:], OP.max, after=[tk])
            t2 = p.ts(lmt[:], pl[:], mxc[:, 0:1], None, OP.subtract, after=[t1])
            t3 = p.ts(ismx[:], lmt[:], 0.0, None, OP.is_ge, after=[t2])
            t4 = p.stt(l2t[:], ismx[:], -1e30, pl[:], OP.mult, OP.add, after=[t3])
            t5 = p.red(mxc[:, 1:2], l2t[:], OP.max, after=[t4])
            t6 = p.ts(masks[:, 8 * t:8 * (t + 1)], pl[:], mxc[:, 1:2], None,
                      OP.is_ge, after=[t5])
            wpl = t6
        p.barrier()
        es_gate.close()

        yacc = p.sb([128, NT * 1024], BF16, es=es)
        p.memset(yacc[:, :8192], 0.0)
        p.memset(yacc[:, 8192:], 0.0)
        p.barrier()

        es2 = contextlib.ExitStack()
        Mb2 = [p.sb([128, 8 * 1024], BF16, es=es2) for _ in range(2)]
        py = [p.ps([128, 1024], F32, es=es2) for _ in range(2)]
        wpy = [None, None]
        mld = [None, None]
        mcons = [None, None]

        def load_m(ei):
            bm = ei % 2
            tl = None
            for c in range(8):
                tl = p.dma("sync", Mb2[bm][:, 1024 * c:1024 * (c + 1)],
                           scr["mall"].rearrange("(e k p) c -> e k p c", e=8, p=128)[ei, c],
                           after=[mcons[bm], p.mall_tok], ch="mld%d" % bm)
            mld[bm] = tl

        load_m(0)
        for e in range(8):
            if e < 7:
                load_m(e + 1)
            Mb = Mb2[e % 2]
            for t in range(NT):
                s = t % 2
                tk = None
                for c in range(8):
                    for n in range(2):
                        tk = p.mm(py[s][:, 512 * n:512 * (n + 1)],
                                  xnfm[:, c * T + 128 * t: c * T + 128 * (t + 1)],
                                  Mb[:, 1024 * c + 512 * n: 1024 * c + 512 * (n + 1)],
                                  start=(c == 0), stop=(c == 7),
                                  after=[mld[e % 2], wpy[s]] if (c == 0 and n == 0) else (),
                                  inc=(c == 7 and n == 1))
                tac = p.stt(yacc[:, 1024 * t:1024 * (t + 1)], py[s][:],
                            masks[:, 8 * t + e: 8 * t + e + 1],
                            yacc[:, 1024 * t:1024 * (t + 1)], OP.mult, OP.add,
                            after=[tk])
                wpy[s] = tac
            mcons[e % 2] = tk
        p.barrier()
        es2.close()
        xt = [p.sb([128, 1024], F32, es=es) for _ in range(2)]
        s0l = [p.sb([128, 1024], F32, es=es) for _ in range(2)]
        syt = [p.sb([128, 1024], F32, es=es) for _ in range(2)]
        x1t = [p.sb([128, 1024], F32, es=es) for _ in range(2)]
        wx = [None, None]
        for t in range(NT):
            s = t % 2
            p.dma("sync", xt[s][:], io["x"][t * 128:(t + 1) * 128, :],
                  after=[wx[s]], ch="cmb%d" % s)
            tls = p.dma("sync", s0l[s][:], scr["s0"][t * 128:(t + 1) * 128, :],
                        after=[wx[s]], ch="cmb%d" % s)
            tlx = tls
            tsy = p.act(syt[s][:], yacc[:, 1024 * t:1024 * (t + 1)], AF.Silu,
                        after=[wx[s]])
            tm1 = p.tt(syt[s][:], syt[s][:], s0l[s][:], OP.mult, after=[tsy, tls])
            tx1 = p.tt(x1t[s][:], syt[s][:], xt[s][:], OP.add, after=[tm1, tlx])
            td = p.dma("gpsimd", scr["x1"][t * 128:(t + 1) * 128, :], x1t[s][:],
                       after=[tx1], ch="x1o%d" % s)
            wx[s] = td
        p.barrier()



def emit_consts(p, io):
    cst = {}
    cst["cc"] = p.sb([128, 6], F32, name="cc_t")
    cst["U2b"] = p.sb([128, 128], BF16, name="u2_t")
    cst["M2f"] = p.sb([128, 128], F32, name="m2_t")
    cst["idb"] = p.sb([128, 128], BF16, name="id_t")
    cst["idf"] = p.sb([128, 128], F32, name="idf_t")
    cst["onesr"] = p.sb([1, 128], BF16, name="or_t")
    cst["onesc"] = p.sb([128, 1], BF16, name="oc_t")
    cst["onesr512"] = p.sb([1, 512], BF16, name="o512_t")
    cst["E4"] = p.sb([128, 4], F32, name="e4_t")
    cst["E4T"] = p.sb([4, 128], F32, name="e4t_t")
    toks = []
    for k in ("cc", "U2b", "M2f", "idb", "idf", "onesr", "onesc", "onesr512", "E4", "E4T"):
        toks.append(p.dma("sync", cst[k][:], io[k], ch="cst"))
    p.barrier()
    return {k: (v[:] if k in ("U2b", "M2f", "idb", "idf", "onesr", "onesc",
                              "onesr512", "E4", "E4T") else v)
            for k, v in cst.items()}


def build_program(upto="l1", dbg=()):
    p = Prog()
    io = declare_io(p)
    scr = {}
    scr["qfm"] = p.dscr("qfm_s", [512, T], BF16)
    scr["kfm"] = p.dscr("kfm_s", [512, T], BF16)
    scr["ktm"] = p.dscr("ktm_s", [T, 512], BF16)
    scr["vtm"] = p.dscr("vtm_s", [T, 1024], BF16)
    scr["gtm"] = p.dscr("gtm_s", [T, 1024], BF16)
    scr["lsz"] = p.dscr("lsz_s", [T, 512], BF16)
    scr["s0"] = p.dscr("s0_s", [T, D], F32)
    scr["mown"] = p.dscr("mown_s", [1024, 1024], BF16)
    scr["mall"] = p.dscr("mall_s", [8192, 1024], BF16)
    scr["xn1"] = p.dscr("xn1_s", [T, D], BF16)
    scr["qfm1"] = p.dscr("qfm1_s", [D, T], BF16)
    scr["kfm1"] = p.dscr("kfm1_s", [D, T], BF16)
    scr["s1"] = p.dscr("s1_s", [T, D], F32)
    if upto == "l0":
        scr["x1"] = io["out"]
    else:
        scr["x1"] = p.dscr("x1_s", [T, D], F32)
    dbg_aps = {}
    for name in dbg:
        shp = {"qfm": [512, T], "kfm": [512, T], "ktm": [T, 512],
               "vtm": [T, 1024], "gtm": [T, 1024], "lsz": [T, 512],
               "s0": [T, D], "x1": [T, D], "xn1": [T, D], "s1": [T, D]}[name]
        dt = F32 if name in ("s0", "x1", "s1") else BF16
        dbg_aps[name] = p.dout("dbg_" + name, shp, dt)

    cst = emit_consts(p, io)
    xnfm = p.sb([128, 8 * T], BF16, name="xnfm0")
    p.wob = p.sb([128, 8 * D], BF16, name="wob", es=p.es_wob)
    emit_moe_mown(p, io, scr, cst)
    emit_ln(p, io["x"], xnfm, cst["cc"], cst["idb"], eps_col_idx=1)
    emit_l0_proj(p, io, scr, cst, xnfm)
    emit_l0_scan(p, io, scr, cst)
    p.es_wob.close()
    emit_l0_moe(p, io, scr, cst, xnfm)
    if upto != "l0":
        emit_l1(p, io, scr, cst, xnfm)
    if dbg_aps:
        with contextlib.ExitStack() as es:
            bt = p.sb([128, 2048], BF16, es=es)
            bt32 = p.sb([128, 1024], F32, es=es)
            dbg_last = None
            tprev = None
            for name, dst in dbg_aps.items():
                srcap = scr[name] if name != "x1" else scr["x1"]
                rows, cols = {"qfm": (512, T), "kfm": (512, T), "ktm": (T, 512),
                              "vtm": (T, 1024), "gtm": (T, 1024), "lsz": (T, 512),
                              "s0": (T, D), "x1": (T, D), "xn1": (T, D),
                              "s1": (T, D)}[name]
                tile = bt32 if name in ("s0", "x1", "s1") else bt
                for r in range(0, rows, 128):
                    t1 = p.dma("sync", tile[:, :cols], srcap[r:r + 128, :],
                               after=[tprev, dbg_last], ch="dbgi")
                    dbg_last = t1
                    tprev = p.dma("gpsimd", dst[r:r + 128, :], tile[:, :cols],
                                  after=[t1], ch="dbgo")
            p.barrier()
    return p


def kernel(**inputs):
    n = 8
    p = build_program(upto=KERNEL_UPTO, dbg=())
    nc = p.finalize()
    in_maps = [host_prep(inputs, b) for b in range(n)]
    res = run_bass_kernel_spmd(nc, in_maps, core_ids=list(range(n)))
    out = np.stack([res.results[b]["out"] for b in range(n)], axis=0)
    return out.astype(np.float32)


KERNEL_UPTO = "l1"


def emit_l1_attn(p, io, scr, cst, xnfm):
    cc = cst["cc"]
    idb = cst["idb"]
    onesr = cst["onesr"]
    onesr512 = cst["onesr512"]
    with contextlib.ExitStack() as es:
        ab8 = p.sb([128, 16], F32, es=es)
        browb = p.sb([1, 3072], BF16, es=es)
        brf = p.sb([1, 3072], F32, es=es)
        vtm = p.sb([128, NT * 1040], BF16, name="vtm1", es=es)
        otm = p.sb([128, NT * 1040], BF16, name="otm1", es=es)
        es_s = contextlib.ExitStack()
        colsum = p.sb([128, 8], F32, es=es_s)
        colsq = p.sb([128, 8], F32, es=es_s)
        sqt = [p.sb([128, T], F32, es=es_s) for _ in range(2)]
        g48 = p.sb([4, 32], F32, es=es_s)
        ps48 = p.ps([4, 8], F32, es=es_s)
        ps128 = p.ps([128, 8], F32, es=es_s)
        gamc = p.sb([128, 8], F32, es=es_s)
        betc = p.sb([128, 8], F32, es=es_s)
        tsq = [None, None]
        for c in range(8):
            s = c % 2
            t1 = p.act(sqt[s][:], xnfm[:, c * T:(c + 1) * T], AF.Square,
                       after=[tsq[s]])
            t2 = p.red(colsq[:, c:c + 1], sqt[s][:], OP.add, after=[t1])
            tsq[s] = t2
            p.red(colsum[:, c:c + 1], xnfm[:, c * T:(c + 1) * T], OP.add)
        p.barrier()
        t1 = p.mm(ps48[:], cst["E4"], colsum[:], True, True)
        t2 = p.scopy(g48[:, 0:8], ps48[:], after=[t1])
        t3 = p.mm(ps48[:], cst["E4"], colsq[:], True, True, after=[t2])
        t4 = p.scopy(g48[:, 8:16], ps48[:], after=[t3])
        inv = 1.0 / (T * 32.0)
        t5 = p.ts(g48[:, 16:24], g48[:, 0:8], inv, None, OP.mult, after=[t2, t4])
        t6 = p.ts(g48[:, 8:16], g48[:, 8:16], inv, None, OP.mult, after=[t4])
        t7 = p.tt(g48[:, 24:32], g48[:, 16:24], g48[:, 16:24], OP.mult, after=[t5])
        t8 = p.tt(g48[:, 24:32], g48[:, 8:16], g48[:, 24:32], OP.subtract,
                  after=[t6, t7])
        t9 = p.act(g48[:, 24:32], g48[:, 24:32], AF.Ln, bias=cc[0:4, 2:3], after=[t8])
        t10 = p.act(g48[:, 24:32], g48[:, 24:32], AF.Exp, scale=-0.5, after=[t9])
        t11 = p.mm(ps128[:], cst["E4T"], g48[:, 24:32], True, True, after=[t10])
        tg = p.dma("sync", gamc[:], io["gam_c"], ch="gnp")
        tb = p.dma("sync", betc[:], io["bet_c"], ch="gnp")
        t12 = p.stt(ab8[:, 0:8], ps128[:], 1.0, gamc[:], OP.mult, OP.mult,
                    after=[t11, tb])
        t13 = p.mm(ps128[:], cst["E4T"], g48[:, 16:24], True, True, after=[t12, t10])
        t14 = p.tt(ab8[:, 8:16], ps128[:], ab8[:, 0:8], OP.mult, after=[t13, t12])
        p.tt(ab8[:, 8:16], betc[:], ab8[:, 8:16], OP.subtract, after=[t14])
        p.barrier()
        es_s.close()

        es_w = contextlib.ExitStack()
        wqkvb = p.sb([128, 8 * 3072], BF16, es=es_w)
        stgq = [p.sb([128, 3072], F32, es=es_w) for _ in range(2)]
        psbr = p.ps([1, 3072], F32, es=es_w)
        stok = [None, None]
        tkbs = [None, None]
        tkb = None
        for c in range(8):
            s = c % 2
            tl = p.dma("sync", stgq[s][:], io["wqkv"][c * 128:(c + 1) * 128, :],
                       after=[stok[s], tkbs[s]], ch="wqkv%d" % s)
            for n in range(6):
                tkb = p.mm(psbr[:, 512 * n:512 * (n + 1)], ab8[:, 8 + c:9 + c],
                           stgq[s][:, 512 * n:512 * (n + 1)],
                           start=(c == 0), stop=(c == 7), after=[tl],
                           inc=(n == 5))
            tkbs[s] = tkb
            tv = p.ts(wqkvb[:, 3072 * c:3072 * (c + 1)], stgq[s][:],
                      ab8[:, c:c + 1], None, OP.mult, after=[tl], eng="gpsimd")
            stok[s] = tv
        tbq = p.dma("sync", brf[:], io["bqkvr"], ch="bqk")
        tba = p.tt(brf[:], brf[:], psbr[:], OP.add, after=[tbq, tkb])
        tbb = p.vcopy(browb[:], brf[:], after=[tba])
        p.barrier()

        vones = vtm[:].rearrange("p (t h c) -> p t h c", h=16, c=65)[:, :, :, 64:65]
        tvo = p.memset(vones, 1.0)
        es_v1 = contextlib.ExitStack()
        psv = [p.ps([128, 512], F32, es=es_v1) for _ in range(2)]
        wpv = [None, None]
        for t in range(NT):
            for n in range(2):
                s = n
                for c in range(8):
                    p.mm(psv[s][:],
                         xnfm[:, c * T + 128 * t: c * T + 128 * (t + 1)],
                         wqkvb[:, 3072 * c + 2048 + 512 * n: 3072 * c + 2048 + 512 * (n + 1)],
                         start=(c == 0), stop=False,
                         after=[wpv[s]] if c == 0 else (), inc=False)
                tk = p.mm(psv[s][:], onesr,
                          browb[0:1, 2048 + 512 * n: 2048 + 512 * (n + 1)],
                          start=False, stop=True, after=[tbb])
                for hh in range(8):
                    h = 8 * n + hh
                    wpv[s] = p.vcopy(
                        vtm[:, 1040 * t + 65 * h: 1040 * t + 65 * h + 64],
                        psv[s][:, 64 * hh:64 * (hh + 1)], after=[tk, tvo])
        stqk = [p.sb([64, 512], BF16, es=es_v1) for _ in range(2)]
        wstq = [None, None]
        wpq = [None, None]
        for h in range(16):
            for part, dst, scl in ((0, scr["qfm1"], 0.125), (1, scr["kfm1"], 1.0)):
                base = 1024 * part + 64 * h
                for g in range(4):
                    s = g % 2
                    for c in range(8):
                        p.mm(psv[s][0:64, :],
                             wqkvb[:, 3072 * c + base: 3072 * c + base + 64],
                             xnfm[:, c * T + 512 * g: c * T + 512 * (g + 1)],
                             start=(c == 0), stop=False,
                             after=[wpv[s]] if c == 0 else (), inc=False)
                    tk = p.mm(psv[s][0:64, :], browb[0:1, base:base + 64],
                              onesr512, start=False, stop=True, after=[tbb])
                    te = p.act(stqk[s][:], psv[s][0:64, :], AF.Copy, scale=scl,
                               after=[tk, wstq[s]])
                    wpv[s] = te
                    td = p.dma("gpsimd",
                               dst[64 * h:64 * (h + 1), 512 * g:512 * (g + 1)],
                               stqk[s][:], after=[te], ch="qko%d" % s)
                    wstq[s] = td
        p.barrier()
        es_v1.close()
        es_w.close()

        es_v = contextlib.ExitStack()
        QA = [p.sb([128, T], BF16, es=es_v) for _ in range(2)]
        KA = [p.sb([128, T], BF16, es=es_v) for _ in range(2)]
        PT = p.sb([128, 8192], BF16, es=es_v)
        sqt2 = p.sb([64, T], BF16, es=es_v)
        qn2 = p.sb([1, T], F32, es=es_v)
        kn2 = p.sb([1, T], F32, es=es_v)
        km = p.sb([1, 1], F32, es=es_v)
        brow = p.sb([1, T], F32, es=es_v)
        oht = p.sb([65, 512], BF16, es=es_v)
        psSt = [p.ps([128, 1024], F32, es=es_v) for _ in range(2)]
        psO = p.ps([65, 512], F32, es=es_v)
        ptrb = p.ps([128, 128], BF16, es=es_v)
        pnrm = p.ps([1, 512], F32, es=es_v)
        tonesa = p.memset(KA[0][64:65, :], 1.0)
        tonesb = p.memset(KA[1][64:65, :], 1.0)
        tldk = [None, None]
        tSTread = [None, None]
        wPT = [None, None]
        wO = None
        woht = None
        wotm = None
        wptrb = None
        wsq = None
        wnrm = None
        wrow = None

        def load_qk(hh):
            b_ = hh % 2
            p.dma("sync", QA[b_][0:64, :], scr["qfm1"][64 * hh:64 * (hh + 1), :],
                  after=[tSTread[b_]], ch="qkl%d" % b_)
            t2 = p.dma("sync", KA[b_][0:64, :], scr["kfm1"][64 * hh:64 * (hh + 1), :],
                       after=[tSTread[b_]], ch="qkl%d" % b_)
            tldk[b_] = t2

        def bound_row(h):
            nonlocal wsq, wnrm, wrow
            b_ = h % 2
            t1 = p.act(sqt2[:], KA[b_][0:64, :], AF.Square, after=[tldk[b_], wsq])
            tlast = None
            tklast = None
            for g in range(4):
                tk = p.mm(pnrm[:], cst["onesc"][0:64, :], sqt2[:, 512 * g:512 * (g + 1)],
                          True, True, after=[t1, wnrm])
                tklast = tk
                tlast = p.vcopy(kn2[0:1, 512 * g:512 * (g + 1)], pnrm[:], after=[tk, wrow])
                wnrm = tlast
            tkm = p.red(km[:], kn2[:], OP.max, after=[tlast])
            t2 = p.act(sqt2[:], QA[b_][0:64, :], AF.Square, after=[tldk[b_], tklast])
            wsq = t2
            for g in range(4):
                tk = p.mm(pnrm[:], cst["onesc"][0:64, :], sqt2[:, 512 * g:512 * (g + 1)],
                          True, True, after=[t2, wnrm])
                tlast = p.vcopy(qn2[0:1, 512 * g:512 * (g + 1)], pnrm[:], after=[tk, wrow])
                wnrm = tlast
            t3 = p.ts(brow[:], qn2[:], km[0:1, 0:1], None, OP.mult, after=[tkm, tlast, wrow])
            t4 = p.act(brow[:], brow[:], AF.Ln, bias=cc[0:1, 4:5], after=[t3])
            t5 = p.act(brow[:], brow[:], AF.Exp, scale=0.5, after=[t4])
            t6 = p.ts(brow[:], brow[:], -1.0, 60.0, OP.mult, OP.add, after=[t5])
            wrow = t6
            tq = p.vcopy(QA[b_][64:65, :], brow[:], after=[t6, tSTread[b_]])
            return tq

        def st_sub(h, qg, tq64):
            nonlocal wO, woht, wotm, wptrb
            b_ = h % 2
            for pr in range(8):
                half = pr % 2
                toks = []
                for j in range(2):
                    kc = 2 * pr + j
                    toks.append(p.mm(
                        psSt[half][:, 512 * j:512 * (j + 1)],
                        KA[b_][0:65, 128 * kc:128 * (kc + 1)],
                        QA[b_][0:65, 512 * qg:512 * (qg + 1)],
                        start=True, stop=True,
                        after=[tq64, tonesa, tonesb, wPT[half]]))
                tex = p.act(PT[:, 1024 * pr:1024 * (pr + 1)], psSt[half][:],
                            AF.Exp, after=[toks[0], toks[1]])
                wPT[half] = tex
            tk = None
            for kc in range(16):
                tk = p.mm(psO[:],
                          vtm[:, 1040 * kc + 65 * h: 1040 * kc + 65 * (h + 1)],
                          PT[:, 512 * kc:512 * (kc + 1)],
                          start=(kc == 0), stop=(kc == 15),
                          after=[wPT[0], wPT[1], wO] if kc == 0 else (),
                          inc=(kc == 15))
            tSTread[b_] = tk
            teo = p.vcopy(oht[:], psO[:], after=[tk, woht])
            wO = teo
            for sub in range(4):
                tt_ = 4 * qg + sub
                ttr = p.tr(ptrb[:, 0:65], oht[:, 128 * sub:128 * (sub + 1)],
                           idb[0:65, 0:65], after=[teo, wptrb])
                tcp = p.vcopy(
                    otm[:, 1040 * tt_ + 65 * h: 1040 * tt_ + 65 * (h + 1)],
                    ptrb[:, 0:65], after=[ttr, wotm])
                wptrb = tcp
                woht = tcp

        load_qk(0)
        pend = None
        for h in range(16):
            tq = bound_row(h)
            for qg in range(4):
                if pend is not None:
                    st_sub(*pend)
                pend = (h, qg, tq)
                if qg == 0 and h + 1 < 16:
                    load_qk(h + 1)
        st_sub(*pend)
        p.barrier()
        es_v.close()

        es_o = contextlib.ExitStack()
        wo1b, two0, two1 = load_weight_bf16(p, io["wo1"], D, D, es_o)
        bo1f = p.sb([1, 1024], F32, es=es_o)
        bo1r = p.sb([1, 1024], BF16, es=es_o)
        tl = p.dma("sync", bo1f[:], io["bo1r"], ch="bo1")
        tbo = p.vcopy(bo1r[:], bo1f[:], after=[tl])
        bo1bc = p.sb([128, 1024], F32, es=es_o)
        psW = p.ps([128, 512], F32, es=es_o)
        tprev = None
        for n in range(2):
            tk = p.mm(psW[:], onesr, bo1r[0:1, 512 * n:512 * (n + 1)], True, True,
                      after=[tbo, tprev])
            tprev = p.scopy(bo1bc[:, 512 * n:512 * (n + 1)], psW[:], after=[tk])
        rcp16 = p.sb([128, 16], F32, es=es_o)
        o2 = p.sb([128, 1024], BF16, es=es_o)
        ofmt = p.sb([128, 1024], BF16, es=es_o)
        ptrb2 = [p.ps([128, 128], BF16, es=es_o) for _ in range(2)]
        xn1l = [p.sb([128, 1024], BF16, es=es_o) for _ in range(2)]
        s1t = [p.sb([128, 1024], F32, es=es_o) for _ in range(2)]
        wrc = None
        wo2 = None
        wofmt = None
        wpt2 = [None, None]
        wps = tprev
        ws1 = [None, None]
        wx1l = [None, None]
        for t in range(NT):
            s = t % 2
            tlx = p.dma("sync", xn1l[s][:], scr["xn1"][128 * t:128 * (t + 1), :],
                        after=[wx1l[s]], ch="xn1l%d" % s)
            ov = otm[:, 1040 * t:1040 * (t + 1)].rearrange("p (h c) -> p h c", c=65)
            trc = p.recip(rcp16[:], ov[:, :, 64], after=[wrc])
            o2toks = []
            for hh in range(16):
                tn = p.ts(o2[:, 64 * hh:64 * (hh + 1)],
                          otm[:, 1040 * t + 65 * hh: 1040 * t + 65 * hh + 64],
                          rcp16[:, hh:hh + 1], None, OP.mult,
                          after=[trc, wo2])
                o2toks.append(tn)
            wrc = o2toks[-1]
            tcpl = None
            for c in range(8):
                ss = c % 2
                ttr = p.tr(ptrb2[ss][:], o2[:, 128 * c:128 * (c + 1)], idb,
                           after=[o2toks[-1], wpt2[ss]])
                tcp = p.vcopy(ofmt[:, 128 * c:128 * (c + 1)], ptrb2[ss][:],
                              after=[ttr, wofmt])
                wpt2[ss] = tcp
                tcpl = tcp
            wo2 = tcpl
            for n in range(2):
                tk = None
                for c in range(8):
                    tk = p.mm(psW[:], ofmt[:, 128 * c:128 * (c + 1)],
                              wo1b[:, 1024 * c + 512 * n: 1024 * c + 512 * (n + 1)],
                              start=(c == 0), stop=(c == 7),
                              after=[tcpl, wps, two0, two1] if c == 0 else (),
                              inc=(c == 7))
                wofmt = tk
                ta = p.tt(s1t[s][:, 512 * n:512 * (n + 1)], psW[:],
                          bo1bc[:, 512 * n:512 * (n + 1)], OP.add,
                          after=[tk, ws1[s]])
                wps = ta
                ws1[s] = ta
            tb2 = p.tt(s1t[s][:], s1t[s][:], xn1l[s][:], OP.add, after=[tlx, ws1[s]])
            td = p.dma("gpsimd", scr["s1"][128 * t:128 * (t + 1), :], s1t[s][:],
                       after=[tb2], ch="s1o%d" % s)
            ws1[s] = td
            wx1l[s] = tb2
        p.barrier()
        es_o.close()


def emit_l1_ffn(p, io, scr, cst, xnfm):
    cc = cst["cc"]
    with contextlib.ExitStack() as es:
        gates = p.sb([128, NT * 8], F32, es=es)
        y1 = p.sb([128, NT * 1024], BF16, es=es)
        tmz = [p.memset(y1[:, :8192], 0.0), p.memset(y1[:, 8192:], 0.0)]
        es_e = contextlib.ExitStack()
        winb = p.sb([128, 8 * 2048], BF16, es=es_e)
        woutb = p.sb([128, 16 * 1024], BF16, es=es_e)
        stgw = [p.sb([128, 2048], F32, es=es_e) for _ in range(2)]
        hfm = p.sb([128, 16 * 512], BF16, es=es_e)
        wstg = [None, None]

        def conv_copy(eng, out, in_, after):
            if eng == "scalar":
                return p.act(out, in_, AF.Copy, after=after)
            return p.vcopy(out, in_, after=after, eng=eng)

        def load_expert(e, win_cons, wout_cons, engs=("gpsimd", "gpsimd")):
            win_rdy = [None, None]
            wout_rdy = [None, None]
            for c in range(8):
                s = c % 2
                tl = p.dma("sync", stgw[s][:],
                           io["win"][e, 128 * c:128 * (c + 1), :],
                           after=[wstg[s]], ch="wie%d" % s)
                win_rdy[s] = conv_copy(engs[s], winb[:, 2048 * c:2048 * (c + 1)],
                                       stgw[s][:], [tl, win_cons])
                wstg[s] = win_rdy[s]
            for c in range(16):
                s = c % 2
                tl = p.dma("sync", stgw[s][:, 0:1024],
                           io["wout"][e, 128 * c:128 * (c + 1), :],
                           after=[wstg[s]], ch="wie%d" % s)
                wout_rdy[s] = conv_copy(engs[s], woutb[:, 1024 * c:1024 * (c + 1)],
                                        stgw[s][:, 0:1024], [tl, wout_cons])
                wstg[s] = wout_rdy[s]
            return win_rdy, wout_rdy

        es_g = contextlib.ExitStack()
        wg1b, twg0, twg1 = load_weight_bf16(p, io["wg1"], D, 256, es_g)
        wg2b = p.sb([128, 16], BF16, es=es_g)
        stg2 = p.sb([128, 8], F32, es=es_g)
        bg1c = p.sb([128, 2], F32, es=es_g)
        tbg = p.dma("sync", bg1c[:], io["bg1c"], ch="bg1")
        tw2 = None
        for c in range(2):
            tl = p.dma("sync", stg2[:], io["wg2"][c * 128:(c + 1) * 128, :],
                       after=[tw2], ch="wg2l")
            tw2 = p.vcopy(wg2b[:, 8 * c:8 * (c + 1)], stg2[:], after=[tl])
        g1fm = p.sb([128, 2 * T], BF16, es=es_g)
        psg = [p.ps([128, 512], F32, es=es_g) for _ in range(2)]
        wg = [None, None]
        for mc in range(2):
            for g in range(4):
                s = (mc * 4 + g) % 2
                tk = None
                for c in range(8):
                    tk = p.mm(psg[s][:],
                              wg1b[:, 256 * c + 128 * mc: 256 * c + 128 * (mc + 1)],
                              xnfm[:, c * T + 512 * g: c * T + 512 * (g + 1)],
                              start=(c == 0), stop=(c == 7),
                              after=[wg[s], twg0, twg1] if c == 0 else (),
                              inc=(c == 7))
                te = p.act(g1fm[:, T * mc + 512 * g: T * mc + 512 * (g + 1)],
                           psg[s][:], AF.Gelu, bias=bg1c[:, mc:mc + 1],
                           after=[tk, tbg])
                wg[s] = te
        psl = [p.ps([128, 8], F32, es=es_g) for _ in range(2)]
        prb = [p.sb([128, 8], F32, es=es_g) for _ in range(2)]
        aux = [p.sb([128, 12], F32, es=es_g) for _ in range(2)]
        junk8 = p.sb([128, 8], BF16, es=es_g)
        wpl = [None, None]
        gate_tok = [None] * NT
        for t in range(NT):
            s = t % 2
            tk = None
            for c in range(2):
                tk = p.mm(psl[s][:], g1fm[:, T * c + 128 * t: T * c + 128 * (t + 1)],
                          wg2b[:, 8 * c:8 * (c + 1)], start=(c == 0), stop=(c == 1),
                          after=[wg[0], wg[1], tw2, wpl[s]] if c == 0 else (),
                          inc=(c == 1))
            t1 = p.red(aux[s][:, 0:1], psl[s][:], OP.max, negate=True,
                       after=[tk, wpl[s]])
            t2 = p.act(junk8[:], psl[s][:], AF.Exp, bias=aux[s][:, 0:1],
                       accum_out=aux[s][:, 1:2], after=[t1])
            t3 = p.recip(aux[s][:, 2:3], aux[s][:, 1:2], after=[t2])
            t4 = p.act(prb[s][:], psl[s][:], AF.Exp, bias=aux[s][:, 0:1], after=[t2])
            t5 = p.ts(prb[s][:], prb[s][:], aux[s][:, 2:3], None, OP.mult,
                      after=[t3, t4])
            t6 = p.red(aux[s][:, 3:4], prb[s][:], OP.max, after=[t5])
            t7 = p.ts(aux[s][:, 4:12], prb[s][:], aux[s][:, 3:4], None, OP.subtract,
                      after=[t6])
            t8 = p.ts(aux[s][:, 4:12], aux[s][:, 4:12], 0.0, None, OP.is_ge,
                      after=[t7])
            t9 = p.stt(aux[s][:, 4:12], aux[s][:, 4:12], -1e30, prb[s][:],
                       OP.mult, OP.add, after=[t8])
            t10 = p.red(aux[s][:, 3:4], aux[s][:, 4:12], OP.max, after=[t9])
            t11 = p.ts(aux[s][:, 4:12], prb[s][:], aux[s][:, 3:4], None, OP.is_ge,
                       after=[t10])
            t12 = p.tt(gates[:, 8 * t:8 * (t + 1)], prb[s][:], aux[s][:, 4:12],
                       OP.mult, after=[t11])
            gate_tok[t] = t12
            wpl[s] = t12
        win_rdy, wout_rdy = load_expert(0, None, None, engs=("scalar", "gpsimd"))
        es_g.close()
        gates_done = [wg[0], wg[1], wpl[0], wpl[1]]
        psh = [p.ps([128, 512], F32, es=es_e) for _ in range(2)]
        pso = [p.ps([128, 1024], F32, es=es_e) for _ in range(2)]
        win_cons = None
        wout_cons = None
        wh = [None, None]
        wo_ = [None, None]
        first_mm = True
        for e in range(8):
            for tg in range(4):
                hlast = None
                for j in range(16):
                    s = j % 2
                    tk = None
                    for c in range(8):
                        tk = p.mm(psh[s][:],
                                  winb[:, 2048 * c + 128 * j: 2048 * c + 128 * (j + 1)],
                                  xnfm[:, c * T + 512 * tg: c * T + 512 * (tg + 1)],
                                  start=(c == 0), stop=(c == 7),
                                  after=([win_rdy[0], win_rdy[1], wh[s]]
                                         + (gates_done if first_mm else []))
                                  if c == 0 else (),
                                  inc=(c == 7))
                        first_mm = False
                    win_cons = tk
                    te = p.act(hfm[:, 512 * j:512 * (j + 1)], psh[s][:], AF.Gelu,
                               after=[tk])
                    wh[s] = te
                    hlast = te
                for sub in range(4):
                    t_ = 4 * tg + sub
                    s = sub % 2
                    tk = None
                    for j in range(16):
                        for n in range(2):
                            tk = p.mm(pso[s][:, 512 * n:512 * (n + 1)],
                                      hfm[:, 512 * j + 128 * sub: 512 * j + 128 * (sub + 1)],
                                      woutb[:, 1024 * j + 512 * n: 1024 * j + 512 * (n + 1)],
                                      start=(j == 0), stop=(j == 15),
                                      after=[hlast, wout_rdy[0], wout_rdy[1], wo_[s]]
                                      if (j == 0 and n == 0) else (),
                                      inc=(j == 15 and n == 1))
                    wout_cons = tk
                    tac = p.stt(y1[:, 1024 * t_:1024 * (t_ + 1)], pso[s][:],
                                gates[:, 8 * t_ + e: 8 * t_ + e + 1],
                                y1[:, 1024 * t_:1024 * (t_ + 1)], OP.mult, OP.add,
                                after=[tk, gate_tok[t_]] + (tmz if e == 0 else []))
                    wo_[s] = tac
            if e < 7:
                win_rdy, wout_rdy = load_expert(e + 1, win_cons, wout_cons)
        p.barrier()
        es_e.close()
        x1t = [p.sb([128, 1024], F32, es=es) for _ in range(2)]
        s1l = [p.sb([128, 1024], F32, es=es) for _ in range(2)]
        syt = [p.sb([128, 1024], F32, es=es) for _ in range(2)]
        ot = [p.sb([128, 1024], F32, es=es) for _ in range(2)]
        wx = [None, None]
        for t in range(NT):
            s = t % 2
            p.dma("sync", x1t[s][:], scr["x1"][t * 128:(t + 1) * 128, :],
                  after=[wx[s]], ch="fc%d" % s)
            tls = p.dma("sync", s1l[s][:], scr["s1"][t * 128:(t + 1) * 128, :],
                        after=[wx[s]], ch="fc%d" % s)
            tsy = p.act(syt[s][:], y1[:, 1024 * t:1024 * (t + 1)], AF.Silu,
                        after=[wx[s]])
            tm1 = p.tt(syt[s][:], syt[s][:], s1l[s][:], OP.mult, after=[tsy, tls])
            tx2 = p.tt(ot[s][:], syt[s][:], x1t[s][:], OP.add, after=[tm1, tls])
            td = p.dma("gpsimd", io["out"][t * 128:(t + 1) * 128, :], ot[s][:],
                       after=[tx2], ch="oo%d" % s)
            wx[s] = td
        p.barrier()


def emit_l1(p, io, scr, cst, xnfm):
    emit_ln(p, scr["x1"], xnfm, cst["cc"], cst["idb"], eps_col_idx=1,
            xn_tm_dram=scr["xn1"])
    emit_l1_attn(p, io, scr, cst, xnfm)
    emit_l1_ffn(p, io, scr, cst, xnfm)


# version 16
# speedup vs baseline: 1.0036x; 1.0036x over previous
import contextlib
import numpy as np
import ml_dtypes

import concourse.bass as bass
import concourse.mybir as mybir
from concourse.alu_op_type import AluOpType as OP
from concourse.bass_utils import run_bass_kernel_spmd

F32 = mybir.dt.float32
BF16 = mybir.dt.bfloat16
AF = mybir.ActivationFunctionType
AX = mybir.AxisListType
BF = ml_dtypes.bfloat16

ENG = ("sync", "tensor", "vector", "scalar", "gpsimd")
AUTO_SELF = {"vector", "scalar", "gpsimd"}

T = 2048
D = 1024
NT = T // 128
H = 8
DK = 64
DV = 128
LNR8 = -2.0794415416798357


class Prog:

    def __init__(self):
        self.nc = bass.Bass("TRN2", target_bir_lowering=False)
        self.es = contextlib.ExitStack()
        self.es_wob = self.es.enter_context(contextlib.ExitStack())
        self.sched = {e: [] for e in ENG}
        self.ctr = {e: 0 for e in ENG}
        self.chctr = {}
        self.waited = {c: {} for c in ENG}
        self._n = 0

    def _nm(self, pfx):
        self._n += 1
        return f"{pfx}{self._n}"

    def sb(self, shape, dt=F32, name=None, es=None):
        return (es or self.es).enter_context(
            self.nc.sbuf_tensor(name or self._nm("sb"), shape, dt))

    def ps(self, shape, dt=F32, name=None, es=None):
        return (es or self.es).enter_context(
            self.nc.psum_tensor(name or self._nm("ps"), shape, dt))

    def din(self, name, shape, dt=F32):
        return self.nc.dram_tensor(name, shape, dt, kind="ExternalInput").ap()

    def dout(self, name, shape, dt=F32):
        return self.nc.dram_tensor(name, shape, dt, kind="ExternalOutput").ap()

    def dscr(self, name, shape, dt=F32):
        return self.nc.dram_tensor(name, shape, dt, kind="Internal").ap()

    def _wait(self, eng, sem, v):
        if self.waited[eng].get(sem, 0) < v:
            self.sched[eng].append(("w", sem, v))
            self.waited[eng][sem] = v

    def op(self, eng, fn, after=(), inc=True, ch=None):
        sch = self.sched[eng]
        if eng in AUTO_SELF and self.ctr[eng] > self.waited[eng].get(eng, 0):
            self._wait(eng, eng, self.ctr[eng])
        for tok in after:
            if tok is None:
                continue
            self._wait(eng, tok[0], tok[1])
        if ch is not None:
            self.chctr[ch] = self.chctr.get(ch, 0) + 16
            sch.append(("op", fn, ("c", ch, self.chctr[ch])))
            return (ch, self.chctr[ch])
        if inc:
            self.ctr[eng] += 1
            sch.append(("op", fn, ("e", eng, self.ctr[eng])))
            return (eng, self.ctr[eng])
        sch.append(("op", fn, None))
        return None

    def tok(self, eng):
        return (eng, self.ctr[eng])

    def barrier(self):
        toks = [(e, self.ctr[e]) for e in ENG]
        toks += [(c, v) for c, v in self.chctr.items()]
        for e in ENG:
            for pe, v in toks:
                if v > 0:
                    self._wait(e, pe, v)

    def finalize(self):
        self.barrier()
        sems = {e: self.es.enter_context(self.nc.semaphore("S" + e)) for e in ENG}
        for c in self.chctr:
            sems[c] = self.es.enter_context(self.nc.semaphore("C" + c))
        block = self.es.enter_context(self.nc.Block())

        def emit2(eng_name):
            def body(engine):
                for item in self.sched[eng_name]:
                    if item[0] == "w":
                        engine.wait_ge(sems[item[1]], item[2])
                    else:
                        inst = item[1](engine)
                        if item[2] is not None:
                            kind, nm, val = item[2]
                            inst.then_inc(sems[nm], 16 if kind == "c" else 1)
            return body

        block.sync(emit2("sync"))
        block.tensor(emit2("tensor"))
        block.vector(emit2("vector"))
        block.scalar(emit2("scalar"))
        block.gpsimd(emit2("gpsimd"))
        self.es.close()
        return self.nc

    def dma(self, eng, out, in_, after=(), ch="c0"):
        return self.op(eng, lambda e, o=out, i=in_: e.dma_start(out=o, in_=i),
                       after=after, ch=ch)

    def mm(self, out, lhsT, rhs, start, stop, after=(), inc=True):
        return self.op("tensor",
                       lambda e, o=out, l=lhsT, r=rhs, s1=start, s2=stop:
                       e.matmul(o, l, r, start=s1, stop=s2),
                       after=after, inc=inc)

    def tr(self, out, in_, ident, after=(), inc=True):
        return self.op("tensor",
                       lambda e, o=out, i=in_, d=ident: e.transpose(o, i, d),
                       after=after, inc=inc)

    def act(self, out, in_, func, bias=None, scale=1.0, accum_out=None, after=()):
        def fn(e, o=out, i=in_, f=func, b=bias, s=scale, a=accum_out):
            kw = dict(scale=s)
            if b is not None:
                kw["bias"] = b
            if a is not None:
                kw["accum_out"] = a
            return e.activation(o, i, f, **kw)
        return self.op("scalar", fn, after=after)

    def scopy(self, out, in_, after=()):
        return self.op("scalar", lambda e, o=out, i=in_: e.copy(o, i), after=after)

    def tt(self, out, in0, in1, op, after=(), eng="vector"):
        return self.op(eng, lambda e, o=out, a=in0, b=in1, p=op:
                       e.tensor_tensor(o, a, b, p), after=after)

    def ts(self, out, in0, s1, s2, op0, op1=None, after=(), accum_out=None, eng="vector"):
        def fn(e, o=out, i=in0, a=s1, b=s2, p0=op0, p1=op1, ac=accum_out):
            kw = {}
            if p1 is not None:
                kw["op1"] = p1
            if ac is not None:
                kw["accum_out"] = ac
            return e.tensor_scalar(o, i, a, b, p0, **kw)
        return self.op(eng, fn, after=after)

    def stt(self, out, in0, scalar, in1, op0, op1, after=()):
        return self.op("vector", lambda e, o=out, i0=in0, s=scalar, i1=in1, p0=op0, p1=op1:
                       e.scalar_tensor_tensor(o, i0, s, i1, p0, p1), after=after)

    def vcopy(self, out, in_, after=(), eng="vector"):
        return self.op(eng, lambda e, o=out, i=in_: e.tensor_copy(o, i), after=after)

    def red(self, out, in_, op, negate=False, after=()):
        return self.op("vector", lambda e, o=out, i=in_, p=op, n=negate:
                       e.tensor_reduce(o, i, axis=AX.X, op=p, negate=n), after=after)

    def recip(self, out, in_, after=()):
        return self.op("vector", lambda e, o=out, i=in_: e.reciprocal(o, i), after=after)

    def memset(self, ap, val, after=(), eng="gpsimd"):
        return self.op(eng, lambda e, a=ap, v=val: e.memset(a, v), after=after)



def host_prep(inputs, core):
    f32 = np.float32
    bf = BF
    d = {}
    d["x"] = np.ascontiguousarray(inputs["L0_attn_x"][core], f32)
    d["wq"] = np.ascontiguousarray(inputs["L0_attn_Wq"], f32)
    d["wk"] = np.ascontiguousarray(inputs["L0_attn_Wk"], f32)
    d["wv"] = np.ascontiguousarray(inputs["L0_attn_Wv"], f32)
    d["wg"] = np.ascontiguousarray(inputs["L0_attn_Wg"], f32)
    d["wgk1"] = np.ascontiguousarray(inputs["L0_attn_Wgk1"], f32)
    d["wgk2"] = np.ascontiguousarray(inputs["L0_attn_Wgk2"], f32)
    d["bgk2r"] = np.ascontiguousarray(inputs["L0_attn_bgk2"].reshape(1, 512), f32)
    d["nwc"] = np.ascontiguousarray(inputs["L0_attn_norm_w"].reshape(128, 1), f32)
    d["wo"] = np.ascontiguousarray(inputs["L0_attn_Wo"], f32)
    d["gwT"] = np.ascontiguousarray(inputs["L0_ffn_gate_w"].T, f32)
    wupT = np.ascontiguousarray(inputs["L0_ffn_w_up"].transpose(0, 2, 1), f32)
    d["wupT_own"] = np.ascontiguousarray(wupT[core])
    d["wdn_own"] = np.ascontiguousarray(inputs["L0_ffn_w_down"][core], f32)
    d["gam_c"] = np.ascontiguousarray(inputs["L1_attn_gn_gamma"].reshape(8, 128).T, f32)
    d["bet_c"] = np.ascontiguousarray(inputs["L1_attn_gn_beta"].reshape(8, 128).T, f32)
    d["wqkv"] = np.ascontiguousarray(inputs["L1_attn_W_qkv"], f32)
    d["bqkvr"] = np.ascontiguousarray(inputs["L1_attn_b_qkv"].reshape(1, 3072), f32)
    d["wo1"] = np.ascontiguousarray(inputs["L1_attn_W_o"], f32)
    d["bo1r"] = np.ascontiguousarray(inputs["L1_attn_b_o"].reshape(1, 1024), f32)
    d["wg1"] = np.ascontiguousarray(inputs["L1_ffn_wg1"], f32)
    d["bg1c"] = np.ascontiguousarray(inputs["L1_ffn_bg1"].reshape(2, 128).T, f32)
    d["wg2"] = np.ascontiguousarray(inputs["L1_ffn_wg2"], f32)
    d["win"] = np.ascontiguousarray(inputs["L1_ffn_w_in"], f32)
    d["wout"] = np.ascontiguousarray(inputs["L1_ffn_w_out"], f32)
    tri = np.triu(np.ones((64, 64), f32))
    u2 = np.zeros((128, 128), f32)
    u2[:64, :64] = tri / 16.0
    u2[64:, 64:] = tri / 16.0
    d["U2b"] = u2.astype(bf)
    m2 = np.zeros((128, 128), f32)
    m2[:64, :64] = tri
    m2[64:, 64:] = tri
    d["M2f"] = m2
    d["idb"] = np.eye(128, dtype=f32).astype(bf)
    d["idf"] = np.eye(128, dtype=f32)
    d["onesr"] = np.ones((1, 128), f32).astype(bf)
    d["onesc"] = np.ones((128, 1), f32).astype(bf)
    cc = np.zeros((128, 6), f32)
    cc[:, 0] = 1.0
    cc[:, 1] = 1e-5
    cc[:, 2] = 1e-6
    cc[:, 3] = LNR8
    cc[:, 4] = 1e-30
    d["cc"] = cc
    d["onesr512"] = np.ones((1, 512), f32).astype(bf)
    e4 = np.zeros((128, 4), f32)
    for j in range(4):
        e4[32 * j:32 * (j + 1), j] = 1.0
    d["E4"] = e4
    d["E4T"] = np.ascontiguousarray(e4.T)
    return d


def declare_io(p):
    io = {}
    io["x"] = p.din("x", [T, D])
    io["wq"] = p.din("wq", [D, 512]); io["wk"] = p.din("wk", [D, 512])
    io["wv"] = p.din("wv", [D, D]); io["wg"] = p.din("wg", [D, D])
    io["wgk1"] = p.din("wgk1", [D, 16]); io["wgk2"] = p.din("wgk2", [16, 512])
    io["bgk2r"] = p.din("bgk2r", [1, 512]); io["nwc"] = p.din("nwc", [128, 1])
    io["wo"] = p.din("wo", [D, D]); io["gwT"] = p.din("gwT", [D, 8])
    io["wupT_own"] = p.din("wupT_own", [4096, D])
    io["wdn_own"] = p.din("wdn_own", [4096, D])
    io["gam_c"] = p.din("gam_c", [128, 8]); io["bet_c"] = p.din("bet_c", [128, 8])
    io["wqkv"] = p.din("wqkv", [D, 3072]); io["bqkvr"] = p.din("bqkvr", [1, 3072])
    io["wo1"] = p.din("wo1", [D, D]); io["bo1r"] = p.din("bo1r", [1, D])
    io["wg1"] = p.din("wg1", [D, 256]); io["bg1c"] = p.din("bg1c", [128, 2])
    io["wg2"] = p.din("wg2", [256, 8])
    io["win"] = p.din("win", [8, D, 2048]); io["wout"] = p.din("wout", [8, 2048, D])
    io["U2b"] = p.din("U2b", [128, 128], BF16)
    io["M2f"] = p.din("M2f", [128, 128])
    io["idb"] = p.din("idb", [128, 128], BF16)
    io["idf"] = p.din("idf", [128, 128])
    io["onesr"] = p.din("onesr", [1, 128], BF16)
    io["onesc"] = p.din("onesc", [128, 1], BF16)
    io["cc"] = p.din("cc", [128, 6])
    io["onesr512"] = p.din("onesr512", [1, 512], BF16)
    io["E4"] = p.din("E4", [128, 4])
    io["E4T"] = p.din("E4T", [4, 128])
    io["out"] = p.dout("out", [T, D])
    return io



def load_weight_bf16(p, src, rows, cols, es, scale_col=None, prev=None):
    nch = rows // 128
    chp = p._nm("lw")
    wbf = p.sb([128, nch * cols], BF16, es=es)
    stg = [p.sb([128, cols], F32, es=es), p.sb([128, cols], F32, es=es)]
    stok = [None, None]
    for c in range(nch):
        s = c % 2
        tl = p.dma("sync", stg[s][:], src[c * 128:(c + 1) * 128, :],
                   after=[stok[s], prev], ch=chp + str(s))
        if scale_col is None:
            tv = p.vcopy(wbf[:, c * cols:(c + 1) * cols], stg[s][:], after=[tl],
                         eng="gpsimd")
        else:
            tv = p.ts(wbf[:, c * cols:(c + 1) * cols], stg[s][:], scale_col, None,
                      OP.mult, after=[tl], eng="gpsimd")
        stok[s] = tv
    return wbf, stok[0], stok[1]


def load_row_bf16(p, src, cols, es):
    stg = p.sb([1, cols], F32, es=es)
    row = p.sb([1, cols], BF16, es=es)
    tl = p.dma("sync", stg[:], src, ch=p._nm("lr"))
    tv = p.vcopy(row[:], stg[:], after=[tl])
    return row, tv



def emit_ln(p, src_dram, xnfm, cc, idb, eps_col_idx=1, xn_tm_dram=None):
    with contextlib.ExitStack() as es:
        xt = [p.sb([128, D], F32, es=es) for _ in range(2)]
        junk = p.sb([128, D], BF16, es=es)
        xnb = [p.sb([128, D], BF16, es=es) for _ in range(2)]
        colp = [p.sb([128, 8], F32, es=es) for _ in range(2)]
        ptb = [p.ps([128, 128], BF16, es=es) for _ in range(2)]
        tx = [None, None]
        txn = [None, None]
        tpt = [None, None]
        tj = None
        for t in range(NT):
            s = t % 2
            ctl = colp[s]
            tl = p.dma("sync", xt[s][:], src_dram[t * 128:(t + 1) * 128, :],
                       after=[tx[s]], ch="lnx%d" % s)
            tssq = p.act(junk[:], xt[s][:], AF.Square, accum_out=ctl[:, 1:2],
                         after=[tl, tj, tx[s]])
            tj = tssq
            tsum = p.red(ctl[:, 0:1], xt[s][:], OP.add, after=[tl, tx[s]])
            tm = p.ts(ctl[:, 2:3], ctl[:, 0:1], 1.0 / D, None, OP.mult, after=[tsum])
            tq = p.ts(ctl[:, 3:4], ctl[:, 1:2], 1.0 / D, None, OP.mult, after=[tssq])
            tmm = p.tt(ctl[:, 4:5], ctl[:, 2:3], ctl[:, 2:3], OP.mult, after=[tm])
            tvar = p.tt(ctl[:, 4:5], ctl[:, 3:4], ctl[:, 4:5], OP.subtract,
                        after=[tq, tmm])
            tln = p.act(ctl[:, 5:6], ctl[:, 4:5], AF.Ln,
                        bias=cc[:, eps_col_idx:eps_col_idx + 1], after=[tvar])
            trr = p.act(ctl[:, 5:6], ctl[:, 5:6], AF.Exp, scale=-0.5, after=[tln])
            tmr = p.tt(ctl[:, 6:7], ctl[:, 2:3], ctl[:, 5:6], OP.mult, after=[trr])
            txnb = p.ts(xnb[s][:], xt[s][:], ctl[:, 5:6], ctl[:, 6:7],
                        OP.mult, OP.subtract, after=[tmr, txn[s]])
            tx[s] = txnb
            if xn_tm_dram is not None:
                txn_d = p.dma("gpsimd", xn_tm_dram[t * 128:(t + 1) * 128, :],
                              xnb[s][:], after=[txnb], ch="lnxo%d" % s)
            else:
                txn_d = None
            last = txnb
            for c in range(8):
                ss = c % 2
                ttr = p.tr(ptb[ss][:], xnb[s][:, c * 128:(c + 1) * 128], idb,
                           after=[txnb, tpt[ss]])
                tcp = p.vcopy(xnfm[:, c * T + t * 128: c * T + (t + 1) * 128],
                              ptb[ss][:], after=[ttr])
                tpt[ss] = tcp
                last = tcp
            txn[s] = last if txn_d is None else txn_d
        p.barrier()


def emit_moe_mown(p, io, scr, cst):
    with contextlib.ExitStack() as es:
        wdnb = p.sb([128, 32 * 1024], BF16, es=es)
        wupb = [p.sb([128, 4096], BF16, es=es) for _ in range(2)]
        stgw = [p.sb([128, 1024], F32, es=es) for _ in range(2)]
        stgu = [p.sb([128, 2048], F32, es=es) for _ in range(2)]
        mstg = p.sb([128, 1024], BF16, es=es)
        pM = [p.ps([128, 1024], F32, es=es) for _ in range(2)]
        wstg = [None, None]
        stu_tok = [None, None]
        wdn_ready = None
        for c in range(32):
            s = c % 2
            tl = p.dma("scalar", stgw[s][:],
                       io["wdn_own"][c * 128:(c + 1) * 128, :], after=[wstg[s]],
                       ch="wdn%d" % s)
            wdn_ready = p.vcopy(wdnb[:, 1024 * c:1024 * (c + 1)], stgw[s][:],
                                after=[tl], eng="gpsimd")
            wstg[s] = wdn_ready
        upcv = [None, None]
        upcons = [None, None]

        def load_up(mi):
            bm = mi % 2
            wsrc = io["wupT_own"].rearrange("(k p) c -> p k c", p=128)
            wsrc = wsrc[:, :, 128 * mi:128 * (mi + 1)]
            tcv = None
            for hh in range(2):
                sl = wsrc[:, 16 * hh:16 * (hh + 1), :]
                tl = p.dma("scalar", stgu[hh][:], sl,
                           after=[stu_tok[hh], upcons[bm]], ch="wup%d" % hh)
                tcv = p.vcopy(wupb[bm][:, 2048 * hh:2048 * (hh + 1)],
                              stgu[hh][:], after=[tl, upcons[bm]], eng="gpsimd")
                stu_tok[hh] = tcv
            upcv[bm] = tcv

        load_up(0)
        wpM = [None, None]
        wms = None
        for m in range(8):
            sm = m % 2
            if m < 7:
                load_up(m + 1)
            tk = None
            for k in range(32):
                for n in range(2):
                    tk = p.mm(pM[sm][:, 512 * n:512 * (n + 1)],
                              wupb[m % 2][:, 128 * k:128 * (k + 1)],
                              wdnb[:, 1024 * k + 512 * n: 1024 * k + 512 * (n + 1)],
                              start=(k == 0), stop=(k == 31),
                              after=[upcv[m % 2], wdn_ready, wpM[sm]] if k == 0 else (),
                              inc=(k == 31 and n == 1))
            upcons[m % 2] = tk
            tev = p.act(mstg[:], pM[sm][:], AF.Copy, after=[tk, wms])
            wpM[sm] = tev
            wms = p.dma("gpsimd", scr["mown"][m * 128:(m + 1) * 128, :], mstg[:],
                        after=[tev], ch="mow")
        p.barrier()
        tcc = p.op("gpsimd",
                   lambda e: e.collective_compute(
                       "AllGather", mybir.AluOpType.bypass,
                       replica_groups=[list(range(8))],
                       ins=[scr["mown"].opt()],
                       outs=[scr["mall"].opt()]),
                   after=[wms])
        p.mall_tok = tcc


def emit_l0_proj(p, io, scr, cst, xnfm, moe):
    cc, onesr = cst["cc"], cst["onesr"]
    with contextlib.ExitStack() as es:
        masks = moe["masks"]
        p.memset(moe["yacc"][:, :8192], 0.0)
        p.memset(moe["yacc"][:, 8192:], 0.0)
        gwTb = p.sb([128, 64], BF16, es=es)
        stg8 = p.sb([128, 8], F32, es=es)
        tgw = None
        for c in range(8):
            tl = p.dma("sync", stg8[:], io["gwT"][c * 128:(c + 1) * 128, :],
                       after=[tgw], ch="gwt")
            tgw = p.vcopy(gwTb[:, 8 * c:8 * (c + 1)], stg8[:], after=[tl])
        pl = p.ps([128, 8], F32, es=es)
        lmt = p.sb([128, 8], F32, es=es)
        ismx = p.sb([128, 8], F32, es=es)
        l2t = p.sb([128, 8], F32, es=es)
        mxc = p.sb([128, 2], F32, es=es)
        wpl = None
        for t in range(NT):
            tk = None
            for c in range(8):
                tk = p.mm(pl[:], xnfm[:, c * T + 128 * t: c * T + 128 * (t + 1)],
                          gwTb[:, 8 * c:8 * (c + 1)], start=(c == 0), stop=(c == 7),
                          after=[tgw, wpl] if c == 0 else (), inc=(c == 7))
            t1 = p.red(mxc[:, 0:1], pl[:], OP.max, after=[tk])
            t2 = p.ts(lmt[:], pl[:], mxc[:, 0:1], None, OP.subtract, after=[t1])
            t3 = p.ts(ismx[:], lmt[:], 0.0, None, OP.is_ge, after=[t2])
            t4 = p.stt(l2t[:], ismx[:], -1e30, pl[:], OP.mult, OP.add, after=[t3])
            t5 = p.red(mxc[:, 1:2], l2t[:], OP.max, after=[t4])
            t6 = p.ts(masks[:, 8 * t:8 * (t + 1)], pl[:], mxc[:, 1:2], None,
                      OP.is_ge, after=[t5])
            wpl = t6
        wqb, _, _ = load_weight_bf16(p, io["wq"], D, 512, es)
        wkb, _, _ = load_weight_bf16(p, io["wk"], D, 512, es)
        wvb, _, _ = load_weight_bf16(p, io["wv"], D, D, es)
        wgb, _, _ = load_weight_bf16(p, io["wg"], D, D, es)
        nwcs = p.sb([128, 1], F32, es=es)
        tnw = p.dma("sync", nwcs[:], io["nwc"], ch="nwc")
        wob = p.wob
        stgo = [p.sb([128, D], F32, es=es) for _ in range(2)]
        stok = [None, None]
        for c in range(8):
            s = c % 2
            tl = p.dma("sync", stgo[s][:], io["wo"][c * 128:(c + 1) * 128, :],
                       after=[stok[s]], ch="wo%d" % s)
            tv = p.ts(wob[:, c * D:(c + 1) * D], stgo[s][:], nwcs[:], None,
                      OP.mult, after=[tl, tnw], eng="gpsimd")
            stok[s] = tv
        wgk1b, _, _ = load_weight_bf16(p, io["wgk1"], D, 16, es)
        wgk2b = p.sb([16, 512], BF16, es=es)
        stg2 = p.sb([16, 512], F32, es=es)
        t2 = p.dma("sync", stg2[:], io["wgk2"], ch="wgk2")
        p.vcopy(wgk2b[:], stg2[:], after=[t2])
        bgk2b, tbg = load_row_bf16(p, io["bgk2r"], 512, es)
        p.barrier()

        psA = [p.ps([128, 512], F32, es=es) for _ in range(2)]
        psB = [p.ps([128, 1024], F32, es=es) for _ in range(2)]
        stgb = [p.sb([128, 1024], BF16, es=es) for _ in range(2)]
        ptoks = [None, None]
        etoksA = [None, None]
        etoksB = [None, None]
        for (wb, dst) in ((wqb, scr["qfm"]), (wkb, scr["kfm"])):
            for m in range(4):
                for g in range(4):
                    s = (m * 4 + g) % 2
                    tk = None
                    for c in range(8):
                        tk = p.mm(psA[s][:],
                                  wb[:, 512 * c + 128 * m: 512 * c + 128 * (m + 1)],
                                  xnfm[:, c * T + 512 * g: c * T + 512 * (g + 1)],
                                  start=(c == 0), stop=(c == 7),
                                  after=[etoksA[s]] if c == 0 else (), inc=(c == 7))
                    te = p.act(stgb[s][:, :512], psA[s][:], AF.Copy,
                               after=[tk, ptoks[s]])
                    td = p.dma("gpsimd",
                               dst[128 * m:128 * (m + 1), 512 * g:512 * (g + 1)],
                               stgb[s][:, :512], after=[te], ch="pev%d" % s)
                    ptoks[s] = td
                    etoksA[s] = te
        for (wb, ncols, dst) in ((wkb, 512, scr["ktm"]), (wvb, D, scr["vtm"]),
                                 (wgb, D, scr["gtm"])):
            ngr = ncols // 512
            for t in range(NT):
                s = t % 2
                ps = psB[s]
                tk = None
                for n in range(ngr):
                    for c in range(8):
                        tk = p.mm(ps[:, 512 * n:512 * (n + 1)],
                                  xnfm[:, c * T + 128 * t: c * T + 128 * (t + 1)],
                                  wb[:, ncols * c + 512 * n: ncols * c + 512 * (n + 1)],
                                  start=(c == 0), stop=(c == 7),
                                  after=[etoksB[s]] if (c == 0 and n == 0) else (),
                                  inc=(c == 7 and n == ngr - 1))
                te = p.act(stgb[s][:, :ncols], ps[:, :ncols], AF.Copy,
                           after=[tk, ptoks[s]])
                td = p.dma("gpsimd", dst[128 * t:128 * (t + 1), :],
                           stgb[s][:, :ncols], after=[te], ch="pev%d" % s)
                ptoks[s] = td
                etoksB[s] = te
        ufmb = p.sb([16, T], BF16, es=es)
        psU = p.ps([16, 512], F32, es=es)
        tprev = None
        for g in range(4):
            tk = None
            for c in range(8):
                tk = p.mm(psU[:], wgk1b[:, 16 * c:16 * (c + 1)],
                          xnfm[:, c * T + 512 * g: c * T + 512 * (g + 1)],
                          start=(c == 0), stop=(c == 7),
                          after=[tprev] if c == 0 else (), inc=(c == 7))
            tprev = p.act(ufmb[:, 512 * g:512 * (g + 1)], psU[:], AF.Copy, after=[tk])
        zcol = [p.sb([128, 512], F32, es=es) for _ in range(2)]
        zc2 = [p.sb([128, 512], F32, es=es) for _ in range(2)]
        lsb = [p.sb([128, 512], BF16, es=es) for _ in range(2)]
        dtoks = [None, None]
        wzc = [None, None]
        for t in range(NT):
            s = t % 2
            ps = psA[s]
            p.mm(ps[:], ufmb[:, 128 * t:128 * (t + 1)], wgk2b[:],
                 start=True, stop=False, after=[tprev, etoksA[s]], inc=False)
            tk = p.mm(ps[:], onesr, bgk2b[:], start=False, stop=True, after=[tbg])
            tabs = p.act(zcol[s][:], ps[:], AF.Abs, after=[tk, wzc[s]])
            texp = p.act(zcol[s][:], zcol[s][:], AF.Exp, scale=-1.0, after=[tabs])
            tl1p = p.act(zcol[s][:], zcol[s][:], AF.Ln, bias=cc[:, 0:1], after=[texp])
            tmin = p.ts(zc2[s][:], ps[:], 0.0, None, OP.min, after=[tk, tabs])
            etoksA[s] = tmin
            tls = p.tt(lsb[s][:], zc2[s][:], zcol[s][:], OP.subtract,
                       after=[tmin, tl1p, dtoks[s]])
            wzc[s] = tls
            td = p.dma("gpsimd", scr["lsz"][128 * t:(t + 1) * 128, :], lsb[s][:],
                       after=[tls], ch="lszo%d" % s)
            dtoks[s] = td
        p.barrier()


def emit_l0_scan(p, io, scr, cst, moe):
    cc, idb, U2b, M2f = cst["cc"], cst["idb"], cst["U2b"], cst["M2f"]
    wob = p.wob
    masks, yacc = moe["masks"], moe["yacc"]
    with contextlib.ExitStack() as es:
        lszt = [p.sb([128, 512], BF16, es=es) for _ in range(2)]
        ktt = [p.sb([128, 512], BF16, es=es) for _ in range(2)]
        vtt = [p.sb([128, 1024], BF16, es=es) for _ in range(2)]
        gtt = [p.sb([128, 1024], BF16, es=es) for _ in range(2)]
        qfh = [[p.sb([64, 128], BF16, es=es) for _ in range(H)] for _ in range(2)]
        kfh = [[p.sb([64, 128], BF16, es=es) for _ in range(H)] for _ in range(2)]
        QE0 = p.sb([128, 128], BF16, es=es)
        QE1 = p.sb([128, 128], BF16, es=es)
        KE0 = p.sb([128, 128], BF16, es=es)
        KE1 = p.sb([128, 128], BF16, es=es)
        KT0 = p.sb([128, 64], BF16, es=es)
        KT1 = p.sb([128, 64], BF16, es=es)
        ebf = p.sb([64, 128], F32, es=es)
        enf = p.sb([64, 128], F32, es=es)
        ent = p.sb([128, 64], F32, es=es)
        e0c = p.sb([64, 1], F32, es=es)
        e1c = p.sb([64, 1], F32, es=es)
        A2m = p.sb([128, 128], BF16, es=es)
        S_A = p.sb([64, 8 * 128], F32, es=es)
        S_B = p.sb([64, 8 * 128], F32, es=es)
        S_Ab = p.sb([128, 8 * 128], BF16, es=es)
        S_Bb = p.sb([128, 8 * 128], BF16, es=es)
        tz = []
        tz.append(p.memset(QE0[:], 0.0))
        tz.append(p.memset(QE1[:], 0.0))
        tz.append(p.memset(KE0[:], 0.0))
        tz.append(p.memset(KE1[:], 0.0))
        tz.append(p.memset(KT0[:], 0.0))
        tz.append(p.memset(KT1[:], 0.0))
        tz.append(p.memset(S_A[:], 0.0))
        tz.append(p.memset(S_B[:], 0.0))
        tz.append(p.memset(S_Ab[:], 0.0))
        tz.append(p.memset(S_Bb[:], 0.0))
        p.barrier()
        pbb = p.ps([128, 192], F32, es=es)
        pbtm = pbb[:, 0:64]
        pbfm = pbb[0:64, 64:192]
        pAP = p.ps([128, 256], F32, es=es)
        pAT = pAP[:, 0:128]
        po = pAP[:, 128:256]
        pkv0 = p.ps([64, 128], F32, es=es)
        pkv1 = pkv0
        py = [p.ps([128, 512], F32, es=es) for _ in range(2)]
        Mb2 = [p.sb([128, 8 * 1024], BF16, es=es) for _ in range(2)]
        mld = [None, None]
        mcons = [None, None]
        wpy = [None, None]

        def load_m(ei):
            bm = ei % 2
            tl = None
            for c in range(8):
                tl = p.dma("sync", Mb2[bm][:, 1024 * c:1024 * (c + 1)],
                           scr["mall"].rearrange("(e k p) c -> e k p c", e=8, p=128)[ei, c],
                           after=[mcons[bm], p.mall_tok], ch="mld%d" % bm)
            mld[bm] = tl

        def emit_y_unit(u):
            e, ty = u // 16, u % 16
            bm = e % 2
            tk = None
            for n in range(2):
                for c in range(8):
                    tk = p.mm(py[n][:],
                              xnfm0[:, c * T + 128 * ty: c * T + 128 * (ty + 1)],
                              Mb2[bm][:, 1024 * c + 512 * n: 1024 * c + 512 * (n + 1)],
                              start=(c == 0), stop=(c == 7),
                              after=[mld[bm], wpy[n]] if c == 0 else (),
                              inc=(c == 7))
                tac = p.stt(yacc[:, 1024 * ty + 512 * n: 1024 * ty + 512 * (n + 1)],
                            py[n][:],
                            masks[:, 8 * ty + e: 8 * ty + e + 1],
                            yacc[:, 1024 * ty + 512 * n: 1024 * ty + 512 * (n + 1)],
                            OP.mult, OP.add, after=[tk])
                wpy[n] = tac
            mcons[bm] = tk

        xnfm0 = moe["xnfm"]
        load_m(0)
        ot = p.sb([128, 1024], F32, es=es)
        sqt = p.sb([128, 1024], F32, es=es)
        sgt = p.sb([128, 1024], F32, es=es)
        rs8 = p.sb([128, 8], F32, es=es)
        ogb = p.sb([128, 1024], BF16, es=es)
        ogf = p.sb([128, 1024], BF16, es=es)
        ptb = [p.ps([128, 128], BF16, es=es) for _ in range(2)]
        ps0 = p.ps([128, 512], F32, es=es)
        s0t = [p.sb([128, 1024], F32, es=es) for _ in range(2)]

        tld = [dict(lsz=None, k=None, v=None, g=None, q=[None] * H, kf=None) for _ in range(2)]
        w = dict(pbtm=None, pbfm=None, pAT=None, pkv0=None, pkv1=None, po=None,
                 ebf=None, enf=None, ent=None, e0=None, e1=None, A2m=None,
                 QE0=None, QE1=None, KE0=None, KE1=None, KT0=None, KT1=None,
                 SA=None, SB=None, SAb=None, SBb=None, ot=None, ogb=None,
                 ogf=None, ps0=None, s0t=[None, None], sqt=None, sgt=None,
                 rs8=None)
        prevdma = [None, None]
        for t in range(NT):
            s = t % 2
            if t % 2 == 1 and t // 2 < 7:
                load_m(t // 2 + 1)
            pd = prevdma[s]
            chn = "scan%d" % s
            p.dma("sync", lszt[s][:], scr["lsz"][t * 128:(t + 1) * 128, :], after=[pd], ch=chn)
            p.dma("sync", ktt[s][:], scr["ktm"][t * 128:(t + 1) * 128, :], after=[pd], ch=chn)
            p.dma("sync", vtt[s][:], scr["vtm"][t * 128:(t + 1) * 128, :], after=[pd], ch=chn)
            p.dma("sync", gtt[s][:], scr["gtm"][t * 128:(t + 1) * 128, :], after=[pd], ch=chn)
            for h in range(H):
                p.dma("sync", qfh[s][h][:],
                      scr["qfm"][64 * h:64 * (h + 1), t * 128:(t + 1) * 128],
                      after=[pd], ch=chn)
                t_ld = p.dma("sync", kfh[s][h][:],
                             scr["kfm"][64 * h:64 * (h + 1), t * 128:(t + 1) * 128],
                             after=[pd], ch=chn)
            t_lsz = t_k = t_v = t_g = t_ld
            t_q = [t_ld] * H
            t_kf = [t_ld] * H
            for h in range(H):
                hs = slice(64 * h, 64 * (h + 1))
                vs = slice(128 * h, 128 * (h + 1))
                tb1 = p.mm(pbtm, U2b, lszt[s][:, hs], True, True,
                           after=[t_lsz, w["pbtm"], w["pbfm"]])
                tb2 = p.mm(pbfm, lszt[s][:, hs], U2b, True, True,
                           after=[t_lsz, w["pbfm"]])
                tent = p.act(ent[:], pbtm, AF.Exp, scale=-1.0,
                             after=[tb1, tb2, w["ent"]])
                w["pbtm"] = tent
                tebf = p.act(ebf[:], pbfm, AF.Exp, bias=cc[0:64, 3:4],
                             after=[tb2, w["ebf"]])
                tenf = p.act(enf[:], pbfm, AF.Exp, scale=-1.0,
                             after=[tb2, w["enf"]])
                te0 = p.act(e0c[:], pbfm[:, 63:64], AF.Exp, after=[tb2, w["e0"]])
                te1 = p.act(e1c[:], pbfm[:, 127:128], AF.Exp, after=[tb2, w["e1"]])
                w["pbfm"] = te1
                tQ0 = p.tt(QE0[0:64, 0:64], qfh[s][h][:, 0:64], ebf[:, 0:64],
                           OP.mult, after=[t_q[h], tebf, w["QE0"]])
                tQ1 = p.tt(QE1[0:64, 64:128], qfh[s][h][:, 64:128], ebf[:, 64:128],
                           OP.mult, after=[t_q[h], tebf, w["QE1"]])
                tK0 = p.tt(KE0[0:64, 0:64], kfh[s][h][:, 0:64], enf[:, 0:64],
                           OP.mult, after=[t_kf[h], tenf, w["KE0"]])
                tK1 = p.tt(KE1[0:64, 64:128], kfh[s][h][:, 64:128], enf[:, 64:128],
                           OP.mult, after=[t_kf[h], tenf, w["KE1"]])
                w["ebf"] = tQ1
                w["enf"] = tK1
                tT0 = p.tt(KT0[0:64, :], ktt[s][0:64, hs], ent[0:64, :], OP.mult,
                           after=[t_k, tent, w["KT0"]])
                tT1 = p.tt(KT1[64:128, :], ktt[s][64:128, hs], ent[64:128, :],
                           OP.mult, after=[t_k, tent, w["KT1"]])
                w["ent"] = tT1
                p.mm(pAT, KE0[:], QE0[:], True, False,
                     after=[tK0, tQ0, w["pAT"], w["po"]], inc=False)
                tat = p.mm(pAT, KE1[:], QE1[:], False, True, after=[tK1, tQ1])
                ta2 = p.tt(A2m[:], pAT, M2f, OP.mult, after=[tat, w["A2m"]])
                w["pAT"] = ta2
                tkv0 = p.mm(pkv0[:], KT0[:], vtt[s][:, vs], True, True,
                            after=[tT0, t_v, w["pkv0"]])
                tSB = p.tt(S_B[:, vs], S_A[:, vs], pkv0[:], OP.add,
                           after=[tkv0, w["SB"], w["SA"]])
                w["pkv0"] = tSB
                tSB2 = p.ts(S_B[:, vs], S_B[:, vs], e0c[:], None, OP.mult,
                            after=[te0, tSB])
                tSBb = p.vcopy(S_Bb[0:64, vs], S_B[:, vs], after=[tSB2, w["SBb"]])
                tkv1 = p.mm(pkv1[:], KT1[:], vtt[s][:, vs], True, True,
                            after=[tT1, t_v, tSB])
                p.mm(po, A2m[:], vtt[s][:, vs], True, False,
                     after=[ta2, t_v, w["po"]], inc=False)
                tom2 = p.mm(po, QE0[:], S_Ab[:, vs], False, False,
                            after=[tQ0, w["SAb"]])
                tom = p.mm(po, QE1[:], S_Bb[:, vs], False, True,
                           after=[tQ1, tSBb])
                tSA = p.tt(S_A[:, vs], S_B[:, vs], pkv1[:], OP.add,
                           after=[tkv1, tSB2, tom])
                w["pkv0"] = tSA
                w["SB"] = tSA
                tSA2 = p.ts(S_A[:, vs], S_A[:, vs], e1c[:], None, OP.mult,
                            after=[te1, tSA])
                w["SA"] = tSA2
                tSAb = p.vcopy(S_Ab[0:64, vs], S_A[:, vs], after=[tSA2, tom2])
                w["SAb"] = tSAb
                w["SBb"] = tom
                w["e0"] = tSB2
                w["e1"] = tSA2
                w["QE0"] = tom2
                w["QE1"] = tom
                w["KE0"] = tat
                w["KE1"] = tat
                w["KT0"] = tkv0
                w["KT1"] = tkv1
                tev = p.scopy(ot[:, vs], po, after=[tom, w["ot"]])
                w["po"] = tev
                if h == H - 1:
                    w["ot"] = None
                oevac_last = tev
                emit_y_unit(t * 8 + h)
            tsq = p.act(sqt[:], ot[:], AF.Square, after=[oevac_last, w["sqt"]])
            trs = p.red(rs8[:], sqt[:].rearrange("p (h c) -> p h c", c=128),
                        OP.add, after=[tsq, w["rs8"]])
            w["sqt"] = trs
            tms = p.ts(rs8[:], rs8[:], 1.0 / 128.0, None, OP.mult, after=[trs])
            tln = p.act(rs8[:], rs8[:], AF.Ln, bias=cc[:, 1:2], after=[tms])
            trr = p.act(rs8[:], rs8[:], AF.Exp, scale=-0.5, after=[tln])
            w["rs8"] = trr
            tsg = p.act(sgt[:], gtt[s][:], AF.Silu, after=[t_g, w["sgt"]])
            ogtoks = []
            for h in range(H):
                vs = slice(128 * h, 128 * (h + 1))
                tog = p.stt(ogb[:, vs], ot[:, vs], rs8[:, h:h + 1], sgt[:, vs],
                            OP.mult, OP.mult, after=[trr, tsg, w["ogb"]])
                ogtoks.append(tog)
            w["ogb"] = ogtoks[-1]
            w["ot"] = ogtoks[-1]
            w["sgt"] = ogtoks[-1]
            tcpl = None
            for c in range(8):
                ss = c % 2
                ttr = p.tr(ptb[ss][:], ogb[:, c * 128:(c + 1) * 128], idb,
                           after=[ogtoks[c], w.get("ptb%d" % ss)])
                tcp = p.vcopy(ogf[:, c * 128:(c + 1) * 128], ptb[ss][:],
                              after=[ttr, w["ogf"]])
                w["ptb%d" % ss] = tcp
                tcpl = tcp
            for n in range(2):
                tk = None
                for c in range(8):
                    tk = p.mm(ps0[:],
                              ogf[:, 128 * c:128 * (c + 1)],
                              wob[:, D * c + 512 * n: D * c + 512 * (n + 1)],
                              start=(c == 0), stop=(c == 7),
                              after=[tcpl, w["ps0"]] if c == 0 else (),
                              inc=(c == 7))
                tev = p.scopy(s0t[s][:, 512 * n:512 * (n + 1)], ps0[:],
                              after=[tk, w["s0t"][s]])
                w["ps0"] = tev
                w["ogf"] = tk
            td = p.dma("gpsimd", scr["s0"][t * 128:(t + 1) * 128, :], s0t[s][:],
                       after=[tev], ch="s0o%d" % s)
            w["s0t"][s] = td
            prevdma[s] = ogtoks[-1]
        p.barrier()


def emit_l0_combine(p, io, scr, cst, moe):
    yacc = moe["yacc"]
    with contextlib.ExitStack() as es:
        xt = [p.sb([128, 1024], F32, es=es) for _ in range(2)]
        s0l = [p.sb([128, 1024], F32, es=es) for _ in range(2)]
        syt = [p.sb([128, 1024], F32, es=es) for _ in range(2)]
        x1t = [p.sb([128, 1024], F32, es=es) for _ in range(2)]
        wx = [None, None]
        for t in range(NT):
            s = t % 2
            p.dma("sync", xt[s][:], io["x"][t * 128:(t + 1) * 128, :],
                  after=[wx[s]], ch="cmb%d" % s)
            tls = p.dma("sync", s0l[s][:], scr["s0"][t * 128:(t + 1) * 128, :],
                        after=[wx[s]], ch="cmb%d" % s)
            tlx = tls
            tsy = p.act(syt[s][:], yacc[:, 1024 * t:1024 * (t + 1)], AF.Silu,
                        after=[wx[s]])
            tm1 = p.tt(syt[s][:], syt[s][:], s0l[s][:], OP.mult, after=[tsy, tls])
            tx1 = p.tt(x1t[s][:], syt[s][:], xt[s][:], OP.add, after=[tm1, tlx])
            td = p.dma("gpsimd", scr["x1"][t * 128:(t + 1) * 128, :], x1t[s][:],
                       after=[tx1], ch="x1o%d" % s)
            wx[s] = td
        p.barrier()



def emit_consts(p, io):
    cst = {}
    cst["cc"] = p.sb([128, 6], F32, name="cc_t")
    cst["U2b"] = p.sb([128, 128], BF16, name="u2_t")
    cst["M2f"] = p.sb([128, 128], F32, name="m2_t")
    cst["idb"] = p.sb([128, 128], BF16, name="id_t")
    cst["idf"] = p.sb([128, 128], F32, name="idf_t")
    cst["onesr"] = p.sb([1, 128], BF16, name="or_t")
    cst["onesc"] = p.sb([128, 1], BF16, name="oc_t")
    cst["onesr512"] = p.sb([1, 512], BF16, name="o512_t")
    cst["E4"] = p.sb([128, 4], F32, name="e4_t")
    cst["E4T"] = p.sb([4, 128], F32, name="e4t_t")
    toks = []
    for k in ("cc", "U2b", "M2f", "idb", "idf", "onesr", "onesc", "onesr512", "E4", "E4T"):
        toks.append(p.dma("sync", cst[k][:], io[k], ch="cst"))
    p.barrier()
    return {k: (v[:] if k in ("U2b", "M2f", "idb", "idf", "onesr", "onesc",
                              "onesr512", "E4", "E4T") else v)
            for k, v in cst.items()}


def build_program(upto="l1", dbg=()):
    p = Prog()
    io = declare_io(p)
    scr = {}
    scr["qfm"] = p.dscr("qfm_s", [512, T], BF16)
    scr["kfm"] = p.dscr("kfm_s", [512, T], BF16)
    scr["ktm"] = p.dscr("ktm_s", [T, 512], BF16)
    scr["vtm"] = p.dscr("vtm_s", [T, 1024], BF16)
    scr["gtm"] = p.dscr("gtm_s", [T, 1024], BF16)
    scr["lsz"] = p.dscr("lsz_s", [T, 512], BF16)
    scr["s0"] = p.dscr("s0_s", [T, D], F32)
    scr["mown"] = p.dscr("mown_s", [1024, 1024], BF16)
    scr["mall"] = p.dscr("mall_s", [8192, 1024], BF16)
    scr["xn1"] = p.dscr("xn1_s", [T, D], BF16)
    scr["qfm1"] = p.dscr("qfm1_s", [D, T], BF16)
    scr["kfm1"] = p.dscr("kfm1_s", [D, T], BF16)
    scr["s1"] = p.dscr("s1_s", [T, D], F32)
    if upto == "l0":
        scr["x1"] = io["out"]
    else:
        scr["x1"] = p.dscr("x1_s", [T, D], F32)
    dbg_aps = {}
    for name in dbg:
        shp = {"qfm": [512, T], "kfm": [512, T], "ktm": [T, 512],
               "vtm": [T, 1024], "gtm": [T, 1024], "lsz": [T, 512],
               "s0": [T, D], "x1": [T, D], "xn1": [T, D], "s1": [T, D]}[name]
        dt = F32 if name in ("s0", "x1", "s1") else BF16
        dbg_aps[name] = p.dout("dbg_" + name, shp, dt)

    cst = emit_consts(p, io)
    xnfm = p.sb([128, 8 * T], BF16, name="xnfm0")
    es_moe = contextlib.ExitStack()
    moe = {
        "masks": p.sb([128, NT * 8], F32, es=es_moe),
        "yacc": p.sb([128, NT * 1024], BF16, es=es_moe),
        "xnfm": xnfm,
    }
    p.wob = p.sb([128, 8 * D], BF16, name="wob", es=p.es_wob)
    emit_moe_mown(p, io, scr, cst)
    emit_ln(p, io["x"], xnfm, cst["cc"], cst["idb"], eps_col_idx=1)
    emit_l0_proj(p, io, scr, cst, xnfm, moe)
    emit_l0_scan(p, io, scr, cst, moe)
    p.es_wob.close()
    emit_l0_combine(p, io, scr, cst, moe)
    es_moe.close()
    if upto != "l0":
        emit_l1(p, io, scr, cst, xnfm)
    if dbg_aps:
        with contextlib.ExitStack() as es:
            bt = p.sb([128, 2048], BF16, es=es)
            bt32 = p.sb([128, 1024], F32, es=es)
            dbg_last = None
            tprev = None
            for name, dst in dbg_aps.items():
                srcap = scr[name] if name != "x1" else scr["x1"]
                rows, cols = {"qfm": (512, T), "kfm": (512, T), "ktm": (T, 512),
                              "vtm": (T, 1024), "gtm": (T, 1024), "lsz": (T, 512),
                              "s0": (T, D), "x1": (T, D), "xn1": (T, D),
                              "s1": (T, D)}[name]
                tile = bt32 if name in ("s0", "x1", "s1") else bt
                for r in range(0, rows, 128):
                    t1 = p.dma("sync", tile[:, :cols], srcap[r:r + 128, :],
                               after=[tprev, dbg_last], ch="dbgi")
                    dbg_last = t1
                    tprev = p.dma("gpsimd", dst[r:r + 128, :], tile[:, :cols],
                                  after=[t1], ch="dbgo")
            p.barrier()
    return p


def kernel(**inputs):
    n = 8
    p = build_program(upto=KERNEL_UPTO, dbg=())
    nc = p.finalize()
    in_maps = [host_prep(inputs, b) for b in range(n)]
    res = run_bass_kernel_spmd(nc, in_maps, core_ids=list(range(n)))
    out = np.stack([res.results[b]["out"] for b in range(n)], axis=0)
    return out.astype(np.float32)


KERNEL_UPTO = "l1"


def emit_l1_attn(p, io, scr, cst, xnfm):
    cc = cst["cc"]
    idb = cst["idb"]
    onesr = cst["onesr"]
    onesr512 = cst["onesr512"]
    with contextlib.ExitStack() as es:
        ab8 = p.sb([128, 16], F32, es=es)
        browb = p.sb([1, 3072], BF16, es=es)
        brf = p.sb([1, 3072], F32, es=es)
        vtm = p.sb([128, NT * 1040], BF16, name="vtm1", es=es)
        otm = p.sb([128, NT * 1040], BF16, name="otm1", es=es)
        es_s = contextlib.ExitStack()
        colsum = p.sb([128, 8], F32, es=es_s)
        colsq = p.sb([128, 8], F32, es=es_s)
        sqt = [p.sb([128, T], F32, es=es_s) for _ in range(2)]
        g48 = p.sb([4, 32], F32, es=es_s)
        ps48 = p.ps([4, 8], F32, es=es_s)
        ps128 = p.ps([128, 8], F32, es=es_s)
        gamc = p.sb([128, 8], F32, es=es_s)
        betc = p.sb([128, 8], F32, es=es_s)
        tsq = [None, None]
        for c in range(8):
            s = c % 2
            t1 = p.act(sqt[s][:], xnfm[:, c * T:(c + 1) * T], AF.Square,
                       after=[tsq[s]])
            t2 = p.red(colsq[:, c:c + 1], sqt[s][:], OP.add, after=[t1])
            tsq[s] = t2
            p.red(colsum[:, c:c + 1], xnfm[:, c * T:(c + 1) * T], OP.add)
        p.barrier()
        t1 = p.mm(ps48[:], cst["E4"], colsum[:], True, True)
        t2 = p.scopy(g48[:, 0:8], ps48[:], after=[t1])
        t3 = p.mm(ps48[:], cst["E4"], colsq[:], True, True, after=[t2])
        t4 = p.scopy(g48[:, 8:16], ps48[:], after=[t3])
        inv = 1.0 / (T * 32.0)
        t5 = p.ts(g48[:, 16:24], g48[:, 0:8], inv, None, OP.mult, after=[t2, t4])
        t6 = p.ts(g48[:, 8:16], g48[:, 8:16], inv, None, OP.mult, after=[t4])
        t7 = p.tt(g48[:, 24:32], g48[:, 16:24], g48[:, 16:24], OP.mult, after=[t5])
        t8 = p.tt(g48[:, 24:32], g48[:, 8:16], g48[:, 24:32], OP.subtract,
                  after=[t6, t7])
        t9 = p.act(g48[:, 24:32], g48[:, 24:32], AF.Ln, bias=cc[0:4, 2:3], after=[t8])
        t10 = p.act(g48[:, 24:32], g48[:, 24:32], AF.Exp, scale=-0.5, after=[t9])
        t11 = p.mm(ps128[:], cst["E4T"], g48[:, 24:32], True, True, after=[t10])
        tg = p.dma("sync", gamc[:], io["gam_c"], ch="gnp")
        tb = p.dma("sync", betc[:], io["bet_c"], ch="gnp")
        t12 = p.stt(ab8[:, 0:8], ps128[:], 1.0, gamc[:], OP.mult, OP.mult,
                    after=[t11, tb])
        t13 = p.mm(ps128[:], cst["E4T"], g48[:, 16:24], True, True, after=[t12, t10])
        t14 = p.tt(ab8[:, 8:16], ps128[:], ab8[:, 0:8], OP.mult, after=[t13, t12])
        p.tt(ab8[:, 8:16], betc[:], ab8[:, 8:16], OP.subtract, after=[t14])
        p.barrier()
        es_s.close()

        es_w = contextlib.ExitStack()
        wqkvb = p.sb([128, 8 * 3072], BF16, es=es_w)
        stgq = [p.sb([128, 3072], F32, es=es_w) for _ in range(2)]
        psbr = p.ps([1, 3072], F32, es=es_w)
        stok = [None, None]
        tkbs = [None, None]
        tkb = None
        for c in range(8):
            s = c % 2
            tl = p.dma("sync", stgq[s][:], io["wqkv"][c * 128:(c + 1) * 128, :],
                       after=[stok[s], tkbs[s]], ch="wqkv%d" % s)
            for n in range(6):
                tkb = p.mm(psbr[:, 512 * n:512 * (n + 1)], ab8[:, 8 + c:9 + c],
                           stgq[s][:, 512 * n:512 * (n + 1)],
                           start=(c == 0), stop=(c == 7), after=[tl],
                           inc=(n == 5))
            tkbs[s] = tkb
            tv = p.ts(wqkvb[:, 3072 * c:3072 * (c + 1)], stgq[s][:],
                      ab8[:, c:c + 1], None, OP.mult, after=[tl], eng="gpsimd")
            stok[s] = tv
        tbq = p.dma("sync", brf[:], io["bqkvr"], ch="bqk")
        tba = p.tt(brf[:], brf[:], psbr[:], OP.add, after=[tbq, tkb])
        tbb = p.vcopy(browb[:], brf[:], after=[tba])
        p.barrier()

        vones = vtm[:].rearrange("p (t h c) -> p t h c", h=16, c=65)[:, :, :, 64:65]
        tvo = p.memset(vones, 1.0)
        es_v1 = contextlib.ExitStack()
        psv = [p.ps([128, 512], F32, es=es_v1) for _ in range(2)]
        wpv = [None, None]
        for t in range(NT):
            for n in range(2):
                s = n
                for c in range(8):
                    p.mm(psv[s][:],
                         xnfm[:, c * T + 128 * t: c * T + 128 * (t + 1)],
                         wqkvb[:, 3072 * c + 2048 + 512 * n: 3072 * c + 2048 + 512 * (n + 1)],
                         start=(c == 0), stop=False,
                         after=[wpv[s]] if c == 0 else (), inc=False)
                tk = p.mm(psv[s][:], onesr,
                          browb[0:1, 2048 + 512 * n: 2048 + 512 * (n + 1)],
                          start=False, stop=True, after=[tbb])
                for hh in range(8):
                    h = 8 * n + hh
                    wpv[s] = p.vcopy(
                        vtm[:, 1040 * t + 65 * h: 1040 * t + 65 * h + 64],
                        psv[s][:, 64 * hh:64 * (hh + 1)], after=[tk, tvo])
        stqk = [p.sb([64, 512], BF16, es=es_v1) for _ in range(2)]
        wstq = [None, None]
        wpq = [None, None]
        for h in range(16):
            for part, dst, scl in ((0, scr["qfm1"], 0.125), (1, scr["kfm1"], 1.0)):
                base = 1024 * part + 64 * h
                for g in range(4):
                    s = g % 2
                    for c in range(8):
                        p.mm(psv[s][0:64, :],
                             wqkvb[:, 3072 * c + base: 3072 * c + base + 64],
                             xnfm[:, c * T + 512 * g: c * T + 512 * (g + 1)],
                             start=(c == 0), stop=False,
                             after=[wpv[s]] if c == 0 else (), inc=False)
                    tk = p.mm(psv[s][0:64, :], browb[0:1, base:base + 64],
                              onesr512, start=False, stop=True, after=[tbb])
                    te = p.act(stqk[s][:], psv[s][0:64, :], AF.Copy, scale=scl,
                               after=[tk, wstq[s]])
                    wpv[s] = te
                    td = p.dma("gpsimd",
                               dst[64 * h:64 * (h + 1), 512 * g:512 * (g + 1)],
                               stqk[s][:], after=[te], ch="qko%d" % s)
                    wstq[s] = td
        p.barrier()
        es_v1.close()
        es_w.close()

        es_v = contextlib.ExitStack()
        QA = [p.sb([128, T], BF16, es=es_v) for _ in range(2)]
        KA = [p.sb([128, T], BF16, es=es_v) for _ in range(2)]
        PT = p.sb([128, 8192], BF16, es=es_v)
        sqt2 = p.sb([64, T], BF16, es=es_v)
        qn2 = p.sb([1, T], F32, es=es_v)
        kn2 = p.sb([1, T], F32, es=es_v)
        km = p.sb([1, 1], F32, es=es_v)
        brow = p.sb([1, T], F32, es=es_v)
        oht = p.sb([65, 512], BF16, es=es_v)
        psSt = [p.ps([128, 1024], F32, es=es_v) for _ in range(2)]
        psO = p.ps([65, 512], F32, es=es_v)
        ptrb = p.ps([128, 128], BF16, es=es_v)
        pnrm = p.ps([1, 512], F32, es=es_v)
        tonesa = p.memset(KA[0][64:65, :], 1.0)
        tonesb = p.memset(KA[1][64:65, :], 1.0)
        tldk = [None, None]
        tSTread = [None, None]
        wPT = [None, None]
        wO = None
        woht = None
        wotm = None
        wptrb = None
        wsq = None
        wnrm = None
        wrow = None

        def load_qk(hh):
            b_ = hh % 2
            p.dma("sync", QA[b_][0:64, :], scr["qfm1"][64 * hh:64 * (hh + 1), :],
                  after=[tSTread[b_]], ch="qkl%d" % b_)
            t2 = p.dma("sync", KA[b_][0:64, :], scr["kfm1"][64 * hh:64 * (hh + 1), :],
                       after=[tSTread[b_]], ch="qkl%d" % b_)
            tldk[b_] = t2

        def bound_row(h):
            nonlocal wsq, wnrm, wrow
            b_ = h % 2
            t1 = p.act(sqt2[:], KA[b_][0:64, :], AF.Square, after=[tldk[b_], wsq])
            tlast = None
            tklast = None
            for g in range(4):
                tk = p.mm(pnrm[:], cst["onesc"][0:64, :], sqt2[:, 512 * g:512 * (g + 1)],
                          True, True, after=[t1, wnrm])
                tklast = tk
                tlast = p.vcopy(kn2[0:1, 512 * g:512 * (g + 1)], pnrm[:], after=[tk, wrow])
                wnrm = tlast
            tkm = p.red(km[:], kn2[:], OP.max, after=[tlast])
            t2 = p.act(sqt2[:], QA[b_][0:64, :], AF.Square, after=[tldk[b_], tklast])
            wsq = t2
            for g in range(4):
                tk = p.mm(pnrm[:], cst["onesc"][0:64, :], sqt2[:, 512 * g:512 * (g + 1)],
                          True, True, after=[t2, wnrm])
                tlast = p.vcopy(qn2[0:1, 512 * g:512 * (g + 1)], pnrm[:], after=[tk, wrow])
                wnrm = tlast
            t3 = p.ts(brow[:], qn2[:], km[0:1, 0:1], None, OP.mult, after=[tkm, tlast, wrow])
            t4 = p.act(brow[:], brow[:], AF.Ln, bias=cc[0:1, 4:5], after=[t3])
            t5 = p.act(brow[:], brow[:], AF.Exp, scale=0.5, after=[t4])
            t6 = p.ts(brow[:], brow[:], -1.0, 60.0, OP.mult, OP.add, after=[t5])
            wrow = t6
            tq = p.vcopy(QA[b_][64:65, :], brow[:], after=[t6, tSTread[b_]])
            return tq

        def st_sub(h, qg, tq64):
            nonlocal wO, woht, wotm, wptrb
            b_ = h % 2
            for pr in range(8):
                half = pr % 2
                toks = []
                for j in range(2):
                    kc = 2 * pr + j
                    toks.append(p.mm(
                        psSt[half][:, 512 * j:512 * (j + 1)],
                        KA[b_][0:65, 128 * kc:128 * (kc + 1)],
                        QA[b_][0:65, 512 * qg:512 * (qg + 1)],
                        start=True, stop=True,
                        after=[tq64, tonesa, tonesb, wPT[half]]))
                tex = p.act(PT[:, 1024 * pr:1024 * (pr + 1)], psSt[half][:],
                            AF.Exp, after=[toks[0], toks[1]])
                wPT[half] = tex
            tk = None
            for kc in range(16):
                tk = p.mm(psO[:],
                          vtm[:, 1040 * kc + 65 * h: 1040 * kc + 65 * (h + 1)],
                          PT[:, 512 * kc:512 * (kc + 1)],
                          start=(kc == 0), stop=(kc == 15),
                          after=[wPT[0], wPT[1], wO] if kc == 0 else (),
                          inc=(kc == 15))
            tSTread[b_] = tk
            teo = p.vcopy(oht[:], psO[:], after=[tk, woht])
            wO = teo
            for sub in range(4):
                tt_ = 4 * qg + sub
                ttr = p.tr(ptrb[:, 0:65], oht[:, 128 * sub:128 * (sub + 1)],
                           idb[0:65, 0:65], after=[teo, wptrb])
                tcp = p.vcopy(
                    otm[:, 1040 * tt_ + 65 * h: 1040 * tt_ + 65 * (h + 1)],
                    ptrb[:, 0:65], after=[ttr, wotm])
                wptrb = tcp
                woht = tcp

        load_qk(0)
        pend = None
        for h in range(16):
            tq = bound_row(h)
            for qg in range(4):
                if pend is not None:
                    st_sub(*pend)
                pend = (h, qg, tq)
                if qg == 0 and h + 1 < 16:
                    load_qk(h + 1)
        st_sub(*pend)
        p.barrier()
        es_v.close()

        es_o = contextlib.ExitStack()
        wo1b, two0, two1 = load_weight_bf16(p, io["wo1"], D, D, es_o)
        bo1f = p.sb([1, 1024], F32, es=es_o)
        bo1r = p.sb([1, 1024], BF16, es=es_o)
        tl = p.dma("sync", bo1f[:], io["bo1r"], ch="bo1")
        tbo = p.vcopy(bo1r[:], bo1f[:], after=[tl])
        bo1bc = p.sb([128, 1024], F32, es=es_o)
        psW = p.ps([128, 512], F32, es=es_o)
        tprev = None
        for n in range(2):
            tk = p.mm(psW[:], onesr, bo1r[0:1, 512 * n:512 * (n + 1)], True, True,
                      after=[tbo, tprev])
            tprev = p.scopy(bo1bc[:, 512 * n:512 * (n + 1)], psW[:], after=[tk])
        rcp16 = p.sb([128, 16], F32, es=es_o)
        o2 = p.sb([128, 1024], BF16, es=es_o)
        ofmt = p.sb([128, 1024], BF16, es=es_o)
        ptrb2 = [p.ps([128, 128], BF16, es=es_o) for _ in range(2)]
        xn1l = [p.sb([128, 1024], BF16, es=es_o) for _ in range(2)]
        s1t = [p.sb([128, 1024], F32, es=es_o) for _ in range(2)]
        wrc = None
        wo2 = None
        wofmt = None
        wpt2 = [None, None]
        wps = tprev
        ws1 = [None, None]
        wx1l = [None, None]
        for t in range(NT):
            s = t % 2
            tlx = p.dma("sync", xn1l[s][:], scr["xn1"][128 * t:128 * (t + 1), :],
                        after=[wx1l[s]], ch="xn1l%d" % s)
            ov = otm[:, 1040 * t:1040 * (t + 1)].rearrange("p (h c) -> p h c", c=65)
            trc = p.recip(rcp16[:], ov[:, :, 64], after=[wrc])
            o2toks = []
            for hh in range(16):
                tn = p.ts(o2[:, 64 * hh:64 * (hh + 1)],
                          otm[:, 1040 * t + 65 * hh: 1040 * t + 65 * hh + 64],
                          rcp16[:, hh:hh + 1], None, OP.mult,
                          after=[trc, wo2])
                o2toks.append(tn)
            wrc = o2toks[-1]
            tcpl = None
            for c in range(8):
                ss = c % 2
                ttr = p.tr(ptrb2[ss][:], o2[:, 128 * c:128 * (c + 1)], idb,
                           after=[o2toks[-1], wpt2[ss]])
                tcp = p.vcopy(ofmt[:, 128 * c:128 * (c + 1)], ptrb2[ss][:],
                              after=[ttr, wofmt])
                wpt2[ss] = tcp
                tcpl = tcp
            wo2 = tcpl
            for n in range(2):
                tk = None
                for c in range(8):
                    tk = p.mm(psW[:], ofmt[:, 128 * c:128 * (c + 1)],
                              wo1b[:, 1024 * c + 512 * n: 1024 * c + 512 * (n + 1)],
                              start=(c == 0), stop=(c == 7),
                              after=[tcpl, wps, two0, two1] if c == 0 else (),
                              inc=(c == 7))
                wofmt = tk
                ta = p.tt(s1t[s][:, 512 * n:512 * (n + 1)], psW[:],
                          bo1bc[:, 512 * n:512 * (n + 1)], OP.add,
                          after=[tk, ws1[s]])
                wps = ta
                ws1[s] = ta
            tb2 = p.tt(s1t[s][:], s1t[s][:], xn1l[s][:], OP.add, after=[tlx, ws1[s]])
            td = p.dma("gpsimd", scr["s1"][128 * t:128 * (t + 1), :], s1t[s][:],
                       after=[tb2], ch="s1o%d" % s)
            ws1[s] = td
            wx1l[s] = tb2
        p.barrier()
        es_o.close()


def emit_l1_ffn(p, io, scr, cst, xnfm):
    cc = cst["cc"]
    with contextlib.ExitStack() as es:
        gates = p.sb([128, NT * 8], F32, es=es)
        y1 = p.sb([128, NT * 1024], BF16, es=es)
        tmz = [p.memset(y1[:, :8192], 0.0), p.memset(y1[:, 8192:], 0.0)]
        es_e = contextlib.ExitStack()
        winb = p.sb([128, 8 * 2048], BF16, es=es_e)
        woutb = p.sb([128, 16 * 1024], BF16, es=es_e)
        stgw = [p.sb([128, 2048], F32, es=es_e) for _ in range(2)]
        hfm = p.sb([128, 16 * 512], BF16, es=es_e)
        wstg = [None, None]

        def conv_copy(eng, out, in_, after):
            if eng == "scalar":
                return p.act(out, in_, AF.Copy, after=after)
            return p.vcopy(out, in_, after=after, eng=eng)

        def load_expert(e, win_cons, wout_cons, engs=("gpsimd", "gpsimd")):
            win_rdy = [None, None]
            wout_rdy = [None, None]
            for c in range(8):
                s = c % 2
                tl = p.dma("sync", stgw[s][:],
                           io["win"][e, 128 * c:128 * (c + 1), :],
                           after=[wstg[s]], ch="wie%d" % s)
                win_rdy[s] = conv_copy(engs[s], winb[:, 2048 * c:2048 * (c + 1)],
                                       stgw[s][:], [tl, win_cons])
                wstg[s] = win_rdy[s]
            for c in range(16):
                s = c % 2
                tl = p.dma("sync", stgw[s][:, 0:1024],
                           io["wout"][e, 128 * c:128 * (c + 1), :],
                           after=[wstg[s]], ch="wie%d" % s)
                wout_rdy[s] = conv_copy(engs[s], woutb[:, 1024 * c:1024 * (c + 1)],
                                        stgw[s][:, 0:1024], [tl, wout_cons])
                wstg[s] = wout_rdy[s]
            return win_rdy, wout_rdy

        es_g = contextlib.ExitStack()
        wg1b, twg0, twg1 = load_weight_bf16(p, io["wg1"], D, 256, es_g)
        wg2b = p.sb([128, 16], BF16, es=es_g)
        stg2 = p.sb([128, 8], F32, es=es_g)
        bg1c = p.sb([128, 2], F32, es=es_g)
        tbg = p.dma("sync", bg1c[:], io["bg1c"], ch="bg1")
        tw2 = None
        for c in range(2):
            tl = p.dma("sync", stg2[:], io["wg2"][c * 128:(c + 1) * 128, :],
                       after=[tw2], ch="wg2l")
            tw2 = p.vcopy(wg2b[:, 8 * c:8 * (c + 1)], stg2[:], after=[tl])
        g1fm = p.sb([128, 2 * T], BF16, es=es_g)
        psg = [p.ps([128, 512], F32, es=es_g) for _ in range(2)]
        wg = [None, None]
        for mc in range(2):
            for g in range(4):
                s = (mc * 4 + g) % 2
                tk = None
                for c in range(8):
                    tk = p.mm(psg[s][:],
                              wg1b[:, 256 * c + 128 * mc: 256 * c + 128 * (mc + 1)],
                              xnfm[:, c * T + 512 * g: c * T + 512 * (g + 1)],
                              start=(c == 0), stop=(c == 7),
                              after=[wg[s], twg0, twg1] if c == 0 else (),
                              inc=(c == 7))
                te = p.act(g1fm[:, T * mc + 512 * g: T * mc + 512 * (g + 1)],
                           psg[s][:], AF.Gelu, bias=bg1c[:, mc:mc + 1],
                           after=[tk, tbg])
                wg[s] = te
        psl = [p.ps([128, 8], F32, es=es_g) for _ in range(2)]
        prb = [p.sb([128, 8], F32, es=es_g) for _ in range(2)]
        aux = [p.sb([128, 12], F32, es=es_g) for _ in range(2)]
        junk8 = p.sb([128, 8], BF16, es=es_g)
        wpl = [None, None]
        gate_tok = [None] * NT
        for t in range(NT):
            s = t % 2
            tk = None
            for c in range(2):
                tk = p.mm(psl[s][:], g1fm[:, T * c + 128 * t: T * c + 128 * (t + 1)],
                          wg2b[:, 8 * c:8 * (c + 1)], start=(c == 0), stop=(c == 1),
                          after=[wg[0], wg[1], tw2, wpl[s]] if c == 0 else (),
                          inc=(c == 1))
            t1 = p.red(aux[s][:, 0:1], psl[s][:], OP.max, negate=True,
                       after=[tk, wpl[s]])
            t2 = p.act(junk8[:], psl[s][:], AF.Exp, bias=aux[s][:, 0:1],
                       accum_out=aux[s][:, 1:2], after=[t1])
            t3 = p.recip(aux[s][:, 2:3], aux[s][:, 1:2], after=[t2])
            t4 = p.act(prb[s][:], psl[s][:], AF.Exp, bias=aux[s][:, 0:1], after=[t2])
            t5 = p.ts(prb[s][:], prb[s][:], aux[s][:, 2:3], None, OP.mult,
                      after=[t3, t4])
            t6 = p.red(aux[s][:, 3:4], prb[s][:], OP.max, after=[t5])
            t7 = p.ts(aux[s][:, 4:12], prb[s][:], aux[s][:, 3:4], None, OP.subtract,
                      after=[t6])
            t8 = p.ts(aux[s][:, 4:12], aux[s][:, 4:12], 0.0, None, OP.is_ge,
                      after=[t7])
            t9 = p.stt(aux[s][:, 4:12], aux[s][:, 4:12], -1e30, prb[s][:],
                       OP.mult, OP.add, after=[t8])
            t10 = p.red(aux[s][:, 3:4], aux[s][:, 4:12], OP.max, after=[t9])
            t11 = p.ts(aux[s][:, 4:12], prb[s][:], aux[s][:, 3:4], None, OP.is_ge,
                       after=[t10])
            t12 = p.tt(gates[:, 8 * t:8 * (t + 1)], prb[s][:], aux[s][:, 4:12],
                       OP.mult, after=[t11])
            gate_tok[t] = t12
            wpl[s] = t12
        win_rdy, wout_rdy = load_expert(0, None, None, engs=("scalar", "gpsimd"))
        es_g.close()
        gates_done = [wg[0], wg[1], wpl[0], wpl[1]]
        psh = [p.ps([128, 512], F32, es=es_e) for _ in range(2)]
        pso = [p.ps([128, 1024], F32, es=es_e) for _ in range(2)]
        win_cons = None
        wout_cons = None
        wh = [None, None]
        wo_ = [None, None]
        first_mm = True
        for e in range(8):
            for tg in range(4):
                hlast = None
                for j in range(16):
                    s = j % 2
                    tk = None
                    for c in range(8):
                        tk = p.mm(psh[s][:],
                                  winb[:, 2048 * c + 128 * j: 2048 * c + 128 * (j + 1)],
                                  xnfm[:, c * T + 512 * tg: c * T + 512 * (tg + 1)],
                                  start=(c == 0), stop=(c == 7),
                                  after=([win_rdy[0], win_rdy[1], wh[s]]
                                         + (gates_done if first_mm else []))
                                  if c == 0 else (),
                                  inc=(c == 7))
                        first_mm = False
                    win_cons = tk
                    te = p.act(hfm[:, 512 * j:512 * (j + 1)], psh[s][:], AF.Gelu,
                               after=[tk])
                    wh[s] = te
                    hlast = te
                for sub in range(4):
                    t_ = 4 * tg + sub
                    s = sub % 2
                    tk = None
                    for j in range(16):
                        for n in range(2):
                            tk = p.mm(pso[s][:, 512 * n:512 * (n + 1)],
                                      hfm[:, 512 * j + 128 * sub: 512 * j + 128 * (sub + 1)],
                                      woutb[:, 1024 * j + 512 * n: 1024 * j + 512 * (n + 1)],
                                      start=(j == 0), stop=(j == 15),
                                      after=[hlast, wout_rdy[0], wout_rdy[1], wo_[s]]
                                      if (j == 0 and n == 0) else (),
                                      inc=(j == 15 and n == 1))
                    wout_cons = tk
                    tac = p.stt(y1[:, 1024 * t_:1024 * (t_ + 1)], pso[s][:],
                                gates[:, 8 * t_ + e: 8 * t_ + e + 1],
                                y1[:, 1024 * t_:1024 * (t_ + 1)], OP.mult, OP.add,
                                after=[tk, gate_tok[t_]] + (tmz if e == 0 else []))
                    wo_[s] = tac
            if e < 7:
                win_rdy, wout_rdy = load_expert(e + 1, win_cons, wout_cons)
        p.barrier()
        es_e.close()
        x1t = [p.sb([128, 1024], F32, es=es) for _ in range(2)]
        s1l = [p.sb([128, 1024], F32, es=es) for _ in range(2)]
        syt = [p.sb([128, 1024], F32, es=es) for _ in range(2)]
        ot = [p.sb([128, 1024], F32, es=es) for _ in range(2)]
        wx = [None, None]
        for t in range(NT):
            s = t % 2
            p.dma("sync", x1t[s][:], scr["x1"][t * 128:(t + 1) * 128, :],
                  after=[wx[s]], ch="fc%d" % s)
            tls = p.dma("sync", s1l[s][:], scr["s1"][t * 128:(t + 1) * 128, :],
                        after=[wx[s]], ch="fc%d" % s)
            tsy = p.act(syt[s][:], y1[:, 1024 * t:1024 * (t + 1)], AF.Silu,
                        after=[wx[s]])
            tm1 = p.tt(syt[s][:], syt[s][:], s1l[s][:], OP.mult, after=[tsy, tls])
            tx2 = p.tt(ot[s][:], syt[s][:], x1t[s][:], OP.add, after=[tm1, tls])
            td = p.dma("gpsimd", io["out"][t * 128:(t + 1) * 128, :], ot[s][:],
                       after=[tx2], ch="oo%d" % s)
            wx[s] = td
        p.barrier()


def emit_l1(p, io, scr, cst, xnfm):
    emit_ln(p, scr["x1"], xnfm, cst["cc"], cst["idb"], eps_col_idx=1,
            xn_tm_dram=scr["xn1"])
    emit_l1_attn(p, io, scr, cst, xnfm)
    emit_l1_ffn(p, io, scr, cst, xnfm)


# version 17
# speedup vs baseline: 1.0912x; 1.0873x over previous
import contextlib
import numpy as np
import ml_dtypes

import concourse.bass as bass
import concourse.mybir as mybir
from concourse.alu_op_type import AluOpType as OP
from concourse.bass_utils import run_bass_kernel_spmd

F32 = mybir.dt.float32
BF16 = mybir.dt.bfloat16
AF = mybir.ActivationFunctionType
AX = mybir.AxisListType
BF = ml_dtypes.bfloat16

ENG = ("sync", "tensor", "vector", "scalar", "gpsimd")
AUTO_SELF = {"vector", "scalar", "gpsimd"}

T = 2048
D = 1024
NT = T // 128
H = 8
DK = 64
DV = 128
LNR8 = -2.0794415416798357


class Prog:

    def __init__(self):
        self.nc = bass.Bass("TRN2", target_bir_lowering=False)
        self.es = contextlib.ExitStack()
        self.es_wob = self.es.enter_context(contextlib.ExitStack())
        self.sched = {e: [] for e in ENG}
        self.ctr = {e: 0 for e in ENG}
        self.chctr = {}
        self.waited = {c: {} for c in ENG}
        self._n = 0

    def _nm(self, pfx):
        self._n += 1
        return f"{pfx}{self._n}"

    def sb(self, shape, dt=F32, name=None, es=None):
        return (es or self.es).enter_context(
            self.nc.sbuf_tensor(name or self._nm("sb"), shape, dt))

    def ps(self, shape, dt=F32, name=None, es=None):
        return (es or self.es).enter_context(
            self.nc.psum_tensor(name or self._nm("ps"), shape, dt))

    def din(self, name, shape, dt=F32):
        return self.nc.dram_tensor(name, shape, dt, kind="ExternalInput").ap()

    def dout(self, name, shape, dt=F32):
        return self.nc.dram_tensor(name, shape, dt, kind="ExternalOutput").ap()

    def dscr(self, name, shape, dt=F32):
        return self.nc.dram_tensor(name, shape, dt, kind="Internal").ap()

    def _wait(self, eng, sem, v):
        if self.waited[eng].get(sem, 0) < v:
            self.sched[eng].append(("w", sem, v))
            self.waited[eng][sem] = v

    def op(self, eng, fn, after=(), inc=True, ch=None):
        sch = self.sched[eng]
        if eng in AUTO_SELF and self.ctr[eng] > self.waited[eng].get(eng, 0):
            self._wait(eng, eng, self.ctr[eng])
        for tok in after:
            if tok is None:
                continue
            self._wait(eng, tok[0], tok[1])
        if ch is not None:
            self.chctr[ch] = self.chctr.get(ch, 0) + 16
            sch.append(("op", fn, ("c", ch, self.chctr[ch])))
            return (ch, self.chctr[ch])
        if inc:
            self.ctr[eng] += 1
            sch.append(("op", fn, ("e", eng, self.ctr[eng])))
            return (eng, self.ctr[eng])
        sch.append(("op", fn, None))
        return None

    def tok(self, eng):
        return (eng, self.ctr[eng])

    def barrier(self):
        toks = [(e, self.ctr[e]) for e in ENG]
        toks += [(c, v) for c, v in self.chctr.items()]
        for e in ENG:
            for pe, v in toks:
                if v > 0:
                    self._wait(e, pe, v)

    def finalize(self):
        self.barrier()
        sems = {e: self.es.enter_context(self.nc.semaphore("S" + e)) for e in ENG}
        for c in self.chctr:
            sems[c] = self.es.enter_context(self.nc.semaphore("C" + c))
        block = self.es.enter_context(self.nc.Block())

        def emit2(eng_name):
            def body(engine):
                for item in self.sched[eng_name]:
                    if item[0] == "w":
                        engine.wait_ge(sems[item[1]], item[2])
                    else:
                        inst = item[1](engine)
                        if item[2] is not None:
                            kind, nm, val = item[2]
                            inst.then_inc(sems[nm], 16 if kind == "c" else 1)
            return body

        block.sync(emit2("sync"))
        block.tensor(emit2("tensor"))
        block.vector(emit2("vector"))
        block.scalar(emit2("scalar"))
        block.gpsimd(emit2("gpsimd"))
        self.es.close()
        return self.nc

    def dma(self, eng, out, in_, after=(), ch="c0"):
        return self.op(eng, lambda e, o=out, i=in_: e.dma_start(out=o, in_=i),
                       after=after, ch=ch)

    def mm(self, out, lhsT, rhs, start, stop, after=(), inc=True):
        return self.op("tensor",
                       lambda e, o=out, l=lhsT, r=rhs, s1=start, s2=stop:
                       e.matmul(o, l, r, start=s1, stop=s2),
                       after=after, inc=inc)

    def tr(self, out, in_, ident, after=(), inc=True):
        return self.op("tensor",
                       lambda e, o=out, i=in_, d=ident: e.transpose(o, i, d),
                       after=after, inc=inc)

    def act(self, out, in_, func, bias=None, scale=1.0, accum_out=None, after=()):
        def fn(e, o=out, i=in_, f=func, b=bias, s=scale, a=accum_out):
            kw = dict(scale=s)
            if b is not None:
                kw["bias"] = b
            if a is not None:
                kw["accum_out"] = a
            return e.activation(o, i, f, **kw)
        return self.op("scalar", fn, after=after)

    def scopy(self, out, in_, after=()):
        return self.op("scalar", lambda e, o=out, i=in_: e.copy(o, i), after=after)

    def tt(self, out, in0, in1, op, after=(), eng="vector"):
        return self.op(eng, lambda e, o=out, a=in0, b=in1, p=op:
                       e.tensor_tensor(o, a, b, p), after=after)

    def ts(self, out, in0, s1, s2, op0, op1=None, after=(), accum_out=None, eng="vector"):
        def fn(e, o=out, i=in0, a=s1, b=s2, p0=op0, p1=op1, ac=accum_out):
            kw = {}
            if p1 is not None:
                kw["op1"] = p1
            if ac is not None:
                kw["accum_out"] = ac
            return e.tensor_scalar(o, i, a, b, p0, **kw)
        return self.op(eng, fn, after=after)

    def stt(self, out, in0, scalar, in1, op0, op1, after=()):
        return self.op("vector", lambda e, o=out, i0=in0, s=scalar, i1=in1, p0=op0, p1=op1:
                       e.scalar_tensor_tensor(o, i0, s, i1, p0, p1), after=after)

    def vcopy(self, out, in_, after=(), eng="vector"):
        return self.op(eng, lambda e, o=out, i=in_: e.tensor_copy(o, i), after=after)

    def red(self, out, in_, op, negate=False, after=()):
        return self.op("vector", lambda e, o=out, i=in_, p=op, n=negate:
                       e.tensor_reduce(o, i, axis=AX.X, op=p, negate=n), after=after)

    def recip(self, out, in_, after=()):
        return self.op("vector", lambda e, o=out, i=in_: e.reciprocal(o, i), after=after)

    def memset(self, ap, val, after=(), eng="gpsimd"):
        return self.op(eng, lambda e, a=ap, v=val: e.memset(a, v), after=after)



def host_prep(inputs, core):
    f32 = np.float32
    bf = BF
    d = {}
    d["x"] = np.ascontiguousarray(inputs["L0_attn_x"][core], f32)
    d["wq"] = np.ascontiguousarray(inputs["L0_attn_Wq"], f32)
    d["wk"] = np.ascontiguousarray(inputs["L0_attn_Wk"], f32)
    d["wv"] = np.ascontiguousarray(inputs["L0_attn_Wv"], f32)
    d["wg"] = np.ascontiguousarray(inputs["L0_attn_Wg"], f32)
    d["wgk1"] = np.ascontiguousarray(inputs["L0_attn_Wgk1"], f32)
    d["wgk2"] = np.ascontiguousarray(inputs["L0_attn_Wgk2"], f32)
    d["bgk2r"] = np.ascontiguousarray(inputs["L0_attn_bgk2"].reshape(1, 512), f32)
    d["nwc"] = np.ascontiguousarray(inputs["L0_attn_norm_w"].reshape(128, 1), f32)
    d["wo"] = np.ascontiguousarray(inputs["L0_attn_Wo"], f32)
    d["gwT"] = np.ascontiguousarray(inputs["L0_ffn_gate_w"].T, f32)
    wupT = np.ascontiguousarray(inputs["L0_ffn_w_up"].transpose(0, 2, 1), f32)
    d["wupT_own"] = np.ascontiguousarray(wupT[core])
    d["wdn_own"] = np.ascontiguousarray(inputs["L0_ffn_w_down"][core], f32)
    d["gam_c"] = np.ascontiguousarray(inputs["L1_attn_gn_gamma"].reshape(8, 128).T, f32)
    d["bet_c"] = np.ascontiguousarray(inputs["L1_attn_gn_beta"].reshape(8, 128).T, f32)
    d["wqkv"] = np.ascontiguousarray(inputs["L1_attn_W_qkv"], f32)
    d["bqkvr"] = np.ascontiguousarray(inputs["L1_attn_b_qkv"].reshape(1, 3072), f32)
    d["wo1"] = np.ascontiguousarray(inputs["L1_attn_W_o"], f32)
    d["bo1r"] = np.ascontiguousarray(inputs["L1_attn_b_o"].reshape(1, 1024), f32)
    d["wg1"] = np.ascontiguousarray(inputs["L1_ffn_wg1"], f32)
    d["bg1c"] = np.ascontiguousarray(inputs["L1_ffn_bg1"].reshape(2, 128).T, f32)
    d["wg2"] = np.ascontiguousarray(inputs["L1_ffn_wg2"], f32)
    d["win"] = np.ascontiguousarray(inputs["L1_ffn_w_in"], f32)
    d["wout"] = np.ascontiguousarray(inputs["L1_ffn_w_out"], f32)
    tri = np.triu(np.ones((64, 64), f32))
    u2 = np.zeros((128, 128), f32)
    u2[:64, :64] = tri / 16.0
    u2[64:, 64:] = tri / 16.0
    d["U2b"] = u2.astype(bf)
    m2 = np.zeros((128, 128), f32)
    m2[:64, :64] = tri
    m2[64:, 64:] = tri
    d["M2f"] = m2
    d["idb"] = np.eye(128, dtype=f32).astype(bf)
    d["idf"] = np.eye(128, dtype=f32)
    d["onesr"] = np.ones((1, 128), f32).astype(bf)
    d["onesc"] = np.ones((128, 1), f32).astype(bf)
    cc = np.zeros((128, 6), f32)
    cc[:, 0] = 1.0
    cc[:, 1] = 1e-5
    cc[:, 2] = 1e-6
    cc[:, 3] = LNR8
    cc[:, 4] = 1e-30
    d["cc"] = cc
    d["onesr512"] = np.ones((1, 512), f32).astype(bf)
    e4 = np.zeros((128, 4), f32)
    for j in range(4):
        e4[32 * j:32 * (j + 1), j] = 1.0
    d["E4"] = e4
    d["E4T"] = np.ascontiguousarray(e4.T)
    return d


def declare_io(p):
    io = {}
    io["x"] = p.din("x", [T, D])
    io["wq"] = p.din("wq", [D, 512]); io["wk"] = p.din("wk", [D, 512])
    io["wv"] = p.din("wv", [D, D]); io["wg"] = p.din("wg", [D, D])
    io["wgk1"] = p.din("wgk1", [D, 16]); io["wgk2"] = p.din("wgk2", [16, 512])
    io["bgk2r"] = p.din("bgk2r", [1, 512]); io["nwc"] = p.din("nwc", [128, 1])
    io["wo"] = p.din("wo", [D, D]); io["gwT"] = p.din("gwT", [D, 8])
    io["wupT_own"] = p.din("wupT_own", [4096, D])
    io["wdn_own"] = p.din("wdn_own", [4096, D])
    io["gam_c"] = p.din("gam_c", [128, 8]); io["bet_c"] = p.din("bet_c", [128, 8])
    io["wqkv"] = p.din("wqkv", [D, 3072]); io["bqkvr"] = p.din("bqkvr", [1, 3072])
    io["wo1"] = p.din("wo1", [D, D]); io["bo1r"] = p.din("bo1r", [1, D])
    io["wg1"] = p.din("wg1", [D, 256]); io["bg1c"] = p.din("bg1c", [128, 2])
    io["wg2"] = p.din("wg2", [256, 8])
    io["win"] = p.din("win", [8, D, 2048]); io["wout"] = p.din("wout", [8, 2048, D])
    io["U2b"] = p.din("U2b", [128, 128], BF16)
    io["M2f"] = p.din("M2f", [128, 128])
    io["idb"] = p.din("idb", [128, 128], BF16)
    io["idf"] = p.din("idf", [128, 128])
    io["onesr"] = p.din("onesr", [1, 128], BF16)
    io["onesc"] = p.din("onesc", [128, 1], BF16)
    io["cc"] = p.din("cc", [128, 6])
    io["onesr512"] = p.din("onesr512", [1, 512], BF16)
    io["E4"] = p.din("E4", [128, 4])
    io["E4T"] = p.din("E4T", [4, 128])
    io["out"] = p.dout("out", [T, D])
    return io



def load_weight_bf16(p, src, rows, cols, es, scale_col=None, prev=None):
    nch = rows // 128
    chp = p._nm("lw")
    wbf = p.sb([128, nch * cols], BF16, es=es)
    stg = [p.sb([128, cols], F32, es=es), p.sb([128, cols], F32, es=es)]
    stok = [None, None]
    for c in range(nch):
        s = c % 2
        tl = p.dma("sync", stg[s][:], src[c * 128:(c + 1) * 128, :],
                   after=[stok[s], prev], ch=chp + str(s))
        if scale_col is None:
            tv = p.vcopy(wbf[:, c * cols:(c + 1) * cols], stg[s][:], after=[tl],
                         eng="gpsimd")
        else:
            tv = p.ts(wbf[:, c * cols:(c + 1) * cols], stg[s][:], scale_col, None,
                      OP.mult, after=[tl], eng="gpsimd")
        stok[s] = tv
    return wbf, stok[0], stok[1]


def load_row_bf16(p, src, cols, es):
    stg = p.sb([1, cols], F32, es=es)
    row = p.sb([1, cols], BF16, es=es)
    tl = p.dma("sync", stg[:], src, ch=p._nm("lr"))
    tv = p.vcopy(row[:], stg[:], after=[tl])
    return row, tv



def emit_ln(p, src_dram, xnfm, cc, idb, eps_col_idx=1, xn_tm_dram=None):
    with contextlib.ExitStack() as es:
        xt = [p.sb([128, D], F32, es=es) for _ in range(2)]
        junk = p.sb([128, D], BF16, es=es)
        xnb = [p.sb([128, D], BF16, es=es) for _ in range(2)]
        colp = [p.sb([128, 8], F32, es=es) for _ in range(2)]
        ptb = [p.ps([128, 128], BF16, es=es) for _ in range(2)]
        tx = [None, None]
        txn = [None, None]
        tpt = [None, None]
        tj = None
        for t in range(NT):
            s = t % 2
            ctl = colp[s]
            tl = p.dma("sync", xt[s][:], src_dram[t * 128:(t + 1) * 128, :],
                       after=[tx[s]], ch="lnx%d" % s)
            tssq = p.act(junk[:], xt[s][:], AF.Square, accum_out=ctl[:, 1:2],
                         after=[tl, tj, tx[s]])
            tj = tssq
            tsum = p.red(ctl[:, 0:1], xt[s][:], OP.add, after=[tl, tx[s]])
            tm = p.ts(ctl[:, 2:3], ctl[:, 0:1], 1.0 / D, None, OP.mult, after=[tsum])
            tq = p.ts(ctl[:, 3:4], ctl[:, 1:2], 1.0 / D, None, OP.mult, after=[tssq])
            tmm = p.tt(ctl[:, 4:5], ctl[:, 2:3], ctl[:, 2:3], OP.mult, after=[tm])
            tvar = p.tt(ctl[:, 4:5], ctl[:, 3:4], ctl[:, 4:5], OP.subtract,
                        after=[tq, tmm])
            tln = p.act(ctl[:, 5:6], ctl[:, 4:5], AF.Ln,
                        bias=cc[:, eps_col_idx:eps_col_idx + 1], after=[tvar])
            trr = p.act(ctl[:, 5:6], ctl[:, 5:6], AF.Exp, scale=-0.5, after=[tln])
            tmr = p.tt(ctl[:, 6:7], ctl[:, 2:3], ctl[:, 5:6], OP.mult, after=[trr])
            txnb = p.ts(xnb[s][:], xt[s][:], ctl[:, 5:6], ctl[:, 6:7],
                        OP.mult, OP.subtract, after=[tmr, txn[s]])
            tx[s] = txnb
            if xn_tm_dram is not None:
                txn_d = p.dma("gpsimd", xn_tm_dram[t * 128:(t + 1) * 128, :],
                              xnb[s][:], after=[txnb], ch="lnxo%d" % s)
            else:
                txn_d = None
            last = txnb
            for c in range(8):
                ss = c % 2
                ttr = p.tr(ptb[ss][:], xnb[s][:, c * 128:(c + 1) * 128], idb,
                           after=[txnb, tpt[ss]])
                tcp = p.vcopy(xnfm[:, c * T + t * 128: c * T + (t + 1) * 128],
                              ptb[ss][:], after=[ttr])
                tpt[ss] = tcp
                last = tcp
            txn[s] = last if txn_d is None else txn_d
        p.barrier()


def emit_moe_mown(p, io, scr, cst):
    with contextlib.ExitStack() as es:
        wdnb = p.sb([128, 32 * 1024], BF16, es=es)
        wupb = [p.sb([128, 4096], BF16, es=es) for _ in range(2)]
        stgw = [p.sb([128, 1024], F32, es=es) for _ in range(2)]
        stgu = [p.sb([128, 2048], F32, es=es) for _ in range(2)]
        mstg = p.sb([128, 1024], BF16, es=es)
        pM = [p.ps([128, 1024], F32, es=es) for _ in range(2)]
        wstg = [None, None]
        stu_tok = [None, None]

        def cast_copy(eng, out, in_, after):
            if eng == "scalar":
                return p.act(out, in_, AF.Copy, after=after)
            return p.vcopy(out, in_, after=after, eng=eng)

        wdn_rdy = [None, None]
        for c in range(32):
            s = c % 2
            tl = p.dma("gpsimd", stgw[s][:],
                       io["wdn_own"][c * 128:(c + 1) * 128, :], after=[wstg[s]],
                       ch="wdn%d" % s)
            wdn_rdy[s] = cast_copy("vector" if s == 0 else "scalar",
                                   wdnb[:, 1024 * c:1024 * (c + 1)], stgw[s][:],
                                   [tl])
            wstg[s] = wdn_rdy[s]
        upcv = [None, None]
        upcons = [None, None]

        def load_up(mi):
            bm = mi % 2
            wsrc = io["wupT_own"].rearrange("(k p) c -> p k c", p=128)
            wsrc = wsrc[:, :, 128 * mi:128 * (mi + 1)]
            tcvs = []
            for hh in range(2):
                sl = wsrc[:, 16 * hh:16 * (hh + 1), :]
                tl = p.dma("gpsimd", stgu[hh][:], sl,
                           after=[stu_tok[hh], upcons[bm]], ch="wup%d" % hh)
                tcv = cast_copy("vector" if hh == 0 else "scalar",
                                wupb[bm][:, 2048 * hh:2048 * (hh + 1)],
                                stgu[hh][:], [tl, upcons[bm]])
                stu_tok[hh] = tcv
                tcvs.append(tcv)
            upcv[bm] = tcvs

        load_up(0)
        wpM = [None, None]
        wms = None
        for m in range(8):
            sm = m % 2
            if m < 7:
                load_up(m + 1)
            tk = None
            for k in range(32):
                for n in range(2):
                    tk = p.mm(pM[sm][:, 512 * n:512 * (n + 1)],
                              wupb[m % 2][:, 128 * k:128 * (k + 1)],
                              wdnb[:, 1024 * k + 512 * n: 1024 * k + 512 * (n + 1)],
                              start=(k == 0), stop=(k == 31),
                              after=([*upcv[m % 2], wdn_rdy[0], wdn_rdy[1],
                                      wpM[sm]] if k == 0 else ()),
                              inc=(k == 31 and n == 1))
            upcons[m % 2] = tk
            tev = p.act(mstg[:], pM[sm][:], AF.Copy, after=[tk, wms])
            wpM[sm] = tev
            wms = p.dma("gpsimd", scr["mown"][m * 128:(m + 1) * 128, :], mstg[:],
                        after=[tev], ch="mow")
        p.barrier()
        tcc = p.op("gpsimd",
                   lambda e: e.collective_compute(
                       "AllGather", mybir.AluOpType.bypass,
                       replica_groups=[list(range(8))],
                       ins=[scr["mown"].opt()],
                       outs=[scr["mall"].opt()]),
                   after=[wms])
        p.mall_tok = tcc


def emit_l0_proj(p, io, scr, cst, xnfm, moe):
    cc, onesr = cst["cc"], cst["onesr"]
    with contextlib.ExitStack() as es:
        masks = moe["masks"]
        p.memset(moe["yacc"][:, :8192], 0.0)
        p.memset(moe["yacc"][:, 8192:], 0.0)
        gwTb = p.sb([128, 64], BF16, es=es)
        stg8 = p.sb([128, 8], F32, es=es)
        tgw = None
        for c in range(8):
            tl = p.dma("sync", stg8[:], io["gwT"][c * 128:(c + 1) * 128, :],
                       after=[tgw], ch="gwt")
            tgw = p.vcopy(gwTb[:, 8 * c:8 * (c + 1)], stg8[:], after=[tl])
        pl = p.ps([128, 8], F32, es=es)
        lmt = p.sb([128, 8], F32, es=es)
        ismx = p.sb([128, 8], F32, es=es)
        l2t = p.sb([128, 8], F32, es=es)
        mxc = p.sb([128, 2], F32, es=es)
        wpl = None
        for t in range(NT):
            tk = None
            for c in range(8):
                tk = p.mm(pl[:], xnfm[:, c * T + 128 * t: c * T + 128 * (t + 1)],
                          gwTb[:, 8 * c:8 * (c + 1)], start=(c == 0), stop=(c == 7),
                          after=[tgw, wpl] if c == 0 else (), inc=(c == 7))
            t1 = p.red(mxc[:, 0:1], pl[:], OP.max, after=[tk])
            t2 = p.ts(lmt[:], pl[:], mxc[:, 0:1], None, OP.subtract, after=[t1])
            t3 = p.ts(ismx[:], lmt[:], 0.0, None, OP.is_ge, after=[t2])
            t4 = p.stt(l2t[:], ismx[:], -1e30, pl[:], OP.mult, OP.add, after=[t3])
            t5 = p.red(mxc[:, 1:2], l2t[:], OP.max, after=[t4])
            t6 = p.ts(masks[:, 8 * t:8 * (t + 1)], pl[:], mxc[:, 1:2], None,
                      OP.is_ge, after=[t5])
            wpl = t6
        wqb, _, _ = load_weight_bf16(p, io["wq"], D, 512, es)
        wkb, _, _ = load_weight_bf16(p, io["wk"], D, 512, es)
        wvb, _, _ = load_weight_bf16(p, io["wv"], D, D, es)
        wgb, _, _ = load_weight_bf16(p, io["wg"], D, D, es)
        nwcs = p.sb([128, 1], F32, es=es)
        tnw = p.dma("sync", nwcs[:], io["nwc"], ch="nwc")
        wob = p.wob
        stgo = [p.sb([128, D], F32, es=es) for _ in range(2)]
        stok = [None, None]
        for c in range(8):
            s = c % 2
            tl = p.dma("sync", stgo[s][:], io["wo"][c * 128:(c + 1) * 128, :],
                       after=[stok[s]], ch="wo%d" % s)
            tv = p.ts(wob[:, c * D:(c + 1) * D], stgo[s][:], nwcs[:], None,
                      OP.mult, after=[tl, tnw], eng="gpsimd")
            stok[s] = tv
        wgk1b, _, _ = load_weight_bf16(p, io["wgk1"], D, 16, es)
        wgk2b = p.sb([16, 512], BF16, es=es)
        stg2 = p.sb([16, 512], F32, es=es)
        t2 = p.dma("sync", stg2[:], io["wgk2"], ch="wgk2")
        p.vcopy(wgk2b[:], stg2[:], after=[t2])
        bgk2b, tbg = load_row_bf16(p, io["bgk2r"], 512, es)
        p.barrier()

        psA = [p.ps([128, 512], F32, es=es) for _ in range(2)]
        psB = [p.ps([128, 1024], F32, es=es) for _ in range(2)]
        stgb = [p.sb([128, 1024], BF16, es=es) for _ in range(2)]
        ptoks = [None, None]
        etoksA = [None, None]
        etoksB = [None, None]
        for (wb, dst) in ((wqb, scr["qfm"]), (wkb, scr["kfm"])):
            for m in range(4):
                for g in range(4):
                    s = (m * 4 + g) % 2
                    tk = None
                    for c in range(8):
                        tk = p.mm(psA[s][:],
                                  wb[:, 512 * c + 128 * m: 512 * c + 128 * (m + 1)],
                                  xnfm[:, c * T + 512 * g: c * T + 512 * (g + 1)],
                                  start=(c == 0), stop=(c == 7),
                                  after=[etoksA[s]] if c == 0 else (), inc=(c == 7))
                    te = p.act(stgb[s][:, :512], psA[s][:], AF.Copy,
                               after=[tk, ptoks[s]])
                    td = p.dma("gpsimd",
                               dst[128 * m:128 * (m + 1), 512 * g:512 * (g + 1)],
                               stgb[s][:, :512], after=[te], ch="pev%d" % s)
                    ptoks[s] = td
                    etoksA[s] = te
        for (wb, ncols, dst) in ((wkb, 512, scr["ktm"]), (wvb, D, scr["vtm"]),
                                 (wgb, D, scr["gtm"])):
            ngr = ncols // 512
            for t in range(NT):
                s = t % 2
                ps = psB[s]
                tk = None
                for n in range(ngr):
                    for c in range(8):
                        tk = p.mm(ps[:, 512 * n:512 * (n + 1)],
                                  xnfm[:, c * T + 128 * t: c * T + 128 * (t + 1)],
                                  wb[:, ncols * c + 512 * n: ncols * c + 512 * (n + 1)],
                                  start=(c == 0), stop=(c == 7),
                                  after=[etoksB[s]] if (c == 0 and n == 0) else (),
                                  inc=(c == 7 and n == ngr - 1))
                te = p.act(stgb[s][:, :ncols], ps[:, :ncols], AF.Copy,
                           after=[tk, ptoks[s]])
                td = p.dma("gpsimd", dst[128 * t:128 * (t + 1), :],
                           stgb[s][:, :ncols], after=[te], ch="pev%d" % s)
                ptoks[s] = td
                etoksB[s] = te
        ufmb = p.sb([16, T], BF16, es=es)
        psU = p.ps([16, 512], F32, es=es)
        tprev = None
        for g in range(4):
            tk = None
            for c in range(8):
                tk = p.mm(psU[:], wgk1b[:, 16 * c:16 * (c + 1)],
                          xnfm[:, c * T + 512 * g: c * T + 512 * (g + 1)],
                          start=(c == 0), stop=(c == 7),
                          after=[tprev] if c == 0 else (), inc=(c == 7))
            tprev = p.act(ufmb[:, 512 * g:512 * (g + 1)], psU[:], AF.Copy, after=[tk])
        zcol = [p.sb([128, 512], F32, es=es) for _ in range(2)]
        zc2 = [p.sb([128, 512], F32, es=es) for _ in range(2)]
        lsb = [p.sb([128, 512], BF16, es=es) for _ in range(2)]
        dtoks = [None, None]
        wzc = [None, None]
        for t in range(NT):
            s = t % 2
            ps = psA[s]
            p.mm(ps[:], ufmb[:, 128 * t:128 * (t + 1)], wgk2b[:],
                 start=True, stop=False, after=[tprev, etoksA[s]], inc=False)
            tk = p.mm(ps[:], onesr, bgk2b[:], start=False, stop=True, after=[tbg])
            tabs = p.act(zcol[s][:], ps[:], AF.Abs, after=[tk, wzc[s]])
            texp = p.act(zcol[s][:], zcol[s][:], AF.Exp, scale=-1.0, after=[tabs])
            tl1p = p.act(zcol[s][:], zcol[s][:], AF.Ln, bias=cc[:, 0:1], after=[texp])
            tmin = p.ts(zc2[s][:], ps[:], 0.0, None, OP.min, after=[tk, tabs])
            etoksA[s] = tmin
            tls = p.tt(lsb[s][:], zc2[s][:], zcol[s][:], OP.subtract,
                       after=[tmin, tl1p, dtoks[s]])
            wzc[s] = tls
            td = p.dma("gpsimd", scr["lsz"][128 * t:(t + 1) * 128, :], lsb[s][:],
                       after=[tls], ch="lszo%d" % s)
            dtoks[s] = td
        p.barrier()


def emit_l0_scan(p, io, scr, cst, moe):
    cc, idb, U2b, M2f = cst["cc"], cst["idb"], cst["U2b"], cst["M2f"]
    wob = p.wob
    masks, yacc = moe["masks"], moe["yacc"]
    with contextlib.ExitStack() as es:
        lszt = [p.sb([128, 512], BF16, es=es) for _ in range(2)]
        ktt = [p.sb([128, 512], BF16, es=es) for _ in range(2)]
        vtt = [p.sb([128, 1024], BF16, es=es) for _ in range(2)]
        gtt = [p.sb([128, 1024], BF16, es=es) for _ in range(2)]
        qfh = [[p.sb([64, 128], BF16, es=es) for _ in range(H)] for _ in range(2)]
        kfh = [[p.sb([64, 128], BF16, es=es) for _ in range(H)] for _ in range(2)]
        QE0 = p.sb([128, 128], BF16, es=es)
        QE1 = p.sb([128, 128], BF16, es=es)
        KE0 = p.sb([128, 128], BF16, es=es)
        KE1 = p.sb([128, 128], BF16, es=es)
        KT0 = p.sb([128, 64], BF16, es=es)
        KT1 = p.sb([128, 64], BF16, es=es)
        ebf = p.sb([64, 128], F32, es=es)
        enf = p.sb([64, 128], F32, es=es)
        ent = p.sb([128, 64], F32, es=es)
        e0c = p.sb([64, 1], F32, es=es)
        e1c = p.sb([64, 1], F32, es=es)
        A2m = p.sb([128, 128], BF16, es=es)
        S_A = p.sb([64, 8 * 128], F32, es=es)
        S_B = p.sb([64, 8 * 128], F32, es=es)
        S_Ab = p.sb([128, 8 * 128], BF16, es=es)
        S_Bb = p.sb([128, 8 * 128], BF16, es=es)
        tz = []
        tz.append(p.memset(QE0[:], 0.0))
        tz.append(p.memset(QE1[:], 0.0))
        tz.append(p.memset(KE0[:], 0.0))
        tz.append(p.memset(KE1[:], 0.0))
        tz.append(p.memset(KT0[:], 0.0))
        tz.append(p.memset(KT1[:], 0.0))
        tz.append(p.memset(S_A[:], 0.0))
        tz.append(p.memset(S_B[:], 0.0))
        tz.append(p.memset(S_Ab[:], 0.0))
        tz.append(p.memset(S_Bb[:], 0.0))
        p.barrier()
        pbb = p.ps([128, 192], F32, es=es)
        pbtm = pbb[:, 0:64]
        pbfm = pbb[0:64, 64:192]
        pAP = p.ps([128, 256], F32, es=es)
        pAT = pAP[:, 0:128]
        po = pAP[:, 128:256]
        pkv0 = p.ps([64, 128], F32, es=es)
        pkv1 = pkv0
        py = [p.ps([128, 512], F32, es=es) for _ in range(2)]
        Mb2 = [p.sb([128, 8 * 1024], BF16, es=es) for _ in range(2)]
        mld = [None, None]
        mcons = [None, None]
        wpy = [None, None]

        def load_m(ei):
            bm = ei % 2
            tl = None
            for c in range(8):
                tl = p.dma("sync", Mb2[bm][:, 1024 * c:1024 * (c + 1)],
                           scr["mall"].rearrange("(e k p) c -> e k p c", e=8, p=128)[ei, c],
                           after=[mcons[bm], p.mall_tok], ch="mld%d" % bm)
            mld[bm] = tl

        def emit_y_unit(u):
            e, ty = u // 16, u % 16
            bm = e % 2
            tk = None
            for n in range(2):
                for c in range(8):
                    tk = p.mm(py[n][:],
                              xnfm0[:, c * T + 128 * ty: c * T + 128 * (ty + 1)],
                              Mb2[bm][:, 1024 * c + 512 * n: 1024 * c + 512 * (n + 1)],
                              start=(c == 0), stop=(c == 7),
                              after=[mld[bm], wpy[n]] if c == 0 else (),
                              inc=(c == 7))
                tac = p.stt(yacc[:, 1024 * ty + 512 * n: 1024 * ty + 512 * (n + 1)],
                            py[n][:],
                            masks[:, 8 * ty + e: 8 * ty + e + 1],
                            yacc[:, 1024 * ty + 512 * n: 1024 * ty + 512 * (n + 1)],
                            OP.mult, OP.add, after=[tk])
                wpy[n] = tac
            mcons[bm] = tk

        xnfm0 = moe["xnfm"]
        load_m(0)
        ot = p.sb([128, 1024], F32, es=es)
        sqt = p.sb([128, 1024], F32, es=es)
        sgt = p.sb([128, 1024], F32, es=es)
        rs8 = p.sb([128, 8], F32, es=es)
        ogb = p.sb([128, 1024], BF16, es=es)
        ogf = p.sb([128, 1024], BF16, es=es)
        ptb = [p.ps([128, 128], BF16, es=es) for _ in range(2)]
        ps0 = p.ps([128, 512], F32, es=es)
        s0t = [p.sb([128, 1024], F32, es=es) for _ in range(2)]

        tld = [dict(lsz=None, k=None, v=None, g=None, q=[None] * H, kf=None) for _ in range(2)]
        w = dict(pbtm=None, pbfm=None, pAT=None, pkv0=None, pkv1=None, po=None,
                 ebf=None, enf=None, ent=None, e0=None, e1=None, A2m=None,
                 QE0=None, QE1=None, KE0=None, KE1=None, KT0=None, KT1=None,
                 SA=None, SB=None, SAb=None, SBb=None, ot=None, ogb=None,
                 ogf=None, ps0=None, s0t=[None, None], sqt=None, sgt=None,
                 rs8=None)
        prevdma = [None, None]
        for t in range(NT):
            s = t % 2
            if t % 2 == 1 and t // 2 < 7:
                load_m(t // 2 + 1)
            pd = prevdma[s]
            chn = "scan%d" % s
            p.dma("sync", lszt[s][:], scr["lsz"][t * 128:(t + 1) * 128, :], after=[pd], ch=chn)
            p.dma("sync", ktt[s][:], scr["ktm"][t * 128:(t + 1) * 128, :], after=[pd], ch=chn)
            p.dma("sync", vtt[s][:], scr["vtm"][t * 128:(t + 1) * 128, :], after=[pd], ch=chn)
            p.dma("sync", gtt[s][:], scr["gtm"][t * 128:(t + 1) * 128, :], after=[pd], ch=chn)
            for h in range(H):
                p.dma("sync", qfh[s][h][:],
                      scr["qfm"][64 * h:64 * (h + 1), t * 128:(t + 1) * 128],
                      after=[pd], ch=chn)
                t_ld = p.dma("sync", kfh[s][h][:],
                             scr["kfm"][64 * h:64 * (h + 1), t * 128:(t + 1) * 128],
                             after=[pd], ch=chn)
            t_lsz = t_k = t_v = t_g = t_ld
            t_q = [t_ld] * H
            t_kf = [t_ld] * H
            for h in range(H):
                hs = slice(64 * h, 64 * (h + 1))
                vs = slice(128 * h, 128 * (h + 1))
                tb1 = p.mm(pbtm, U2b, lszt[s][:, hs], True, True,
                           after=[t_lsz, w["pbtm"], w["pbfm"]])
                tb2 = p.mm(pbfm, lszt[s][:, hs], U2b, True, True,
                           after=[t_lsz, w["pbfm"]])
                tent = p.act(ent[:], pbtm, AF.Exp, scale=-1.0,
                             after=[tb1, tb2, w["ent"]])
                w["pbtm"] = tent
                tebf = p.act(ebf[:], pbfm, AF.Exp, bias=cc[0:64, 3:4],
                             after=[tb2, w["ebf"]])
                tenf = p.act(enf[:], pbfm, AF.Exp, scale=-1.0,
                             after=[tb2, w["enf"]])
                te0 = p.act(e0c[:], pbfm[:, 63:64], AF.Exp, after=[tb2, w["e0"]])
                te1 = p.act(e1c[:], pbfm[:, 127:128], AF.Exp, after=[tb2, w["e1"]])
                w["pbfm"] = te1
                tQ0 = p.tt(QE0[0:64, 0:64], qfh[s][h][:, 0:64], ebf[:, 0:64],
                           OP.mult, after=[t_q[h], tebf, w["QE0"]])
                tQ1 = p.tt(QE1[0:64, 64:128], qfh[s][h][:, 64:128], ebf[:, 64:128],
                           OP.mult, after=[t_q[h], tebf, w["QE1"]])
                tK0 = p.tt(KE0[0:64, 0:64], kfh[s][h][:, 0:64], enf[:, 0:64],
                           OP.mult, after=[t_kf[h], tenf, w["KE0"]])
                tK1 = p.tt(KE1[0:64, 64:128], kfh[s][h][:, 64:128], enf[:, 64:128],
                           OP.mult, after=[t_kf[h], tenf, w["KE1"]])
                w["ebf"] = tQ1
                w["enf"] = tK1
                tT0 = p.tt(KT0[0:64, :], ktt[s][0:64, hs], ent[0:64, :], OP.mult,
                           after=[t_k, tent, w["KT0"]])
                tT1 = p.tt(KT1[64:128, :], ktt[s][64:128, hs], ent[64:128, :],
                           OP.mult, after=[t_k, tent, w["KT1"]])
                w["ent"] = tT1
                p.mm(pAT, KE0[:], QE0[:], True, False,
                     after=[tK0, tQ0, w["pAT"], w["po"]], inc=False)
                tat = p.mm(pAT, KE1[:], QE1[:], False, True, after=[tK1, tQ1])
                ta2 = p.tt(A2m[:], pAT, M2f, OP.mult, after=[tat, w["A2m"]])
                w["pAT"] = ta2
                tkv0 = p.mm(pkv0[:], KT0[:], vtt[s][:, vs], True, True,
                            after=[tT0, t_v, w["pkv0"]])
                tSB = p.tt(S_B[:, vs], S_A[:, vs], pkv0[:], OP.add,
                           after=[tkv0, w["SB"], w["SA"]])
                w["pkv0"] = tSB
                tSB2 = p.ts(S_B[:, vs], S_B[:, vs], e0c[:], None, OP.mult,
                            after=[te0, tSB])
                tSBb = p.vcopy(S_Bb[0:64, vs], S_B[:, vs], after=[tSB2, w["SBb"]])
                tkv1 = p.mm(pkv1[:], KT1[:], vtt[s][:, vs], True, True,
                            after=[tT1, t_v, tSB])
                p.mm(po, A2m[:], vtt[s][:, vs], True, False,
                     after=[ta2, t_v, w["po"]], inc=False)
                tom2 = p.mm(po, QE0[:], S_Ab[:, vs], False, False,
                            after=[tQ0, w["SAb"]])
                tom = p.mm(po, QE1[:], S_Bb[:, vs], False, True,
                           after=[tQ1, tSBb])
                tSA = p.tt(S_A[:, vs], S_B[:, vs], pkv1[:], OP.add,
                           after=[tkv1, tSB2, tom])
                w["pkv0"] = tSA
                w["SB"] = tSA
                tSA2 = p.ts(S_A[:, vs], S_A[:, vs], e1c[:], None, OP.mult,
                            after=[te1, tSA])
                w["SA"] = tSA2
                tSAb = p.vcopy(S_Ab[0:64, vs], S_A[:, vs], after=[tSA2, tom2])
                w["SAb"] = tSAb
                w["SBb"] = tom
                w["e0"] = tSB2
                w["e1"] = tSA2
                w["QE0"] = tom2
                w["QE1"] = tom
                w["KE0"] = tat
                w["KE1"] = tat
                w["KT0"] = tkv0
                w["KT1"] = tkv1
                tev = p.scopy(ot[:, vs], po, after=[tom, w["ot"]])
                w["po"] = tev
                if h == H - 1:
                    w["ot"] = None
                oevac_last = tev
                emit_y_unit(t * 8 + h)
            tsq = p.act(sqt[:], ot[:], AF.Square, after=[oevac_last, w["sqt"]])
            trs = p.red(rs8[:], sqt[:].rearrange("p (h c) -> p h c", c=128),
                        OP.add, after=[tsq, w["rs8"]])
            w["sqt"] = trs
            tms = p.ts(rs8[:], rs8[:], 1.0 / 128.0, None, OP.mult, after=[trs])
            tln = p.act(rs8[:], rs8[:], AF.Ln, bias=cc[:, 1:2], after=[tms])
            trr = p.act(rs8[:], rs8[:], AF.Exp, scale=-0.5, after=[tln])
            w["rs8"] = trr
            tsg = p.act(sgt[:], gtt[s][:], AF.Silu, after=[t_g, w["sgt"]])
            ogtoks = []
            for h in range(H):
                vs = slice(128 * h, 128 * (h + 1))
                tog = p.stt(ogb[:, vs], ot[:, vs], rs8[:, h:h + 1], sgt[:, vs],
                            OP.mult, OP.mult, after=[trr, tsg, w["ogb"]])
                ogtoks.append(tog)
            w["ogb"] = ogtoks[-1]
            w["ot"] = ogtoks[-1]
            w["sgt"] = ogtoks[-1]
            tcpl = None
            for c in range(8):
                ss = c % 2
                ttr = p.tr(ptb[ss][:], ogb[:, c * 128:(c + 1) * 128], idb,
                           after=[ogtoks[c], w.get("ptb%d" % ss)])
                tcp = p.vcopy(ogf[:, c * 128:(c + 1) * 128], ptb[ss][:],
                              after=[ttr, w["ogf"]])
                w["ptb%d" % ss] = tcp
                tcpl = tcp
            for n in range(2):
                tk = None
                for c in range(8):
                    tk = p.mm(ps0[:],
                              ogf[:, 128 * c:128 * (c + 1)],
                              wob[:, D * c + 512 * n: D * c + 512 * (n + 1)],
                              start=(c == 0), stop=(c == 7),
                              after=[tcpl, w["ps0"]] if c == 0 else (),
                              inc=(c == 7))
                tev = p.scopy(s0t[s][:, 512 * n:512 * (n + 1)], ps0[:],
                              after=[tk, w["s0t"][s]])
                w["ps0"] = tev
                w["ogf"] = tk
            td = p.dma("gpsimd", scr["s0"][t * 128:(t + 1) * 128, :], s0t[s][:],
                       after=[tev], ch="s0o%d" % s)
            w["s0t"][s] = td
            prevdma[s] = ogtoks[-1]
        p.barrier()


def emit_l0_combine(p, io, scr, cst, moe):
    yacc = moe["yacc"]
    with contextlib.ExitStack() as es:
        xt = [p.sb([128, 1024], F32, es=es) for _ in range(2)]
        s0l = [p.sb([128, 1024], F32, es=es) for _ in range(2)]
        syt = [p.sb([128, 1024], F32, es=es) for _ in range(2)]
        x1t = [p.sb([128, 1024], F32, es=es) for _ in range(2)]
        wx = [None, None]
        for t in range(NT):
            s = t % 2
            p.dma("sync", xt[s][:], io["x"][t * 128:(t + 1) * 128, :],
                  after=[wx[s]], ch="cmb%d" % s)
            tls = p.dma("sync", s0l[s][:], scr["s0"][t * 128:(t + 1) * 128, :],
                        after=[wx[s]], ch="cmb%d" % s)
            tlx = tls
            tsy = p.act(syt[s][:], yacc[:, 1024 * t:1024 * (t + 1)], AF.Silu,
                        after=[wx[s]])
            tm1 = p.tt(syt[s][:], syt[s][:], s0l[s][:], OP.mult, after=[tsy, tls])
            tx1 = p.tt(x1t[s][:], syt[s][:], xt[s][:], OP.add, after=[tm1, tlx])
            td = p.dma("gpsimd", scr["x1"][t * 128:(t + 1) * 128, :], x1t[s][:],
                       after=[tx1], ch="x1o%d" % s)
            wx[s] = td
        p.barrier()



def emit_consts(p, io):
    cst = {}
    cst["cc"] = p.sb([128, 6], F32, name="cc_t")
    cst["U2b"] = p.sb([128, 128], BF16, name="u2_t")
    cst["M2f"] = p.sb([128, 128], F32, name="m2_t")
    cst["idb"] = p.sb([128, 128], BF16, name="id_t")
    cst["idf"] = p.sb([128, 128], F32, name="idf_t")
    cst["onesr"] = p.sb([1, 128], BF16, name="or_t")
    cst["onesc"] = p.sb([128, 1], BF16, name="oc_t")
    cst["onesr512"] = p.sb([1, 512], BF16, name="o512_t")
    cst["E4"] = p.sb([128, 4], F32, name="e4_t")
    cst["E4T"] = p.sb([4, 128], F32, name="e4t_t")
    toks = []
    for k in ("cc", "U2b", "M2f", "idb", "idf", "onesr", "onesc", "onesr512", "E4", "E4T"):
        toks.append(p.dma("sync", cst[k][:], io[k], ch="cst"))
    p.barrier()
    return {k: (v[:] if k in ("U2b", "M2f", "idb", "idf", "onesr", "onesc",
                              "onesr512", "E4", "E4T") else v)
            for k, v in cst.items()}


def build_program(upto="l1", dbg=()):
    p = Prog()
    io = declare_io(p)
    scr = {}
    scr["qfm"] = p.dscr("qfm_s", [512, T], BF16)
    scr["kfm"] = p.dscr("kfm_s", [512, T], BF16)
    scr["ktm"] = p.dscr("ktm_s", [T, 512], BF16)
    scr["vtm"] = p.dscr("vtm_s", [T, 1024], BF16)
    scr["gtm"] = p.dscr("gtm_s", [T, 1024], BF16)
    scr["lsz"] = p.dscr("lsz_s", [T, 512], BF16)
    scr["s0"] = p.dscr("s0_s", [T, D], F32)
    scr["mown"] = p.dscr("mown_s", [1024, 1024], BF16)
    scr["mall"] = p.dscr("mall_s", [8192, 1024], BF16)
    scr["xn1"] = p.dscr("xn1_s", [T, D], BF16)
    scr["qfm1"] = p.dscr("qfm1_s", [D, T], BF16)
    scr["kfm1"] = p.dscr("kfm1_s", [D, T], BF16)
    scr["s1"] = p.dscr("s1_s", [T, D], F32)
    if upto == "l0":
        scr["x1"] = io["out"]
    else:
        scr["x1"] = p.dscr("x1_s", [T, D], F32)
    dbg_aps = {}
    for name in dbg:
        shp = {"qfm": [512, T], "kfm": [512, T], "ktm": [T, 512],
               "vtm": [T, 1024], "gtm": [T, 1024], "lsz": [T, 512],
               "s0": [T, D], "x1": [T, D], "xn1": [T, D], "s1": [T, D]}[name]
        dt = F32 if name in ("s0", "x1", "s1") else BF16
        dbg_aps[name] = p.dout("dbg_" + name, shp, dt)

    cst = emit_consts(p, io)
    xnfm = p.sb([128, 8 * T], BF16, name="xnfm0")
    es_moe = contextlib.ExitStack()
    moe = {
        "masks": p.sb([128, NT * 8], F32, es=es_moe),
        "yacc": p.sb([128, NT * 1024], BF16, es=es_moe),
        "xnfm": xnfm,
    }
    p.wob = p.sb([128, 8 * D], BF16, name="wob", es=p.es_wob)
    emit_moe_mown(p, io, scr, cst)
    emit_ln(p, io["x"], xnfm, cst["cc"], cst["idb"], eps_col_idx=1)
    emit_l0_proj(p, io, scr, cst, xnfm, moe)
    emit_l0_scan(p, io, scr, cst, moe)
    p.es_wob.close()
    emit_l0_combine(p, io, scr, cst, moe)
    es_moe.close()
    if upto != "l0":
        emit_l1(p, io, scr, cst, xnfm)
    if dbg_aps:
        with contextlib.ExitStack() as es:
            bt = p.sb([128, 2048], BF16, es=es)
            bt32 = p.sb([128, 1024], F32, es=es)
            dbg_last = None
            tprev = None
            for name, dst in dbg_aps.items():
                srcap = scr[name] if name != "x1" else scr["x1"]
                rows, cols = {"qfm": (512, T), "kfm": (512, T), "ktm": (T, 512),
                              "vtm": (T, 1024), "gtm": (T, 1024), "lsz": (T, 512),
                              "s0": (T, D), "x1": (T, D), "xn1": (T, D),
                              "s1": (T, D)}[name]
                tile = bt32 if name in ("s0", "x1", "s1") else bt
                for r in range(0, rows, 128):
                    t1 = p.dma("sync", tile[:, :cols], srcap[r:r + 128, :],
                               after=[tprev, dbg_last], ch="dbgi")
                    dbg_last = t1
                    tprev = p.dma("gpsimd", dst[r:r + 128, :], tile[:, :cols],
                                  after=[t1], ch="dbgo")
            p.barrier()
    return p


def kernel(**inputs):
    n = 8
    p = build_program(upto=KERNEL_UPTO, dbg=())
    nc = p.finalize()
    in_maps = [host_prep(inputs, b) for b in range(n)]
    res = run_bass_kernel_spmd(nc, in_maps, core_ids=list(range(n)))
    out = np.stack([res.results[b]["out"] for b in range(n)], axis=0)
    return out.astype(np.float32)


KERNEL_UPTO = "l1"


def emit_l1_attn(p, io, scr, cst, xnfm):
    cc = cst["cc"]
    idb = cst["idb"]
    onesr = cst["onesr"]
    onesr512 = cst["onesr512"]
    with contextlib.ExitStack() as es:
        ab8 = p.sb([128, 16], F32, es=es)
        browb = p.sb([1, 3072], BF16, es=es)
        brf = p.sb([1, 3072], F32, es=es)
        vtm = p.sb([128, NT * 1040], BF16, name="vtm1", es=es)
        otm = p.sb([128, NT * 1040], BF16, name="otm1", es=es)
        es_s = contextlib.ExitStack()
        colsum = p.sb([128, 8], F32, es=es_s)
        colsq = p.sb([128, 8], F32, es=es_s)
        sqt = [p.sb([128, T], F32, es=es_s) for _ in range(2)]
        g48 = p.sb([4, 32], F32, es=es_s)
        ps48 = p.ps([4, 8], F32, es=es_s)
        ps128 = p.ps([128, 8], F32, es=es_s)
        gamc = p.sb([128, 8], F32, es=es_s)
        betc = p.sb([128, 8], F32, es=es_s)
        tsq = [None, None]
        for c in range(8):
            s = c % 2
            t1 = p.act(sqt[s][:], xnfm[:, c * T:(c + 1) * T], AF.Square,
                       after=[tsq[s]])
            t2 = p.red(colsq[:, c:c + 1], sqt[s][:], OP.add, after=[t1])
            tsq[s] = t2
            p.red(colsum[:, c:c + 1], xnfm[:, c * T:(c + 1) * T], OP.add)
        p.barrier()
        t1 = p.mm(ps48[:], cst["E4"], colsum[:], True, True)
        t2 = p.scopy(g48[:, 0:8], ps48[:], after=[t1])
        t3 = p.mm(ps48[:], cst["E4"], colsq[:], True, True, after=[t2])
        t4 = p.scopy(g48[:, 8:16], ps48[:], after=[t3])
        inv = 1.0 / (T * 32.0)
        t5 = p.ts(g48[:, 16:24], g48[:, 0:8], inv, None, OP.mult, after=[t2, t4])
        t6 = p.ts(g48[:, 8:16], g48[:, 8:16], inv, None, OP.mult, after=[t4])
        t7 = p.tt(g48[:, 24:32], g48[:, 16:24], g48[:, 16:24], OP.mult, after=[t5])
        t8 = p.tt(g48[:, 24:32], g48[:, 8:16], g48[:, 24:32], OP.subtract,
                  after=[t6, t7])
        t9 = p.act(g48[:, 24:32], g48[:, 24:32], AF.Ln, bias=cc[0:4, 2:3], after=[t8])
        t10 = p.act(g48[:, 24:32], g48[:, 24:32], AF.Exp, scale=-0.5, after=[t9])
        t11 = p.mm(ps128[:], cst["E4T"], g48[:, 24:32], True, True, after=[t10])
        tg = p.dma("sync", gamc[:], io["gam_c"], ch="gnp")
        tb = p.dma("sync", betc[:], io["bet_c"], ch="gnp")
        t12 = p.stt(ab8[:, 0:8], ps128[:], 1.0, gamc[:], OP.mult, OP.mult,
                    after=[t11, tb])
        t13 = p.mm(ps128[:], cst["E4T"], g48[:, 16:24], True, True, after=[t12, t10])
        t14 = p.tt(ab8[:, 8:16], ps128[:], ab8[:, 0:8], OP.mult, after=[t13, t12])
        p.tt(ab8[:, 8:16], betc[:], ab8[:, 8:16], OP.subtract, after=[t14])
        p.barrier()
        es_s.close()

        es_w = contextlib.ExitStack()
        wqkvb = p.sb([128, 8 * 3072], BF16, es=es_w)
        stgq = [p.sb([128, 3072], F32, es=es_w) for _ in range(2)]
        psbr = p.ps([1, 3072], F32, es=es_w)
        stok = [None, None]
        tkbs = [None, None]
        tkb = None
        for c in range(8):
            s = c % 2
            tl = p.dma("sync", stgq[s][:], io["wqkv"][c * 128:(c + 1) * 128, :],
                       after=[stok[s], tkbs[s]], ch="wqkv%d" % s)
            for n in range(6):
                tkb = p.mm(psbr[:, 512 * n:512 * (n + 1)], ab8[:, 8 + c:9 + c],
                           stgq[s][:, 512 * n:512 * (n + 1)],
                           start=(c == 0), stop=(c == 7), after=[tl],
                           inc=(n == 5))
            tkbs[s] = tkb
            tv = p.ts(wqkvb[:, 3072 * c:3072 * (c + 1)], stgq[s][:],
                      ab8[:, c:c + 1], None, OP.mult, after=[tl], eng="vector")
            stok[s] = tv
        tbq = p.dma("sync", brf[:], io["bqkvr"], ch="bqk")
        tba = p.tt(brf[:], brf[:], psbr[:], OP.add, after=[tbq, tkb])
        tbb = p.vcopy(browb[:], brf[:], after=[tba])
        p.barrier()

        vones = vtm[:].rearrange("p (t h c) -> p t h c", h=16, c=65)[:, :, :, 64:65]
        tvo = p.memset(vones, 1.0)
        es_v1 = contextlib.ExitStack()
        psv = [p.ps([128, 512], F32, es=es_v1) for _ in range(2)]
        wpv = [None, None]
        for t in range(NT):
            for n in range(2):
                s = n
                for c in range(8):
                    p.mm(psv[s][:],
                         xnfm[:, c * T + 128 * t: c * T + 128 * (t + 1)],
                         wqkvb[:, 3072 * c + 2048 + 512 * n: 3072 * c + 2048 + 512 * (n + 1)],
                         start=(c == 0), stop=False,
                         after=[wpv[s]] if c == 0 else (), inc=False)
                tk = p.mm(psv[s][:], onesr,
                          browb[0:1, 2048 + 512 * n: 2048 + 512 * (n + 1)],
                          start=False, stop=True, after=[tbb])
                for hh in range(8):
                    h = 8 * n + hh
                    wpv[s] = p.vcopy(
                        vtm[:, 1040 * t + 65 * h: 1040 * t + 65 * h + 64],
                        psv[s][:, 64 * hh:64 * (hh + 1)], after=[tk, tvo])
        stqk = [p.sb([64, 512], BF16, es=es_v1) for _ in range(2)]
        wstq = [None, None]
        wpq = [None, None]
        for h in range(16):
            for part, dst, scl in ((0, scr["qfm1"], 0.125), (1, scr["kfm1"], 1.0)):
                base = 1024 * part + 64 * h
                for g in range(4):
                    s = g % 2
                    for c in range(8):
                        p.mm(psv[s][0:64, :],
                             wqkvb[:, 3072 * c + base: 3072 * c + base + 64],
                             xnfm[:, c * T + 512 * g: c * T + 512 * (g + 1)],
                             start=(c == 0), stop=False,
                             after=[wpv[s]] if c == 0 else (), inc=False)
                    tk = p.mm(psv[s][0:64, :], browb[0:1, base:base + 64],
                              onesr512, start=False, stop=True, after=[tbb])
                    te = p.act(stqk[s][:], psv[s][0:64, :], AF.Copy, scale=scl,
                               after=[tk, wstq[s]])
                    wpv[s] = te
                    td = p.dma("gpsimd",
                               dst[64 * h:64 * (h + 1), 512 * g:512 * (g + 1)],
                               stqk[s][:], after=[te], ch="qko%d" % s)
                    wstq[s] = td
        p.barrier()
        es_v1.close()
        es_w.close()

        es_v = contextlib.ExitStack()
        QA = [p.sb([128, T], BF16, es=es_v) for _ in range(2)]
        KA = [p.sb([128, T], BF16, es=es_v) for _ in range(2)]
        PT = p.sb([128, 8192], BF16, es=es_v)
        sqt2 = p.sb([64, T], BF16, es=es_v)
        qn2 = p.sb([1, T], F32, es=es_v)
        kn2 = p.sb([1, T], F32, es=es_v)
        km = p.sb([1, 1], F32, es=es_v)
        brow = p.sb([1, T], F32, es=es_v)
        oht = p.sb([65, 512], BF16, es=es_v)
        psSt = [p.ps([128, 1024], F32, es=es_v) for _ in range(2)]
        psO = p.ps([65, 512], F32, es=es_v)
        ptrb = p.ps([128, 128], BF16, es=es_v)
        pnrm = p.ps([1, 512], F32, es=es_v)
        tonesa = p.memset(KA[0][64:65, :], 1.0)
        tonesb = p.memset(KA[1][64:65, :], 1.0)
        tldk = [None, None]
        tSTread = [None, None]
        wPT = [None, None]
        wO = None
        woht = None
        wotm = None
        wptrb = None
        wsq = None
        wnrm = None
        wrow = None

        def load_qk(hh):
            b_ = hh % 2
            p.dma("sync", QA[b_][0:64, :], scr["qfm1"][64 * hh:64 * (hh + 1), :],
                  after=[tSTread[b_]], ch="qkl%d" % b_)
            t2 = p.dma("sync", KA[b_][0:64, :], scr["kfm1"][64 * hh:64 * (hh + 1), :],
                       after=[tSTread[b_]], ch="qkl%d" % b_)
            tldk[b_] = t2

        def bound_row(h):
            nonlocal wsq, wnrm, wrow
            b_ = h % 2
            t1 = p.act(sqt2[:], KA[b_][0:64, :], AF.Square, after=[tldk[b_], wsq])
            tlast = None
            tklast = None
            for g in range(4):
                tk = p.mm(pnrm[:], cst["onesc"][0:64, :], sqt2[:, 512 * g:512 * (g + 1)],
                          True, True, after=[t1, wnrm])
                tklast = tk
                tlast = p.vcopy(kn2[0:1, 512 * g:512 * (g + 1)], pnrm[:], after=[tk, wrow])
                wnrm = tlast
            tkm = p.red(km[:], kn2[:], OP.max, after=[tlast])
            t2 = p.act(sqt2[:], QA[b_][0:64, :], AF.Square, after=[tldk[b_], tklast])
            wsq = t2
            for g in range(4):
                tk = p.mm(pnrm[:], cst["onesc"][0:64, :], sqt2[:, 512 * g:512 * (g + 1)],
                          True, True, after=[t2, wnrm])
                tlast = p.vcopy(qn2[0:1, 512 * g:512 * (g + 1)], pnrm[:], after=[tk, wrow])
                wnrm = tlast
            t3 = p.ts(brow[:], qn2[:], km[0:1, 0:1], None, OP.mult, after=[tkm, tlast, wrow])
            t4 = p.act(brow[:], brow[:], AF.Ln, bias=cc[0:1, 4:5], after=[t3])
            t5 = p.act(brow[:], brow[:], AF.Exp, scale=0.5, after=[t4])
            t6 = p.ts(brow[:], brow[:], -1.0, 60.0, OP.mult, OP.add, after=[t5])
            wrow = t6
            tq = p.vcopy(QA[b_][64:65, :], brow[:], after=[t6, tSTread[b_]])
            return tq

        def st_sub(h, qg, tq64):
            nonlocal wO, woht, wotm, wptrb
            b_ = h % 2
            for pr in range(8):
                half = pr % 2
                toks = []
                for j in range(2):
                    kc = 2 * pr + j
                    toks.append(p.mm(
                        psSt[half][:, 512 * j:512 * (j + 1)],
                        KA[b_][0:65, 128 * kc:128 * (kc + 1)],
                        QA[b_][0:65, 512 * qg:512 * (qg + 1)],
                        start=True, stop=True,
                        after=[tq64, tonesa, tonesb, wPT[half]]))
                tex = p.act(PT[:, 1024 * pr:1024 * (pr + 1)], psSt[half][:],
                            AF.Exp, after=[toks[0], toks[1]])
                wPT[half] = tex
            tk = None
            for kc in range(16):
                tk = p.mm(psO[:],
                          vtm[:, 1040 * kc + 65 * h: 1040 * kc + 65 * (h + 1)],
                          PT[:, 512 * kc:512 * (kc + 1)],
                          start=(kc == 0), stop=(kc == 15),
                          after=[wPT[0], wPT[1], wO] if kc == 0 else (),
                          inc=(kc == 15))
            tSTread[b_] = tk
            teo = p.vcopy(oht[:], psO[:], after=[tk, woht])
            wO = teo
            for sub in range(4):
                tt_ = 4 * qg + sub
                ttr = p.tr(ptrb[:, 0:65], oht[:, 128 * sub:128 * (sub + 1)],
                           idb[0:65, 0:65], after=[teo, wptrb])
                tcp = p.vcopy(
                    otm[:, 1040 * tt_ + 65 * h: 1040 * tt_ + 65 * (h + 1)],
                    ptrb[:, 0:65], after=[ttr, wotm])
                wptrb = tcp
                woht = tcp

        load_qk(0)
        pend = None
        for h in range(16):
            tq = bound_row(h)
            for qg in range(4):
                if pend is not None:
                    st_sub(*pend)
                pend = (h, qg, tq)
                if qg == 0 and h + 1 < 16:
                    load_qk(h + 1)
        st_sub(*pend)
        p.barrier()
        es_v.close()

        es_o = contextlib.ExitStack()
        wo1b, two0, two1 = load_weight_bf16(p, io["wo1"], D, D, es_o)
        bo1f = p.sb([1, 1024], F32, es=es_o)
        bo1r = p.sb([1, 1024], BF16, es=es_o)
        tl = p.dma("sync", bo1f[:], io["bo1r"], ch="bo1")
        tbo = p.vcopy(bo1r[:], bo1f[:], after=[tl])
        bo1bc = p.sb([128, 1024], F32, es=es_o)
        psW = p.ps([128, 512], F32, es=es_o)
        tprev = None
        for n in range(2):
            tk = p.mm(psW[:], onesr, bo1r[0:1, 512 * n:512 * (n + 1)], True, True,
                      after=[tbo, tprev])
            tprev = p.scopy(bo1bc[:, 512 * n:512 * (n + 1)], psW[:], after=[tk])
        rcp16 = p.sb([128, 16], F32, es=es_o)
        o2 = p.sb([128, 1024], BF16, es=es_o)
        ofmt = p.sb([128, 1024], BF16, es=es_o)
        ptrb2 = [p.ps([128, 128], BF16, es=es_o) for _ in range(2)]
        xn1l = [p.sb([128, 1024], BF16, es=es_o) for _ in range(2)]
        s1t = [p.sb([128, 1024], F32, es=es_o) for _ in range(2)]
        wrc = None
        wo2 = None
        wofmt = None
        wpt2 = [None, None]
        wps = tprev
        ws1 = [None, None]
        wx1l = [None, None]
        for t in range(NT):
            s = t % 2
            tlx = p.dma("sync", xn1l[s][:], scr["xn1"][128 * t:128 * (t + 1), :],
                        after=[wx1l[s]], ch="xn1l%d" % s)
            ov = otm[:, 1040 * t:1040 * (t + 1)].rearrange("p (h c) -> p h c", c=65)
            trc = p.recip(rcp16[:], ov[:, :, 64], after=[wrc])
            o2toks = []
            for hh in range(16):
                tn = p.ts(o2[:, 64 * hh:64 * (hh + 1)],
                          otm[:, 1040 * t + 65 * hh: 1040 * t + 65 * hh + 64],
                          rcp16[:, hh:hh + 1], None, OP.mult,
                          after=[trc, wo2])
                o2toks.append(tn)
            wrc = o2toks[-1]
            tcpl = None
            for c in range(8):
                ss = c % 2
                ttr = p.tr(ptrb2[ss][:], o2[:, 128 * c:128 * (c + 1)], idb,
                           after=[o2toks[-1], wpt2[ss]])
                tcp = p.vcopy(ofmt[:, 128 * c:128 * (c + 1)], ptrb2[ss][:],
                              after=[ttr, wofmt])
                wpt2[ss] = tcp
                tcpl = tcp
            wo2 = tcpl
            for n in range(2):
                tk = None
                for c in range(8):
                    tk = p.mm(psW[:], ofmt[:, 128 * c:128 * (c + 1)],
                              wo1b[:, 1024 * c + 512 * n: 1024 * c + 512 * (n + 1)],
                              start=(c == 0), stop=(c == 7),
                              after=[tcpl, wps, two0, two1] if c == 0 else (),
                              inc=(c == 7))
                wofmt = tk
                ta = p.tt(s1t[s][:, 512 * n:512 * (n + 1)], psW[:],
                          bo1bc[:, 512 * n:512 * (n + 1)], OP.add,
                          after=[tk, ws1[s]])
                wps = ta
                ws1[s] = ta
            tb2 = p.tt(s1t[s][:], s1t[s][:], xn1l[s][:], OP.add, after=[tlx, ws1[s]])
            td = p.dma("gpsimd", scr["s1"][128 * t:128 * (t + 1), :], s1t[s][:],
                       after=[tb2], ch="s1o%d" % s)
            ws1[s] = td
            wx1l[s] = tb2
        p.barrier()
        es_o.close()


def emit_l1_ffn(p, io, scr, cst, xnfm):
    cc = cst["cc"]
    with contextlib.ExitStack() as es:
        gates = p.sb([128, NT * 8], F32, es=es)
        y1 = p.sb([128, NT * 1024], BF16, es=es)
        tmz = [p.memset(y1[:, :8192], 0.0), p.memset(y1[:, 8192:], 0.0)]
        es_e = contextlib.ExitStack()
        winb = p.sb([128, 8 * 2048], BF16, es=es_e)
        woutb = p.sb([128, 16 * 1024], BF16, es=es_e)
        stgw = [p.sb([128, 2048], F32, es=es_e) for _ in range(2)]
        hfm = p.sb([128, 16 * 512], BF16, es=es_e)
        wstg = [None, None]

        def conv_copy(eng, out, in_, after):
            if eng == "scalar":
                return p.act(out, in_, AF.Copy, after=after)
            return p.vcopy(out, in_, after=after, eng=eng)

        def load_expert(e, win_cons, wout_cons, engs=("scalar", "vector")):
            win_rdy = [None, None]
            wout_rdy = [None, None]
            for c in range(8):
                s = c % 2
                tl = p.dma("sync", stgw[s][:],
                           io["win"][e, 128 * c:128 * (c + 1), :],
                           after=[wstg[s]], ch="wie%d" % s)
                win_rdy[s] = conv_copy(engs[0], winb[:, 2048 * c:2048 * (c + 1)],
                                       stgw[s][:], [tl, win_cons])
                wstg[s] = win_rdy[s]
            for c in range(16):
                s = c % 2
                tl = p.dma("sync", stgw[s][:, 0:1024],
                           io["wout"][e, 128 * c:128 * (c + 1), :],
                           after=[wstg[s]], ch="wie%d" % s)
                wout_rdy[s] = conv_copy(engs[1], woutb[:, 1024 * c:1024 * (c + 1)],
                                        stgw[s][:, 0:1024], [tl, wout_cons])
                wstg[s] = wout_rdy[s]
            return win_rdy, wout_rdy

        es_g = contextlib.ExitStack()
        wg1b, twg0, twg1 = load_weight_bf16(p, io["wg1"], D, 256, es_g)
        wg2b = p.sb([128, 16], BF16, es=es_g)
        stg2 = p.sb([128, 8], F32, es=es_g)
        bg1c = p.sb([128, 2], F32, es=es_g)
        tbg = p.dma("sync", bg1c[:], io["bg1c"], ch="bg1")
        tw2 = None
        for c in range(2):
            tl = p.dma("sync", stg2[:], io["wg2"][c * 128:(c + 1) * 128, :],
                       after=[tw2], ch="wg2l")
            tw2 = p.vcopy(wg2b[:, 8 * c:8 * (c + 1)], stg2[:], after=[tl])
        g1fm = p.sb([128, 2 * T], BF16, es=es_g)
        psg = [p.ps([128, 512], F32, es=es_g) for _ in range(2)]
        wg = [None, None]
        for mc in range(2):
            for g in range(4):
                s = (mc * 4 + g) % 2
                tk = None
                for c in range(8):
                    tk = p.mm(psg[s][:],
                              wg1b[:, 256 * c + 128 * mc: 256 * c + 128 * (mc + 1)],
                              xnfm[:, c * T + 512 * g: c * T + 512 * (g + 1)],
                              start=(c == 0), stop=(c == 7),
                              after=[wg[s], twg0, twg1] if c == 0 else (),
                              inc=(c == 7))
                te = p.act(g1fm[:, T * mc + 512 * g: T * mc + 512 * (g + 1)],
                           psg[s][:], AF.Gelu, bias=bg1c[:, mc:mc + 1],
                           after=[tk, tbg])
                wg[s] = te
        psl = [p.ps([128, 8], F32, es=es_g) for _ in range(2)]
        prb = [p.sb([128, 8], F32, es=es_g) for _ in range(2)]
        aux = [p.sb([128, 12], F32, es=es_g) for _ in range(2)]
        junk8 = p.sb([128, 8], BF16, es=es_g)
        wpl = [None, None]
        gate_tok = [None] * NT
        for t in range(NT):
            s = t % 2
            tk = None
            for c in range(2):
                tk = p.mm(psl[s][:], g1fm[:, T * c + 128 * t: T * c + 128 * (t + 1)],
                          wg2b[:, 8 * c:8 * (c + 1)], start=(c == 0), stop=(c == 1),
                          after=[wg[0], wg[1], tw2, wpl[s]] if c == 0 else (),
                          inc=(c == 1))
            t1 = p.red(aux[s][:, 0:1], psl[s][:], OP.max, negate=True,
                       after=[tk, wpl[s]])
            t2 = p.act(junk8[:], psl[s][:], AF.Exp, bias=aux[s][:, 0:1],
                       accum_out=aux[s][:, 1:2], after=[t1])
            t3 = p.recip(aux[s][:, 2:3], aux[s][:, 1:2], after=[t2])
            t4 = p.act(prb[s][:], psl[s][:], AF.Exp, bias=aux[s][:, 0:1], after=[t2])
            t5 = p.ts(prb[s][:], prb[s][:], aux[s][:, 2:3], None, OP.mult,
                      after=[t3, t4])
            t6 = p.red(aux[s][:, 3:4], prb[s][:], OP.max, after=[t5])
            t7 = p.ts(aux[s][:, 4:12], prb[s][:], aux[s][:, 3:4], None, OP.subtract,
                      after=[t6])
            t8 = p.ts(aux[s][:, 4:12], aux[s][:, 4:12], 0.0, None, OP.is_ge,
                      after=[t7])
            t9 = p.stt(aux[s][:, 4:12], aux[s][:, 4:12], -1e30, prb[s][:],
                       OP.mult, OP.add, after=[t8])
            t10 = p.red(aux[s][:, 3:4], aux[s][:, 4:12], OP.max, after=[t9])
            t11 = p.ts(aux[s][:, 4:12], prb[s][:], aux[s][:, 3:4], None, OP.is_ge,
                       after=[t10])
            t12 = p.tt(gates[:, 8 * t:8 * (t + 1)], prb[s][:], aux[s][:, 4:12],
                       OP.mult, after=[t11])
            gate_tok[t] = t12
            wpl[s] = t12
        win_rdy, wout_rdy = load_expert(0, None, None)
        es_g.close()
        gates_done = [wg[0], wg[1], wpl[0], wpl[1]]
        psh = [p.ps([128, 512], F32, es=es_e) for _ in range(2)]
        pso = [p.ps([128, 1024], F32, es=es_e) for _ in range(2)]
        win_cons = None
        wout_cons = None
        wh = [None, None]
        wo_ = [None, None]
        first_mm = True
        for e in range(8):
            for tg in range(4):
                hlast = None
                for j in range(16):
                    s = j % 2
                    tk = None
                    for c in range(8):
                        tk = p.mm(psh[s][:],
                                  winb[:, 2048 * c + 128 * j: 2048 * c + 128 * (j + 1)],
                                  xnfm[:, c * T + 512 * tg: c * T + 512 * (tg + 1)],
                                  start=(c == 0), stop=(c == 7),
                                  after=([win_rdy[0], win_rdy[1], wh[s]]
                                         + (gates_done if first_mm else []))
                                  if c == 0 else (),
                                  inc=(c == 7))
                        first_mm = False
                    win_cons = tk
                    te = p.act(hfm[:, 512 * j:512 * (j + 1)], psh[s][:], AF.Gelu,
                               after=[tk])
                    wh[s] = te
                    hlast = te
                for sub in range(4):
                    t_ = 4 * tg + sub
                    s = sub % 2
                    tk = None
                    for j in range(16):
                        for n in range(2):
                            tk = p.mm(pso[s][:, 512 * n:512 * (n + 1)],
                                      hfm[:, 512 * j + 128 * sub: 512 * j + 128 * (sub + 1)],
                                      woutb[:, 1024 * j + 512 * n: 1024 * j + 512 * (n + 1)],
                                      start=(j == 0), stop=(j == 15),
                                      after=[hlast, wout_rdy[0], wout_rdy[1], wo_[s]]
                                      if (j == 0 and n == 0) else (),
                                      inc=(j == 15 and n == 1))
                    wout_cons = tk
                    tac = p.stt(y1[:, 1024 * t_:1024 * (t_ + 1)], pso[s][:],
                                gates[:, 8 * t_ + e: 8 * t_ + e + 1],
                                y1[:, 1024 * t_:1024 * (t_ + 1)], OP.mult, OP.add,
                                after=[tk, gate_tok[t_]] + (tmz if e == 0 else []))
                    wo_[s] = tac
            if e < 7:
                win_rdy, wout_rdy = load_expert(e + 1, win_cons, wout_cons)
        p.barrier()
        es_e.close()
        x1t = [p.sb([128, 1024], F32, es=es) for _ in range(2)]
        s1l = [p.sb([128, 1024], F32, es=es) for _ in range(2)]
        syt = [p.sb([128, 1024], F32, es=es) for _ in range(2)]
        ot = [p.sb([128, 1024], F32, es=es) for _ in range(2)]
        wx = [None, None]
        for t in range(NT):
            s = t % 2
            p.dma("sync", x1t[s][:], scr["x1"][t * 128:(t + 1) * 128, :],
                  after=[wx[s]], ch="fc%d" % s)
            tls = p.dma("sync", s1l[s][:], scr["s1"][t * 128:(t + 1) * 128, :],
                        after=[wx[s]], ch="fc%d" % s)
            tsy = p.act(syt[s][:], y1[:, 1024 * t:1024 * (t + 1)], AF.Silu,
                        after=[wx[s]])
            tm1 = p.tt(syt[s][:], syt[s][:], s1l[s][:], OP.mult, after=[tsy, tls])
            tx2 = p.tt(ot[s][:], syt[s][:], x1t[s][:], OP.add, after=[tm1, tls])
            td = p.dma("gpsimd", io["out"][t * 128:(t + 1) * 128, :], ot[s][:],
                       after=[tx2], ch="oo%d" % s)
            wx[s] = td
        p.barrier()


def emit_l1(p, io, scr, cst, xnfm):
    emit_ln(p, scr["x1"], xnfm, cst["cc"], cst["idb"], eps_col_idx=1,
            xn_tm_dram=scr["xn1"])
    emit_l1_attn(p, io, scr, cst, xnfm)
    emit_l1_ffn(p, io, scr, cst, xnfm)


# version 18
# speedup vs baseline: 1.1246x; 1.0306x over previous
import contextlib
import numpy as np
import ml_dtypes

import concourse.bass as bass
import concourse.mybir as mybir
from concourse.alu_op_type import AluOpType as OP
from concourse.bass_utils import run_bass_kernel_spmd

F32 = mybir.dt.float32
BF16 = mybir.dt.bfloat16
AF = mybir.ActivationFunctionType
AX = mybir.AxisListType
BF = ml_dtypes.bfloat16

ENG = ("sync", "tensor", "vector", "scalar", "gpsimd")
AUTO_SELF = {"vector", "scalar", "gpsimd"}

T = 2048
D = 1024
NT = T // 128
H = 8
DK = 64
DV = 128
LNR8 = -2.0794415416798357


class Prog:

    def __init__(self):
        self.nc = bass.Bass("TRN2", target_bir_lowering=False)
        self.es = contextlib.ExitStack()
        self.es_wob = self.es.enter_context(contextlib.ExitStack())
        self.sched = {e: [] for e in ENG}
        self.ctr = {e: 0 for e in ENG}
        self.chctr = {}
        self.waited = {c: {} for c in ENG}
        self._n = 0

    def _nm(self, pfx):
        self._n += 1
        return f"{pfx}{self._n}"

    def sb(self, shape, dt=F32, name=None, es=None):
        return (es or self.es).enter_context(
            self.nc.sbuf_tensor(name or self._nm("sb"), shape, dt))

    def ps(self, shape, dt=F32, name=None, es=None):
        return (es or self.es).enter_context(
            self.nc.psum_tensor(name or self._nm("ps"), shape, dt))

    def din(self, name, shape, dt=F32):
        return self.nc.dram_tensor(name, shape, dt, kind="ExternalInput").ap()

    def dout(self, name, shape, dt=F32):
        return self.nc.dram_tensor(name, shape, dt, kind="ExternalOutput").ap()

    def dscr(self, name, shape, dt=F32):
        return self.nc.dram_tensor(name, shape, dt, kind="Internal").ap()

    def _wait(self, eng, sem, v):
        if self.waited[eng].get(sem, 0) < v:
            self.sched[eng].append(("w", sem, v))
            self.waited[eng][sem] = v

    def op(self, eng, fn, after=(), inc=True, ch=None):
        sch = self.sched[eng]
        if eng in AUTO_SELF and self.ctr[eng] > self.waited[eng].get(eng, 0):
            self._wait(eng, eng, self.ctr[eng])
        for tok in after:
            if tok is None:
                continue
            self._wait(eng, tok[0], tok[1])
        if ch is not None:
            self.chctr[ch] = self.chctr.get(ch, 0) + 16
            sch.append(("op", fn, ("c", ch, self.chctr[ch])))
            return (ch, self.chctr[ch])
        if inc:
            self.ctr[eng] += 1
            sch.append(("op", fn, ("e", eng, self.ctr[eng])))
            return (eng, self.ctr[eng])
        sch.append(("op", fn, None))
        return None

    def tok(self, eng):
        return (eng, self.ctr[eng])

    def barrier(self):
        toks = [(e, self.ctr[e]) for e in ENG]
        toks += [(c, v) for c, v in self.chctr.items()]
        for e in ENG:
            for pe, v in toks:
                if v > 0:
                    self._wait(e, pe, v)

    def finalize(self):
        self.barrier()
        sems = {e: self.es.enter_context(self.nc.semaphore("S" + e)) for e in ENG}
        for c in self.chctr:
            sems[c] = self.es.enter_context(self.nc.semaphore("C" + c))
        block = self.es.enter_context(self.nc.Block())

        def emit2(eng_name):
            def body(engine):
                for item in self.sched[eng_name]:
                    if item[0] == "w":
                        engine.wait_ge(sems[item[1]], item[2])
                    else:
                        inst = item[1](engine)
                        if item[2] is not None:
                            kind, nm, val = item[2]
                            inst.then_inc(sems[nm], 16 if kind == "c" else 1)
            return body

        block.sync(emit2("sync"))
        block.tensor(emit2("tensor"))
        block.vector(emit2("vector"))
        block.scalar(emit2("scalar"))
        block.gpsimd(emit2("gpsimd"))
        self.es.close()
        return self.nc

    def dma(self, eng, out, in_, after=(), ch="c0"):
        return self.op(eng, lambda e, o=out, i=in_: e.dma_start(out=o, in_=i),
                       after=after, ch=ch)

    def mm(self, out, lhsT, rhs, start, stop, after=(), inc=True):
        return self.op("tensor",
                       lambda e, o=out, l=lhsT, r=rhs, s1=start, s2=stop:
                       e.matmul(o, l, r, start=s1, stop=s2),
                       after=after, inc=inc)

    def tr(self, out, in_, ident, after=(), inc=True):
        return self.op("tensor",
                       lambda e, o=out, i=in_, d=ident: e.transpose(o, i, d),
                       after=after, inc=inc)

    def act(self, out, in_, func, bias=None, scale=1.0, accum_out=None, after=()):
        def fn(e, o=out, i=in_, f=func, b=bias, s=scale, a=accum_out):
            kw = dict(scale=s)
            if b is not None:
                kw["bias"] = b
            if a is not None:
                kw["accum_out"] = a
            return e.activation(o, i, f, **kw)
        return self.op("scalar", fn, after=after)

    def scopy(self, out, in_, after=()):
        return self.op("scalar", lambda e, o=out, i=in_: e.copy(o, i), after=after)

    def tt(self, out, in0, in1, op, after=(), eng="vector"):
        return self.op(eng, lambda e, o=out, a=in0, b=in1, p=op:
                       e.tensor_tensor(o, a, b, p), after=after)

    def ts(self, out, in0, s1, s2, op0, op1=None, after=(), accum_out=None, eng="vector"):
        def fn(e, o=out, i=in0, a=s1, b=s2, p0=op0, p1=op1, ac=accum_out):
            kw = {}
            if p1 is not None:
                kw["op1"] = p1
            if ac is not None:
                kw["accum_out"] = ac
            return e.tensor_scalar(o, i, a, b, p0, **kw)
        return self.op(eng, fn, after=after)

    def stt(self, out, in0, scalar, in1, op0, op1, after=()):
        return self.op("vector", lambda e, o=out, i0=in0, s=scalar, i1=in1, p0=op0, p1=op1:
                       e.scalar_tensor_tensor(o, i0, s, i1, p0, p1), after=after)

    def vcopy(self, out, in_, after=(), eng="vector"):
        return self.op(eng, lambda e, o=out, i=in_: e.tensor_copy(o, i), after=after)

    def red(self, out, in_, op, negate=False, after=()):
        return self.op("vector", lambda e, o=out, i=in_, p=op, n=negate:
                       e.tensor_reduce(o, i, axis=AX.X, op=p, negate=n), after=after)

    def recip(self, out, in_, after=()):
        return self.op("vector", lambda e, o=out, i=in_: e.reciprocal(o, i), after=after)

    def memset(self, ap, val, after=(), eng="gpsimd"):
        return self.op(eng, lambda e, a=ap, v=val: e.memset(a, v), after=after)



def host_prep(inputs, core):
    f32 = np.float32
    bf = BF
    d = {}
    d["x"] = np.ascontiguousarray(inputs["L0_attn_x"][core], f32)
    d["wq"] = np.ascontiguousarray(inputs["L0_attn_Wq"], f32)
    d["wk"] = np.ascontiguousarray(inputs["L0_attn_Wk"], f32)
    d["wv"] = np.ascontiguousarray(inputs["L0_attn_Wv"], f32)
    d["wg"] = np.ascontiguousarray(inputs["L0_attn_Wg"], f32)
    d["wgk1"] = np.ascontiguousarray(inputs["L0_attn_Wgk1"], f32)
    d["wgk2"] = np.ascontiguousarray(inputs["L0_attn_Wgk2"], f32)
    d["bgk2r"] = np.ascontiguousarray(inputs["L0_attn_bgk2"].reshape(1, 512), f32)
    d["nwc"] = np.ascontiguousarray(inputs["L0_attn_norm_w"].reshape(128, 1), f32)
    d["wo"] = np.ascontiguousarray(inputs["L0_attn_Wo"], f32)
    d["gwT"] = np.ascontiguousarray(inputs["L0_ffn_gate_w"].T, f32)
    wupT = np.ascontiguousarray(inputs["L0_ffn_w_up"].transpose(0, 2, 1), f32)
    d["wupT_own"] = np.ascontiguousarray(wupT[core])
    d["wdn_own"] = np.ascontiguousarray(inputs["L0_ffn_w_down"][core], f32)
    d["gam_c"] = np.ascontiguousarray(inputs["L1_attn_gn_gamma"].reshape(8, 128).T, f32)
    d["bet_c"] = np.ascontiguousarray(inputs["L1_attn_gn_beta"].reshape(8, 128).T, f32)
    d["wqkv"] = np.ascontiguousarray(inputs["L1_attn_W_qkv"], f32)
    d["bqkvr"] = np.ascontiguousarray(inputs["L1_attn_b_qkv"].reshape(1, 3072), f32)
    d["wo1"] = np.ascontiguousarray(inputs["L1_attn_W_o"], f32)
    d["bo1r"] = np.ascontiguousarray(inputs["L1_attn_b_o"].reshape(1, 1024), f32)
    d["wg1"] = np.ascontiguousarray(inputs["L1_ffn_wg1"], f32)
    d["bg1c"] = np.ascontiguousarray(inputs["L1_ffn_bg1"].reshape(2, 128).T, f32)
    d["wg2"] = np.ascontiguousarray(inputs["L1_ffn_wg2"], f32)
    d["win"] = np.ascontiguousarray(inputs["L1_ffn_w_in"], f32)
    d["wout"] = np.ascontiguousarray(inputs["L1_ffn_w_out"], f32)
    tri = np.triu(np.ones((64, 64), f32))
    u2 = np.zeros((128, 128), f32)
    u2[:64, :64] = tri / 16.0
    u2[64:, 64:] = tri / 16.0
    d["U2b"] = u2.astype(bf)
    m2 = np.zeros((128, 128), f32)
    m2[:64, :64] = tri
    m2[64:, 64:] = tri
    d["M2f"] = m2
    d["idb"] = np.eye(128, dtype=f32).astype(bf)
    d["idf"] = np.eye(128, dtype=f32)
    d["onesr"] = np.ones((1, 128), f32).astype(bf)
    d["onesc"] = np.ones((128, 1), f32).astype(bf)
    cc = np.zeros((128, 6), f32)
    cc[:, 0] = 1.0
    cc[:, 1] = 1e-5
    cc[:, 2] = 1e-6
    cc[:, 3] = LNR8
    cc[:, 4] = 1e-30
    d["cc"] = cc
    d["onesr512"] = np.ones((1, 512), f32).astype(bf)
    e4 = np.zeros((128, 4), f32)
    for j in range(4):
        e4[32 * j:32 * (j + 1), j] = 1.0
    d["E4"] = e4
    d["E4T"] = np.ascontiguousarray(e4.T)
    return d


def declare_io(p):
    io = {}
    io["x"] = p.din("x", [T, D])
    io["wq"] = p.din("wq", [D, 512]); io["wk"] = p.din("wk", [D, 512])
    io["wv"] = p.din("wv", [D, D]); io["wg"] = p.din("wg", [D, D])
    io["wgk1"] = p.din("wgk1", [D, 16]); io["wgk2"] = p.din("wgk2", [16, 512])
    io["bgk2r"] = p.din("bgk2r", [1, 512]); io["nwc"] = p.din("nwc", [128, 1])
    io["wo"] = p.din("wo", [D, D]); io["gwT"] = p.din("gwT", [D, 8])
    io["wupT_own"] = p.din("wupT_own", [4096, D])
    io["wdn_own"] = p.din("wdn_own", [4096, D])
    io["gam_c"] = p.din("gam_c", [128, 8]); io["bet_c"] = p.din("bet_c", [128, 8])
    io["wqkv"] = p.din("wqkv", [D, 3072]); io["bqkvr"] = p.din("bqkvr", [1, 3072])
    io["wo1"] = p.din("wo1", [D, D]); io["bo1r"] = p.din("bo1r", [1, D])
    io["wg1"] = p.din("wg1", [D, 256]); io["bg1c"] = p.din("bg1c", [128, 2])
    io["wg2"] = p.din("wg2", [256, 8])
    io["win"] = p.din("win", [8, D, 2048]); io["wout"] = p.din("wout", [8, 2048, D])
    io["U2b"] = p.din("U2b", [128, 128], BF16)
    io["M2f"] = p.din("M2f", [128, 128])
    io["idb"] = p.din("idb", [128, 128], BF16)
    io["idf"] = p.din("idf", [128, 128])
    io["onesr"] = p.din("onesr", [1, 128], BF16)
    io["onesc"] = p.din("onesc", [128, 1], BF16)
    io["cc"] = p.din("cc", [128, 6])
    io["onesr512"] = p.din("onesr512", [1, 512], BF16)
    io["E4"] = p.din("E4", [128, 4])
    io["E4T"] = p.din("E4T", [4, 128])
    io["out"] = p.dout("out", [T, D])
    return io



def load_weight_bf16(p, src, rows, cols, es, scale_col=None, prev=None):
    nch = rows // 128
    chp = p._nm("lw")
    wbf = p.sb([128, nch * cols], BF16, es=es)
    stg = [p.sb([128, cols], F32, es=es), p.sb([128, cols], F32, es=es)]
    stok = [None, None]
    for c in range(nch):
        s = c % 2
        tl = p.dma("sync", stg[s][:], src[c * 128:(c + 1) * 128, :],
                   after=[stok[s], prev], ch=chp + str(s))
        if scale_col is None:
            tv = p.vcopy(wbf[:, c * cols:(c + 1) * cols], stg[s][:], after=[tl],
                         eng="gpsimd")
        else:
            tv = p.ts(wbf[:, c * cols:(c + 1) * cols], stg[s][:], scale_col, None,
                      OP.mult, after=[tl], eng="gpsimd")
        stok[s] = tv
    return wbf, stok[0], stok[1]


def load_row_bf16(p, src, cols, es):
    stg = p.sb([1, cols], F32, es=es)
    row = p.sb([1, cols], BF16, es=es)
    tl = p.dma("sync", stg[:], src, ch=p._nm("lr"))
    tv = p.vcopy(row[:], stg[:], after=[tl])
    return row, tv



def emit_ln(p, src_dram, xnfm, cc, idb, eps_col_idx=1, xn_tm_dram=None):
    with contextlib.ExitStack() as es:
        xt = [p.sb([128, D], F32, es=es) for _ in range(2)]
        junk = p.sb([128, D], BF16, es=es)
        xnb = [p.sb([128, D], BF16, es=es) for _ in range(2)]
        colp = [p.sb([128, 8], F32, es=es) for _ in range(2)]
        ptb = [p.ps([128, 128], BF16, es=es) for _ in range(2)]
        tx = [None, None]
        txn = [None, None]
        tpt = [None, None]
        tj = None
        for t in range(NT):
            s = t % 2
            ctl = colp[s]
            tl = p.dma("sync", xt[s][:], src_dram[t * 128:(t + 1) * 128, :],
                       after=[tx[s]], ch="lnx%d" % s)
            tssq = p.act(junk[:], xt[s][:], AF.Square, accum_out=ctl[:, 1:2],
                         after=[tl, tj, tx[s]])
            tj = tssq
            tsum = p.red(ctl[:, 0:1], xt[s][:], OP.add, after=[tl, tx[s]])
            tm = p.ts(ctl[:, 2:3], ctl[:, 0:1], 1.0 / D, None, OP.mult, after=[tsum])
            tq = p.ts(ctl[:, 3:4], ctl[:, 1:2], 1.0 / D, None, OP.mult, after=[tssq])
            tmm = p.tt(ctl[:, 4:5], ctl[:, 2:3], ctl[:, 2:3], OP.mult, after=[tm])
            tvar = p.tt(ctl[:, 4:5], ctl[:, 3:4], ctl[:, 4:5], OP.subtract,
                        after=[tq, tmm])
            tln = p.act(ctl[:, 5:6], ctl[:, 4:5], AF.Ln,
                        bias=cc[:, eps_col_idx:eps_col_idx + 1], after=[tvar])
            trr = p.act(ctl[:, 5:6], ctl[:, 5:6], AF.Exp, scale=-0.5, after=[tln])
            tmr = p.tt(ctl[:, 6:7], ctl[:, 2:3], ctl[:, 5:6], OP.mult, after=[trr])
            txnb = p.ts(xnb[s][:], xt[s][:], ctl[:, 5:6], ctl[:, 6:7],
                        OP.mult, OP.subtract, after=[tmr, txn[s]])
            tx[s] = txnb
            if xn_tm_dram is not None:
                txn_d = p.dma("gpsimd", xn_tm_dram[t * 128:(t + 1) * 128, :],
                              xnb[s][:], after=[txnb], ch="lnxo%d" % s)
            else:
                txn_d = None
            last = txnb
            for c in range(8):
                ss = c % 2
                ttr = p.tr(ptb[ss][:], xnb[s][:, c * 128:(c + 1) * 128], idb,
                           after=[txnb, tpt[ss]])
                tcp = p.vcopy(xnfm[:, c * T + t * 128: c * T + (t + 1) * 128],
                              ptb[ss][:], after=[ttr])
                tpt[ss] = tcp
                last = tcp
            txn[s] = last if txn_d is None else txn_d
        p.barrier()


def emit_moe_mown(p, io, scr, cst):
    with contextlib.ExitStack() as es:
        wdnb = p.sb([128, 32 * 1024], BF16, es=es)
        wupb = [p.sb([128, 4096], BF16, es=es) for _ in range(2)]
        stgw = [p.sb([128, 1024], F32, es=es) for _ in range(2)]
        stgu = [p.sb([128, 2048], F32, es=es) for _ in range(2)]
        mstg = p.sb([128, 1024], BF16, es=es)
        pM = [p.ps([128, 1024], F32, es=es) for _ in range(2)]
        wstg = [None, None]
        stu_tok = [None, None]

        def cast_copy(eng, out, in_, after):
            if eng == "scalar":
                return p.act(out, in_, AF.Copy, after=after)
            return p.vcopy(out, in_, after=after, eng=eng)

        wdn_rdy = [None, None]
        for c in range(32):
            s = c % 2
            tl = p.dma("gpsimd", stgw[s][:],
                       io["wdn_own"][c * 128:(c + 1) * 128, :], after=[wstg[s]],
                       ch="wdn%d" % s)
            wdn_rdy[s] = cast_copy("vector" if s == 0 else "scalar",
                                   wdnb[:, 1024 * c:1024 * (c + 1)], stgw[s][:],
                                   [tl])
            wstg[s] = wdn_rdy[s]
        upcv = [None, None]
        upcons = [None, None]

        def load_up(mi):
            bm = mi % 2
            wsrc = io["wupT_own"].rearrange("(k p) c -> p k c", p=128)
            wsrc = wsrc[:, :, 128 * mi:128 * (mi + 1)]
            tcvs = []
            for hh in range(2):
                sl = wsrc[:, 16 * hh:16 * (hh + 1), :]
                tl = p.dma("gpsimd", stgu[hh][:], sl,
                           after=[stu_tok[hh], upcons[bm]], ch="wup%d" % hh)
                tcv = cast_copy("vector" if hh == 0 else "scalar",
                                wupb[bm][:, 2048 * hh:2048 * (hh + 1)],
                                stgu[hh][:], [tl, upcons[bm]])
                stu_tok[hh] = tcv
                tcvs.append(tcv)
            upcv[bm] = tcvs

        load_up(0)
        wpM = [None, None]
        wms = None
        for m in range(8):
            sm = m % 2
            if m < 7:
                load_up(m + 1)
            tk = None
            for k in range(32):
                for n in range(2):
                    tk = p.mm(pM[sm][:, 512 * n:512 * (n + 1)],
                              wupb[m % 2][:, 128 * k:128 * (k + 1)],
                              wdnb[:, 1024 * k + 512 * n: 1024 * k + 512 * (n + 1)],
                              start=(k == 0), stop=(k == 31),
                              after=([*upcv[m % 2], wdn_rdy[0], wdn_rdy[1],
                                      wpM[sm]] if k == 0 else ()),
                              inc=(k == 31 and n == 1))
            upcons[m % 2] = tk
            tev = p.act(mstg[:], pM[sm][:], AF.Copy, after=[tk, wms])
            wpM[sm] = tev
            wms = p.dma("gpsimd", scr["mown"][m * 128:(m + 1) * 128, :], mstg[:],
                        after=[tev], ch="mow")
        p.barrier()
        tcc = p.op("gpsimd",
                   lambda e: e.collective_compute(
                       "AllGather", mybir.AluOpType.bypass,
                       replica_groups=[list(range(8))],
                       ins=[scr["mown"].opt()],
                       outs=[scr["mall"].opt()]),
                   after=[wms])
        p.mall_tok = tcc


def emit_l0_proj(p, io, scr, cst, xnfm, moe):
    cc, onesr = cst["cc"], cst["onesr"]
    with contextlib.ExitStack() as es:
        masks = moe["masks"]
        p.memset(moe["yacc"][:, :8192], 0.0)
        p.memset(moe["yacc"][:, 8192:], 0.0)
        gwTb = p.sb([128, 64], BF16, es=es)
        stg8 = p.sb([128, 8], F32, es=es)
        tgw = None
        for c in range(8):
            tl = p.dma("sync", stg8[:], io["gwT"][c * 128:(c + 1) * 128, :],
                       after=[tgw], ch="gwt")
            tgw = p.vcopy(gwTb[:, 8 * c:8 * (c + 1)], stg8[:], after=[tl])
        pl = p.ps([128, 8], F32, es=es)
        lmt = p.sb([128, 8], F32, es=es)
        ismx = p.sb([128, 8], F32, es=es)
        l2t = p.sb([128, 8], F32, es=es)
        mxc = p.sb([128, 2], F32, es=es)
        wpl = None
        for t in range(NT):
            tk = None
            for c in range(8):
                tk = p.mm(pl[:], xnfm[:, c * T + 128 * t: c * T + 128 * (t + 1)],
                          gwTb[:, 8 * c:8 * (c + 1)], start=(c == 0), stop=(c == 7),
                          after=[tgw, wpl] if c == 0 else (), inc=(c == 7))
            t1 = p.red(mxc[:, 0:1], pl[:], OP.max, after=[tk])
            t2 = p.ts(lmt[:], pl[:], mxc[:, 0:1], None, OP.subtract, after=[t1])
            t3 = p.ts(ismx[:], lmt[:], 0.0, None, OP.is_ge, after=[t2])
            t4 = p.stt(l2t[:], ismx[:], -1e30, pl[:], OP.mult, OP.add, after=[t3])
            t5 = p.red(mxc[:, 1:2], l2t[:], OP.max, after=[t4])
            t6 = p.ts(masks[:, 8 * t:8 * (t + 1)], pl[:], mxc[:, 1:2], None,
                      OP.is_ge, after=[t5])
            wpl = t6
        wqb, _, _ = load_weight_bf16(p, io["wq"], D, 512, es)
        wkb, _, _ = load_weight_bf16(p, io["wk"], D, 512, es)
        wvb, _, _ = load_weight_bf16(p, io["wv"], D, D, es)
        wgb, _, _ = load_weight_bf16(p, io["wg"], D, D, es)
        nwcs = p.sb([128, 1], F32, es=es)
        tnw = p.dma("sync", nwcs[:], io["nwc"], ch="nwc")
        wob = p.wob
        stgo = [p.sb([128, D], F32, es=es) for _ in range(2)]
        stok = [None, None]
        for c in range(8):
            s = c % 2
            tl = p.dma("sync", stgo[s][:], io["wo"][c * 128:(c + 1) * 128, :],
                       after=[stok[s]], ch="wo%d" % s)
            tv = p.ts(wob[:, c * D:(c + 1) * D], stgo[s][:], nwcs[:], None,
                      OP.mult, after=[tl, tnw], eng="gpsimd")
            stok[s] = tv
        wgk1b, _, _ = load_weight_bf16(p, io["wgk1"], D, 16, es)
        wgk2b = p.sb([16, 512], BF16, es=es)
        stg2 = p.sb([16, 512], F32, es=es)
        t2 = p.dma("sync", stg2[:], io["wgk2"], ch="wgk2")
        p.vcopy(wgk2b[:], stg2[:], after=[t2])
        bgk2b, tbg = load_row_bf16(p, io["bgk2r"], 512, es)
        p.barrier()

        psA = [p.ps([128, 512], F32, es=es) for _ in range(2)]
        psB = [p.ps([128, 1024], F32, es=es) for _ in range(2)]
        stgb = [p.sb([128, 1024], BF16, es=es) for _ in range(2)]
        ptoks = [None, None]
        etoksA = [None, None]
        etoksB = [None, None]
        for (wb, dst) in ((wqb, scr["qfm"]), (wkb, scr["kfm"])):
            for m in range(4):
                for g in range(4):
                    s = (m * 4 + g) % 2
                    tk = None
                    for c in range(8):
                        tk = p.mm(psA[s][:],
                                  wb[:, 512 * c + 128 * m: 512 * c + 128 * (m + 1)],
                                  xnfm[:, c * T + 512 * g: c * T + 512 * (g + 1)],
                                  start=(c == 0), stop=(c == 7),
                                  after=[etoksA[s]] if c == 0 else (), inc=(c == 7))
                    te = p.act(stgb[s][:, :512], psA[s][:], AF.Copy,
                               after=[tk, ptoks[s]])
                    td = p.dma("gpsimd",
                               dst[128 * m:128 * (m + 1), 512 * g:512 * (g + 1)],
                               stgb[s][:, :512], after=[te], ch="pev%d" % s)
                    ptoks[s] = td
                    etoksA[s] = te
        for (wb, ncols, dst) in ((wkb, 512, scr["ktm"]), (wvb, D, scr["vtm"]),
                                 (wgb, D, scr["gtm"])):
            ngr = ncols // 512
            for t in range(NT):
                s = t % 2
                ps = psB[s]
                tk = None
                for n in range(ngr):
                    for c in range(8):
                        tk = p.mm(ps[:, 512 * n:512 * (n + 1)],
                                  xnfm[:, c * T + 128 * t: c * T + 128 * (t + 1)],
                                  wb[:, ncols * c + 512 * n: ncols * c + 512 * (n + 1)],
                                  start=(c == 0), stop=(c == 7),
                                  after=[etoksB[s]] if (c == 0 and n == 0) else (),
                                  inc=(c == 7 and n == ngr - 1))
                te = p.act(stgb[s][:, :ncols], ps[:, :ncols], AF.Copy,
                           after=[tk, ptoks[s]])
                td = p.dma("gpsimd", dst[128 * t:128 * (t + 1), :],
                           stgb[s][:, :ncols], after=[te], ch="pev%d" % s)
                ptoks[s] = td
                etoksB[s] = te
        ufmb = p.sb([16, T], BF16, es=es)
        psU = p.ps([16, 512], F32, es=es)
        tprev = None
        for g in range(4):
            tk = None
            for c in range(8):
                tk = p.mm(psU[:], wgk1b[:, 16 * c:16 * (c + 1)],
                          xnfm[:, c * T + 512 * g: c * T + 512 * (g + 1)],
                          start=(c == 0), stop=(c == 7),
                          after=[tprev] if c == 0 else (), inc=(c == 7))
            tprev = p.act(ufmb[:, 512 * g:512 * (g + 1)], psU[:], AF.Copy, after=[tk])
        zcol = [p.sb([128, 512], F32, es=es) for _ in range(2)]
        zc2 = [p.sb([128, 512], F32, es=es) for _ in range(2)]
        lsb = [p.sb([128, 512], BF16, es=es) for _ in range(2)]
        dtoks = [None, None]
        wzc = [None, None]
        for t in range(NT):
            s = t % 2
            ps = psA[s]
            p.mm(ps[:], ufmb[:, 128 * t:128 * (t + 1)], wgk2b[:],
                 start=True, stop=False, after=[tprev, etoksA[s]], inc=False)
            tk = p.mm(ps[:], onesr, bgk2b[:], start=False, stop=True, after=[tbg])
            tabs = p.act(zcol[s][:], ps[:], AF.Abs, after=[tk, wzc[s]])
            texp = p.act(zcol[s][:], zcol[s][:], AF.Exp, scale=-1.0, after=[tabs])
            tl1p = p.act(zcol[s][:], zcol[s][:], AF.Ln, bias=cc[:, 0:1], after=[texp])
            tmin = p.ts(zc2[s][:], ps[:], 0.0, None, OP.min, after=[tk, tabs])
            etoksA[s] = tmin
            tls = p.tt(lsb[s][:], zc2[s][:], zcol[s][:], OP.subtract,
                       after=[tmin, tl1p, dtoks[s]])
            wzc[s] = tls
            td = p.dma("gpsimd", scr["lsz"][128 * t:(t + 1) * 128, :], lsb[s][:],
                       after=[tls], ch="lszo%d" % s)
            dtoks[s] = td
        p.barrier()


def emit_l0_scan(p, io, scr, cst, moe):
    cc, idb, U2b, M2f = cst["cc"], cst["idb"], cst["U2b"], cst["M2f"]
    wob = p.wob
    masks, yacc = moe["masks"], moe["yacc"]
    with contextlib.ExitStack() as es:
        lszt = [p.sb([128, 512], BF16, es=es) for _ in range(2)]
        ktt = [p.sb([128, 512], BF16, es=es) for _ in range(2)]
        vtt = [p.sb([128, 1024], BF16, es=es) for _ in range(2)]
        gtt = [p.sb([128, 1024], BF16, es=es) for _ in range(2)]
        qfh = [[p.sb([64, 128], BF16, es=es) for _ in range(H)] for _ in range(2)]
        kfh = [[p.sb([64, 128], BF16, es=es) for _ in range(H)] for _ in range(2)]
        QE0 = p.sb([128, 128], BF16, es=es)
        QE1 = p.sb([128, 128], BF16, es=es)
        KE0 = p.sb([128, 128], BF16, es=es)
        KE1 = p.sb([128, 128], BF16, es=es)
        KT0 = p.sb([128, 64], BF16, es=es)
        KT1 = p.sb([128, 64], BF16, es=es)
        ebf = p.sb([64, 128], F32, es=es)
        enf = p.sb([64, 128], F32, es=es)
        ent = p.sb([128, 64], F32, es=es)
        e0c = p.sb([64, 1], F32, es=es)
        e1c = p.sb([64, 1], F32, es=es)
        A2m = p.sb([128, 128], BF16, es=es)
        S_A = p.sb([64, 8 * 128], F32, es=es)
        S_B = p.sb([64, 8 * 128], F32, es=es)
        S_Ab = p.sb([128, 8 * 128], BF16, es=es)
        S_Bb = p.sb([128, 8 * 128], BF16, es=es)
        tz = []
        tz.append(p.memset(QE0[:], 0.0))
        tz.append(p.memset(QE1[:], 0.0))
        tz.append(p.memset(KE0[:], 0.0))
        tz.append(p.memset(KE1[:], 0.0))
        tz.append(p.memset(KT0[:], 0.0))
        tz.append(p.memset(KT1[:], 0.0))
        tz.append(p.memset(S_A[:], 0.0))
        tz.append(p.memset(S_B[:], 0.0))
        tz.append(p.memset(S_Ab[:], 0.0))
        tz.append(p.memset(S_Bb[:], 0.0))
        p.barrier()
        pbb = p.ps([128, 192], F32, es=es)
        pbtm = pbb[:, 0:64]
        pbfm = pbb[0:64, 64:192]
        pAP = p.ps([128, 256], F32, es=es)
        pAT = pAP[:, 0:128]
        po = pAP[:, 128:256]
        pkv0 = p.ps([64, 128], F32, es=es)
        pkv1 = pkv0
        py = [p.ps([128, 512], F32, es=es) for _ in range(2)]
        Mb2 = [p.sb([128, 8 * 1024], BF16, es=es) for _ in range(2)]
        mld = [None, None]
        mcons = [None, None]
        wpy = [None, None]

        def load_m(ei):
            bm = ei % 2
            tl = None
            for c in range(8):
                tl = p.dma("sync", Mb2[bm][:, 1024 * c:1024 * (c + 1)],
                           scr["mall"].rearrange("(e k p) c -> e k p c", e=8, p=128)[ei, c],
                           after=[mcons[bm], p.mall_tok], ch="mld%d" % bm)
            mld[bm] = tl

        def emit_y_half(u, n):
            e, ty = u // 16, u % 16
            bm = e % 2
            tk = None
            for c in range(8):
                tk = p.mm(py[n][:],
                          xnfm0[:, c * T + 128 * ty: c * T + 128 * (ty + 1)],
                          Mb2[bm][:, 1024 * c + 512 * n: 1024 * c + 512 * (n + 1)],
                          start=(c == 0), stop=(c == 7),
                          after=[mld[bm], wpy[n]] if c == 0 else (),
                          inc=(c == 7))
            tac = p.stt(yacc[:, 1024 * ty + 512 * n: 1024 * ty + 512 * (n + 1)],
                        py[n][:],
                        masks[:, 8 * ty + e: 8 * ty + e + 1],
                        yacc[:, 1024 * ty + 512 * n: 1024 * ty + 512 * (n + 1)],
                        OP.mult, OP.add, after=[tk])
            wpy[n] = tac
            mcons[bm] = tk

        xnfm0 = moe["xnfm"]
        load_m(0)
        ot = p.sb([128, 1024], F32, es=es)
        sqt = p.sb([128, 1024], F32, es=es)
        sgt = p.sb([128, 1024], F32, es=es)
        rs8 = p.sb([128, 8], F32, es=es)
        ogb = p.sb([128, 1024], BF16, es=es)
        ogf = p.sb([128, 1024], BF16, es=es)
        ptb = [p.ps([128, 128], BF16, es=es) for _ in range(2)]
        ps0 = p.ps([128, 512], F32, es=es)
        s0t = [p.sb([128, 1024], F32, es=es) for _ in range(2)]

        tld = [dict(lsz=None, k=None, v=None, g=None, q=[None] * H, kf=None) for _ in range(2)]
        w = dict(pbtm=None, pbfm=None, pAT=None, pkv0=None, pkv1=None, po=None,
                 ebf=None, enf=None, ent=None, e0=None, e1=None, A2m=None,
                 QE0=None, QE1=None, KE0=None, KE1=None, KT0=None, KT1=None,
                 SA=None, SB=None, SAb=None, SBb=None, ot=None, ogb=None,
                 ogf=None, ps0=None, s0t=[None, None], sqt=None, sgt=None,
                 rs8=None)
        prevdma = [None, None]
        for t in range(NT):
            s = t % 2
            if t % 2 == 1 and t // 2 < 7:
                load_m(t // 2 + 1)
            pd = prevdma[s]
            chn = "scan%d" % s
            p.dma("sync", lszt[s][:], scr["lsz"][t * 128:(t + 1) * 128, :], after=[pd], ch=chn)
            p.dma("sync", ktt[s][:], scr["ktm"][t * 128:(t + 1) * 128, :], after=[pd], ch=chn)
            p.dma("sync", vtt[s][:], scr["vtm"][t * 128:(t + 1) * 128, :], after=[pd], ch=chn)
            p.dma("sync", gtt[s][:], scr["gtm"][t * 128:(t + 1) * 128, :], after=[pd], ch=chn)
            for h in range(H):
                p.dma("sync", qfh[s][h][:],
                      scr["qfm"][64 * h:64 * (h + 1), t * 128:(t + 1) * 128],
                      after=[pd], ch=chn)
                t_ld = p.dma("sync", kfh[s][h][:],
                             scr["kfm"][64 * h:64 * (h + 1), t * 128:(t + 1) * 128],
                             after=[pd], ch=chn)
            t_lsz = t_k = t_v = t_g = t_ld
            t_q = [t_ld] * H
            t_kf = [t_ld] * H
            for h in range(H):
                hs = slice(64 * h, 64 * (h + 1))
                vs = slice(128 * h, 128 * (h + 1))
                tb1 = p.mm(pbtm, U2b, lszt[s][:, hs], True, True,
                           after=[t_lsz, w["pbtm"], w["pbfm"]])
                tb2 = p.mm(pbfm, lszt[s][:, hs], U2b, True, True,
                           after=[t_lsz, w["pbfm"]])
                emit_y_half(t * 8 + h, 0)
                tent = p.act(ent[:], pbtm, AF.Exp, scale=-1.0,
                             after=[tb1, tb2, w["ent"]])
                w["pbtm"] = tent
                tebf = p.act(ebf[:], pbfm, AF.Exp, bias=cc[0:64, 3:4],
                             after=[tb2, w["ebf"]])
                tenf = p.act(enf[:], pbfm, AF.Exp, scale=-1.0,
                             after=[tb2, w["enf"]])
                te0 = p.act(e0c[:], pbfm[:, 63:64], AF.Exp, after=[tb2, w["e0"]])
                te1 = p.act(e1c[:], pbfm[:, 127:128], AF.Exp, after=[tb2, w["e1"]])
                w["pbfm"] = te1
                tQ0 = p.tt(QE0[0:64, 0:64], qfh[s][h][:, 0:64], ebf[:, 0:64],
                           OP.mult, after=[t_q[h], tebf, w["QE0"]])
                tQ1 = p.tt(QE1[0:64, 64:128], qfh[s][h][:, 64:128], ebf[:, 64:128],
                           OP.mult, after=[t_q[h], tebf, w["QE1"]])
                tK0 = p.tt(KE0[0:64, 0:64], kfh[s][h][:, 0:64], enf[:, 0:64],
                           OP.mult, after=[t_kf[h], tenf, w["KE0"]])
                tK1 = p.tt(KE1[0:64, 64:128], kfh[s][h][:, 64:128], enf[:, 64:128],
                           OP.mult, after=[t_kf[h], tenf, w["KE1"]])
                w["ebf"] = tQ1
                w["enf"] = tK1
                tT0 = p.tt(KT0[0:64, :], ktt[s][0:64, hs], ent[0:64, :], OP.mult,
                           after=[t_k, tent, w["KT0"]])
                tT1 = p.tt(KT1[64:128, :], ktt[s][64:128, hs], ent[64:128, :],
                           OP.mult, after=[t_k, tent, w["KT1"]])
                w["ent"] = tT1
                p.mm(pAT, KE0[:], QE0[:], True, False,
                     after=[tK0, tQ0, w["pAT"], w["po"]], inc=False)
                tat = p.mm(pAT, KE1[:], QE1[:], False, True, after=[tK1, tQ1])
                ta2 = p.tt(A2m[:], pAT, M2f, OP.mult, after=[tat, w["A2m"]])
                w["pAT"] = ta2
                tkv0 = p.mm(pkv0[:], KT0[:], vtt[s][:, vs], True, True,
                            after=[tT0, t_v, w["pkv0"]])
                tSB = p.tt(S_B[:, vs], S_A[:, vs], pkv0[:], OP.add,
                           after=[tkv0, w["SB"], w["SA"]])
                w["pkv0"] = tSB
                tSB2 = p.ts(S_B[:, vs], S_B[:, vs], e0c[:], None, OP.mult,
                            after=[te0, tSB])
                tSBb = p.vcopy(S_Bb[0:64, vs], S_B[:, vs], after=[tSB2, w["SBb"]])
                emit_y_half(t * 8 + h, 1)
                tkv1 = p.mm(pkv1[:], KT1[:], vtt[s][:, vs], True, True,
                            after=[tT1, t_v, tSB])
                p.mm(po, A2m[:], vtt[s][:, vs], True, False,
                     after=[ta2, t_v, w["po"]], inc=False)
                tom2 = p.mm(po, QE0[:], S_Ab[:, vs], False, False,
                            after=[tQ0, w["SAb"]])
                tom = p.mm(po, QE1[:], S_Bb[:, vs], False, True,
                           after=[tQ1, tSBb])
                tSA = p.tt(S_A[:, vs], S_B[:, vs], pkv1[:], OP.add,
                           after=[tkv1, tSB2, tom])
                w["pkv0"] = tSA
                w["SB"] = tSA
                tSA2 = p.ts(S_A[:, vs], S_A[:, vs], e1c[:], None, OP.mult,
                            after=[te1, tSA])
                w["SA"] = tSA2
                tSAb = p.vcopy(S_Ab[0:64, vs], S_A[:, vs], after=[tSA2, tom2])
                w["SAb"] = tSAb
                w["SBb"] = tom
                w["e0"] = tSB2
                w["e1"] = tSA2
                w["QE0"] = tom2
                w["QE1"] = tom
                w["KE0"] = tat
                w["KE1"] = tat
                w["KT0"] = tkv0
                w["KT1"] = tkv1
                tev = p.scopy(ot[:, vs], po, after=[tom, w["ot"]])
                w["po"] = tev
                if h == H - 1:
                    w["ot"] = None
                oevac_last = tev
            tsq = p.act(sqt[:], ot[:], AF.Square, after=[oevac_last, w["sqt"]])
            trs = p.red(rs8[:], sqt[:].rearrange("p (h c) -> p h c", c=128),
                        OP.add, after=[tsq, w["rs8"]])
            w["sqt"] = trs
            tms = p.ts(rs8[:], rs8[:], 1.0 / 128.0, None, OP.mult, after=[trs])
            tln = p.act(rs8[:], rs8[:], AF.Ln, bias=cc[:, 1:2], after=[tms])
            trr = p.act(rs8[:], rs8[:], AF.Exp, scale=-0.5, after=[tln])
            w["rs8"] = trr
            tsg = p.act(sgt[:], gtt[s][:], AF.Silu, after=[t_g, w["sgt"]])
            ogtoks = []
            for h in range(H):
                vs = slice(128 * h, 128 * (h + 1))
                tog = p.stt(ogb[:, vs], ot[:, vs], rs8[:, h:h + 1], sgt[:, vs],
                            OP.mult, OP.mult, after=[trr, tsg, w["ogb"]])
                ogtoks.append(tog)
            w["ogb"] = ogtoks[-1]
            w["ot"] = ogtoks[-1]
            w["sgt"] = ogtoks[-1]
            tcpl = None
            for c in range(8):
                ss = c % 2
                ttr = p.tr(ptb[ss][:], ogb[:, c * 128:(c + 1) * 128], idb,
                           after=[ogtoks[c], w.get("ptb%d" % ss)])
                tcp = p.vcopy(ogf[:, c * 128:(c + 1) * 128], ptb[ss][:],
                              after=[ttr, w["ogf"]])
                w["ptb%d" % ss] = tcp
                tcpl = tcp
            for n in range(2):
                tk = None
                for c in range(8):
                    tk = p.mm(ps0[:],
                              ogf[:, 128 * c:128 * (c + 1)],
                              wob[:, D * c + 512 * n: D * c + 512 * (n + 1)],
                              start=(c == 0), stop=(c == 7),
                              after=[tcpl, w["ps0"]] if c == 0 else (),
                              inc=(c == 7))
                tev = p.scopy(s0t[s][:, 512 * n:512 * (n + 1)], ps0[:],
                              after=[tk, w["s0t"][s]])
                w["ps0"] = tev
                w["ogf"] = tk
            td = p.dma("gpsimd", scr["s0"][t * 128:(t + 1) * 128, :], s0t[s][:],
                       after=[tev], ch="s0o%d" % s)
            w["s0t"][s] = td
            prevdma[s] = ogtoks[-1]
        p.barrier()


def emit_l0_combine(p, io, scr, cst, moe):
    yacc = moe["yacc"]
    with contextlib.ExitStack() as es:
        xt = [p.sb([128, 1024], F32, es=es) for _ in range(2)]
        s0l = [p.sb([128, 1024], F32, es=es) for _ in range(2)]
        syt = [p.sb([128, 1024], F32, es=es) for _ in range(2)]
        x1t = [p.sb([128, 1024], F32, es=es) for _ in range(2)]
        wx = [None, None]
        for t in range(NT):
            s = t % 2
            p.dma("sync", xt[s][:], io["x"][t * 128:(t + 1) * 128, :],
                  after=[wx[s]], ch="cmb%d" % s)
            tls = p.dma("sync", s0l[s][:], scr["s0"][t * 128:(t + 1) * 128, :],
                        after=[wx[s]], ch="cmb%d" % s)
            tlx = tls
            tsy = p.act(syt[s][:], yacc[:, 1024 * t:1024 * (t + 1)], AF.Silu,
                        after=[wx[s]])
            tm1 = p.tt(syt[s][:], syt[s][:], s0l[s][:], OP.mult, after=[tsy, tls])
            tx1 = p.tt(x1t[s][:], syt[s][:], xt[s][:], OP.add, after=[tm1, tlx])
            td = p.dma("gpsimd", scr["x1"][t * 128:(t + 1) * 128, :], x1t[s][:],
                       after=[tx1], ch="x1o%d" % s)
            wx[s] = td
        p.barrier()



def emit_consts(p, io):
    cst = {}
    cst["cc"] = p.sb([128, 6], F32, name="cc_t")
    cst["U2b"] = p.sb([128, 128], BF16, name="u2_t")
    cst["M2f"] = p.sb([128, 128], F32, name="m2_t")
    cst["idb"] = p.sb([128, 128], BF16, name="id_t")
    cst["idf"] = p.sb([128, 128], F32, name="idf_t")
    cst["onesr"] = p.sb([1, 128], BF16, name="or_t")
    cst["onesc"] = p.sb([128, 1], BF16, name="oc_t")
    cst["onesr512"] = p.sb([1, 512], BF16, name="o512_t")
    cst["E4"] = p.sb([128, 4], F32, name="e4_t")
    cst["E4T"] = p.sb([4, 128], F32, name="e4t_t")
    toks = []
    for k in ("cc", "U2b", "M2f", "idb", "idf", "onesr", "onesc", "onesr512", "E4", "E4T"):
        toks.append(p.dma("sync", cst[k][:], io[k], ch="cst"))
    p.barrier()
    return {k: (v[:] if k in ("U2b", "M2f", "idb", "idf", "onesr", "onesc",
                              "onesr512", "E4", "E4T") else v)
            for k, v in cst.items()}


def build_program(upto="l1", dbg=()):
    p = Prog()
    io = declare_io(p)
    scr = {}
    scr["qfm"] = p.dscr("qfm_s", [512, T], BF16)
    scr["kfm"] = p.dscr("kfm_s", [512, T], BF16)
    scr["ktm"] = p.dscr("ktm_s", [T, 512], BF16)
    scr["vtm"] = p.dscr("vtm_s", [T, 1024], BF16)
    scr["gtm"] = p.dscr("gtm_s", [T, 1024], BF16)
    scr["lsz"] = p.dscr("lsz_s", [T, 512], BF16)
    scr["s0"] = p.dscr("s0_s", [T, D], F32)
    scr["mown"] = p.dscr("mown_s", [1024, 1024], BF16)
    scr["mall"] = p.dscr("mall_s", [8192, 1024], BF16)
    scr["xn1"] = p.dscr("xn1_s", [T, D], BF16)
    scr["qfm1"] = p.dscr("qfm1_s", [D, T], BF16)
    scr["kfm1"] = p.dscr("kfm1_s", [D, T], BF16)
    scr["s1"] = p.dscr("s1_s", [T, D], F32)
    if upto == "l0":
        scr["x1"] = io["out"]
    else:
        scr["x1"] = p.dscr("x1_s", [T, D], F32)
    dbg_aps = {}
    for name in dbg:
        shp = {"qfm": [512, T], "kfm": [512, T], "ktm": [T, 512],
               "vtm": [T, 1024], "gtm": [T, 1024], "lsz": [T, 512],
               "s0": [T, D], "x1": [T, D], "xn1": [T, D], "s1": [T, D]}[name]
        dt = F32 if name in ("s0", "x1", "s1") else BF16
        dbg_aps[name] = p.dout("dbg_" + name, shp, dt)

    cst = emit_consts(p, io)
    xnfm = p.sb([128, 8 * T], BF16, name="xnfm0")
    es_moe = contextlib.ExitStack()
    moe = {
        "masks": p.sb([128, NT * 8], F32, es=es_moe),
        "yacc": p.sb([128, NT * 1024], BF16, es=es_moe),
        "xnfm": xnfm,
    }
    p.wob = p.sb([128, 8 * D], BF16, name="wob", es=p.es_wob)
    emit_moe_mown(p, io, scr, cst)
    emit_ln(p, io["x"], xnfm, cst["cc"], cst["idb"], eps_col_idx=1)
    emit_l0_proj(p, io, scr, cst, xnfm, moe)
    emit_l0_scan(p, io, scr, cst, moe)
    p.es_wob.close()
    emit_l0_combine(p, io, scr, cst, moe)
    es_moe.close()
    if upto != "l0":
        emit_l1(p, io, scr, cst, xnfm)
    if dbg_aps:
        with contextlib.ExitStack() as es:
            bt = p.sb([128, 2048], BF16, es=es)
            bt32 = p.sb([128, 1024], F32, es=es)
            dbg_last = None
            tprev = None
            for name, dst in dbg_aps.items():
                srcap = scr[name] if name != "x1" else scr["x1"]
                rows, cols = {"qfm": (512, T), "kfm": (512, T), "ktm": (T, 512),
                              "vtm": (T, 1024), "gtm": (T, 1024), "lsz": (T, 512),
                              "s0": (T, D), "x1": (T, D), "xn1": (T, D),
                              "s1": (T, D)}[name]
                tile = bt32 if name in ("s0", "x1", "s1") else bt
                for r in range(0, rows, 128):
                    t1 = p.dma("sync", tile[:, :cols], srcap[r:r + 128, :],
                               after=[tprev, dbg_last], ch="dbgi")
                    dbg_last = t1
                    tprev = p.dma("gpsimd", dst[r:r + 128, :], tile[:, :cols],
                                  after=[t1], ch="dbgo")
            p.barrier()
    return p


def kernel(**inputs):
    n = 8
    p = build_program(upto=KERNEL_UPTO, dbg=())
    nc = p.finalize()
    in_maps = [host_prep(inputs, b) for b in range(n)]
    res = run_bass_kernel_spmd(nc, in_maps, core_ids=list(range(n)))
    out = np.stack([res.results[b]["out"] for b in range(n)], axis=0)
    return out.astype(np.float32)


KERNEL_UPTO = "l1"


def emit_l1_attn(p, io, scr, cst, xnfm):
    cc = cst["cc"]
    idb = cst["idb"]
    onesr = cst["onesr"]
    onesr512 = cst["onesr512"]
    with contextlib.ExitStack() as es:
        ab8 = p.sb([128, 16], F32, es=es)
        browb = p.sb([1, 3072], BF16, es=es)
        brf = p.sb([1, 3072], F32, es=es)
        vtm = p.sb([128, NT * 1040], BF16, name="vtm1", es=es)
        otm = p.sb([128, NT * 1040], BF16, name="otm1", es=es)
        es_s = contextlib.ExitStack()
        colsum = p.sb([128, 8], F32, es=es_s)
        colsq = p.sb([128, 8], F32, es=es_s)
        sqt = [p.sb([128, T], F32, es=es_s) for _ in range(2)]
        g48 = p.sb([4, 32], F32, es=es_s)
        ps48 = p.ps([4, 8], F32, es=es_s)
        ps128 = p.ps([128, 8], F32, es=es_s)
        gamc = p.sb([128, 8], F32, es=es_s)
        betc = p.sb([128, 8], F32, es=es_s)
        tsq = [None, None]
        for c in range(8):
            s = c % 2
            t1 = p.act(sqt[s][:], xnfm[:, c * T:(c + 1) * T], AF.Square,
                       after=[tsq[s]])
            t2 = p.red(colsq[:, c:c + 1], sqt[s][:], OP.add, after=[t1])
            tsq[s] = t2
            p.red(colsum[:, c:c + 1], xnfm[:, c * T:(c + 1) * T], OP.add)
        p.barrier()
        t1 = p.mm(ps48[:], cst["E4"], colsum[:], True, True)
        t2 = p.scopy(g48[:, 0:8], ps48[:], after=[t1])
        t3 = p.mm(ps48[:], cst["E4"], colsq[:], True, True, after=[t2])
        t4 = p.scopy(g48[:, 8:16], ps48[:], after=[t3])
        inv = 1.0 / (T * 32.0)
        t5 = p.ts(g48[:, 16:24], g48[:, 0:8], inv, None, OP.mult, after=[t2, t4])
        t6 = p.ts(g48[:, 8:16], g48[:, 8:16], inv, None, OP.mult, after=[t4])
        t7 = p.tt(g48[:, 24:32], g48[:, 16:24], g48[:, 16:24], OP.mult, after=[t5])
        t8 = p.tt(g48[:, 24:32], g48[:, 8:16], g48[:, 24:32], OP.subtract,
                  after=[t6, t7])
        t9 = p.act(g48[:, 24:32], g48[:, 24:32], AF.Ln, bias=cc[0:4, 2:3], after=[t8])
        t10 = p.act(g48[:, 24:32], g48[:, 24:32], AF.Exp, scale=-0.5, after=[t9])
        t11 = p.mm(ps128[:], cst["E4T"], g48[:, 24:32], True, True, after=[t10])
        tg = p.dma("sync", gamc[:], io["gam_c"], ch="gnp")
        tb = p.dma("sync", betc[:], io["bet_c"], ch="gnp")
        t12 = p.stt(ab8[:, 0:8], ps128[:], 1.0, gamc[:], OP.mult, OP.mult,
                    after=[t11, tb])
        t13 = p.mm(ps128[:], cst["E4T"], g48[:, 16:24], True, True, after=[t12, t10])
        t14 = p.tt(ab8[:, 8:16], ps128[:], ab8[:, 0:8], OP.mult, after=[t13, t12])
        p.tt(ab8[:, 8:16], betc[:], ab8[:, 8:16], OP.subtract, after=[t14])
        p.barrier()
        es_s.close()

        es_w = contextlib.ExitStack()
        wqkvb = p.sb([128, 8 * 3072], BF16, es=es_w)
        stgq = [p.sb([128, 3072], F32, es=es_w) for _ in range(2)]
        psbr = p.ps([1, 3072], F32, es=es_w)
        stok = [None, None]
        tkbs = [None, None]
        tkb = None
        for c in range(8):
            s = c % 2
            tl = p.dma("sync", stgq[s][:], io["wqkv"][c * 128:(c + 1) * 128, :],
                       after=[stok[s], tkbs[s]], ch="wqkv%d" % s)
            for n in range(6):
                tkb = p.mm(psbr[:, 512 * n:512 * (n + 1)], ab8[:, 8 + c:9 + c],
                           stgq[s][:, 512 * n:512 * (n + 1)],
                           start=(c == 0), stop=(c == 7), after=[tl],
                           inc=(n == 5))
            tkbs[s] = tkb
            tv = p.ts(wqkvb[:, 3072 * c:3072 * (c + 1)], stgq[s][:],
                      ab8[:, c:c + 1], None, OP.mult, after=[tl], eng="vector")
            stok[s] = tv
        tbq = p.dma("sync", brf[:], io["bqkvr"], ch="bqk")
        tba = p.tt(brf[:], brf[:], psbr[:], OP.add, after=[tbq, tkb])
        tbb = p.vcopy(browb[:], brf[:], after=[tba])
        p.barrier()

        vones = vtm[:].rearrange("p (t h c) -> p t h c", h=16, c=65)[:, :, :, 64:65]
        tvo = p.memset(vones, 1.0)
        es_v1 = contextlib.ExitStack()
        psv = [p.ps([128, 512], F32, es=es_v1) for _ in range(2)]
        wpv = [None, None]
        for t in range(NT):
            for n in range(2):
                s = n
                for c in range(8):
                    p.mm(psv[s][:],
                         xnfm[:, c * T + 128 * t: c * T + 128 * (t + 1)],
                         wqkvb[:, 3072 * c + 2048 + 512 * n: 3072 * c + 2048 + 512 * (n + 1)],
                         start=(c == 0), stop=False,
                         after=[wpv[s]] if c == 0 else (), inc=False)
                tk = p.mm(psv[s][:], onesr,
                          browb[0:1, 2048 + 512 * n: 2048 + 512 * (n + 1)],
                          start=False, stop=True, after=[tbb])
                for hh in range(8):
                    h = 8 * n + hh
                    wpv[s] = p.vcopy(
                        vtm[:, 1040 * t + 65 * h: 1040 * t + 65 * h + 64],
                        psv[s][:, 64 * hh:64 * (hh + 1)], after=[tk, tvo])
        stqk = [p.sb([64, 512], BF16, es=es_v1) for _ in range(2)]
        wstq = [None, None]
        wpq = [None, None]
        for h in range(16):
            for part, dst, scl in ((0, scr["qfm1"], 0.125), (1, scr["kfm1"], 1.0)):
                base = 1024 * part + 64 * h
                for g in range(4):
                    s = g % 2
                    for c in range(8):
                        p.mm(psv[s][0:64, :],
                             wqkvb[:, 3072 * c + base: 3072 * c + base + 64],
                             xnfm[:, c * T + 512 * g: c * T + 512 * (g + 1)],
                             start=(c == 0), stop=False,
                             after=[wpv[s]] if c == 0 else (), inc=False)
                    tk = p.mm(psv[s][0:64, :], browb[0:1, base:base + 64],
                              onesr512, start=False, stop=True, after=[tbb])
                    te = p.act(stqk[s][:], psv[s][0:64, :], AF.Copy, scale=scl,
                               after=[tk, wstq[s]])
                    wpv[s] = te
                    td = p.dma("gpsimd",
                               dst[64 * h:64 * (h + 1), 512 * g:512 * (g + 1)],
                               stqk[s][:], after=[te], ch="qko%d" % s)
                    wstq[s] = td
        p.barrier()
        es_v1.close()
        es_w.close()

        es_v = contextlib.ExitStack()
        QA = [p.sb([128, T], BF16, es=es_v) for _ in range(2)]
        KA = [p.sb([128, T], BF16, es=es_v) for _ in range(2)]
        PT = p.sb([128, 8192], BF16, es=es_v)
        sqt2 = p.sb([64, T], BF16, es=es_v)
        qn2 = p.sb([1, T], F32, es=es_v)
        kn2 = p.sb([1, T], F32, es=es_v)
        km = p.sb([1, 1], F32, es=es_v)
        brow = p.sb([1, T], F32, es=es_v)
        oht = p.sb([65, 512], BF16, es=es_v)
        psSt = [p.ps([128, 1024], F32, es=es_v) for _ in range(2)]
        psO = p.ps([65, 512], F32, es=es_v)
        ptrb = p.ps([128, 128], BF16, es=es_v)
        pnrm = p.ps([1, 512], F32, es=es_v)
        tonesa = p.memset(KA[0][64:65, :], 1.0)
        tonesb = p.memset(KA[1][64:65, :], 1.0)
        tldk = [None, None]
        tSTread = [None, None]
        wPT = [None, None]
        wO = None
        woht = None
        wotm = None
        wptrb = None
        wsq = None
        wnrm = None
        wrow = None

        def load_qk(hh):
            b_ = hh % 2
            p.dma("sync", QA[b_][0:64, :], scr["qfm1"][64 * hh:64 * (hh + 1), :],
                  after=[tSTread[b_]], ch="qkl%d" % b_)
            t2 = p.dma("sync", KA[b_][0:64, :], scr["kfm1"][64 * hh:64 * (hh + 1), :],
                       after=[tSTread[b_]], ch="qkl%d" % b_)
            tldk[b_] = t2

        def bound_row(h):
            nonlocal wsq, wnrm, wrow
            b_ = h % 2
            t1 = p.act(sqt2[:], KA[b_][0:64, :], AF.Square, after=[tldk[b_], wsq])
            tlast = None
            tklast = None
            for g in range(4):
                tk = p.mm(pnrm[:], cst["onesc"][0:64, :], sqt2[:, 512 * g:512 * (g + 1)],
                          True, True, after=[t1, wnrm])
                tklast = tk
                tlast = p.vcopy(kn2[0:1, 512 * g:512 * (g + 1)], pnrm[:], after=[tk, wrow])
                wnrm = tlast
            tkm = p.red(km[:], kn2[:], OP.max, after=[tlast])
            t2 = p.act(sqt2[:], QA[b_][0:64, :], AF.Square, after=[tldk[b_], tklast])
            wsq = t2
            for g in range(4):
                tk = p.mm(pnrm[:], cst["onesc"][0:64, :], sqt2[:, 512 * g:512 * (g + 1)],
                          True, True, after=[t2, wnrm])
                tlast = p.vcopy(qn2[0:1, 512 * g:512 * (g + 1)], pnrm[:], after=[tk, wrow])
                wnrm = tlast
            t3 = p.ts(brow[:], qn2[:], km[0:1, 0:1], None, OP.mult, after=[tkm, tlast, wrow])
            t4 = p.act(brow[:], brow[:], AF.Ln, bias=cc[0:1, 4:5], after=[t3])
            t5 = p.act(brow[:], brow[:], AF.Exp, scale=0.5, after=[t4])
            t6 = p.ts(brow[:], brow[:], -1.0, 60.0, OP.mult, OP.add, after=[t5])
            wrow = t6
            tq = p.vcopy(QA[b_][64:65, :], brow[:], after=[t6, tSTread[b_]])
            return tq

        def st_sub(h, qg, tq64):
            nonlocal wO, woht, wotm, wptrb
            b_ = h % 2
            for pr in range(8):
                half = pr % 2
                toks = []
                for j in range(2):
                    kc = 2 * pr + j
                    toks.append(p.mm(
                        psSt[half][:, 512 * j:512 * (j + 1)],
                        KA[b_][0:65, 128 * kc:128 * (kc + 1)],
                        QA[b_][0:65, 512 * qg:512 * (qg + 1)],
                        start=True, stop=True,
                        after=[tq64, tonesa, tonesb, wPT[half]]))
                tex = p.act(PT[:, 1024 * pr:1024 * (pr + 1)], psSt[half][:],
                            AF.Exp, after=[toks[0], toks[1]])
                wPT[half] = tex
            tk = None
            for kc in range(16):
                tk = p.mm(psO[:],
                          vtm[:, 1040 * kc + 65 * h: 1040 * kc + 65 * (h + 1)],
                          PT[:, 512 * kc:512 * (kc + 1)],
                          start=(kc == 0), stop=(kc == 15),
                          after=[wPT[0], wPT[1], wO] if kc == 0 else (),
                          inc=(kc == 15))
            tSTread[b_] = tk
            teo = p.vcopy(oht[:], psO[:], after=[tk, woht])
            wO = teo
            for sub in range(4):
                tt_ = 4 * qg + sub
                ttr = p.tr(ptrb[:, 0:65], oht[:, 128 * sub:128 * (sub + 1)],
                           idb[0:65, 0:65], after=[teo, wptrb])
                tcp = p.vcopy(
                    otm[:, 1040 * tt_ + 65 * h: 1040 * tt_ + 65 * (h + 1)],
                    ptrb[:, 0:65], after=[ttr, wotm])
                wptrb = tcp
                woht = tcp

        load_qk(0)
        pend = None
        for h in range(16):
            tq = bound_row(h)
            for qg in range(4):
                if pend is not None:
                    st_sub(*pend)
                pend = (h, qg, tq)
                if qg == 0 and h + 1 < 16:
                    load_qk(h + 1)
        st_sub(*pend)
        p.barrier()
        es_v.close()

        es_o = contextlib.ExitStack()
        wo1b, two0, two1 = load_weight_bf16(p, io["wo1"], D, D, es_o)
        bo1f = p.sb([1, 1024], F32, es=es_o)
        bo1r = p.sb([1, 1024], BF16, es=es_o)
        tl = p.dma("sync", bo1f[:], io["bo1r"], ch="bo1")
        tbo = p.vcopy(bo1r[:], bo1f[:], after=[tl])
        bo1bc = p.sb([128, 1024], F32, es=es_o)
        psW = p.ps([128, 512], F32, es=es_o)
        tprev = None
        for n in range(2):
            tk = p.mm(psW[:], onesr, bo1r[0:1, 512 * n:512 * (n + 1)], True, True,
                      after=[tbo, tprev])
            tprev = p.scopy(bo1bc[:, 512 * n:512 * (n + 1)], psW[:], after=[tk])
        rcp16 = p.sb([128, 16], F32, es=es_o)
        o2 = p.sb([128, 1024], BF16, es=es_o)
        ofmt = p.sb([128, 1024], BF16, es=es_o)
        ptrb2 = [p.ps([128, 128], BF16, es=es_o) for _ in range(2)]
        xn1l = [p.sb([128, 1024], BF16, es=es_o) for _ in range(2)]
        s1t = [p.sb([128, 1024], F32, es=es_o) for _ in range(2)]
        wrc = None
        wo2 = None
        wofmt = None
        wpt2 = [None, None]
        wps = tprev
        ws1 = [None, None]
        wx1l = [None, None]
        for t in range(NT):
            s = t % 2
            tlx = p.dma("sync", xn1l[s][:], scr["xn1"][128 * t:128 * (t + 1), :],
                        after=[wx1l[s]], ch="xn1l%d" % s)
            ov = otm[:, 1040 * t:1040 * (t + 1)].rearrange("p (h c) -> p h c", c=65)
            trc = p.recip(rcp16[:], ov[:, :, 64], after=[wrc])
            o2toks = []
            for hh in range(16):
                tn = p.ts(o2[:, 64 * hh:64 * (hh + 1)],
                          otm[:, 1040 * t + 65 * hh: 1040 * t + 65 * hh + 64],
                          rcp16[:, hh:hh + 1], None, OP.mult,
                          after=[trc, wo2])
                o2toks.append(tn)
            wrc = o2toks[-1]
            tcpl = None
            for c in range(8):
                ss = c % 2
                ttr = p.tr(ptrb2[ss][:], o2[:, 128 * c:128 * (c + 1)], idb,
                           after=[o2toks[-1], wpt2[ss]])
                tcp = p.vcopy(ofmt[:, 128 * c:128 * (c + 1)], ptrb2[ss][:],
                              after=[ttr, wofmt])
                wpt2[ss] = tcp
                tcpl = tcp
            wo2 = tcpl
            for n in range(2):
                tk = None
                for c in range(8):
                    tk = p.mm(psW[:], ofmt[:, 128 * c:128 * (c + 1)],
                              wo1b[:, 1024 * c + 512 * n: 1024 * c + 512 * (n + 1)],
                              start=(c == 0), stop=(c == 7),
                              after=[tcpl, wps, two0, two1] if c == 0 else (),
                              inc=(c == 7))
                wofmt = tk
                ta = p.tt(s1t[s][:, 512 * n:512 * (n + 1)], psW[:],
                          bo1bc[:, 512 * n:512 * (n + 1)], OP.add,
                          after=[tk, ws1[s]])
                wps = ta
                ws1[s] = ta
            tb2 = p.tt(s1t[s][:], s1t[s][:], xn1l[s][:], OP.add, after=[tlx, ws1[s]])
            td = p.dma("gpsimd", scr["s1"][128 * t:128 * (t + 1), :], s1t[s][:],
                       after=[tb2], ch="s1o%d" % s)
            ws1[s] = td
            wx1l[s] = tb2
        p.barrier()
        es_o.close()


def emit_l1_ffn(p, io, scr, cst, xnfm):
    cc = cst["cc"]
    with contextlib.ExitStack() as es:
        gates = p.sb([128, NT * 8], F32, es=es)
        y1 = p.sb([128, NT * 1024], BF16, es=es)
        tmz = [p.memset(y1[:, :8192], 0.0), p.memset(y1[:, 8192:], 0.0)]
        es_e = contextlib.ExitStack()
        winb = p.sb([128, 8 * 2048], BF16, es=es_e)
        woutb = p.sb([128, 16 * 1024], BF16, es=es_e)
        stgw = [p.sb([128, 2048], F32, es=es_e) for _ in range(2)]
        hfm = p.sb([128, 16 * 512], BF16, es=es_e)
        wstg = [None, None]

        def conv_copy(eng, out, in_, after):
            if eng == "scalar":
                return p.act(out, in_, AF.Copy, after=after)
            return p.vcopy(out, in_, after=after, eng=eng)

        def load_expert(e, win_cons, wout_cons, engs=("scalar", "vector")):
            win_rdy = [None, None]
            wout_rdy = [None, None]
            for c in range(8):
                s = c % 2
                tl = p.dma("sync", stgw[s][:],
                           io["win"][e, 128 * c:128 * (c + 1), :],
                           after=[wstg[s]], ch="wie%d" % s)
                win_rdy[s] = conv_copy(engs[0], winb[:, 2048 * c:2048 * (c + 1)],
                                       stgw[s][:], [tl, win_cons])
                wstg[s] = win_rdy[s]
            for c in range(16):
                s = c % 2
                tl = p.dma("sync", stgw[s][:, 0:1024],
                           io["wout"][e, 128 * c:128 * (c + 1), :],
                           after=[wstg[s]], ch="wie%d" % s)
                wout_rdy[s] = conv_copy(engs[1], woutb[:, 1024 * c:1024 * (c + 1)],
                                        stgw[s][:, 0:1024], [tl, wout_cons])
                wstg[s] = wout_rdy[s]
            return win_rdy, wout_rdy

        es_g = contextlib.ExitStack()
        wg1b, twg0, twg1 = load_weight_bf16(p, io["wg1"], D, 256, es_g)
        wg2b = p.sb([128, 16], BF16, es=es_g)
        stg2 = p.sb([128, 8], F32, es=es_g)
        bg1c = p.sb([128, 2], F32, es=es_g)
        tbg = p.dma("sync", bg1c[:], io["bg1c"], ch="bg1")
        tw2 = None
        for c in range(2):
            tl = p.dma("sync", stg2[:], io["wg2"][c * 128:(c + 1) * 128, :],
                       after=[tw2], ch="wg2l")
            tw2 = p.vcopy(wg2b[:, 8 * c:8 * (c + 1)], stg2[:], after=[tl])
        g1fm = p.sb([128, 2 * T], BF16, es=es_g)
        psg = [p.ps([128, 512], F32, es=es_g) for _ in range(2)]
        wg = [None, None]
        for mc in range(2):
            for g in range(4):
                s = (mc * 4 + g) % 2
                tk = None
                for c in range(8):
                    tk = p.mm(psg[s][:],
                              wg1b[:, 256 * c + 128 * mc: 256 * c + 128 * (mc + 1)],
                              xnfm[:, c * T + 512 * g: c * T + 512 * (g + 1)],
                              start=(c == 0), stop=(c == 7),
                              after=[wg[s], twg0, twg1] if c == 0 else (),
                              inc=(c == 7))
                te = p.act(g1fm[:, T * mc + 512 * g: T * mc + 512 * (g + 1)],
                           psg[s][:], AF.Gelu, bias=bg1c[:, mc:mc + 1],
                           after=[tk, tbg])
                wg[s] = te
        psl = [p.ps([128, 8], F32, es=es_g) for _ in range(2)]
        prb = [p.sb([128, 8], F32, es=es_g) for _ in range(2)]
        aux = [p.sb([128, 12], F32, es=es_g) for _ in range(2)]
        junk8 = p.sb([128, 8], BF16, es=es_g)
        wpl = [None, None]
        gate_tok = [None] * NT
        for t in range(NT):
            s = t % 2
            tk = None
            for c in range(2):
                tk = p.mm(psl[s][:], g1fm[:, T * c + 128 * t: T * c + 128 * (t + 1)],
                          wg2b[:, 8 * c:8 * (c + 1)], start=(c == 0), stop=(c == 1),
                          after=[wg[0], wg[1], tw2, wpl[s]] if c == 0 else (),
                          inc=(c == 1))
            t1 = p.red(aux[s][:, 0:1], psl[s][:], OP.max, negate=True,
                       after=[tk, wpl[s]])
            t2 = p.act(junk8[:], psl[s][:], AF.Exp, bias=aux[s][:, 0:1],
                       accum_out=aux[s][:, 1:2], after=[t1])
            t3 = p.recip(aux[s][:, 2:3], aux[s][:, 1:2], after=[t2])
            t4 = p.act(prb[s][:], psl[s][:], AF.Exp, bias=aux[s][:, 0:1], after=[t2])
            t5 = p.ts(prb[s][:], prb[s][:], aux[s][:, 2:3], None, OP.mult,
                      after=[t3, t4])
            t6 = p.red(aux[s][:, 3:4], prb[s][:], OP.max, after=[t5])
            t7 = p.ts(aux[s][:, 4:12], prb[s][:], aux[s][:, 3:4], None, OP.subtract,
                      after=[t6])
            t8 = p.ts(aux[s][:, 4:12], aux[s][:, 4:12], 0.0, None, OP.is_ge,
                      after=[t7])
            t9 = p.stt(aux[s][:, 4:12], aux[s][:, 4:12], -1e30, prb[s][:],
                       OP.mult, OP.add, after=[t8])
            t10 = p.red(aux[s][:, 3:4], aux[s][:, 4:12], OP.max, after=[t9])
            t11 = p.ts(aux[s][:, 4:12], prb[s][:], aux[s][:, 3:4], None, OP.is_ge,
                       after=[t10])
            t12 = p.tt(gates[:, 8 * t:8 * (t + 1)], prb[s][:], aux[s][:, 4:12],
                       OP.mult, after=[t11])
            gate_tok[t] = t12
            wpl[s] = t12
        win_rdy, wout_rdy = load_expert(0, None, None)
        es_g.close()
        gates_done = [wg[0], wg[1], wpl[0], wpl[1]]
        psh = [p.ps([128, 512], F32, es=es_e) for _ in range(2)]
        pso = [p.ps([128, 1024], F32, es=es_e) for _ in range(2)]
        win_cons = None
        wout_cons = None
        wh = [None, None]
        wo_ = [None, None]
        first_mm = True
        for e in range(8):
            for tg in range(4):
                hlast = None
                for j in range(16):
                    s = j % 2
                    tk = None
                    for c in range(8):
                        tk = p.mm(psh[s][:],
                                  winb[:, 2048 * c + 128 * j: 2048 * c + 128 * (j + 1)],
                                  xnfm[:, c * T + 512 * tg: c * T + 512 * (tg + 1)],
                                  start=(c == 0), stop=(c == 7),
                                  after=([win_rdy[0], win_rdy[1], wh[s]]
                                         + (gates_done if first_mm else []))
                                  if c == 0 else (),
                                  inc=(c == 7))
                        first_mm = False
                    win_cons = tk
                    te = p.act(hfm[:, 512 * j:512 * (j + 1)], psh[s][:], AF.Gelu,
                               after=[tk])
                    wh[s] = te
                    hlast = te
                for sub in range(4):
                    t_ = 4 * tg + sub
                    s = sub % 2
                    tk = None
                    for j in range(16):
                        for n in range(2):
                            tk = p.mm(pso[s][:, 512 * n:512 * (n + 1)],
                                      hfm[:, 512 * j + 128 * sub: 512 * j + 128 * (sub + 1)],
                                      woutb[:, 1024 * j + 512 * n: 1024 * j + 512 * (n + 1)],
                                      start=(j == 0), stop=(j == 15),
                                      after=[hlast, wout_rdy[0], wout_rdy[1], wo_[s]]
                                      if (j == 0 and n == 0) else (),
                                      inc=(j == 15 and n == 1))
                    wout_cons = tk
                    tac = p.stt(y1[:, 1024 * t_:1024 * (t_ + 1)], pso[s][:],
                                gates[:, 8 * t_ + e: 8 * t_ + e + 1],
                                y1[:, 1024 * t_:1024 * (t_ + 1)], OP.mult, OP.add,
                                after=[tk, gate_tok[t_]] + (tmz if e == 0 else []))
                    wo_[s] = tac
            if e < 7:
                win_rdy, wout_rdy = load_expert(e + 1, win_cons, wout_cons)
        p.barrier()
        es_e.close()
        x1t = [p.sb([128, 1024], F32, es=es) for _ in range(2)]
        s1l = [p.sb([128, 1024], F32, es=es) for _ in range(2)]
        syt = [p.sb([128, 1024], F32, es=es) for _ in range(2)]
        ot = [p.sb([128, 1024], F32, es=es) for _ in range(2)]
        wx = [None, None]
        for t in range(NT):
            s = t % 2
            p.dma("sync", x1t[s][:], scr["x1"][t * 128:(t + 1) * 128, :],
                  after=[wx[s]], ch="fc%d" % s)
            tls = p.dma("sync", s1l[s][:], scr["s1"][t * 128:(t + 1) * 128, :],
                        after=[wx[s]], ch="fc%d" % s)
            tsy = p.act(syt[s][:], y1[:, 1024 * t:1024 * (t + 1)], AF.Silu,
                        after=[wx[s]])
            tm1 = p.tt(syt[s][:], syt[s][:], s1l[s][:], OP.mult, after=[tsy, tls])
            tx2 = p.tt(ot[s][:], syt[s][:], x1t[s][:], OP.add, after=[tm1, tls])
            td = p.dma("gpsimd", io["out"][t * 128:(t + 1) * 128, :], ot[s][:],
                       after=[tx2], ch="oo%d" % s)
            wx[s] = td
        p.barrier()


def emit_l1(p, io, scr, cst, xnfm):
    emit_ln(p, scr["x1"], xnfm, cst["cc"], cst["idb"], eps_col_idx=1,
            xn_tm_dram=scr["xn1"])
    emit_l1_attn(p, io, scr, cst, xnfm)
    emit_l1_ffn(p, io, scr, cst, xnfm)


# version 19
# speedup vs baseline: 1.1328x; 1.0073x over previous
import contextlib
import numpy as np
import ml_dtypes

import concourse.bass as bass
import concourse.mybir as mybir
from concourse.alu_op_type import AluOpType as OP
from concourse.bass_utils import run_bass_kernel_spmd

F32 = mybir.dt.float32
BF16 = mybir.dt.bfloat16
AF = mybir.ActivationFunctionType
AX = mybir.AxisListType
BF = ml_dtypes.bfloat16

ENG = ("sync", "tensor", "vector", "scalar", "gpsimd")
AUTO_SELF = {"vector", "scalar", "gpsimd"}

T = 2048
D = 1024
NT = T // 128
H = 8
DK = 64
DV = 128
LNR8 = -2.0794415416798357


class Prog:

    def __init__(self):
        self.nc = bass.Bass("TRN2", target_bir_lowering=False)
        self.es = contextlib.ExitStack()
        self.es_wob = self.es.enter_context(contextlib.ExitStack())
        self.sched = {e: [] for e in ENG}
        self.ctr = {e: 0 for e in ENG}
        self.chctr = {}
        self.waited = {c: {} for c in ENG}
        self._n = 0

    def _nm(self, pfx):
        self._n += 1
        return f"{pfx}{self._n}"

    def sb(self, shape, dt=F32, name=None, es=None):
        return (es or self.es).enter_context(
            self.nc.sbuf_tensor(name or self._nm("sb"), shape, dt))

    def ps(self, shape, dt=F32, name=None, es=None):
        return (es or self.es).enter_context(
            self.nc.psum_tensor(name or self._nm("ps"), shape, dt))

    def din(self, name, shape, dt=F32):
        return self.nc.dram_tensor(name, shape, dt, kind="ExternalInput").ap()

    def dout(self, name, shape, dt=F32):
        return self.nc.dram_tensor(name, shape, dt, kind="ExternalOutput").ap()

    def dscr(self, name, shape, dt=F32):
        return self.nc.dram_tensor(name, shape, dt, kind="Internal").ap()

    def _wait(self, eng, sem, v):
        if self.waited[eng].get(sem, 0) < v:
            self.sched[eng].append(("w", sem, v))
            self.waited[eng][sem] = v

    def op(self, eng, fn, after=(), inc=True, ch=None):
        sch = self.sched[eng]
        if eng in AUTO_SELF and self.ctr[eng] > self.waited[eng].get(eng, 0):
            self._wait(eng, eng, self.ctr[eng])
        for tok in after:
            if tok is None:
                continue
            self._wait(eng, tok[0], tok[1])
        if ch is not None:
            self.chctr[ch] = self.chctr.get(ch, 0) + 16
            sch.append(("op", fn, ("c", ch, self.chctr[ch])))
            return (ch, self.chctr[ch])
        if inc:
            self.ctr[eng] += 1
            sch.append(("op", fn, ("e", eng, self.ctr[eng])))
            return (eng, self.ctr[eng])
        sch.append(("op", fn, None))
        return None

    def tok(self, eng):
        return (eng, self.ctr[eng])

    def barrier(self):
        toks = [(e, self.ctr[e]) for e in ENG]
        toks += [(c, v) for c, v in self.chctr.items()]
        for e in ENG:
            for pe, v in toks:
                if v > 0:
                    self._wait(e, pe, v)

    def finalize(self):
        self.barrier()
        sems = {e: self.es.enter_context(self.nc.semaphore("S" + e)) for e in ENG}
        for c in self.chctr:
            sems[c] = self.es.enter_context(self.nc.semaphore("C" + c))
        block = self.es.enter_context(self.nc.Block())

        def emit2(eng_name):
            def body(engine):
                for item in self.sched[eng_name]:
                    if item[0] == "w":
                        engine.wait_ge(sems[item[1]], item[2])
                    else:
                        inst = item[1](engine)
                        if item[2] is not None:
                            kind, nm, val = item[2]
                            inst.then_inc(sems[nm], 16 if kind == "c" else 1)
            return body

        block.sync(emit2("sync"))
        block.tensor(emit2("tensor"))
        block.vector(emit2("vector"))
        block.scalar(emit2("scalar"))
        block.gpsimd(emit2("gpsimd"))
        self.es.close()
        return self.nc

    def dma(self, eng, out, in_, after=(), ch="c0"):
        return self.op(eng, lambda e, o=out, i=in_: e.dma_start(out=o, in_=i),
                       after=after, ch=ch)

    def mm(self, out, lhsT, rhs, start, stop, after=(), inc=True):
        return self.op("tensor",
                       lambda e, o=out, l=lhsT, r=rhs, s1=start, s2=stop:
                       e.matmul(o, l, r, start=s1, stop=s2),
                       after=after, inc=inc)

    def tr(self, out, in_, ident, after=(), inc=True):
        return self.op("tensor",
                       lambda e, o=out, i=in_, d=ident: e.transpose(o, i, d),
                       after=after, inc=inc)

    def act(self, out, in_, func, bias=None, scale=1.0, accum_out=None, after=()):
        def fn(e, o=out, i=in_, f=func, b=bias, s=scale, a=accum_out):
            kw = dict(scale=s)
            if b is not None:
                kw["bias"] = b
            if a is not None:
                kw["accum_out"] = a
            return e.activation(o, i, f, **kw)
        return self.op("scalar", fn, after=after)

    def scopy(self, out, in_, after=()):
        return self.op("scalar", lambda e, o=out, i=in_: e.copy(o, i), after=after)

    def tt(self, out, in0, in1, op, after=(), eng="vector"):
        return self.op(eng, lambda e, o=out, a=in0, b=in1, p=op:
                       e.tensor_tensor(o, a, b, p), after=after)

    def ts(self, out, in0, s1, s2, op0, op1=None, after=(), accum_out=None, eng="vector"):
        def fn(e, o=out, i=in0, a=s1, b=s2, p0=op0, p1=op1, ac=accum_out):
            kw = {}
            if p1 is not None:
                kw["op1"] = p1
            if ac is not None:
                kw["accum_out"] = ac
            return e.tensor_scalar(o, i, a, b, p0, **kw)
        return self.op(eng, fn, after=after)

    def stt(self, out, in0, scalar, in1, op0, op1, after=()):
        return self.op("vector", lambda e, o=out, i0=in0, s=scalar, i1=in1, p0=op0, p1=op1:
                       e.scalar_tensor_tensor(o, i0, s, i1, p0, p1), after=after)

    def vcopy(self, out, in_, after=(), eng="vector"):
        return self.op(eng, lambda e, o=out, i=in_: e.tensor_copy(o, i), after=after)

    def red(self, out, in_, op, negate=False, after=()):
        return self.op("vector", lambda e, o=out, i=in_, p=op, n=negate:
                       e.tensor_reduce(o, i, axis=AX.X, op=p, negate=n), after=after)

    def recip(self, out, in_, after=()):
        return self.op("vector", lambda e, o=out, i=in_: e.reciprocal(o, i), after=after)

    def memset(self, ap, val, after=(), eng="gpsimd"):
        return self.op(eng, lambda e, a=ap, v=val: e.memset(a, v), after=after)



def host_prep(inputs, core):
    f32 = np.float32
    bf = BF
    d = {}
    d["x"] = np.ascontiguousarray(inputs["L0_attn_x"][core], f32)
    d["wq"] = np.ascontiguousarray(inputs["L0_attn_Wq"], f32)
    d["wk"] = np.ascontiguousarray(inputs["L0_attn_Wk"], f32)
    d["wv"] = np.ascontiguousarray(inputs["L0_attn_Wv"], f32)
    d["wg"] = np.ascontiguousarray(inputs["L0_attn_Wg"], f32)
    d["wgk1"] = np.ascontiguousarray(inputs["L0_attn_Wgk1"], f32)
    d["wgk2"] = np.ascontiguousarray(inputs["L0_attn_Wgk2"], f32)
    d["bgk2r"] = np.ascontiguousarray(inputs["L0_attn_bgk2"].reshape(1, 512), f32)
    d["nwc"] = np.ascontiguousarray(inputs["L0_attn_norm_w"].reshape(128, 1), f32)
    d["wo"] = np.ascontiguousarray(inputs["L0_attn_Wo"], f32)
    d["gwT"] = np.ascontiguousarray(inputs["L0_ffn_gate_w"].T, f32)
    wupT = np.ascontiguousarray(inputs["L0_ffn_w_up"].transpose(0, 2, 1), f32)
    d["wupT_own"] = np.ascontiguousarray(wupT[core])
    d["wdn_own"] = np.ascontiguousarray(inputs["L0_ffn_w_down"][core], f32)
    d["gam_c"] = np.ascontiguousarray(inputs["L1_attn_gn_gamma"].reshape(8, 128).T, f32)
    d["bet_c"] = np.ascontiguousarray(inputs["L1_attn_gn_beta"].reshape(8, 128).T, f32)
    d["wqkv"] = np.ascontiguousarray(inputs["L1_attn_W_qkv"], f32)
    d["bqkvr"] = np.ascontiguousarray(inputs["L1_attn_b_qkv"].reshape(1, 3072), f32)
    d["wo1"] = np.ascontiguousarray(inputs["L1_attn_W_o"], f32)
    d["bo1r"] = np.ascontiguousarray(inputs["L1_attn_b_o"].reshape(1, 1024), f32)
    d["wg1"] = np.ascontiguousarray(inputs["L1_ffn_wg1"], f32)
    d["bg1c"] = np.ascontiguousarray(inputs["L1_ffn_bg1"].reshape(2, 128).T, f32)
    d["wg2"] = np.ascontiguousarray(inputs["L1_ffn_wg2"], f32)
    d["win"] = np.ascontiguousarray(inputs["L1_ffn_w_in"], f32)
    d["wout"] = np.ascontiguousarray(inputs["L1_ffn_w_out"], f32)
    tri = np.triu(np.ones((64, 64), f32))
    u2 = np.zeros((128, 128), f32)
    u2[:64, :64] = tri / 16.0
    u2[64:, 64:] = tri / 16.0
    d["U2b"] = u2.astype(bf)
    m2 = np.zeros((128, 128), f32)
    m2[:64, :64] = tri
    m2[64:, 64:] = tri
    d["M2f"] = m2
    d["idb"] = np.eye(128, dtype=f32).astype(bf)
    d["idf"] = np.eye(128, dtype=f32)
    d["onesr"] = np.ones((1, 128), f32).astype(bf)
    d["onesc"] = np.ones((128, 1), f32).astype(bf)
    cc = np.zeros((128, 6), f32)
    cc[:, 0] = 1.0
    cc[:, 1] = 1e-5
    cc[:, 2] = 1e-6
    cc[:, 3] = LNR8
    cc[:, 4] = 1e-30
    d["cc"] = cc
    d["onesr512"] = np.ones((1, 512), f32).astype(bf)
    e4 = np.zeros((128, 4), f32)
    for j in range(4):
        e4[32 * j:32 * (j + 1), j] = 1.0
    d["E4"] = e4
    d["E4T"] = np.ascontiguousarray(e4.T)
    return d


def declare_io(p):
    io = {}
    io["x"] = p.din("x", [T, D])
    io["wq"] = p.din("wq", [D, 512]); io["wk"] = p.din("wk", [D, 512])
    io["wv"] = p.din("wv", [D, D]); io["wg"] = p.din("wg", [D, D])
    io["wgk1"] = p.din("wgk1", [D, 16]); io["wgk2"] = p.din("wgk2", [16, 512])
    io["bgk2r"] = p.din("bgk2r", [1, 512]); io["nwc"] = p.din("nwc", [128, 1])
    io["wo"] = p.din("wo", [D, D]); io["gwT"] = p.din("gwT", [D, 8])
    io["wupT_own"] = p.din("wupT_own", [4096, D])
    io["wdn_own"] = p.din("wdn_own", [4096, D])
    io["gam_c"] = p.din("gam_c", [128, 8]); io["bet_c"] = p.din("bet_c", [128, 8])
    io["wqkv"] = p.din("wqkv", [D, 3072]); io["bqkvr"] = p.din("bqkvr", [1, 3072])
    io["wo1"] = p.din("wo1", [D, D]); io["bo1r"] = p.din("bo1r", [1, D])
    io["wg1"] = p.din("wg1", [D, 256]); io["bg1c"] = p.din("bg1c", [128, 2])
    io["wg2"] = p.din("wg2", [256, 8])
    io["win"] = p.din("win", [8, D, 2048]); io["wout"] = p.din("wout", [8, 2048, D])
    io["U2b"] = p.din("U2b", [128, 128], BF16)
    io["M2f"] = p.din("M2f", [128, 128])
    io["idb"] = p.din("idb", [128, 128], BF16)
    io["idf"] = p.din("idf", [128, 128])
    io["onesr"] = p.din("onesr", [1, 128], BF16)
    io["onesc"] = p.din("onesc", [128, 1], BF16)
    io["cc"] = p.din("cc", [128, 6])
    io["onesr512"] = p.din("onesr512", [1, 512], BF16)
    io["E4"] = p.din("E4", [128, 4])
    io["E4T"] = p.din("E4T", [4, 128])
    io["out"] = p.dout("out", [T, D])
    return io



def load_weight_bf16(p, src, rows, cols, es, scale_col=None, prev=None):
    nch = rows // 128
    chp = p._nm("lw")
    wbf = p.sb([128, nch * cols], BF16, es=es)
    stg = [p.sb([128, cols], F32, es=es), p.sb([128, cols], F32, es=es)]
    stok = [None, None]
    for c in range(nch):
        s = c % 2
        tl = p.dma("sync", stg[s][:], src[c * 128:(c + 1) * 128, :],
                   after=[stok[s], prev], ch=chp + str(s))
        if scale_col is None:
            if s == 0:
                tv = p.vcopy(wbf[:, c * cols:(c + 1) * cols], stg[s][:],
                             after=[tl], eng="vector")
            else:
                tv = p.act(wbf[:, c * cols:(c + 1) * cols], stg[s][:], AF.Copy,
                           after=[tl])
        else:
            tv = p.ts(wbf[:, c * cols:(c + 1) * cols], stg[s][:], scale_col, None,
                      OP.mult, after=[tl], eng=("vector" if s == 0 else "gpsimd"))
        stok[s] = tv
    return wbf, stok[0], stok[1]


def load_row_bf16(p, src, cols, es):
    stg = p.sb([1, cols], F32, es=es)
    row = p.sb([1, cols], BF16, es=es)
    tl = p.dma("sync", stg[:], src, ch=p._nm("lr"))
    tv = p.vcopy(row[:], stg[:], after=[tl])
    return row, tv



def emit_ln(p, src_dram, xnfm, cc, idb, eps_col_idx=1, xn_tm_dram=None):
    with contextlib.ExitStack() as es:
        xt = [p.sb([128, D], F32, es=es) for _ in range(2)]
        junk = p.sb([128, D], BF16, es=es)
        xnb = [p.sb([128, D], BF16, es=es) for _ in range(2)]
        colp = [p.sb([128, 8], F32, es=es) for _ in range(2)]
        ptb = [p.ps([128, 128], BF16, es=es) for _ in range(2)]
        tx = [None, None]
        txn = [None, None]
        tpt = [None, None]
        tj = None
        for t in range(NT):
            s = t % 2
            ctl = colp[s]
            tl = p.dma("sync", xt[s][:], src_dram[t * 128:(t + 1) * 128, :],
                       after=[tx[s]], ch="lnx%d" % s)
            tssq = p.act(junk[:], xt[s][:], AF.Square, accum_out=ctl[:, 1:2],
                         after=[tl, tj, tx[s]])
            tj = tssq
            tsum = p.red(ctl[:, 0:1], xt[s][:], OP.add, after=[tl, tx[s]])
            tm = p.ts(ctl[:, 2:3], ctl[:, 0:1], 1.0 / D, None, OP.mult, after=[tsum])
            tq = p.ts(ctl[:, 3:4], ctl[:, 1:2], 1.0 / D, None, OP.mult, after=[tssq])
            tmm = p.tt(ctl[:, 4:5], ctl[:, 2:3], ctl[:, 2:3], OP.mult, after=[tm])
            tvar = p.tt(ctl[:, 4:5], ctl[:, 3:4], ctl[:, 4:5], OP.subtract,
                        after=[tq, tmm])
            tln = p.act(ctl[:, 5:6], ctl[:, 4:5], AF.Ln,
                        bias=cc[:, eps_col_idx:eps_col_idx + 1], after=[tvar])
            trr = p.act(ctl[:, 5:6], ctl[:, 5:6], AF.Exp, scale=-0.5, after=[tln])
            tmr = p.tt(ctl[:, 6:7], ctl[:, 2:3], ctl[:, 5:6], OP.mult, after=[trr])
            txnb = p.ts(xnb[s][:], xt[s][:], ctl[:, 5:6], ctl[:, 6:7],
                        OP.mult, OP.subtract, after=[tmr, txn[s]])
            tx[s] = txnb
            if xn_tm_dram is not None:
                txn_d = p.dma("gpsimd", xn_tm_dram[t * 128:(t + 1) * 128, :],
                              xnb[s][:], after=[txnb], ch="lnxo%d" % s)
            else:
                txn_d = None
            last = txnb
            for c in range(8):
                ss = c % 2
                ttr = p.tr(ptb[ss][:], xnb[s][:, c * 128:(c + 1) * 128], idb,
                           after=[txnb, tpt[ss]])
                tcp = p.vcopy(xnfm[:, c * T + t * 128: c * T + (t + 1) * 128],
                              ptb[ss][:], after=[ttr])
                tpt[ss] = tcp
                last = tcp
            txn[s] = last if txn_d is None else txn_d
        p.barrier()


def emit_moe_mown(p, io, scr, cst):
    with contextlib.ExitStack() as es:
        wdnb = p.sb([128, 32 * 1024], BF16, es=es)
        wupb = [p.sb([128, 4096], BF16, es=es) for _ in range(2)]
        stgw = [p.sb([128, 1024], F32, es=es) for _ in range(2)]
        stgu = [p.sb([128, 2048], F32, es=es) for _ in range(2)]
        mstg = p.sb([128, 1024], BF16, es=es)
        pM = [p.ps([128, 1024], F32, es=es) for _ in range(2)]
        wstg = [None, None]
        stu_tok = [None, None]

        def cast_copy(eng, out, in_, after):
            if eng == "scalar":
                return p.act(out, in_, AF.Copy, after=after)
            return p.vcopy(out, in_, after=after, eng=eng)

        wdn_rdy = [None, None]
        for c in range(32):
            s = c % 2
            tl = p.dma("gpsimd", stgw[s][:],
                       io["wdn_own"][c * 128:(c + 1) * 128, :], after=[wstg[s]],
                       ch="wdn%d" % s)
            wdn_rdy[s] = cast_copy("vector" if s == 0 else "scalar",
                                   wdnb[:, 1024 * c:1024 * (c + 1)], stgw[s][:],
                                   [tl])
            wstg[s] = wdn_rdy[s]
        upcv = [None, None]
        upcons = [None, None]

        def load_up(mi):
            bm = mi % 2
            wsrc = io["wupT_own"].rearrange("(k p) c -> p k c", p=128)
            wsrc = wsrc[:, :, 128 * mi:128 * (mi + 1)]
            tcvs = []
            for hh in range(2):
                sl = wsrc[:, 16 * hh:16 * (hh + 1), :]
                tl = p.dma("scalar", stgu[hh][:], sl,
                           after=[stu_tok[hh], upcons[bm]], ch="wup%d" % hh)
                tcv = cast_copy("vector" if hh == 0 else "scalar",
                                wupb[bm][:, 2048 * hh:2048 * (hh + 1)],
                                stgu[hh][:], [tl, upcons[bm]])
                stu_tok[hh] = tcv
                tcvs.append(tcv)
            upcv[bm] = tcvs

        load_up(0)
        wpM = [None, None]
        wms = None
        for m in range(8):
            sm = m % 2
            if m < 7:
                load_up(m + 1)
            tk = None
            for k in range(32):
                for n in range(2):
                    tk = p.mm(pM[sm][:, 512 * n:512 * (n + 1)],
                              wupb[m % 2][:, 128 * k:128 * (k + 1)],
                              wdnb[:, 1024 * k + 512 * n: 1024 * k + 512 * (n + 1)],
                              start=(k == 0), stop=(k == 31),
                              after=([*upcv[m % 2], wdn_rdy[0], wdn_rdy[1],
                                      wpM[sm]] if k == 0 else ()),
                              inc=(k == 31 and n == 1))
            upcons[m % 2] = tk
            tev = p.act(mstg[:], pM[sm][:], AF.Copy, after=[tk, wms])
            wpM[sm] = tev
            wms = p.dma("gpsimd", scr["mown"][m * 128:(m + 1) * 128, :], mstg[:],
                        after=[tev], ch="mow")
        p.barrier()
        tcc = p.op("gpsimd",
                   lambda e: e.collective_compute(
                       "AllGather", mybir.AluOpType.bypass,
                       replica_groups=[list(range(8))],
                       ins=[scr["mown"].opt()],
                       outs=[scr["mall"].opt()]),
                   after=[wms])
        p.mall_tok = tcc


def emit_l0_proj(p, io, scr, cst, xnfm, moe):
    cc, onesr = cst["cc"], cst["onesr"]
    with contextlib.ExitStack() as es:
        masks = moe["masks"]
        p.memset(moe["yacc"][:, :8192], 0.0)
        p.memset(moe["yacc"][:, 8192:], 0.0)
        gwTb = p.sb([128, 64], BF16, es=es)
        stg8 = p.sb([128, 8], F32, es=es)
        tgw = None
        for c in range(8):
            tl = p.dma("sync", stg8[:], io["gwT"][c * 128:(c + 1) * 128, :],
                       after=[tgw], ch="gwt")
            tgw = p.vcopy(gwTb[:, 8 * c:8 * (c + 1)], stg8[:], after=[tl])
        pl = p.ps([128, 8], F32, es=es)
        lmt = p.sb([128, 8], F32, es=es)
        ismx = p.sb([128, 8], F32, es=es)
        l2t = p.sb([128, 8], F32, es=es)
        mxc = p.sb([128, 2], F32, es=es)
        wpl = None
        for t in range(NT):
            tk = None
            for c in range(8):
                tk = p.mm(pl[:], xnfm[:, c * T + 128 * t: c * T + 128 * (t + 1)],
                          gwTb[:, 8 * c:8 * (c + 1)], start=(c == 0), stop=(c == 7),
                          after=[tgw, wpl] if c == 0 else (), inc=(c == 7))
            t1 = p.red(mxc[:, 0:1], pl[:], OP.max, after=[tk])
            t2 = p.ts(lmt[:], pl[:], mxc[:, 0:1], None, OP.subtract, after=[t1])
            t3 = p.ts(ismx[:], lmt[:], 0.0, None, OP.is_ge, after=[t2])
            t4 = p.stt(l2t[:], ismx[:], -1e30, pl[:], OP.mult, OP.add, after=[t3])
            t5 = p.red(mxc[:, 1:2], l2t[:], OP.max, after=[t4])
            t6 = p.ts(masks[:, 8 * t:8 * (t + 1)], pl[:], mxc[:, 1:2], None,
                      OP.is_ge, after=[t5])
            wpl = t6
        wqb, _, _ = load_weight_bf16(p, io["wq"], D, 512, es)
        wkb, _, _ = load_weight_bf16(p, io["wk"], D, 512, es)
        wvb, _, _ = load_weight_bf16(p, io["wv"], D, D, es)
        wgb, _, _ = load_weight_bf16(p, io["wg"], D, D, es)
        nwcs = p.sb([128, 1], F32, es=es)
        tnw = p.dma("sync", nwcs[:], io["nwc"], ch="nwc")
        wob = p.wob
        stgo = [p.sb([128, D], F32, es=es) for _ in range(2)]
        stok = [None, None]
        for c in range(8):
            s = c % 2
            tl = p.dma("sync", stgo[s][:], io["wo"][c * 128:(c + 1) * 128, :],
                       after=[stok[s]], ch="wo%d" % s)
            tv = p.ts(wob[:, c * D:(c + 1) * D], stgo[s][:], nwcs[:], None,
                      OP.mult, after=[tl, tnw],
                      eng=("vector" if s == 0 else "gpsimd"))
            stok[s] = tv
        wgk1b, _, _ = load_weight_bf16(p, io["wgk1"], D, 16, es)
        wgk2b = p.sb([16, 512], BF16, es=es)
        stg2 = p.sb([16, 512], F32, es=es)
        t2 = p.dma("sync", stg2[:], io["wgk2"], ch="wgk2")
        p.vcopy(wgk2b[:], stg2[:], after=[t2])
        bgk2b, tbg = load_row_bf16(p, io["bgk2r"], 512, es)
        p.barrier()

        psA = [p.ps([128, 512], F32, es=es) for _ in range(2)]
        psB = [p.ps([128, 1024], F32, es=es) for _ in range(2)]
        stgb = [p.sb([128, 1024], BF16, es=es) for _ in range(2)]
        ptoks = [None, None]
        etoksA = [None, None]
        etoksB = [None, None]
        for (wb, dst) in ((wqb, scr["qfm"]), (wkb, scr["kfm"])):
            for m in range(4):
                for g in range(4):
                    s = (m * 4 + g) % 2
                    tk = None
                    for c in range(8):
                        tk = p.mm(psA[s][:],
                                  wb[:, 512 * c + 128 * m: 512 * c + 128 * (m + 1)],
                                  xnfm[:, c * T + 512 * g: c * T + 512 * (g + 1)],
                                  start=(c == 0), stop=(c == 7),
                                  after=[etoksA[s]] if c == 0 else (), inc=(c == 7))
                    te = p.act(stgb[s][:, :512], psA[s][:], AF.Copy,
                               after=[tk, ptoks[s]])
                    td = p.dma("gpsimd",
                               dst[128 * m:128 * (m + 1), 512 * g:512 * (g + 1)],
                               stgb[s][:, :512], after=[te], ch="pev%d" % s)
                    ptoks[s] = td
                    etoksA[s] = te
        for (wb, ncols, dst) in ((wkb, 512, scr["ktm"]), (wvb, D, scr["vtm"]),
                                 (wgb, D, scr["gtm"])):
            ngr = ncols // 512
            for t in range(NT):
                s = t % 2
                ps = psB[s]
                tk = None
                for n in range(ngr):
                    for c in range(8):
                        tk = p.mm(ps[:, 512 * n:512 * (n + 1)],
                                  xnfm[:, c * T + 128 * t: c * T + 128 * (t + 1)],
                                  wb[:, ncols * c + 512 * n: ncols * c + 512 * (n + 1)],
                                  start=(c == 0), stop=(c == 7),
                                  after=[etoksB[s]] if (c == 0 and n == 0) else (),
                                  inc=(c == 7 and n == ngr - 1))
                te = p.act(stgb[s][:, :ncols], ps[:, :ncols], AF.Copy,
                           after=[tk, ptoks[s]])
                td = p.dma("gpsimd", dst[128 * t:128 * (t + 1), :],
                           stgb[s][:, :ncols], after=[te], ch="pev%d" % s)
                ptoks[s] = td
                etoksB[s] = te
        ufmb = p.sb([16, T], BF16, es=es)
        psU = p.ps([16, 512], F32, es=es)
        tprev = None
        for g in range(4):
            tk = None
            for c in range(8):
                tk = p.mm(psU[:], wgk1b[:, 16 * c:16 * (c + 1)],
                          xnfm[:, c * T + 512 * g: c * T + 512 * (g + 1)],
                          start=(c == 0), stop=(c == 7),
                          after=[tprev] if c == 0 else (), inc=(c == 7))
            tprev = p.act(ufmb[:, 512 * g:512 * (g + 1)], psU[:], AF.Copy, after=[tk])
        zcol = [p.sb([128, 512], F32, es=es) for _ in range(2)]
        zc2 = [p.sb([128, 512], F32, es=es) for _ in range(2)]
        lsb = [p.sb([128, 512], BF16, es=es) for _ in range(2)]
        dtoks = [None, None]
        wzc = [None, None]
        for t in range(NT):
            s = t % 2
            ps = psA[s]
            p.mm(ps[:], ufmb[:, 128 * t:128 * (t + 1)], wgk2b[:],
                 start=True, stop=False, after=[tprev, etoksA[s]], inc=False)
            tk = p.mm(ps[:], onesr, bgk2b[:], start=False, stop=True, after=[tbg])
            tabs = p.act(zcol[s][:], ps[:], AF.Abs, after=[tk, wzc[s]])
            texp = p.act(zcol[s][:], zcol[s][:], AF.Exp, scale=-1.0, after=[tabs])
            tl1p = p.act(zcol[s][:], zcol[s][:], AF.Ln, bias=cc[:, 0:1], after=[texp])
            tmin = p.ts(zc2[s][:], ps[:], 0.0, None, OP.min, after=[tk, tabs])
            etoksA[s] = tmin
            tls = p.tt(lsb[s][:], zc2[s][:], zcol[s][:], OP.subtract,
                       after=[tmin, tl1p, dtoks[s]])
            wzc[s] = tls
            td = p.dma("gpsimd", scr["lsz"][128 * t:(t + 1) * 128, :], lsb[s][:],
                       after=[tls], ch="lszo%d" % s)
            dtoks[s] = td
        p.barrier()


def emit_l0_scan(p, io, scr, cst, moe):
    cc, idb, U2b, M2f = cst["cc"], cst["idb"], cst["U2b"], cst["M2f"]
    wob = p.wob
    masks, yacc = moe["masks"], moe["yacc"]
    with contextlib.ExitStack() as es:
        lszt = [p.sb([128, 512], BF16, es=es) for _ in range(2)]
        ktt = [p.sb([128, 512], BF16, es=es) for _ in range(2)]
        vtt = [p.sb([128, 1024], BF16, es=es) for _ in range(2)]
        gtt = [p.sb([128, 1024], BF16, es=es) for _ in range(2)]
        qfh = [[p.sb([64, 128], BF16, es=es) for _ in range(H)] for _ in range(2)]
        kfh = [[p.sb([64, 128], BF16, es=es) for _ in range(H)] for _ in range(2)]
        QE0 = p.sb([128, 128], BF16, es=es)
        QE1 = p.sb([128, 128], BF16, es=es)
        KE0 = p.sb([128, 128], BF16, es=es)
        KE1 = p.sb([128, 128], BF16, es=es)
        KT0 = p.sb([128, 64], BF16, es=es)
        KT1 = p.sb([128, 64], BF16, es=es)
        ebf = p.sb([64, 128], F32, es=es)
        enf = p.sb([64, 128], F32, es=es)
        ent = p.sb([128, 64], F32, es=es)
        e0c = p.sb([64, 1], F32, es=es)
        e1c = p.sb([64, 1], F32, es=es)
        A2m = p.sb([128, 128], BF16, es=es)
        S_A = p.sb([64, 8 * 128], F32, es=es)
        S_B = p.sb([64, 8 * 128], F32, es=es)
        S_Ab = p.sb([128, 8 * 128], BF16, es=es)
        S_Bb = p.sb([128, 8 * 128], BF16, es=es)
        tz = []
        tz.append(p.memset(QE0[:], 0.0))
        tz.append(p.memset(QE1[:], 0.0))
        tz.append(p.memset(KE0[:], 0.0))
        tz.append(p.memset(KE1[:], 0.0))
        tz.append(p.memset(KT0[:], 0.0))
        tz.append(p.memset(KT1[:], 0.0))
        tz.append(p.memset(S_A[:], 0.0))
        tz.append(p.memset(S_B[:], 0.0))
        tz.append(p.memset(S_Ab[:], 0.0))
        tz.append(p.memset(S_Bb[:], 0.0))
        p.barrier()
        pbb = p.ps([128, 192], F32, es=es)
        pbtm = pbb[:, 0:64]
        pbfm = pbb[0:64, 64:192]
        pAP = p.ps([128, 256], F32, es=es)
        pAT = pAP[:, 0:128]
        po = pAP[:, 128:256]
        pkv0 = p.ps([64, 128], F32, es=es)
        pkv1 = pkv0
        py = [p.ps([128, 512], F32, es=es) for _ in range(2)]
        Mb2 = [p.sb([128, 8 * 1024], BF16, es=es) for _ in range(2)]
        mld = [None, None]
        mcons = [None, None]
        wpy = [None, None]

        def load_m(ei):
            bm = ei % 2
            tl = None
            for c in range(8):
                tl = p.dma("sync", Mb2[bm][:, 1024 * c:1024 * (c + 1)],
                           scr["mall"].rearrange("(e k p) c -> e k p c", e=8, p=128)[ei, c],
                           after=[mcons[bm], p.mall_tok], ch="mld%d" % bm)
            mld[bm] = tl

        def emit_y_half(u, n):
            e, ty = u // 16, u % 16
            bm = e % 2
            tk = None
            for c in range(8):
                tk = p.mm(py[n][:],
                          xnfm0[:, c * T + 128 * ty: c * T + 128 * (ty + 1)],
                          Mb2[bm][:, 1024 * c + 512 * n: 1024 * c + 512 * (n + 1)],
                          start=(c == 0), stop=(c == 7),
                          after=[mld[bm], wpy[n]] if c == 0 else (),
                          inc=(c == 7))
            tac = p.stt(yacc[:, 1024 * ty + 512 * n: 1024 * ty + 512 * (n + 1)],
                        py[n][:],
                        masks[:, 8 * ty + e: 8 * ty + e + 1],
                        yacc[:, 1024 * ty + 512 * n: 1024 * ty + 512 * (n + 1)],
                        OP.mult, OP.add, after=[tk])
            wpy[n] = tac
            mcons[bm] = tk

        xnfm0 = moe["xnfm"]
        load_m(0)
        ot = p.sb([128, 1024], F32, es=es)
        sqt = p.sb([128, 1024], F32, es=es)
        sgt = p.sb([128, 1024], F32, es=es)
        rs8 = p.sb([128, 8], F32, es=es)
        ogb = p.sb([128, 1024], BF16, es=es)
        ogf = p.sb([128, 1024], BF16, es=es)
        ptb = [p.ps([128, 128], BF16, es=es) for _ in range(2)]
        ps0 = p.ps([128, 512], F32, es=es)
        s0t = [p.sb([128, 1024], F32, es=es) for _ in range(2)]

        tld = [dict(lsz=None, k=None, v=None, g=None, q=[None] * H, kf=None) for _ in range(2)]
        w = dict(pbtm=None, pbfm=None, pAT=None, pkv0=None, pkv1=None, po=None,
                 ebf=None, enf=None, ent=None, e0=None, e1=None, A2m=None,
                 QE0=None, QE1=None, KE0=None, KE1=None, KT0=None, KT1=None,
                 SA=None, SB=None, SAb=None, SBb=None, ot=None, ogb=None,
                 ogf=None, ps0=None, s0t=[None, None], sqt=None, sgt=None,
                 rs8=None)
        prevdma = [None, None]
        for t in range(NT):
            s = t % 2
            if t % 2 == 1 and t // 2 < 7:
                load_m(t // 2 + 1)
            pd = prevdma[s]
            chn = "scan%d" % s
            p.dma("sync", lszt[s][:], scr["lsz"][t * 128:(t + 1) * 128, :], after=[pd], ch=chn)
            p.dma("sync", ktt[s][:], scr["ktm"][t * 128:(t + 1) * 128, :], after=[pd], ch=chn)
            p.dma("sync", vtt[s][:], scr["vtm"][t * 128:(t + 1) * 128, :], after=[pd], ch=chn)
            p.dma("sync", gtt[s][:], scr["gtm"][t * 128:(t + 1) * 128, :], after=[pd], ch=chn)
            for h in range(H):
                p.dma("sync", qfh[s][h][:],
                      scr["qfm"][64 * h:64 * (h + 1), t * 128:(t + 1) * 128],
                      after=[pd], ch=chn)
                t_ld = p.dma("sync", kfh[s][h][:],
                             scr["kfm"][64 * h:64 * (h + 1), t * 128:(t + 1) * 128],
                             after=[pd], ch=chn)
            t_lsz = t_k = t_v = t_g = t_ld
            t_q = [t_ld] * H
            t_kf = [t_ld] * H
            for h in range(H):
                hs = slice(64 * h, 64 * (h + 1))
                vs = slice(128 * h, 128 * (h + 1))
                tb1 = p.mm(pbtm, U2b, lszt[s][:, hs], True, True,
                           after=[t_lsz, w["pbtm"], w["pbfm"]])
                tb2 = p.mm(pbfm, lszt[s][:, hs], U2b, True, True,
                           after=[t_lsz, w["pbfm"]])
                emit_y_half(t * 8 + h, 0)
                tent = p.act(ent[:], pbtm, AF.Exp, scale=-1.0,
                             after=[tb1, tb2, w["ent"]])
                w["pbtm"] = tent
                tebf = p.act(ebf[:], pbfm, AF.Exp, bias=cc[0:64, 3:4],
                             after=[tb2, w["ebf"]])
                tenf = p.act(enf[:], pbfm, AF.Exp, scale=-1.0,
                             after=[tb2, w["enf"]])
                te0 = p.act(e0c[:], pbfm[:, 63:64], AF.Exp, after=[tb2, w["e0"]])
                te1 = p.act(e1c[:], pbfm[:, 127:128], AF.Exp, after=[tb2, w["e1"]])
                w["pbfm"] = te1
                tQ0 = p.tt(QE0[0:64, 0:64], qfh[s][h][:, 0:64], ebf[:, 0:64],
                           OP.mult, after=[t_q[h], tebf, w["QE0"]])
                tQ1 = p.tt(QE1[0:64, 64:128], qfh[s][h][:, 64:128], ebf[:, 64:128],
                           OP.mult, after=[t_q[h], tebf, w["QE1"]])
                tK0 = p.tt(KE0[0:64, 0:64], kfh[s][h][:, 0:64], enf[:, 0:64],
                           OP.mult, after=[t_kf[h], tenf, w["KE0"]])
                tK1 = p.tt(KE1[0:64, 64:128], kfh[s][h][:, 64:128], enf[:, 64:128],
                           OP.mult, after=[t_kf[h], tenf, w["KE1"]])
                w["ebf"] = tQ1
                w["enf"] = tK1
                tT0 = p.tt(KT0[0:64, :], ktt[s][0:64, hs], ent[0:64, :], OP.mult,
                           after=[t_k, tent, w["KT0"]])
                tT1 = p.tt(KT1[64:128, :], ktt[s][64:128, hs], ent[64:128, :],
                           OP.mult, after=[t_k, tent, w["KT1"]])
                w["ent"] = tT1
                p.mm(pAT, KE0[:], QE0[:], True, False,
                     after=[tK0, tQ0, w["pAT"], w["po"]], inc=False)
                tat = p.mm(pAT, KE1[:], QE1[:], False, True, after=[tK1, tQ1])
                ta2 = p.tt(A2m[:], pAT, M2f, OP.mult, after=[tat, w["A2m"]])
                w["pAT"] = ta2
                tkv0 = p.mm(pkv0[:], KT0[:], vtt[s][:, vs], True, True,
                            after=[tT0, t_v, w["pkv0"]])
                tSB = p.tt(S_B[:, vs], S_A[:, vs], pkv0[:], OP.add,
                           after=[tkv0, w["SB"], w["SA"]])
                w["pkv0"] = tSB
                tSB2 = p.ts(S_B[:, vs], S_B[:, vs], e0c[:], None, OP.mult,
                            after=[te0, tSB])
                tSBb = p.vcopy(S_Bb[0:64, vs], S_B[:, vs], after=[tSB2, w["SBb"]])
                emit_y_half(t * 8 + h, 1)
                tkv1 = p.mm(pkv1[:], KT1[:], vtt[s][:, vs], True, True,
                            after=[tT1, t_v, tSB])
                p.mm(po, A2m[:], vtt[s][:, vs], True, False,
                     after=[ta2, t_v, w["po"]], inc=False)
                tom2 = p.mm(po, QE0[:], S_Ab[:, vs], False, False,
                            after=[tQ0, w["SAb"]])
                tom = p.mm(po, QE1[:], S_Bb[:, vs], False, True,
                           after=[tQ1, tSBb])
                tSA = p.tt(S_A[:, vs], S_B[:, vs], pkv1[:], OP.add,
                           after=[tkv1, tSB2, tom])
                w["pkv0"] = tSA
                w["SB"] = tSA
                tSA2 = p.ts(S_A[:, vs], S_A[:, vs], e1c[:], None, OP.mult,
                            after=[te1, tSA])
                w["SA"] = tSA2
                tSAb = p.vcopy(S_Ab[0:64, vs], S_A[:, vs], after=[tSA2, tom2])
                w["SAb"] = tSAb
                w["SBb"] = tom
                w["e0"] = tSB2
                w["e1"] = tSA2
                w["QE0"] = tom2
                w["QE1"] = tom
                w["KE0"] = tat
                w["KE1"] = tat
                w["KT0"] = tkv0
                w["KT1"] = tkv1
                tev = p.scopy(ot[:, vs], po, after=[tom, w["ot"]])
                w["po"] = tev
                if h == H - 1:
                    w["ot"] = None
                oevac_last = tev
            tsq = p.act(sqt[:], ot[:], AF.Square, after=[oevac_last, w["sqt"]])
            trs = p.red(rs8[:], sqt[:].rearrange("p (h c) -> p h c", c=128),
                        OP.add, after=[tsq, w["rs8"]])
            w["sqt"] = trs
            tms = p.ts(rs8[:], rs8[:], 1.0 / 128.0, None, OP.mult, after=[trs])
            tln = p.act(rs8[:], rs8[:], AF.Ln, bias=cc[:, 1:2], after=[tms])
            trr = p.act(rs8[:], rs8[:], AF.Exp, scale=-0.5, after=[tln])
            w["rs8"] = trr
            tsg = p.act(sgt[:], gtt[s][:], AF.Silu, after=[t_g, w["sgt"]])
            ogtoks = []
            for h in range(H):
                vs = slice(128 * h, 128 * (h + 1))
                tog = p.stt(ogb[:, vs], ot[:, vs], rs8[:, h:h + 1], sgt[:, vs],
                            OP.mult, OP.mult, after=[trr, tsg, w["ogb"]])
                ogtoks.append(tog)
            w["ogb"] = ogtoks[-1]
            w["ot"] = ogtoks[-1]
            w["sgt"] = ogtoks[-1]
            tcpl = None
            for c in range(8):
                ss = c % 2
                ttr = p.tr(ptb[ss][:], ogb[:, c * 128:(c + 1) * 128], idb,
                           after=[ogtoks[c], w.get("ptb%d" % ss)])
                tcp = p.vcopy(ogf[:, c * 128:(c + 1) * 128], ptb[ss][:],
                              after=[ttr, w["ogf"]])
                w["ptb%d" % ss] = tcp
                tcpl = tcp
            for n in range(2):
                tk = None
                for c in range(8):
                    tk = p.mm(ps0[:],
                              ogf[:, 128 * c:128 * (c + 1)],
                              wob[:, D * c + 512 * n: D * c + 512 * (n + 1)],
                              start=(c == 0), stop=(c == 7),
                              after=[tcpl, w["ps0"]] if c == 0 else (),
                              inc=(c == 7))
                tev = p.scopy(s0t[s][:, 512 * n:512 * (n + 1)], ps0[:],
                              after=[tk, w["s0t"][s]])
                w["ps0"] = tev
                w["ogf"] = tk
            td = p.dma("gpsimd", scr["s0"][t * 128:(t + 1) * 128, :], s0t[s][:],
                       after=[tev], ch="s0o%d" % s)
            w["s0t"][s] = td
            prevdma[s] = ogtoks[-1]
        p.barrier()


def emit_l0_combine(p, io, scr, cst, moe):
    yacc = moe["yacc"]
    with contextlib.ExitStack() as es:
        xt = [p.sb([128, 1024], F32, es=es) for _ in range(2)]
        s0l = [p.sb([128, 1024], F32, es=es) for _ in range(2)]
        syt = [p.sb([128, 1024], F32, es=es) for _ in range(2)]
        x1t = [p.sb([128, 1024], F32, es=es) for _ in range(2)]
        wx = [None, None]
        for t in range(NT):
            s = t % 2
            p.dma("sync", xt[s][:], io["x"][t * 128:(t + 1) * 128, :],
                  after=[wx[s]], ch="cmb%d" % s)
            tls = p.dma("sync", s0l[s][:], scr["s0"][t * 128:(t + 1) * 128, :],
                        after=[wx[s]], ch="cmb%d" % s)
            tlx = tls
            tsy = p.act(syt[s][:], yacc[:, 1024 * t:1024 * (t + 1)], AF.Silu,
                        after=[wx[s]])
            tm1 = p.tt(syt[s][:], syt[s][:], s0l[s][:], OP.mult, after=[tsy, tls])
            tx1 = p.tt(x1t[s][:], syt[s][:], xt[s][:], OP.add, after=[tm1, tlx])
            td = p.dma("gpsimd", scr["x1"][t * 128:(t + 1) * 128, :], x1t[s][:],
                       after=[tx1], ch="x1o%d" % s)
            wx[s] = td
        p.barrier()



def emit_consts(p, io):
    cst = {}
    cst["cc"] = p.sb([128, 6], F32, name="cc_t")
    cst["U2b"] = p.sb([128, 128], BF16, name="u2_t")
    cst["M2f"] = p.sb([128, 128], F32, name="m2_t")
    cst["idb"] = p.sb([128, 128], BF16, name="id_t")
    cst["idf"] = p.sb([128, 128], F32, name="idf_t")
    cst["onesr"] = p.sb([1, 128], BF16, name="or_t")
    cst["onesc"] = p.sb([128, 1], BF16, name="oc_t")
    cst["onesr512"] = p.sb([1, 512], BF16, name="o512_t")
    cst["E4"] = p.sb([128, 4], F32, name="e4_t")
    cst["E4T"] = p.sb([4, 128], F32, name="e4t_t")
    toks = []
    for k in ("cc", "U2b", "M2f", "idb", "idf", "onesr", "onesc", "onesr512", "E4", "E4T"):
        toks.append(p.dma("sync", cst[k][:], io[k], ch="cst"))
    p.barrier()
    return {k: (v[:] if k in ("U2b", "M2f", "idb", "idf", "onesr", "onesc",
                              "onesr512", "E4", "E4T") else v)
            for k, v in cst.items()}


def build_program(upto="l1", dbg=()):
    p = Prog()
    io = declare_io(p)
    scr = {}
    scr["qfm"] = p.dscr("qfm_s", [512, T], BF16)
    scr["kfm"] = p.dscr("kfm_s", [512, T], BF16)
    scr["ktm"] = p.dscr("ktm_s", [T, 512], BF16)
    scr["vtm"] = p.dscr("vtm_s", [T, 1024], BF16)
    scr["gtm"] = p.dscr("gtm_s", [T, 1024], BF16)
    scr["lsz"] = p.dscr("lsz_s", [T, 512], BF16)
    scr["s0"] = p.dscr("s0_s", [T, D], F32)
    scr["mown"] = p.dscr("mown_s", [1024, 1024], BF16)
    scr["mall"] = p.dscr("mall_s", [8192, 1024], BF16)
    scr["xn1"] = p.dscr("xn1_s", [T, D], BF16)
    scr["qfm1"] = p.dscr("qfm1_s", [D, T], BF16)
    scr["kfm1"] = p.dscr("kfm1_s", [D, T], BF16)
    scr["s1"] = p.dscr("s1_s", [T, D], F32)
    if upto == "l0":
        scr["x1"] = io["out"]
    else:
        scr["x1"] = p.dscr("x1_s", [T, D], F32)
    dbg_aps = {}
    for name in dbg:
        shp = {"qfm": [512, T], "kfm": [512, T], "ktm": [T, 512],
               "vtm": [T, 1024], "gtm": [T, 1024], "lsz": [T, 512],
               "s0": [T, D], "x1": [T, D], "xn1": [T, D], "s1": [T, D]}[name]
        dt = F32 if name in ("s0", "x1", "s1") else BF16
        dbg_aps[name] = p.dout("dbg_" + name, shp, dt)

    cst = emit_consts(p, io)
    xnfm = p.sb([128, 8 * T], BF16, name="xnfm0")
    es_moe = contextlib.ExitStack()
    moe = {
        "masks": p.sb([128, NT * 8], F32, es=es_moe),
        "yacc": p.sb([128, NT * 1024], BF16, es=es_moe),
        "xnfm": xnfm,
    }
    p.wob = p.sb([128, 8 * D], BF16, name="wob", es=p.es_wob)
    emit_moe_mown(p, io, scr, cst)
    emit_ln(p, io["x"], xnfm, cst["cc"], cst["idb"], eps_col_idx=1)
    emit_l0_proj(p, io, scr, cst, xnfm, moe)
    emit_l0_scan(p, io, scr, cst, moe)
    p.es_wob.close()
    emit_l0_combine(p, io, scr, cst, moe)
    es_moe.close()
    if upto != "l0":
        emit_l1(p, io, scr, cst, xnfm)
    if dbg_aps:
        with contextlib.ExitStack() as es:
            bt = p.sb([128, 2048], BF16, es=es)
            bt32 = p.sb([128, 1024], F32, es=es)
            dbg_last = None
            tprev = None
            for name, dst in dbg_aps.items():
                srcap = scr[name] if name != "x1" else scr["x1"]
                rows, cols = {"qfm": (512, T), "kfm": (512, T), "ktm": (T, 512),
                              "vtm": (T, 1024), "gtm": (T, 1024), "lsz": (T, 512),
                              "s0": (T, D), "x1": (T, D), "xn1": (T, D),
                              "s1": (T, D)}[name]
                tile = bt32 if name in ("s0", "x1", "s1") else bt
                for r in range(0, rows, 128):
                    t1 = p.dma("sync", tile[:, :cols], srcap[r:r + 128, :],
                               after=[tprev, dbg_last], ch="dbgi")
                    dbg_last = t1
                    tprev = p.dma("gpsimd", dst[r:r + 128, :], tile[:, :cols],
                                  after=[t1], ch="dbgo")
            p.barrier()
    return p


def kernel(**inputs):
    n = 8
    p = build_program(upto=KERNEL_UPTO, dbg=())
    nc = p.finalize()
    in_maps = [host_prep(inputs, b) for b in range(n)]
    res = run_bass_kernel_spmd(nc, in_maps, core_ids=list(range(n)))
    out = np.stack([res.results[b]["out"] for b in range(n)], axis=0)
    return out.astype(np.float32)


KERNEL_UPTO = "l1"


def emit_l1_attn(p, io, scr, cst, xnfm):
    cc = cst["cc"]
    idb = cst["idb"]
    onesr = cst["onesr"]
    onesr512 = cst["onesr512"]
    with contextlib.ExitStack() as es:
        ab8 = p.sb([128, 16], F32, es=es)
        browb = p.sb([1, 3072], BF16, es=es)
        brf = p.sb([1, 3072], F32, es=es)
        vtm = p.sb([128, NT * 1040], BF16, name="vtm1", es=es)
        otm = p.sb([128, NT * 1040], BF16, name="otm1", es=es)
        es_s = contextlib.ExitStack()
        colsum = p.sb([128, 8], F32, es=es_s)
        colsq = p.sb([128, 8], F32, es=es_s)
        sqt = [p.sb([128, T], F32, es=es_s) for _ in range(2)]
        g48 = p.sb([4, 32], F32, es=es_s)
        ps48 = p.ps([4, 8], F32, es=es_s)
        ps128 = p.ps([128, 8], F32, es=es_s)
        gamc = p.sb([128, 8], F32, es=es_s)
        betc = p.sb([128, 8], F32, es=es_s)
        tsq = [None, None]
        for c in range(8):
            s = c % 2
            t1 = p.act(sqt[s][:], xnfm[:, c * T:(c + 1) * T], AF.Square,
                       after=[tsq[s]])
            t2 = p.red(colsq[:, c:c + 1], sqt[s][:], OP.add, after=[t1])
            tsq[s] = t2
            p.red(colsum[:, c:c + 1], xnfm[:, c * T:(c + 1) * T], OP.add)
        p.barrier()
        t1 = p.mm(ps48[:], cst["E4"], colsum[:], True, True)
        t2 = p.scopy(g48[:, 0:8], ps48[:], after=[t1])
        t3 = p.mm(ps48[:], cst["E4"], colsq[:], True, True, after=[t2])
        t4 = p.scopy(g48[:, 8:16], ps48[:], after=[t3])
        inv = 1.0 / (T * 32.0)
        t5 = p.ts(g48[:, 16:24], g48[:, 0:8], inv, None, OP.mult, after=[t2, t4])
        t6 = p.ts(g48[:, 8:16], g48[:, 8:16], inv, None, OP.mult, after=[t4])
        t7 = p.tt(g48[:, 24:32], g48[:, 16:24], g48[:, 16:24], OP.mult, after=[t5])
        t8 = p.tt(g48[:, 24:32], g48[:, 8:16], g48[:, 24:32], OP.subtract,
                  after=[t6, t7])
        t9 = p.act(g48[:, 24:32], g48[:, 24:32], AF.Ln, bias=cc[0:4, 2:3], after=[t8])
        t10 = p.act(g48[:, 24:32], g48[:, 24:32], AF.Exp, scale=-0.5, after=[t9])
        t11 = p.mm(ps128[:], cst["E4T"], g48[:, 24:32], True, True, after=[t10])
        tg = p.dma("sync", gamc[:], io["gam_c"], ch="gnp")
        tb = p.dma("sync", betc[:], io["bet_c"], ch="gnp")
        t12 = p.stt(ab8[:, 0:8], ps128[:], 1.0, gamc[:], OP.mult, OP.mult,
                    after=[t11, tb])
        t13 = p.mm(ps128[:], cst["E4T"], g48[:, 16:24], True, True, after=[t12, t10])
        t14 = p.tt(ab8[:, 8:16], ps128[:], ab8[:, 0:8], OP.mult, after=[t13, t12])
        p.tt(ab8[:, 8:16], betc[:], ab8[:, 8:16], OP.subtract, after=[t14])
        p.barrier()
        es_s.close()

        es_w = contextlib.ExitStack()
        wqkvb = p.sb([128, 8 * 3072], BF16, es=es_w)
        stgq = [p.sb([128, 3072], F32, es=es_w) for _ in range(2)]
        psbr = p.ps([1, 3072], F32, es=es_w)
        stok = [None, None]
        tkbs = [None, None]
        tkb = None
        for c in range(8):
            s = c % 2
            tl = p.dma("sync", stgq[s][:], io["wqkv"][c * 128:(c + 1) * 128, :],
                       after=[stok[s], tkbs[s]], ch="wqkv%d" % s)
            for n in range(6):
                tkb = p.mm(psbr[:, 512 * n:512 * (n + 1)], ab8[:, 8 + c:9 + c],
                           stgq[s][:, 512 * n:512 * (n + 1)],
                           start=(c == 0), stop=(c == 7), after=[tl],
                           inc=(n == 5))
            tkbs[s] = tkb
            tv = p.ts(wqkvb[:, 3072 * c:3072 * (c + 1)], stgq[s][:],
                      ab8[:, c:c + 1], None, OP.mult, after=[tl], eng="vector")
            stok[s] = tv
        tbq = p.dma("sync", brf[:], io["bqkvr"], ch="bqk")
        tba = p.tt(brf[:], brf[:], psbr[:], OP.add, after=[tbq, tkb])
        tbb = p.vcopy(browb[:], brf[:], after=[tba])
        p.barrier()

        vones = vtm[:].rearrange("p (t h c) -> p t h c", h=16, c=65)[:, :, :, 64:65]
        tvo = p.memset(vones, 1.0)
        es_v1 = contextlib.ExitStack()
        psv = [p.ps([128, 512], F32, es=es_v1) for _ in range(2)]
        wpv = [None, None]
        for t in range(NT):
            for n in range(2):
                s = n
                for c in range(8):
                    p.mm(psv[s][:],
                         xnfm[:, c * T + 128 * t: c * T + 128 * (t + 1)],
                         wqkvb[:, 3072 * c + 2048 + 512 * n: 3072 * c + 2048 + 512 * (n + 1)],
                         start=(c == 0), stop=False,
                         after=[wpv[s]] if c == 0 else (), inc=False)
                tk = p.mm(psv[s][:], onesr,
                          browb[0:1, 2048 + 512 * n: 2048 + 512 * (n + 1)],
                          start=False, stop=True, after=[tbb])
                for hh in range(8):
                    h = 8 * n + hh
                    wpv[s] = p.vcopy(
                        vtm[:, 1040 * t + 65 * h: 1040 * t + 65 * h + 64],
                        psv[s][:, 64 * hh:64 * (hh + 1)], after=[tk, tvo])
        stqk = [p.sb([64, 512], BF16, es=es_v1) for _ in range(2)]
        wstq = [None, None]
        wpq = [None, None]
        for h in range(16):
            for part, dst, scl in ((0, scr["qfm1"], 0.125), (1, scr["kfm1"], 1.0)):
                base = 1024 * part + 64 * h
                for g in range(4):
                    s = g % 2
                    for c in range(8):
                        p.mm(psv[s][0:64, :],
                             wqkvb[:, 3072 * c + base: 3072 * c + base + 64],
                             xnfm[:, c * T + 512 * g: c * T + 512 * (g + 1)],
                             start=(c == 0), stop=False,
                             after=[wpv[s]] if c == 0 else (), inc=False)
                    tk = p.mm(psv[s][0:64, :], browb[0:1, base:base + 64],
                              onesr512, start=False, stop=True, after=[tbb])
                    te = p.act(stqk[s][:], psv[s][0:64, :], AF.Copy, scale=scl,
                               after=[tk, wstq[s]])
                    wpv[s] = te
                    td = p.dma("gpsimd",
                               dst[64 * h:64 * (h + 1), 512 * g:512 * (g + 1)],
                               stqk[s][:], after=[te], ch="qko%d" % s)
                    wstq[s] = td
        p.barrier()
        es_v1.close()
        es_w.close()

        es_v = contextlib.ExitStack()
        QA = [p.sb([128, T], BF16, es=es_v) for _ in range(2)]
        KA = [p.sb([128, T], BF16, es=es_v) for _ in range(2)]
        PT = p.sb([128, 8192], BF16, es=es_v)
        sqt2 = p.sb([64, T], BF16, es=es_v)
        qn2 = p.sb([1, T], F32, es=es_v)
        kn2 = p.sb([1, T], F32, es=es_v)
        km = p.sb([1, 1], F32, es=es_v)
        brow = p.sb([1, T], F32, es=es_v)
        oht = [p.sb([65, 512], BF16, es=es_v) for _ in range(2)]
        psSt = [p.ps([128, 1024], F32, es=es_v) for _ in range(2)]
        psO = [p.ps([65, 512], F32, es=es_v) for _ in range(2)]
        ptrb = p.ps([128, 128], BF16, es=es_v)
        pnrm = p.ps([1, 512], F32, es=es_v)
        tonesa = p.memset(KA[0][64:65, :], 1.0)
        tonesb = p.memset(KA[1][64:65, :], 1.0)
        tldk = [None, None]
        tSTread = [None, None]
        wPT = [None, None]
        wO = [None, None]
        woht = [None, None]
        wotm = None
        wptrb = [None, None]
        wsq = None
        wnrm = None
        wrow = None
        st_ctr = [0]

        def load_qk(hh):
            b_ = hh % 2
            p.dma("sync", QA[b_][0:64, :], scr["qfm1"][64 * hh:64 * (hh + 1), :],
                  after=[tSTread[b_]], ch="qkl%d" % b_)
            t2 = p.dma("sync", KA[b_][0:64, :], scr["kfm1"][64 * hh:64 * (hh + 1), :],
                       after=[tSTread[b_]], ch="qkl%d" % b_)
            tldk[b_] = t2

        def bound_row(h):
            nonlocal wsq, wnrm, wrow
            b_ = h % 2
            t1 = p.act(sqt2[:], KA[b_][0:64, :], AF.Square, after=[tldk[b_], wsq])
            tlast = None
            tklast = None
            for g in range(4):
                tk = p.mm(pnrm[:], cst["onesc"][0:64, :], sqt2[:, 512 * g:512 * (g + 1)],
                          True, True, after=[t1, wnrm])
                tklast = tk
                tlast = p.vcopy(kn2[0:1, 512 * g:512 * (g + 1)], pnrm[:], after=[tk, wrow])
                wnrm = tlast
            tkm = p.red(km[:], kn2[:], OP.max, after=[tlast])
            t2 = p.act(sqt2[:], QA[b_][0:64, :], AF.Square, after=[tldk[b_], tklast])
            wsq = t2
            for g in range(4):
                tk = p.mm(pnrm[:], cst["onesc"][0:64, :], sqt2[:, 512 * g:512 * (g + 1)],
                          True, True, after=[t2, wnrm])
                tlast = p.vcopy(qn2[0:1, 512 * g:512 * (g + 1)], pnrm[:], after=[tk, wrow])
                wnrm = tlast
            t3 = p.ts(brow[:], qn2[:], km[0:1, 0:1], None, OP.mult, after=[tkm, tlast, wrow])
            t4 = p.act(brow[:], brow[:], AF.Ln, bias=cc[0:1, 4:5], after=[t3])
            t5 = p.act(brow[:], brow[:], AF.Exp, scale=0.5, after=[t4])
            t6 = p.ts(brow[:], brow[:], -1.0, 60.0, OP.mult, OP.add, after=[t5])
            wrow = t6
            tq = p.vcopy(QA[b_][64:65, :], brow[:], after=[t6, tSTread[b_]])
            return tq

        def st_sub(h, qg, tq64):
            nonlocal wotm
            b_ = h % 2
            par = st_ctr[0] % 2
            st_ctr[0] += 1
            for pr in range(8):
                half = pr % 2
                toks = []
                for j in range(2):
                    kc = 2 * pr + j
                    toks.append(p.mm(
                        psSt[half][:, 512 * j:512 * (j + 1)],
                        KA[b_][0:65, 128 * kc:128 * (kc + 1)],
                        QA[b_][0:65, 512 * qg:512 * (qg + 1)],
                        start=True, stop=True,
                        after=[tq64, tonesa, tonesb, wPT[half]]))
                tex = p.act(PT[:, 1024 * pr:1024 * (pr + 1)], psSt[half][:],
                            AF.Exp, after=[toks[0], toks[1]])
                wPT[half] = tex
            tk = None
            for kc in range(16):
                tk = p.mm(psO[par][:],
                          vtm[:, 1040 * kc + 65 * h: 1040 * kc + 65 * (h + 1)],
                          PT[:, 512 * kc:512 * (kc + 1)],
                          start=(kc == 0), stop=(kc == 15),
                          after=[wPT[0], wPT[1], wO[par]] if kc == 0 else (),
                          inc=(kc == 15))
            tSTread[b_] = tk
            teo = p.vcopy(oht[par][:], psO[par][:], after=[tk, woht[par]])
            wO[par] = teo
            for sub in range(4):
                tt_ = 4 * qg + sub
                ttr = p.tr(ptrb[:, 0:65],
                           oht[par][:, 128 * sub:128 * (sub + 1)],
                           idb[0:65, 0:65], after=[teo, wptrb[0]])
                tcp = p.vcopy(
                    otm[:, 1040 * tt_ + 65 * h: 1040 * tt_ + 65 * (h + 1)],
                    ptrb[:, 0:65], after=[ttr, wotm])
                wptrb[0] = tcp
                woht[par] = tcp

        load_qk(0)
        pend = None
        for h in range(16):
            tq = bound_row(h)
            for qg in range(4):
                if pend is not None:
                    st_sub(*pend)
                pend = (h, qg, tq)
                if qg == 0 and h + 1 < 16:
                    load_qk(h + 1)
        st_sub(*pend)
        p.barrier()
        es_v.close()

        es_o = contextlib.ExitStack()
        wo1b, two0, two1 = load_weight_bf16(p, io["wo1"], D, D, es_o)
        bo1f = p.sb([1, 1024], F32, es=es_o)
        bo1r = p.sb([1, 1024], BF16, es=es_o)
        tl = p.dma("sync", bo1f[:], io["bo1r"], ch="bo1")
        tbo = p.vcopy(bo1r[:], bo1f[:], after=[tl])
        bo1bc = p.sb([128, 1024], F32, es=es_o)
        psW = p.ps([128, 512], F32, es=es_o)
        tprev = None
        for n in range(2):
            tk = p.mm(psW[:], onesr, bo1r[0:1, 512 * n:512 * (n + 1)], True, True,
                      after=[tbo, tprev])
            tprev = p.scopy(bo1bc[:, 512 * n:512 * (n + 1)], psW[:], after=[tk])
        rcp16 = p.sb([128, 16], F32, es=es_o)
        o2 = p.sb([128, 1024], BF16, es=es_o)
        ofmt = p.sb([128, 1024], BF16, es=es_o)
        ptrb2 = [p.ps([128, 128], BF16, es=es_o) for _ in range(2)]
        xn1l = [p.sb([128, 1024], BF16, es=es_o) for _ in range(2)]
        s1t = [p.sb([128, 1024], F32, es=es_o) for _ in range(2)]
        wrc = None
        wo2 = None
        wofmt = None
        wpt2 = [None, None]
        wps = tprev
        ws1 = [None, None]
        wx1l = [None, None]
        for t in range(NT):
            s = t % 2
            tlx = p.dma("sync", xn1l[s][:], scr["xn1"][128 * t:128 * (t + 1), :],
                        after=[wx1l[s]], ch="xn1l%d" % s)
            ov = otm[:, 1040 * t:1040 * (t + 1)].rearrange("p (h c) -> p h c", c=65)
            trc = p.recip(rcp16[:], ov[:, :, 64], after=[wrc])
            o2toks = []
            for hh in range(16):
                tn = p.ts(o2[:, 64 * hh:64 * (hh + 1)],
                          otm[:, 1040 * t + 65 * hh: 1040 * t + 65 * hh + 64],
                          rcp16[:, hh:hh + 1], None, OP.mult,
                          after=[trc, wo2])
                o2toks.append(tn)
            wrc = o2toks[-1]
            tcpl = None
            for c in range(8):
                ss = c % 2
                ttr = p.tr(ptrb2[ss][:], o2[:, 128 * c:128 * (c + 1)], idb,
                           after=[o2toks[-1], wpt2[ss]])
                tcp = p.vcopy(ofmt[:, 128 * c:128 * (c + 1)], ptrb2[ss][:],
                              after=[ttr, wofmt])
                wpt2[ss] = tcp
                tcpl = tcp
            wo2 = tcpl
            for n in range(2):
                tk = None
                for c in range(8):
                    tk = p.mm(psW[:], ofmt[:, 128 * c:128 * (c + 1)],
                              wo1b[:, 1024 * c + 512 * n: 1024 * c + 512 * (n + 1)],
                              start=(c == 0), stop=(c == 7),
                              after=[tcpl, wps, two0, two1] if c == 0 else (),
                              inc=(c == 7))
                wofmt = tk
                ta = p.tt(s1t[s][:, 512 * n:512 * (n + 1)], psW[:],
                          bo1bc[:, 512 * n:512 * (n + 1)], OP.add,
                          after=[tk, ws1[s]])
                wps = ta
                ws1[s] = ta
            tb2 = p.tt(s1t[s][:], s1t[s][:], xn1l[s][:], OP.add, after=[tlx, ws1[s]])
            td = p.dma("gpsimd", scr["s1"][128 * t:128 * (t + 1), :], s1t[s][:],
                       after=[tb2], ch="s1o%d" % s)
            ws1[s] = td
            wx1l[s] = tb2
        p.barrier()
        es_o.close()


def emit_l1_ffn(p, io, scr, cst, xnfm):
    cc = cst["cc"]
    with contextlib.ExitStack() as es:
        gates = p.sb([128, NT * 8], F32, es=es)
        y1 = p.sb([128, NT * 1024], BF16, es=es)
        tmz = [p.memset(y1[:, :8192], 0.0), p.memset(y1[:, 8192:], 0.0)]
        es_e = contextlib.ExitStack()
        winb = p.sb([128, 8 * 2048], BF16, es=es_e)
        woutb = p.sb([128, 16 * 1024], BF16, es=es_e)
        stgw = [p.sb([128, 2048], F32, es=es_e) for _ in range(2)]
        hfm = p.sb([128, 16 * 512], BF16, es=es_e)
        wstg = [None, None]

        def conv_copy(eng, out, in_, after):
            if eng == "scalar":
                return p.act(out, in_, AF.Copy, after=after)
            return p.vcopy(out, in_, after=after, eng=eng)

        def load_expert(e, win_cons, wout_cons, engs=("scalar", "vector")):
            win_rdy = [None, None]
            wout_rdy = [None, None]
            for c in range(8):
                s = c % 2
                tl = p.dma("sync", stgw[s][:],
                           io["win"][e, 128 * c:128 * (c + 1), :],
                           after=[wstg[s]], ch="wie%d" % s)
                win_rdy[s] = conv_copy(engs[0], winb[:, 2048 * c:2048 * (c + 1)],
                                       stgw[s][:], [tl, win_cons])
                wstg[s] = win_rdy[s]
            for c in range(16):
                s = c % 2
                tl = p.dma("sync", stgw[s][:, 0:1024],
                           io["wout"][e, 128 * c:128 * (c + 1), :],
                           after=[wstg[s]], ch="wie%d" % s)
                wout_rdy[s] = conv_copy(engs[1], woutb[:, 1024 * c:1024 * (c + 1)],
                                        stgw[s][:, 0:1024], [tl, wout_cons])
                wstg[s] = wout_rdy[s]
            return win_rdy, wout_rdy

        es_g = contextlib.ExitStack()
        wg1b, twg0, twg1 = load_weight_bf16(p, io["wg1"], D, 256, es_g)
        wg2b = p.sb([128, 16], BF16, es=es_g)
        stg2 = p.sb([128, 8], F32, es=es_g)
        bg1c = p.sb([128, 2], F32, es=es_g)
        tbg = p.dma("sync", bg1c[:], io["bg1c"], ch="bg1")
        tw2 = None
        for c in range(2):
            tl = p.dma("sync", stg2[:], io["wg2"][c * 128:(c + 1) * 128, :],
                       after=[tw2], ch="wg2l")
            tw2 = p.vcopy(wg2b[:, 8 * c:8 * (c + 1)], stg2[:], after=[tl])
        g1fm = p.sb([128, 2 * T], BF16, es=es_g)
        psg = [p.ps([128, 512], F32, es=es_g) for _ in range(2)]
        wg = [None, None]
        for mc in range(2):
            for g in range(4):
                s = (mc * 4 + g) % 2
                tk = None
                for c in range(8):
                    tk = p.mm(psg[s][:],
                              wg1b[:, 256 * c + 128 * mc: 256 * c + 128 * (mc + 1)],
                              xnfm[:, c * T + 512 * g: c * T + 512 * (g + 1)],
                              start=(c == 0), stop=(c == 7),
                              after=[wg[s], twg0, twg1] if c == 0 else (),
                              inc=(c == 7))
                te = p.act(g1fm[:, T * mc + 512 * g: T * mc + 512 * (g + 1)],
                           psg[s][:], AF.Gelu, bias=bg1c[:, mc:mc + 1],
                           after=[tk, tbg])
                wg[s] = te
        psl = [p.ps([128, 8], F32, es=es_g) for _ in range(2)]
        prb = [p.sb([128, 8], F32, es=es_g) for _ in range(2)]
        aux = [p.sb([128, 12], F32, es=es_g) for _ in range(2)]
        junk8 = p.sb([128, 8], BF16, es=es_g)
        wpl = [None, None]
        gate_tok = [None] * NT
        for t in range(NT):
            s = t % 2
            tk = None
            for c in range(2):
                tk = p.mm(psl[s][:], g1fm[:, T * c + 128 * t: T * c + 128 * (t + 1)],
                          wg2b[:, 8 * c:8 * (c + 1)], start=(c == 0), stop=(c == 1),
                          after=[wg[0], wg[1], tw2, wpl[s]] if c == 0 else (),
                          inc=(c == 1))
            t1 = p.red(aux[s][:, 0:1], psl[s][:], OP.max, negate=True,
                       after=[tk, wpl[s]])
            t2 = p.act(junk8[:], psl[s][:], AF.Exp, bias=aux[s][:, 0:1],
                       accum_out=aux[s][:, 1:2], after=[t1])
            t3 = p.recip(aux[s][:, 2:3], aux[s][:, 1:2], after=[t2])
            t4 = p.act(prb[s][:], psl[s][:], AF.Exp, bias=aux[s][:, 0:1], after=[t2])
            t5 = p.ts(prb[s][:], prb[s][:], aux[s][:, 2:3], None, OP.mult,
                      after=[t3, t4])
            t6 = p.red(aux[s][:, 3:4], prb[s][:], OP.max, after=[t5])
            t7 = p.ts(aux[s][:, 4:12], prb[s][:], aux[s][:, 3:4], None, OP.subtract,
                      after=[t6])
            t8 = p.ts(aux[s][:, 4:12], aux[s][:, 4:12], 0.0, None, OP.is_ge,
                      after=[t7])
            t9 = p.stt(aux[s][:, 4:12], aux[s][:, 4:12], -1e30, prb[s][:],
                       OP.mult, OP.add, after=[t8])
            t10 = p.red(aux[s][:, 3:4], aux[s][:, 4:12], OP.max, after=[t9])
            t11 = p.ts(aux[s][:, 4:12], prb[s][:], aux[s][:, 3:4], None, OP.is_ge,
                       after=[t10])
            t12 = p.tt(gates[:, 8 * t:8 * (t + 1)], prb[s][:], aux[s][:, 4:12],
                       OP.mult, after=[t11])
            gate_tok[t] = t12
            wpl[s] = t12
        win_rdy, wout_rdy = load_expert(0, None, None)
        es_g.close()
        gates_done = [wg[0], wg[1], wpl[0], wpl[1]]
        psh = [p.ps([128, 512], F32, es=es_e) for _ in range(2)]
        pso = [p.ps([128, 1024], F32, es=es_e) for _ in range(2)]
        win_cons = None
        wout_cons = None
        wh = [None, None]
        wo_ = [None, None]
        first_mm = True
        for e in range(8):
            for tg in range(4):
                hlast = None
                for j in range(16):
                    s = j % 2
                    tk = None
                    for c in range(8):
                        tk = p.mm(psh[s][:],
                                  winb[:, 2048 * c + 128 * j: 2048 * c + 128 * (j + 1)],
                                  xnfm[:, c * T + 512 * tg: c * T + 512 * (tg + 1)],
                                  start=(c == 0), stop=(c == 7),
                                  after=([win_rdy[0], win_rdy[1], wh[s]]
                                         + (gates_done if first_mm else []))
                                  if c == 0 else (),
                                  inc=(c == 7))
                        first_mm = False
                    win_cons = tk
                    te = p.act(hfm[:, 512 * j:512 * (j + 1)], psh[s][:], AF.Gelu,
                               after=[tk])
                    wh[s] = te
                    hlast = te
                for sub in range(4):
                    t_ = 4 * tg + sub
                    s = sub % 2
                    tk = None
                    for j in range(16):
                        for n in range(2):
                            tk = p.mm(pso[s][:, 512 * n:512 * (n + 1)],
                                      hfm[:, 512 * j + 128 * sub: 512 * j + 128 * (sub + 1)],
                                      woutb[:, 1024 * j + 512 * n: 1024 * j + 512 * (n + 1)],
                                      start=(j == 0), stop=(j == 15),
                                      after=[hlast, wout_rdy[0], wout_rdy[1], wo_[s]]
                                      if (j == 0 and n == 0) else (),
                                      inc=(j == 15 and n == 1))
                    wout_cons = tk
                    tac = p.stt(y1[:, 1024 * t_:1024 * (t_ + 1)], pso[s][:],
                                gates[:, 8 * t_ + e: 8 * t_ + e + 1],
                                y1[:, 1024 * t_:1024 * (t_ + 1)], OP.mult, OP.add,
                                after=[tk, gate_tok[t_]] + (tmz if e == 0 else []))
                    wo_[s] = tac
            if e < 7:
                win_rdy, wout_rdy = load_expert(e + 1, win_cons, wout_cons)
        p.barrier()
        es_e.close()
        x1t = [p.sb([128, 1024], F32, es=es) for _ in range(2)]
        s1l = [p.sb([128, 1024], F32, es=es) for _ in range(2)]
        syt = [p.sb([128, 1024], F32, es=es) for _ in range(2)]
        ot = [p.sb([128, 1024], F32, es=es) for _ in range(2)]
        wx = [None, None]
        for t in range(NT):
            s = t % 2
            p.dma("sync", x1t[s][:], scr["x1"][t * 128:(t + 1) * 128, :],
                  after=[wx[s]], ch="fc%d" % s)
            tls = p.dma("sync", s1l[s][:], scr["s1"][t * 128:(t + 1) * 128, :],
                        after=[wx[s]], ch="fc%d" % s)
            tsy = p.act(syt[s][:], y1[:, 1024 * t:1024 * (t + 1)], AF.Silu,
                        after=[wx[s]])
            tm1 = p.tt(syt[s][:], syt[s][:], s1l[s][:], OP.mult, after=[tsy, tls])
            tx2 = p.tt(ot[s][:], syt[s][:], x1t[s][:], OP.add, after=[tm1, tls])
            td = p.dma("gpsimd", io["out"][t * 128:(t + 1) * 128, :], ot[s][:],
                       after=[tx2], ch="oo%d" % s)
            wx[s] = td
        p.barrier()


def emit_l1(p, io, scr, cst, xnfm):
    emit_ln(p, scr["x1"], xnfm, cst["cc"], cst["idb"], eps_col_idx=1,
            xn_tm_dram=scr["xn1"])
    emit_l1_attn(p, io, scr, cst, xnfm)
    emit_l1_ffn(p, io, scr, cst, xnfm)


# version 20
# speedup vs baseline: 1.1621x; 1.0258x over previous
import contextlib
import numpy as np
import ml_dtypes

import concourse.bass as bass
import concourse.mybir as mybir
from concourse.alu_op_type import AluOpType as OP
from concourse.bass_utils import run_bass_kernel_spmd

F32 = mybir.dt.float32
BF16 = mybir.dt.bfloat16
AF = mybir.ActivationFunctionType
AX = mybir.AxisListType
BF = ml_dtypes.bfloat16

ENG = ("sync", "tensor", "vector", "scalar", "gpsimd")
AUTO_SELF = {"vector", "scalar", "gpsimd"}

T = 2048
D = 1024
NT = T // 128
H = 8
DK = 64
DV = 128
LNR8 = -2.0794415416798357


class Prog:

    def __init__(self):
        self.nc = bass.Bass("TRN2", target_bir_lowering=False)
        self.es = contextlib.ExitStack()
        self.es_wob = self.es.enter_context(contextlib.ExitStack())
        self.sched = {e: [] for e in ENG}
        self.ctr = {e: 0 for e in ENG}
        self.chctr = {}
        self.waited = {c: {} for c in ENG}
        self._n = 0

    def _nm(self, pfx):
        self._n += 1
        return f"{pfx}{self._n}"

    def sb(self, shape, dt=F32, name=None, es=None):
        return (es or self.es).enter_context(
            self.nc.sbuf_tensor(name or self._nm("sb"), shape, dt))

    def ps(self, shape, dt=F32, name=None, es=None):
        return (es or self.es).enter_context(
            self.nc.psum_tensor(name or self._nm("ps"), shape, dt))

    def din(self, name, shape, dt=F32):
        return self.nc.dram_tensor(name, shape, dt, kind="ExternalInput").ap()

    def dout(self, name, shape, dt=F32):
        return self.nc.dram_tensor(name, shape, dt, kind="ExternalOutput").ap()

    def dscr(self, name, shape, dt=F32):
        return self.nc.dram_tensor(name, shape, dt, kind="Internal").ap()

    def _wait(self, eng, sem, v):
        if self.waited[eng].get(sem, 0) < v:
            self.sched[eng].append(("w", sem, v))
            self.waited[eng][sem] = v

    def op(self, eng, fn, after=(), inc=True, ch=None):
        sch = self.sched[eng]
        if eng in AUTO_SELF and self.ctr[eng] > self.waited[eng].get(eng, 0):
            self._wait(eng, eng, self.ctr[eng])
        for tok in after:
            if tok is None:
                continue
            self._wait(eng, tok[0], tok[1])
        if ch is not None:
            self.chctr[ch] = self.chctr.get(ch, 0) + 16
            sch.append(("op", fn, ("c", ch, self.chctr[ch])))
            return (ch, self.chctr[ch])
        if inc:
            self.ctr[eng] += 1
            sch.append(("op", fn, ("e", eng, self.ctr[eng])))
            return (eng, self.ctr[eng])
        sch.append(("op", fn, None))
        return None

    def tok(self, eng):
        return (eng, self.ctr[eng])

    def barrier(self):
        toks = [(e, self.ctr[e]) for e in ENG]
        toks += [(c, v) for c, v in self.chctr.items()]
        for e in ENG:
            for pe, v in toks:
                if v > 0:
                    self._wait(e, pe, v)

    def finalize(self):
        self.barrier()
        sems = {e: self.es.enter_context(self.nc.semaphore("S" + e)) for e in ENG}
        for c in self.chctr:
            sems[c] = self.es.enter_context(self.nc.semaphore("C" + c))
        block = self.es.enter_context(self.nc.Block())

        def emit2(eng_name):
            def body(engine):
                for item in self.sched[eng_name]:
                    if item[0] == "w":
                        engine.wait_ge(sems[item[1]], item[2])
                    else:
                        inst = item[1](engine)
                        if item[2] is not None:
                            kind, nm, val = item[2]
                            inst.then_inc(sems[nm], 16 if kind == "c" else 1)
            return body

        block.sync(emit2("sync"))
        block.tensor(emit2("tensor"))
        block.vector(emit2("vector"))
        block.scalar(emit2("scalar"))
        block.gpsimd(emit2("gpsimd"))
        self.es.close()
        return self.nc

    def dma(self, eng, out, in_, after=(), ch="c0"):
        return self.op(eng, lambda e, o=out, i=in_: e.dma_start(out=o, in_=i),
                       after=after, ch=ch)

    def mm(self, out, lhsT, rhs, start, stop, after=(), inc=True):
        return self.op("tensor",
                       lambda e, o=out, l=lhsT, r=rhs, s1=start, s2=stop:
                       e.matmul(o, l, r, start=s1, stop=s2),
                       after=after, inc=inc)

    def tr(self, out, in_, ident, after=(), inc=True):
        return self.op("tensor",
                       lambda e, o=out, i=in_, d=ident: e.transpose(o, i, d),
                       after=after, inc=inc)

    def act(self, out, in_, func, bias=None, scale=1.0, accum_out=None, after=()):
        def fn(e, o=out, i=in_, f=func, b=bias, s=scale, a=accum_out):
            kw = dict(scale=s)
            if b is not None:
                kw["bias"] = b
            if a is not None:
                kw["accum_out"] = a
            return e.activation(o, i, f, **kw)
        return self.op("scalar", fn, after=after)

    def scopy(self, out, in_, after=()):
        return self.op("scalar", lambda e, o=out, i=in_: e.copy(o, i), after=after)

    def tt(self, out, in0, in1, op, after=(), eng="vector"):
        return self.op(eng, lambda e, o=out, a=in0, b=in1, p=op:
                       e.tensor_tensor(o, a, b, p), after=after)

    def ts(self, out, in0, s1, s2, op0, op1=None, after=(), accum_out=None, eng="vector"):
        def fn(e, o=out, i=in0, a=s1, b=s2, p0=op0, p1=op1, ac=accum_out):
            kw = {}
            if p1 is not None:
                kw["op1"] = p1
            if ac is not None:
                kw["accum_out"] = ac
            return e.tensor_scalar(o, i, a, b, p0, **kw)
        return self.op(eng, fn, after=after)

    def stt(self, out, in0, scalar, in1, op0, op1, after=()):
        return self.op("vector", lambda e, o=out, i0=in0, s=scalar, i1=in1, p0=op0, p1=op1:
                       e.scalar_tensor_tensor(o, i0, s, i1, p0, p1), after=after)

    def vcopy(self, out, in_, after=(), eng="vector"):
        return self.op(eng, lambda e, o=out, i=in_: e.tensor_copy(o, i), after=after)

    def red(self, out, in_, op, negate=False, after=()):
        return self.op("vector", lambda e, o=out, i=in_, p=op, n=negate:
                       e.tensor_reduce(o, i, axis=AX.X, op=p, negate=n), after=after)

    def recip(self, out, in_, after=()):
        return self.op("vector", lambda e, o=out, i=in_: e.reciprocal(o, i), after=after)

    def memset(self, ap, val, after=(), eng="gpsimd"):
        return self.op(eng, lambda e, a=ap, v=val: e.memset(a, v), after=after)



def host_prep(inputs, core):
    f32 = np.float32
    bf = BF
    d = {}
    d["x"] = np.ascontiguousarray(inputs["L0_attn_x"][core], f32)
    d["wq"] = np.ascontiguousarray(inputs["L0_attn_Wq"], f32)
    d["wk"] = np.ascontiguousarray(inputs["L0_attn_Wk"], f32)
    d["wv"] = np.ascontiguousarray(inputs["L0_attn_Wv"], f32)
    d["wg"] = np.ascontiguousarray(inputs["L0_attn_Wg"], f32)
    d["wgk1"] = np.ascontiguousarray(inputs["L0_attn_Wgk1"], f32)
    d["wgk2"] = np.ascontiguousarray(inputs["L0_attn_Wgk2"], f32)
    d["bgk2r"] = np.ascontiguousarray(inputs["L0_attn_bgk2"].reshape(1, 512), f32)
    d["nwc"] = np.ascontiguousarray(inputs["L0_attn_norm_w"].reshape(128, 1), f32)
    d["wo"] = np.ascontiguousarray(inputs["L0_attn_Wo"], f32)
    d["gwT"] = np.ascontiguousarray(inputs["L0_ffn_gate_w"].T, f32)
    wupT = np.ascontiguousarray(inputs["L0_ffn_w_up"].transpose(0, 2, 1), f32)
    d["wupT_own"] = np.ascontiguousarray(wupT[core])
    d["wdn_own"] = np.ascontiguousarray(inputs["L0_ffn_w_down"][core], f32)
    d["gam_c"] = np.ascontiguousarray(inputs["L1_attn_gn_gamma"].reshape(8, 128).T, f32)
    d["bet_c"] = np.ascontiguousarray(inputs["L1_attn_gn_beta"].reshape(8, 128).T, f32)
    d["wqkv"] = np.ascontiguousarray(inputs["L1_attn_W_qkv"], f32)
    d["bqkvr"] = np.ascontiguousarray(inputs["L1_attn_b_qkv"].reshape(1, 3072), f32)
    d["wo1"] = np.ascontiguousarray(inputs["L1_attn_W_o"], f32)
    d["bo1r"] = np.ascontiguousarray(inputs["L1_attn_b_o"].reshape(1, 1024), f32)
    d["wg1"] = np.ascontiguousarray(inputs["L1_ffn_wg1"], f32)
    d["bg1c"] = np.ascontiguousarray(inputs["L1_ffn_bg1"].reshape(2, 128).T, f32)
    d["wg2"] = np.ascontiguousarray(inputs["L1_ffn_wg2"], f32)
    d["win"] = np.ascontiguousarray(inputs["L1_ffn_w_in"], f32)
    d["wout"] = np.ascontiguousarray(inputs["L1_ffn_w_out"], f32)
    tri = np.triu(np.ones((64, 64), f32))
    u2 = np.zeros((128, 128), f32)
    u2[:64, :64] = tri / 16.0
    u2[64:, 64:] = tri / 16.0
    d["U2b"] = u2.astype(bf)
    m2 = np.zeros((128, 128), f32)
    m2[:64, :64] = tri
    m2[64:, 64:] = tri
    d["M2f"] = m2
    d["idb"] = np.eye(128, dtype=f32).astype(bf)
    d["idf"] = np.eye(128, dtype=f32)
    d["onesr"] = np.ones((1, 128), f32).astype(bf)
    d["onesc"] = np.ones((128, 1), f32).astype(bf)
    cc = np.zeros((128, 6), f32)
    cc[:, 0] = 1.0
    cc[:, 1] = 1e-5
    cc[:, 2] = 1e-6
    cc[:, 3] = LNR8
    cc[:, 4] = 1e-30
    d["cc"] = cc
    d["onesr512"] = np.ones((1, 512), f32).astype(bf)
    e4 = np.zeros((128, 4), f32)
    for j in range(4):
        e4[32 * j:32 * (j + 1), j] = 1.0
    d["E4"] = e4
    d["E4T"] = np.ascontiguousarray(e4.T)
    return d


def declare_io(p):
    io = {}
    io["x"] = p.din("x", [T, D])
    io["wq"] = p.din("wq", [D, 512]); io["wk"] = p.din("wk", [D, 512])
    io["wv"] = p.din("wv", [D, D]); io["wg"] = p.din("wg", [D, D])
    io["wgk1"] = p.din("wgk1", [D, 16]); io["wgk2"] = p.din("wgk2", [16, 512])
    io["bgk2r"] = p.din("bgk2r", [1, 512]); io["nwc"] = p.din("nwc", [128, 1])
    io["wo"] = p.din("wo", [D, D]); io["gwT"] = p.din("gwT", [D, 8])
    io["wupT_own"] = p.din("wupT_own", [4096, D])
    io["wdn_own"] = p.din("wdn_own", [4096, D])
    io["gam_c"] = p.din("gam_c", [128, 8]); io["bet_c"] = p.din("bet_c", [128, 8])
    io["wqkv"] = p.din("wqkv", [D, 3072]); io["bqkvr"] = p.din("bqkvr", [1, 3072])
    io["wo1"] = p.din("wo1", [D, D]); io["bo1r"] = p.din("bo1r", [1, D])
    io["wg1"] = p.din("wg1", [D, 256]); io["bg1c"] = p.din("bg1c", [128, 2])
    io["wg2"] = p.din("wg2", [256, 8])
    io["win"] = p.din("win", [8, D, 2048]); io["wout"] = p.din("wout", [8, 2048, D])
    io["U2b"] = p.din("U2b", [128, 128], BF16)
    io["M2f"] = p.din("M2f", [128, 128])
    io["idb"] = p.din("idb", [128, 128], BF16)
    io["idf"] = p.din("idf", [128, 128])
    io["onesr"] = p.din("onesr", [1, 128], BF16)
    io["onesc"] = p.din("onesc", [128, 1], BF16)
    io["cc"] = p.din("cc", [128, 6])
    io["onesr512"] = p.din("onesr512", [1, 512], BF16)
    io["E4"] = p.din("E4", [128, 4])
    io["E4T"] = p.din("E4T", [4, 128])
    io["out"] = p.dout("out", [T, D])
    return io



def load_weight_bf16(p, src, rows, cols, es, scale_col=None, prev=None):
    nch = rows // 128
    chp = p._nm("lw")
    wbf = p.sb([128, nch * cols], BF16, es=es)
    stg = [p.sb([128, cols], F32, es=es), p.sb([128, cols], F32, es=es)]
    stok = [None, None]
    for c in range(nch):
        s = c % 2
        tl = p.dma("sync", stg[s][:], src[c * 128:(c + 1) * 128, :],
                   after=[stok[s], prev], ch=chp + str(s))
        if scale_col is None:
            if s == 0:
                tv = p.vcopy(wbf[:, c * cols:(c + 1) * cols], stg[s][:],
                             after=[tl], eng="vector")
            else:
                tv = p.act(wbf[:, c * cols:(c + 1) * cols], stg[s][:], AF.Copy,
                           after=[tl])
        else:
            tv = p.ts(wbf[:, c * cols:(c + 1) * cols], stg[s][:], scale_col, None,
                      OP.mult, after=[tl], eng=("vector" if s == 0 else "gpsimd"))
        stok[s] = tv
    return wbf, stok[0], stok[1]


def load_row_bf16(p, src, cols, es):
    stg = p.sb([1, cols], F32, es=es)
    row = p.sb([1, cols], BF16, es=es)
    tl = p.dma("sync", stg[:], src, ch=p._nm("lr"))
    tv = p.vcopy(row[:], stg[:], after=[tl])
    return row, tv



def emit_ln(p, src_dram, xnfm, cc, idb, eps_col_idx=1, xn_tm_dram=None):
    with contextlib.ExitStack() as es:
        xt = [p.sb([128, D], F32, es=es) for _ in range(2)]
        junk = p.sb([128, D], BF16, es=es)
        xnb = [p.sb([128, D], BF16, es=es) for _ in range(2)]
        colp = [p.sb([128, 8], F32, es=es) for _ in range(2)]
        ptb = [p.ps([128, 128], BF16, es=es) for _ in range(2)]
        tx = [None, None]
        txn = [None, None]
        tpt = [None, None]
        tj = None
        for t in range(NT):
            s = t % 2
            ctl = colp[s]
            tl = p.dma("sync", xt[s][:], src_dram[t * 128:(t + 1) * 128, :],
                       after=[tx[s]], ch="lnx%d" % s)
            tssq = p.act(junk[:], xt[s][:], AF.Square, accum_out=ctl[:, 1:2],
                         after=[tl, tj, tx[s]])
            tj = tssq
            tsum = p.red(ctl[:, 0:1], xt[s][:], OP.add, after=[tl, tx[s]])
            tm = p.ts(ctl[:, 2:3], ctl[:, 0:1], 1.0 / D, None, OP.mult, after=[tsum])
            tq = p.ts(ctl[:, 3:4], ctl[:, 1:2], 1.0 / D, None, OP.mult, after=[tssq])
            tmm = p.tt(ctl[:, 4:5], ctl[:, 2:3], ctl[:, 2:3], OP.mult, after=[tm])
            tvar = p.tt(ctl[:, 4:5], ctl[:, 3:4], ctl[:, 4:5], OP.subtract,
                        after=[tq, tmm])
            tln = p.act(ctl[:, 5:6], ctl[:, 4:5], AF.Ln,
                        bias=cc[:, eps_col_idx:eps_col_idx + 1], after=[tvar])
            trr = p.act(ctl[:, 5:6], ctl[:, 5:6], AF.Exp, scale=-0.5, after=[tln])
            tmr = p.tt(ctl[:, 6:7], ctl[:, 2:3], ctl[:, 5:6], OP.mult, after=[trr])
            txnb = p.ts(xnb[s][:], xt[s][:], ctl[:, 5:6], ctl[:, 6:7],
                        OP.mult, OP.subtract, after=[tmr, txn[s]])
            tx[s] = txnb
            if xn_tm_dram is not None:
                txn_d = p.dma("gpsimd", xn_tm_dram[t * 128:(t + 1) * 128, :],
                              xnb[s][:], after=[txnb], ch="lnxo%d" % s)
            else:
                txn_d = None
            last = txnb
            for c in range(8):
                ss = c % 2
                ttr = p.tr(ptb[ss][:], xnb[s][:, c * 128:(c + 1) * 128], idb,
                           after=[txnb, tpt[ss]])
                tcp = p.vcopy(xnfm[:, c * T + t * 128: c * T + (t + 1) * 128],
                              ptb[ss][:], after=[ttr])
                tpt[ss] = tcp
                last = tcp
            txn[s] = last if txn_d is None else txn_d
        p.barrier()


def emit_moe_mown(p, io, scr, cst):
    with contextlib.ExitStack() as es:
        wdnb = p.sb([128, 32 * 1024], BF16, es=es)
        wupb = [p.sb([128, 4096], BF16, es=es) for _ in range(2)]
        stgw = [p.sb([128, 1024], F32, es=es) for _ in range(2)]
        stgu = [p.sb([128, 2048], F32, es=es) for _ in range(2)]
        mstg = p.sb([128, 1024], BF16, es=es)
        pM = [p.ps([128, 1024], F32, es=es) for _ in range(2)]
        wstg = [None, None]
        stu_tok = [None, None]

        def cast_copy(eng, out, in_, after):
            if eng == "scalar":
                return p.act(out, in_, AF.Copy, after=after)
            return p.vcopy(out, in_, after=after, eng=eng)

        wdn_rdy = [None, None]
        for c in range(32):
            s = c % 2
            tl = p.dma("gpsimd", stgw[s][:],
                       io["wdn_own"][c * 128:(c + 1) * 128, :], after=[wstg[s]],
                       ch="wdn%d" % s)
            wdn_rdy[s] = cast_copy("vector" if s == 0 else "scalar",
                                   wdnb[:, 1024 * c:1024 * (c + 1)], stgw[s][:],
                                   [tl])
            wstg[s] = wdn_rdy[s]
        upcv = [None, None]
        upcons = [None, None]

        def load_up(mi):
            bm = mi % 2
            wsrc = io["wupT_own"].rearrange("(k p) c -> p k c", p=128)
            wsrc = wsrc[:, :, 128 * mi:128 * (mi + 1)]
            tcvs = []
            for hh in range(2):
                sl = wsrc[:, 16 * hh:16 * (hh + 1), :]
                tl = p.dma("scalar", stgu[hh][:], sl,
                           after=[stu_tok[hh], upcons[bm]], ch="wup%d" % hh)
                tcv = cast_copy("vector" if hh == 0 else "scalar",
                                wupb[bm][:, 2048 * hh:2048 * (hh + 1)],
                                stgu[hh][:], [tl, upcons[bm]])
                stu_tok[hh] = tcv
                tcvs.append(tcv)
            upcv[bm] = tcvs

        load_up(0)
        wpM = [None, None]
        wms = None
        for m in range(8):
            sm = m % 2
            if m < 7:
                load_up(m + 1)
            tk = None
            for k in range(32):
                for n in range(2):
                    tk = p.mm(pM[sm][:, 512 * n:512 * (n + 1)],
                              wupb[m % 2][:, 128 * k:128 * (k + 1)],
                              wdnb[:, 1024 * k + 512 * n: 1024 * k + 512 * (n + 1)],
                              start=(k == 0), stop=(k == 31),
                              after=([*upcv[m % 2], wdn_rdy[0], wdn_rdy[1],
                                      wpM[sm]] if k == 0 else ()),
                              inc=(k == 31 and n == 1))
            upcons[m % 2] = tk
            tev = p.act(mstg[:], pM[sm][:], AF.Copy, after=[tk, wms])
            wpM[sm] = tev
            wms = p.dma("gpsimd", scr["mown"][m * 128:(m + 1) * 128, :], mstg[:],
                        after=[tev], ch="mow")
        p.barrier()
        tcc = p.op("gpsimd",
                   lambda e: e.collective_compute(
                       "AllGather", mybir.AluOpType.bypass,
                       replica_groups=[list(range(8))],
                       ins=[scr["mown"].opt()],
                       outs=[scr["mall"].opt()]),
                   after=[wms])
        p.mall_tok = tcc


def emit_l0_proj(p, io, scr, cst, xnfm, moe):
    cc, onesr = cst["cc"], cst["onesr"]
    with contextlib.ExitStack() as es:
        masks = moe["masks"]
        p.memset(moe["yacc"][:, :8192], 0.0)
        p.memset(moe["yacc"][:, 8192:], 0.0)
        gwTb = p.sb([128, 64], BF16, es=es)
        stg8 = p.sb([128, 8], F32, es=es)
        tgw = None
        for c in range(8):
            tl = p.dma("sync", stg8[:], io["gwT"][c * 128:(c + 1) * 128, :],
                       after=[tgw], ch="gwt")
            tgw = p.vcopy(gwTb[:, 8 * c:8 * (c + 1)], stg8[:], after=[tl])
        pl = p.ps([128, 8], F32, es=es)
        lmt = p.sb([128, 8], F32, es=es)
        ismx = p.sb([128, 8], F32, es=es)
        l2t = p.sb([128, 8], F32, es=es)
        mxc = p.sb([128, 2], F32, es=es)
        wpl = None
        for t in range(NT):
            tk = None
            for c in range(8):
                tk = p.mm(pl[:], xnfm[:, c * T + 128 * t: c * T + 128 * (t + 1)],
                          gwTb[:, 8 * c:8 * (c + 1)], start=(c == 0), stop=(c == 7),
                          after=[tgw, wpl] if c == 0 else (), inc=(c == 7))
            t1 = p.red(mxc[:, 0:1], pl[:], OP.max, after=[tk])
            t2 = p.ts(lmt[:], pl[:], mxc[:, 0:1], None, OP.subtract, after=[t1])
            t3 = p.ts(ismx[:], lmt[:], 0.0, None, OP.is_ge, after=[t2])
            t4 = p.stt(l2t[:], ismx[:], -1e30, pl[:], OP.mult, OP.add, after=[t3])
            t5 = p.red(mxc[:, 1:2], l2t[:], OP.max, after=[t4])
            t6 = p.ts(masks[:, 8 * t:8 * (t + 1)], pl[:], mxc[:, 1:2], None,
                      OP.is_ge, after=[t5])
            wpl = t6
        wqb, _, _ = load_weight_bf16(p, io["wq"], D, 512, es)
        wkb, _, _ = load_weight_bf16(p, io["wk"], D, 512, es)
        wvb, _, _ = load_weight_bf16(p, io["wv"], D, D, es)
        wgb, _, _ = load_weight_bf16(p, io["wg"], D, D, es)
        nwcs = p.sb([128, 1], F32, es=es)
        tnw = p.dma("sync", nwcs[:], io["nwc"], ch="nwc")
        wob = p.wob
        stgo = [p.sb([128, D], F32, es=es) for _ in range(2)]
        stok = [None, None]
        for c in range(8):
            s = c % 2
            tl = p.dma("sync", stgo[s][:], io["wo"][c * 128:(c + 1) * 128, :],
                       after=[stok[s]], ch="wo%d" % s)
            tv = p.ts(wob[:, c * D:(c + 1) * D], stgo[s][:], nwcs[:], None,
                      OP.mult, after=[tl, tnw],
                      eng=("vector" if s == 0 else "gpsimd"))
            stok[s] = tv
        wgk1b, _, _ = load_weight_bf16(p, io["wgk1"], D, 16, es)
        wgk2b = p.sb([16, 512], BF16, es=es)
        stg2 = p.sb([16, 512], F32, es=es)
        t2 = p.dma("sync", stg2[:], io["wgk2"], ch="wgk2")
        p.vcopy(wgk2b[:], stg2[:], after=[t2])
        bgk2b, tbg = load_row_bf16(p, io["bgk2r"], 512, es)
        p.barrier()

        psA = [p.ps([128, 512], F32, es=es) for _ in range(2)]
        psB = [p.ps([128, 1024], F32, es=es) for _ in range(2)]
        stgb = [p.sb([128, 1024], BF16, es=es) for _ in range(2)]
        ptoks = [None, None]
        etoksA = [None, None]
        etoksB = [None, None]
        for (wb, dst) in ((wqb, scr["qfm"]), (wkb, scr["kfm"])):
            for m in range(4):
                for g in range(4):
                    s = (m * 4 + g) % 2
                    tk = None
                    for c in range(8):
                        tk = p.mm(psA[s][:],
                                  wb[:, 512 * c + 128 * m: 512 * c + 128 * (m + 1)],
                                  xnfm[:, c * T + 512 * g: c * T + 512 * (g + 1)],
                                  start=(c == 0), stop=(c == 7),
                                  after=[etoksA[s]] if c == 0 else (), inc=(c == 7))
                    te = p.act(stgb[s][:, :512], psA[s][:], AF.Copy,
                               after=[tk, ptoks[s]])
                    td = p.dma("gpsimd",
                               dst[128 * m:128 * (m + 1), 512 * g:512 * (g + 1)],
                               stgb[s][:, :512], after=[te], ch="pev%d" % s)
                    ptoks[s] = td
                    etoksA[s] = te
        for (wb, ncols, dst) in ((wkb, 512, scr["ktm"]), (wvb, D, scr["vtm"]),
                                 (wgb, D, scr["gtm"])):
            ngr = ncols // 512
            for t in range(NT):
                s = t % 2
                ps = psB[s]
                tk = None
                for n in range(ngr):
                    for c in range(8):
                        tk = p.mm(ps[:, 512 * n:512 * (n + 1)],
                                  xnfm[:, c * T + 128 * t: c * T + 128 * (t + 1)],
                                  wb[:, ncols * c + 512 * n: ncols * c + 512 * (n + 1)],
                                  start=(c == 0), stop=(c == 7),
                                  after=[etoksB[s]] if (c == 0 and n == 0) else (),
                                  inc=(c == 7 and n == ngr - 1))
                te = p.act(stgb[s][:, :ncols], ps[:, :ncols], AF.Copy,
                           after=[tk, ptoks[s]])
                td = p.dma("gpsimd", dst[128 * t:128 * (t + 1), :],
                           stgb[s][:, :ncols], after=[te], ch="pev%d" % s)
                ptoks[s] = td
                etoksB[s] = te
        ufmb = p.sb([16, T], BF16, es=es)
        psU = p.ps([16, 512], F32, es=es)
        tprev = None
        for g in range(4):
            tk = None
            for c in range(8):
                tk = p.mm(psU[:], wgk1b[:, 16 * c:16 * (c + 1)],
                          xnfm[:, c * T + 512 * g: c * T + 512 * (g + 1)],
                          start=(c == 0), stop=(c == 7),
                          after=[tprev] if c == 0 else (), inc=(c == 7))
            tprev = p.act(ufmb[:, 512 * g:512 * (g + 1)], psU[:], AF.Copy, after=[tk])
        zcol = [p.sb([128, 512], F32, es=es) for _ in range(2)]
        zc2 = [p.sb([128, 512], F32, es=es) for _ in range(2)]
        lsb = [p.sb([128, 512], BF16, es=es) for _ in range(2)]
        dtoks = [None, None]
        wzc = [None, None]
        for t in range(NT):
            s = t % 2
            ps = psA[s]
            p.mm(ps[:], ufmb[:, 128 * t:128 * (t + 1)], wgk2b[:],
                 start=True, stop=False, after=[tprev, etoksA[s]], inc=False)
            tk = p.mm(ps[:], onesr, bgk2b[:], start=False, stop=True, after=[tbg])
            tabs = p.act(zcol[s][:], ps[:], AF.Abs, after=[tk, wzc[s]])
            texp = p.act(zcol[s][:], zcol[s][:], AF.Exp, scale=-1.0, after=[tabs])
            tl1p = p.act(zcol[s][:], zcol[s][:], AF.Ln, bias=cc[:, 0:1], after=[texp])
            tmin = p.ts(zc2[s][:], ps[:], 0.0, None, OP.min, after=[tk, tabs])
            etoksA[s] = tmin
            tls = p.tt(lsb[s][:], zc2[s][:], zcol[s][:], OP.subtract,
                       after=[tmin, tl1p, dtoks[s]])
            wzc[s] = tls
            td = p.dma("gpsimd", scr["lsz"][128 * t:(t + 1) * 128, :], lsb[s][:],
                       after=[tls], ch="lszo%d" % s)
            dtoks[s] = td
        p.barrier()


def emit_l0_scan(p, io, scr, cst, moe):
    cc, idb, U2b, M2f = cst["cc"], cst["idb"], cst["U2b"], cst["M2f"]
    wob = p.wob
    masks, yacc = moe["masks"], moe["yacc"]
    with contextlib.ExitStack() as es:
        lszt = [p.sb([128, 512], BF16, es=es) for _ in range(2)]
        ktt = [p.sb([128, 512], BF16, es=es) for _ in range(2)]
        vtt = [p.sb([128, 1024], BF16, es=es) for _ in range(2)]
        gtt = [p.sb([128, 1024], BF16, es=es) for _ in range(2)]
        qfh = [[p.sb([64, 128], BF16, es=es) for _ in range(H)] for _ in range(2)]
        kfh = [[p.sb([64, 128], BF16, es=es) for _ in range(H)] for _ in range(2)]
        QE0 = p.sb([128, 128], BF16, es=es)
        QE1 = p.sb([128, 128], BF16, es=es)
        KE0 = p.sb([128, 128], BF16, es=es)
        KE1 = p.sb([128, 128], BF16, es=es)
        KT0 = p.sb([128, 64], BF16, es=es)
        KT1 = p.sb([128, 64], BF16, es=es)
        ebf = p.sb([64, 128], F32, es=es)
        enf = p.sb([64, 128], F32, es=es)
        ent = p.sb([128, 64], F32, es=es)
        e0c = p.sb([64, 1], F32, es=es)
        e1c = p.sb([64, 1], F32, es=es)
        A2m = p.sb([128, 128], BF16, es=es)
        S_A = p.sb([64, 8 * 128], F32, es=es)
        S_B = p.sb([64, 8 * 128], F32, es=es)
        S_Ab = p.sb([128, 8 * 128], BF16, es=es)
        S_Bb = p.sb([128, 8 * 128], BF16, es=es)
        tz = []
        tz.append(p.memset(QE0[:], 0.0))
        tz.append(p.memset(QE1[:], 0.0))
        tz.append(p.memset(KE0[:], 0.0))
        tz.append(p.memset(KE1[:], 0.0))
        tz.append(p.memset(KT0[:], 0.0))
        tz.append(p.memset(KT1[:], 0.0))
        tz.append(p.memset(S_A[:], 0.0))
        tz.append(p.memset(S_B[:], 0.0))
        tz.append(p.memset(S_Ab[:], 0.0))
        tz.append(p.memset(S_Bb[:], 0.0))
        p.barrier()
        pbb = p.ps([128, 192], F32, es=es)
        pbtm = pbb[:, 0:64]
        pbfm = pbb[0:64, 64:192]
        pAP = p.ps([128, 256], F32, es=es)
        pAT = pAP[:, 0:128]
        po = pAP[:, 128:256]
        pkv0 = p.ps([64, 128], F32, es=es)
        pkv1 = pkv0
        py = [p.ps([128, 512], F32, es=es) for _ in range(2)]
        Mb2 = [p.sb([128, 8 * 1024], BF16, es=es) for _ in range(2)]
        mld = [None, None]
        mcons = [None, None]
        wpy = [None, None]

        def load_m(ei):
            bm = ei % 2
            tl = None
            for c in range(8):
                tl = p.dma("sync", Mb2[bm][:, 1024 * c:1024 * (c + 1)],
                           scr["mall"].rearrange("(e k p) c -> e k p c", e=8, p=128)[ei, c],
                           after=[mcons[bm], p.mall_tok], ch="mld%d" % bm)
            mld[bm] = tl

        def emit_y_half(u, n):
            e, ty = u // 16, u % 16
            bm = e % 2
            tk = None
            for c in range(8):
                tk = p.mm(py[n][:],
                          xnfm0[:, c * T + 128 * ty: c * T + 128 * (ty + 1)],
                          Mb2[bm][:, 1024 * c + 512 * n: 1024 * c + 512 * (n + 1)],
                          start=(c == 0), stop=(c == 7),
                          after=[mld[bm], wpy[n]] if c == 0 else (),
                          inc=(c == 7))
            tac = p.stt(yacc[:, 1024 * ty + 512 * n: 1024 * ty + 512 * (n + 1)],
                        py[n][:],
                        masks[:, 8 * ty + e: 8 * ty + e + 1],
                        yacc[:, 1024 * ty + 512 * n: 1024 * ty + 512 * (n + 1)],
                        OP.mult, OP.add, after=[tk])
            wpy[n] = tac
            mcons[bm] = tk

        xnfm0 = moe["xnfm"]
        load_m(0)
        ot = p.sb([128, 1024], F32, es=es)
        sqt = p.sb([128, 1024], F32, es=es)
        sgt = p.sb([128, 1024], F32, es=es)
        rs8 = p.sb([128, 8], F32, es=es)
        ogb = p.sb([128, 1024], BF16, es=es)
        ogf = p.sb([128, 1024], BF16, es=es)
        ptb = [p.ps([128, 128], BF16, es=es) for _ in range(2)]
        ps0 = p.ps([128, 512], F32, es=es)
        s0t = [p.sb([128, 1024], F32, es=es) for _ in range(2)]

        tld = [dict(lsz=None, k=None, v=None, g=None, q=[None] * H, kf=None) for _ in range(2)]
        w = dict(pbtm=None, pbfm=None, pAT=None, pkv0=None, pkv1=None, po=None,
                 ebf=None, enf=None, ent=None, e0=None, e1=None, A2m=None,
                 QE0=None, QE1=None, KE0=None, KE1=None, KT0=None, KT1=None,
                 SA=None, SB=None, SAb=None, SBb=None, ot=None, ogb=None,
                 ogf=None, ps0=None, s0t=[None, None], sqt=None, sgt=None,
                 rs8=None)
        prevdma = [None, None]
        for t in range(NT):
            s = t % 2
            if t % 2 == 1 and t // 2 < 7:
                load_m(t // 2 + 1)
            pd = prevdma[s]
            chn = "scan%d" % s
            p.dma("sync", lszt[s][:], scr["lsz"][t * 128:(t + 1) * 128, :], after=[pd], ch=chn)
            p.dma("sync", ktt[s][:], scr["ktm"][t * 128:(t + 1) * 128, :], after=[pd], ch=chn)
            p.dma("sync", vtt[s][:], scr["vtm"][t * 128:(t + 1) * 128, :], after=[pd], ch=chn)
            p.dma("sync", gtt[s][:], scr["gtm"][t * 128:(t + 1) * 128, :], after=[pd], ch=chn)
            for h in range(H):
                p.dma("sync", qfh[s][h][:],
                      scr["qfm"][64 * h:64 * (h + 1), t * 128:(t + 1) * 128],
                      after=[pd], ch=chn)
                t_ld = p.dma("sync", kfh[s][h][:],
                             scr["kfm"][64 * h:64 * (h + 1), t * 128:(t + 1) * 128],
                             after=[pd], ch=chn)
            t_lsz = t_k = t_v = t_g = t_ld
            t_q = [t_ld] * H
            t_kf = [t_ld] * H
            for h in range(H):
                hs = slice(64 * h, 64 * (h + 1))
                vs = slice(128 * h, 128 * (h + 1))
                tb1 = p.mm(pbtm, U2b, lszt[s][:, hs], True, True,
                           after=[t_lsz, w["pbtm"], w["pbfm"]])
                tb2 = p.mm(pbfm, lszt[s][:, hs], U2b, True, True,
                           after=[t_lsz, w["pbfm"]])
                emit_y_half(t * 8 + h, 0)
                tent = p.act(ent[:], pbtm, AF.Exp, scale=-1.0,
                             after=[tb1, tb2, w["ent"]])
                w["pbtm"] = tent
                tebf = p.act(ebf[:], pbfm, AF.Exp, bias=cc[0:64, 3:4],
                             after=[tb2, w["ebf"]])
                tenf = p.act(enf[:], pbfm, AF.Exp, scale=-1.0,
                             after=[tb2, w["enf"]])
                te0 = p.act(e0c[:], pbfm[:, 63:64], AF.Exp, after=[tb2, w["e0"]])
                te1 = p.act(e1c[:], pbfm[:, 127:128], AF.Exp, after=[tb2, w["e1"]])
                w["pbfm"] = te1
                tQ0 = p.tt(QE0[0:64, 0:64], qfh[s][h][:, 0:64], ebf[:, 0:64],
                           OP.mult, after=[t_q[h], tebf, w["QE0"]])
                tQ1 = p.tt(QE1[0:64, 64:128], qfh[s][h][:, 64:128], ebf[:, 64:128],
                           OP.mult, after=[t_q[h], tebf, w["QE1"]])
                tK0 = p.tt(KE0[0:64, 0:64], kfh[s][h][:, 0:64], enf[:, 0:64],
                           OP.mult, after=[t_kf[h], tenf, w["KE0"]])
                tK1 = p.tt(KE1[0:64, 64:128], kfh[s][h][:, 64:128], enf[:, 64:128],
                           OP.mult, after=[t_kf[h], tenf, w["KE1"]])
                w["ebf"] = tQ1
                w["enf"] = tK1
                tT0 = p.tt(KT0[0:64, :], ktt[s][0:64, hs], ent[0:64, :], OP.mult,
                           after=[t_k, tent, w["KT0"]])
                tT1 = p.tt(KT1[64:128, :], ktt[s][64:128, hs], ent[64:128, :],
                           OP.mult, after=[t_k, tent, w["KT1"]])
                w["ent"] = tT1
                p.mm(pAT, KE0[:], QE0[:], True, False,
                     after=[tK0, tQ0, w["pAT"], w["po"]], inc=False)
                tat = p.mm(pAT, KE1[:], QE1[:], False, True, after=[tK1, tQ1])
                ta2 = p.tt(A2m[:], pAT, M2f, OP.mult, after=[tat, w["A2m"]])
                w["pAT"] = ta2
                tkv0 = p.mm(pkv0[:], KT0[:], vtt[s][:, vs], True, True,
                            after=[tT0, t_v, w["pkv0"]])
                tSB = p.tt(S_B[:, vs], S_A[:, vs], pkv0[:], OP.add,
                           after=[tkv0, w["SB"], w["SA"]])
                w["pkv0"] = tSB
                tSB2 = p.ts(S_B[:, vs], S_B[:, vs], e0c[:], None, OP.mult,
                            after=[te0, tSB])
                tSBb = p.vcopy(S_Bb[0:64, vs], S_B[:, vs], after=[tSB2, w["SBb"]])
                emit_y_half(t * 8 + h, 1)
                tkv1 = p.mm(pkv1[:], KT1[:], vtt[s][:, vs], True, True,
                            after=[tT1, t_v, tSB])
                p.mm(po, A2m[:], vtt[s][:, vs], True, False,
                     after=[ta2, t_v, w["po"]], inc=False)
                tom2 = p.mm(po, QE0[:], S_Ab[:, vs], False, False,
                            after=[tQ0, w["SAb"]])
                tom = p.mm(po, QE1[:], S_Bb[:, vs], False, True,
                           after=[tQ1, tSBb])
                tSA = p.tt(S_A[:, vs], S_B[:, vs], pkv1[:], OP.add,
                           after=[tkv1, tSB2, tom])
                w["pkv0"] = tSA
                w["SB"] = tSA
                tSA2 = p.ts(S_A[:, vs], S_A[:, vs], e1c[:], None, OP.mult,
                            after=[te1, tSA])
                w["SA"] = tSA2
                tSAb = p.vcopy(S_Ab[0:64, vs], S_A[:, vs], after=[tSA2, tom2])
                w["SAb"] = tSAb
                w["SBb"] = tom
                w["e0"] = tSB2
                w["e1"] = tSA2
                w["QE0"] = tom2
                w["QE1"] = tom
                w["KE0"] = tat
                w["KE1"] = tat
                w["KT0"] = tkv0
                w["KT1"] = tkv1
                tev = p.scopy(ot[:, vs], po, after=[tom, w["ot"]])
                w["po"] = tev
                if h == H - 1:
                    w["ot"] = None
                oevac_last = tev
            tsq = p.act(sqt[:], ot[:], AF.Square, after=[oevac_last, w["sqt"]])
            trs = p.red(rs8[:], sqt[:].rearrange("p (h c) -> p h c", c=128),
                        OP.add, after=[tsq, w["rs8"]])
            w["sqt"] = trs
            tms = p.ts(rs8[:], rs8[:], 1.0 / 128.0, None, OP.mult, after=[trs])
            tln = p.act(rs8[:], rs8[:], AF.Ln, bias=cc[:, 1:2], after=[tms])
            trr = p.act(rs8[:], rs8[:], AF.Exp, scale=-0.5, after=[tln])
            w["rs8"] = trr
            tsg = p.act(sgt[:], gtt[s][:], AF.Silu, after=[t_g, w["sgt"]])
            ogtoks = []
            for h in range(H):
                vs = slice(128 * h, 128 * (h + 1))
                tog = p.stt(ogb[:, vs], ot[:, vs], rs8[:, h:h + 1], sgt[:, vs],
                            OP.mult, OP.mult, after=[trr, tsg, w["ogb"]])
                ogtoks.append(tog)
            w["ogb"] = ogtoks[-1]
            w["ot"] = ogtoks[-1]
            w["sgt"] = ogtoks[-1]
            tcpl = None
            for c in range(8):
                ss = c % 2
                ttr = p.tr(ptb[ss][:], ogb[:, c * 128:(c + 1) * 128], idb,
                           after=[ogtoks[c], w.get("ptb%d" % ss)])
                tcp = p.vcopy(ogf[:, c * 128:(c + 1) * 128], ptb[ss][:],
                              after=[ttr, w["ogf"]])
                w["ptb%d" % ss] = tcp
                tcpl = tcp
            for n in range(2):
                tk = None
                for c in range(8):
                    tk = p.mm(ps0[:],
                              ogf[:, 128 * c:128 * (c + 1)],
                              wob[:, D * c + 512 * n: D * c + 512 * (n + 1)],
                              start=(c == 0), stop=(c == 7),
                              after=[tcpl, w["ps0"]] if c == 0 else (),
                              inc=(c == 7))
                tev = p.scopy(s0t[s][:, 512 * n:512 * (n + 1)], ps0[:],
                              after=[tk, w["s0t"][s]])
                w["ps0"] = tev
                w["ogf"] = tk
            td = p.dma("gpsimd", scr["s0"][t * 128:(t + 1) * 128, :], s0t[s][:],
                       after=[tev], ch="s0o%d" % s)
            w["s0t"][s] = td
            prevdma[s] = ogtoks[-1]
        p.barrier()


def emit_l0_combine(p, io, scr, cst, moe):
    yacc = moe["yacc"]
    with contextlib.ExitStack() as es:
        xt = [p.sb([128, 1024], F32, es=es) for _ in range(2)]
        s0l = [p.sb([128, 1024], F32, es=es) for _ in range(2)]
        syt = [p.sb([128, 1024], F32, es=es) for _ in range(2)]
        x1t = [p.sb([128, 1024], F32, es=es) for _ in range(2)]
        wx = [None, None]
        for t in range(NT):
            s = t % 2
            p.dma("sync", xt[s][:], io["x"][t * 128:(t + 1) * 128, :],
                  after=[wx[s]], ch="cmb%d" % s)
            tls = p.dma("sync", s0l[s][:], scr["s0"][t * 128:(t + 1) * 128, :],
                        after=[wx[s]], ch="cmb%d" % s)
            tlx = tls
            tsy = p.act(syt[s][:], yacc[:, 1024 * t:1024 * (t + 1)], AF.Silu,
                        after=[wx[s]])
            tm1 = p.tt(syt[s][:], syt[s][:], s0l[s][:], OP.mult, after=[tsy, tls])
            tx1 = p.tt(x1t[s][:], syt[s][:], xt[s][:], OP.add, after=[tm1, tlx])
            td = p.dma("gpsimd", scr["x1"][t * 128:(t + 1) * 128, :], x1t[s][:],
                       after=[tx1], ch="x1o%d" % s)
            wx[s] = td
        p.barrier()



def emit_consts(p, io):
    cst = {}
    cst["cc"] = p.sb([128, 6], F32, name="cc_t")
    cst["U2b"] = p.sb([128, 128], BF16, name="u2_t")
    cst["M2f"] = p.sb([128, 128], F32, name="m2_t")
    cst["idb"] = p.sb([128, 128], BF16, name="id_t")
    cst["idf"] = p.sb([128, 128], F32, name="idf_t")
    cst["onesr"] = p.sb([1, 128], BF16, name="or_t")
    cst["onesc"] = p.sb([128, 1], BF16, name="oc_t")
    cst["onesr512"] = p.sb([1, 512], BF16, name="o512_t")
    cst["E4"] = p.sb([128, 4], F32, name="e4_t")
    cst["E4T"] = p.sb([4, 128], F32, name="e4t_t")
    toks = []
    for k in ("cc", "U2b", "M2f", "idb", "idf", "onesr", "onesc", "onesr512", "E4", "E4T"):
        toks.append(p.dma("sync", cst[k][:], io[k], ch="cst"))
    p.barrier()
    return {k: (v[:] if k in ("U2b", "M2f", "idb", "idf", "onesr", "onesc",
                              "onesr512", "E4", "E4T") else v)
            for k, v in cst.items()}


def build_program(upto="l1", dbg=()):
    p = Prog()
    io = declare_io(p)
    scr = {}
    scr["qfm"] = p.dscr("qfm_s", [512, T], BF16)
    scr["kfm"] = p.dscr("kfm_s", [512, T], BF16)
    scr["ktm"] = p.dscr("ktm_s", [T, 512], BF16)
    scr["vtm"] = p.dscr("vtm_s", [T, 1024], BF16)
    scr["gtm"] = p.dscr("gtm_s", [T, 1024], BF16)
    scr["lsz"] = p.dscr("lsz_s", [T, 512], BF16)
    scr["s0"] = p.dscr("s0_s", [T, D], F32)
    scr["mown"] = p.dscr("mown_s", [1024, 1024], BF16)
    scr["mall"] = p.dscr("mall_s", [8192, 1024], BF16)
    scr["xn1"] = p.dscr("xn1_s", [T, D], BF16)
    scr["qfm1"] = p.dscr("qfm1_s", [D, T], BF16)
    scr["kfm1"] = p.dscr("kfm1_s", [D, T], BF16)
    scr["s1"] = p.dscr("s1_s", [T, D], F32)
    if upto == "l0":
        scr["x1"] = io["out"]
    else:
        scr["x1"] = p.dscr("x1_s", [T, D], F32)
    dbg_aps = {}
    for name in dbg:
        shp = {"qfm": [512, T], "kfm": [512, T], "ktm": [T, 512],
               "vtm": [T, 1024], "gtm": [T, 1024], "lsz": [T, 512],
               "s0": [T, D], "x1": [T, D], "xn1": [T, D], "s1": [T, D]}[name]
        dt = F32 if name in ("s0", "x1", "s1") else BF16
        dbg_aps[name] = p.dout("dbg_" + name, shp, dt)

    cst = emit_consts(p, io)
    xnfm = p.sb([128, 8 * T], BF16, name="xnfm0")
    es_moe = contextlib.ExitStack()
    moe = {
        "masks": p.sb([128, NT * 8], F32, es=es_moe),
        "yacc": p.sb([128, NT * 1024], BF16, es=es_moe),
        "xnfm": xnfm,
    }
    p.wob = p.sb([128, 8 * D], BF16, name="wob", es=p.es_wob)
    emit_moe_mown(p, io, scr, cst)
    emit_ln(p, io["x"], xnfm, cst["cc"], cst["idb"], eps_col_idx=1)
    emit_l0_proj(p, io, scr, cst, xnfm, moe)
    emit_l0_scan(p, io, scr, cst, moe)
    p.es_wob.close()
    emit_l0_combine(p, io, scr, cst, moe)
    es_moe.close()
    if upto != "l0":
        emit_l1(p, io, scr, cst, xnfm)
    if dbg_aps:
        with contextlib.ExitStack() as es:
            bt = p.sb([128, 2048], BF16, es=es)
            bt32 = p.sb([128, 1024], F32, es=es)
            dbg_last = None
            tprev = None
            for name, dst in dbg_aps.items():
                srcap = scr[name] if name != "x1" else scr["x1"]
                rows, cols = {"qfm": (512, T), "kfm": (512, T), "ktm": (T, 512),
                              "vtm": (T, 1024), "gtm": (T, 1024), "lsz": (T, 512),
                              "s0": (T, D), "x1": (T, D), "xn1": (T, D),
                              "s1": (T, D)}[name]
                tile = bt32 if name in ("s0", "x1", "s1") else bt
                for r in range(0, rows, 128):
                    t1 = p.dma("sync", tile[:, :cols], srcap[r:r + 128, :],
                               after=[tprev, dbg_last], ch="dbgi")
                    dbg_last = t1
                    tprev = p.dma("gpsimd", dst[r:r + 128, :], tile[:, :cols],
                                  after=[t1], ch="dbgo")
            p.barrier()
    return p


def kernel(**inputs):
    n = 8
    p = build_program(upto=KERNEL_UPTO, dbg=())
    nc = p.finalize()
    in_maps = [host_prep(inputs, b) for b in range(n)]
    res = run_bass_kernel_spmd(nc, in_maps, core_ids=list(range(n)))
    out = np.stack([res.results[b]["out"] for b in range(n)], axis=0)
    return out.astype(np.float32)


KERNEL_UPTO = "l1"


def emit_l1_attn(p, io, scr, cst, xnfm):
    cc = cst["cc"]
    idb = cst["idb"]
    onesr = cst["onesr"]
    onesr512 = cst["onesr512"]
    with contextlib.ExitStack() as es:
        ab8 = p.sb([128, 16], F32, es=es)
        browb = p.sb([1, 3072], BF16, es=es)
        brf = p.sb([1, 3072], F32, es=es)
        vtm = p.sb([128, NT * 1040], BF16, name="vtm1", es=es)
        otm = p.sb([128, NT * 1040], BF16, name="otm1", es=es)
        es_s = contextlib.ExitStack()
        colsum = p.sb([128, 8], F32, es=es_s)
        colsq = p.sb([128, 8], F32, es=es_s)
        sqt = [p.sb([128, T], F32, es=es_s) for _ in range(2)]
        g48 = p.sb([4, 32], F32, es=es_s)
        ps48 = p.ps([4, 8], F32, es=es_s)
        ps128 = p.ps([128, 8], F32, es=es_s)
        gamc = p.sb([128, 8], F32, es=es_s)
        betc = p.sb([128, 8], F32, es=es_s)
        tsq = [None, None]
        for c in range(8):
            s = c % 2
            t1 = p.act(sqt[s][:], xnfm[:, c * T:(c + 1) * T], AF.Square,
                       after=[tsq[s]])
            t2 = p.red(colsq[:, c:c + 1], sqt[s][:], OP.add, after=[t1])
            tsq[s] = t2
            p.red(colsum[:, c:c + 1], xnfm[:, c * T:(c + 1) * T], OP.add)
        p.barrier()
        t1 = p.mm(ps48[:], cst["E4"], colsum[:], True, True)
        t2 = p.scopy(g48[:, 0:8], ps48[:], after=[t1])
        t3 = p.mm(ps48[:], cst["E4"], colsq[:], True, True, after=[t2])
        t4 = p.scopy(g48[:, 8:16], ps48[:], after=[t3])
        inv = 1.0 / (T * 32.0)
        t5 = p.ts(g48[:, 16:24], g48[:, 0:8], inv, None, OP.mult, after=[t2, t4])
        t6 = p.ts(g48[:, 8:16], g48[:, 8:16], inv, None, OP.mult, after=[t4])
        t7 = p.tt(g48[:, 24:32], g48[:, 16:24], g48[:, 16:24], OP.mult, after=[t5])
        t8 = p.tt(g48[:, 24:32], g48[:, 8:16], g48[:, 24:32], OP.subtract,
                  after=[t6, t7])
        t9 = p.act(g48[:, 24:32], g48[:, 24:32], AF.Ln, bias=cc[0:4, 2:3], after=[t8])
        t10 = p.act(g48[:, 24:32], g48[:, 24:32], AF.Exp, scale=-0.5, after=[t9])
        t11 = p.mm(ps128[:], cst["E4T"], g48[:, 24:32], True, True, after=[t10])
        tg = p.dma("sync", gamc[:], io["gam_c"], ch="gnp")
        tb = p.dma("sync", betc[:], io["bet_c"], ch="gnp")
        t12 = p.stt(ab8[:, 0:8], ps128[:], 1.0, gamc[:], OP.mult, OP.mult,
                    after=[t11, tb])
        t13 = p.mm(ps128[:], cst["E4T"], g48[:, 16:24], True, True, after=[t12, t10])
        t14 = p.tt(ab8[:, 8:16], ps128[:], ab8[:, 0:8], OP.mult, after=[t13, t12])
        p.tt(ab8[:, 8:16], betc[:], ab8[:, 8:16], OP.subtract, after=[t14])
        p.barrier()
        es_s.close()

        es_w = contextlib.ExitStack()
        wqkvb = p.sb([128, 8 * 3072], BF16, es=es_w)
        stgq = [p.sb([128, 3072], F32, es=es_w) for _ in range(2)]
        psbr = p.ps([1, 3072], F32, es=es_w)
        stok = [None, None]
        tkbs = [None, None]
        tkb = None
        for c in range(8):
            s = c % 2
            tl = p.dma("sync", stgq[s][:], io["wqkv"][c * 128:(c + 1) * 128, :],
                       after=[stok[s], tkbs[s]], ch="wqkv%d" % s)
            for n in range(6):
                tkb = p.mm(psbr[:, 512 * n:512 * (n + 1)], ab8[:, 8 + c:9 + c],
                           stgq[s][:, 512 * n:512 * (n + 1)],
                           start=(c == 0), stop=(c == 7), after=[tl],
                           inc=(n == 5))
            tkbs[s] = tkb
            tv = p.ts(wqkvb[:, 3072 * c:3072 * (c + 1)], stgq[s][:],
                      ab8[:, c:c + 1], None, OP.mult, after=[tl], eng="vector")
            stok[s] = tv
        tbq = p.dma("sync", brf[:], io["bqkvr"], ch="bqk")
        tba = p.tt(brf[:], brf[:], psbr[:], OP.add, after=[tbq, tkb])
        tbb = p.vcopy(browb[:], brf[:], after=[tba])
        p.barrier()

        vones = vtm[:].rearrange("p (t h c) -> p t h c", h=16, c=65)[:, :, :, 64:65]
        tvo = p.memset(vones, 1.0)
        es_v1 = contextlib.ExitStack()
        psv = [p.ps([128, 512], F32, es=es_v1) for _ in range(2)]
        wpv = [None, None]
        for t in range(NT):
            for n in range(2):
                s = n
                for c in range(8):
                    p.mm(psv[s][:],
                         xnfm[:, c * T + 128 * t: c * T + 128 * (t + 1)],
                         wqkvb[:, 3072 * c + 2048 + 512 * n: 3072 * c + 2048 + 512 * (n + 1)],
                         start=(c == 0), stop=False,
                         after=[wpv[s]] if c == 0 else (), inc=False)
                tk = p.mm(psv[s][:], onesr,
                          browb[0:1, 2048 + 512 * n: 2048 + 512 * (n + 1)],
                          start=False, stop=True, after=[tbb])
                for hh in range(8):
                    h = 8 * n + hh
                    wpv[s] = p.vcopy(
                        vtm[:, 1040 * t + 65 * h: 1040 * t + 65 * h + 64],
                        psv[s][:, 64 * hh:64 * (hh + 1)], after=[tk, tvo])
        stqk = [p.sb([64, 512], BF16, es=es_v1) for _ in range(2)]
        wstq = [None, None]
        wpq = [None, None]
        for h in range(16):
            for part, dst, scl in ((0, scr["qfm1"], 0.125), (1, scr["kfm1"], 1.0)):
                base = 1024 * part + 64 * h
                for g in range(4):
                    s = g % 2
                    for c in range(8):
                        p.mm(psv[s][0:64, :],
                             wqkvb[:, 3072 * c + base: 3072 * c + base + 64],
                             xnfm[:, c * T + 512 * g: c * T + 512 * (g + 1)],
                             start=(c == 0), stop=False,
                             after=[wpv[s]] if c == 0 else (), inc=False)
                    tk = p.mm(psv[s][0:64, :], browb[0:1, base:base + 64],
                              onesr512, start=False, stop=True, after=[tbb])
                    te = p.act(stqk[s][:], psv[s][0:64, :], AF.Copy, scale=scl,
                               after=[tk, wstq[s]])
                    wpv[s] = te
                    td = p.dma("gpsimd",
                               dst[64 * h:64 * (h + 1), 512 * g:512 * (g + 1)],
                               stqk[s][:], after=[te], ch="qko%d" % s)
                    wstq[s] = td
        p.barrier()
        es_v1.close()
        es_w.close()

        es_v = contextlib.ExitStack()
        QA = [p.sb([128, T], BF16, es=es_v) for _ in range(2)]
        KA = [p.sb([128, T], BF16, es=es_v) for _ in range(2)]
        PT = p.sb([128, 8192], BF16, es=es_v)
        sqt2 = p.sb([64, T], BF16, es=es_v)
        qn2 = p.sb([1, T], F32, es=es_v)
        kn2 = p.sb([1, T], F32, es=es_v)
        km = p.sb([1, 1], F32, es=es_v)
        brow = p.sb([1, T], F32, es=es_v)
        oht = [p.sb([65, 512], BF16, es=es_v) for _ in range(2)]
        psSt = [p.ps([128, 1024], F32, es=es_v) for _ in range(2)]
        psO = [p.ps([65, 512], F32, es=es_v) for _ in range(2)]
        ptrb = p.ps([128, 128], BF16, es=es_v)
        pnrm = p.ps([1, 512], F32, es=es_v)
        tonesa = p.memset(KA[0][64:65, :], 1.0)
        tonesb = p.memset(KA[1][64:65, :], 1.0)
        tldk = [None, None]
        tSTread = [None, None]
        wPT = [None, None]
        wO = [None, None]
        woht = [None, None]
        wotm = None
        wptrb = [None, None]
        wsq = None
        wnrm = None
        wrow = None
        st_ctr = [0]

        def load_qk(hh):
            b_ = hh % 2
            p.dma("sync", QA[b_][0:64, :], scr["qfm1"][64 * hh:64 * (hh + 1), :],
                  after=[tSTread[b_]], ch="qkl%d" % b_)
            t2 = p.dma("sync", KA[b_][0:64, :], scr["kfm1"][64 * hh:64 * (hh + 1), :],
                       after=[tSTread[b_]], ch="qkl%d" % b_)
            tldk[b_] = t2

        def bound_row(h):
            nonlocal wsq, wnrm, wrow
            b_ = h % 2
            t1 = p.act(sqt2[:], KA[b_][0:64, :], AF.Square, after=[tldk[b_], wsq])
            tlast = None
            tklast = None
            for g in range(4):
                tk = p.mm(pnrm[:], cst["onesc"][0:64, :], sqt2[:, 512 * g:512 * (g + 1)],
                          True, True, after=[t1, wnrm])
                tklast = tk
                tlast = p.vcopy(kn2[0:1, 512 * g:512 * (g + 1)], pnrm[:], after=[tk, wrow])
                wnrm = tlast
            tkm = p.red(km[:], kn2[:], OP.max, after=[tlast])
            t2 = p.act(sqt2[:], QA[b_][0:64, :], AF.Square, after=[tldk[b_], tklast])
            wsq = t2
            for g in range(4):
                tk = p.mm(pnrm[:], cst["onesc"][0:64, :], sqt2[:, 512 * g:512 * (g + 1)],
                          True, True, after=[t2, wnrm])
                tlast = p.vcopy(qn2[0:1, 512 * g:512 * (g + 1)], pnrm[:], after=[tk, wrow])
                wnrm = tlast
            t3 = p.ts(brow[:], qn2[:], km[0:1, 0:1], None, OP.mult, after=[tkm, tlast, wrow])
            t4 = p.act(brow[:], brow[:], AF.Ln, bias=cc[0:1, 4:5], after=[t3])
            t5 = p.act(brow[:], brow[:], AF.Exp, scale=0.5, after=[t4])
            t6 = p.ts(brow[:], brow[:], -1.0, 60.0, OP.mult, OP.add, after=[t5])
            wrow = t6
            tq = p.vcopy(QA[b_][64:65, :], brow[:], after=[t6, tSTread[b_]])
            return tq

        def st_sub(h, qg, tq64):
            nonlocal wotm
            b_ = h % 2
            par = st_ctr[0] % 2
            st_ctr[0] += 1
            extoks = []
            for pr in range(8):
                half = pr % 2
                toks = []
                for j in range(2):
                    kc = 2 * pr + j
                    toks.append(p.mm(
                        psSt[half][:, 512 * j:512 * (j + 1)],
                        KA[b_][0:65, 128 * kc:128 * (kc + 1)],
                        QA[b_][0:65, 512 * qg:512 * (qg + 1)],
                        start=True, stop=True,
                        after=[tq64, tonesa, tonesb, wPT[half]]))
                tex = p.act(PT[:, 1024 * pr:1024 * (pr + 1)], psSt[half][:],
                            AF.Exp, after=[toks[0], toks[1]])
                wPT[half] = tex
                extoks.append(tex)
            tk = None
            for kc in range(16):
                tk = p.mm(psO[par][:],
                          vtm[:, 1040 * kc + 65 * h: 1040 * kc + 65 * (h + 1)],
                          PT[:, 512 * kc:512 * (kc + 1)],
                          start=(kc == 0), stop=(kc == 15),
                          after=[extoks[kc // 2]] + ([wO[par]] if kc == 0 else []),
                          inc=(kc == 15))
            tSTread[b_] = tk
            teo = p.vcopy(oht[par][:], psO[par][:], after=[tk, woht[par]])
            wO[par] = teo
            for sub in range(4):
                tt_ = 4 * qg + sub
                ttr = p.tr(ptrb[:, 0:65],
                           oht[par][:, 128 * sub:128 * (sub + 1)],
                           idb[0:65, 0:65], after=[teo, wptrb[0]])
                tcp = p.vcopy(
                    otm[:, 1040 * tt_ + 65 * h: 1040 * tt_ + 65 * (h + 1)],
                    ptrb[:, 0:65], after=[ttr, wotm])
                wptrb[0] = tcp
                woht[par] = tcp

        load_qk(0)
        pend = None
        for h in range(16):
            tq = bound_row(h)
            for qg in range(4):
                if pend is not None:
                    st_sub(*pend)
                pend = (h, qg, tq)
                if qg == 0 and h + 1 < 16:
                    load_qk(h + 1)
        st_sub(*pend)
        p.barrier()
        es_v.close()

        es_o = contextlib.ExitStack()
        wo1b, two0, two1 = load_weight_bf16(p, io["wo1"], D, D, es_o)
        bo1f = p.sb([1, 1024], F32, es=es_o)
        bo1r = p.sb([1, 1024], BF16, es=es_o)
        tl = p.dma("sync", bo1f[:], io["bo1r"], ch="bo1")
        tbo = p.vcopy(bo1r[:], bo1f[:], after=[tl])
        bo1bc = p.sb([128, 1024], F32, es=es_o)
        psW = p.ps([128, 512], F32, es=es_o)
        tprev = None
        for n in range(2):
            tk = p.mm(psW[:], onesr, bo1r[0:1, 512 * n:512 * (n + 1)], True, True,
                      after=[tbo, tprev])
            tprev = p.scopy(bo1bc[:, 512 * n:512 * (n + 1)], psW[:], after=[tk])
        rcp16 = p.sb([128, 16], F32, es=es_o)
        o2 = p.sb([128, 1024], BF16, es=es_o)
        ofmt = p.sb([128, 1024], BF16, es=es_o)
        ptrb2 = [p.ps([128, 128], BF16, es=es_o) for _ in range(2)]
        xn1l = [p.sb([128, 1024], BF16, es=es_o) for _ in range(2)]
        s1t = [p.sb([128, 1024], F32, es=es_o) for _ in range(2)]
        wrc = None
        wo2 = None
        wofmt = None
        wpt2 = [None, None]
        wps = tprev
        ws1 = [None, None]
        wx1l = [None, None]
        for t in range(NT):
            s = t % 2
            tlx = p.dma("sync", xn1l[s][:], scr["xn1"][128 * t:128 * (t + 1), :],
                        after=[wx1l[s]], ch="xn1l%d" % s)
            ov = otm[:, 1040 * t:1040 * (t + 1)].rearrange("p (h c) -> p h c", c=65)
            trc = p.recip(rcp16[:], ov[:, :, 64], after=[wrc])
            o2toks = []
            for hh in range(16):
                tn = p.ts(o2[:, 64 * hh:64 * (hh + 1)],
                          otm[:, 1040 * t + 65 * hh: 1040 * t + 65 * hh + 64],
                          rcp16[:, hh:hh + 1], None, OP.mult,
                          after=[trc, wo2])
                o2toks.append(tn)
            wrc = o2toks[-1]
            tcpl = None
            for c in range(8):
                ss = c % 2
                ttr = p.tr(ptrb2[ss][:], o2[:, 128 * c:128 * (c + 1)], idb,
                           after=[o2toks[-1], wpt2[ss]])
                tcp = p.vcopy(ofmt[:, 128 * c:128 * (c + 1)], ptrb2[ss][:],
                              after=[ttr, wofmt])
                wpt2[ss] = tcp
                tcpl = tcp
            wo2 = tcpl
            for n in range(2):
                tk = None
                for c in range(8):
                    tk = p.mm(psW[:], ofmt[:, 128 * c:128 * (c + 1)],
                              wo1b[:, 1024 * c + 512 * n: 1024 * c + 512 * (n + 1)],
                              start=(c == 0), stop=(c == 7),
                              after=[tcpl, wps, two0, two1] if c == 0 else (),
                              inc=(c == 7))
                wofmt = tk
                ta = p.tt(s1t[s][:, 512 * n:512 * (n + 1)], psW[:],
                          bo1bc[:, 512 * n:512 * (n + 1)], OP.add,
                          after=[tk, ws1[s]])
                wps = ta
                ws1[s] = ta
            tb2 = p.tt(s1t[s][:], s1t[s][:], xn1l[s][:], OP.add, after=[tlx, ws1[s]])
            td = p.dma("gpsimd", scr["s1"][128 * t:128 * (t + 1), :], s1t[s][:],
                       after=[tb2], ch="s1o%d" % s)
            ws1[s] = td
            wx1l[s] = tb2
        p.barrier()
        es_o.close()


def emit_l1_ffn(p, io, scr, cst, xnfm):
    cc = cst["cc"]
    with contextlib.ExitStack() as es:
        gates = p.sb([128, NT * 8], F32, es=es)
        y1 = p.sb([128, NT * 1024], BF16, es=es)
        tmz = [p.memset(y1[:, :8192], 0.0), p.memset(y1[:, 8192:], 0.0)]
        es_e = contextlib.ExitStack()
        winb = p.sb([128, 8 * 2048], BF16, es=es_e)
        woutb = p.sb([128, 16 * 1024], BF16, es=es_e)
        stgw = [p.sb([128, 2048], F32, es=es_e) for _ in range(2)]
        hfm = p.sb([128, 16 * 512], BF16, es=es_e)
        wstg = [None, None]

        def conv_copy(eng, out, in_, after):
            if eng == "scalar":
                return p.act(out, in_, AF.Copy, after=after)
            return p.vcopy(out, in_, after=after, eng=eng)

        def load_expert(e, win_cons, wout_cons, engs=("scalar", "vector")):
            win_rdy = [None, None]
            wout_rdy = [None, None]
            for c in range(8):
                s = c % 2
                tl = p.dma("sync", stgw[s][:],
                           io["win"][e, 128 * c:128 * (c + 1), :],
                           after=[wstg[s]], ch="wie%d" % s)
                win_rdy[s] = conv_copy(engs[0], winb[:, 2048 * c:2048 * (c + 1)],
                                       stgw[s][:], [tl, win_cons])
                wstg[s] = win_rdy[s]
            for c in range(16):
                s = c % 2
                tl = p.dma("sync", stgw[s][:, 0:1024],
                           io["wout"][e, 128 * c:128 * (c + 1), :],
                           after=[wstg[s]], ch="wie%d" % s)
                wout_rdy[s] = conv_copy(engs[1], woutb[:, 1024 * c:1024 * (c + 1)],
                                        stgw[s][:, 0:1024], [tl, wout_cons])
                wstg[s] = wout_rdy[s]
            return win_rdy, wout_rdy

        es_g = contextlib.ExitStack()
        wg1b, twg0, twg1 = load_weight_bf16(p, io["wg1"], D, 256, es_g)
        wg2b = p.sb([128, 16], BF16, es=es_g)
        stg2 = p.sb([128, 8], F32, es=es_g)
        bg1c = p.sb([128, 2], F32, es=es_g)
        tbg = p.dma("sync", bg1c[:], io["bg1c"], ch="bg1")
        tw2 = None
        for c in range(2):
            tl = p.dma("sync", stg2[:], io["wg2"][c * 128:(c + 1) * 128, :],
                       after=[tw2], ch="wg2l")
            tw2 = p.vcopy(wg2b[:, 8 * c:8 * (c + 1)], stg2[:], after=[tl])
        g1fm = p.sb([128, 2 * T], BF16, es=es_g)
        psg = [p.ps([128, 512], F32, es=es_g) for _ in range(2)]
        wg = [None, None]
        for mc in range(2):
            for g in range(4):
                s = (mc * 4 + g) % 2
                tk = None
                for c in range(8):
                    tk = p.mm(psg[s][:],
                              wg1b[:, 256 * c + 128 * mc: 256 * c + 128 * (mc + 1)],
                              xnfm[:, c * T + 512 * g: c * T + 512 * (g + 1)],
                              start=(c == 0), stop=(c == 7),
                              after=[wg[s], twg0, twg1] if c == 0 else (),
                              inc=(c == 7))
                te = p.act(g1fm[:, T * mc + 512 * g: T * mc + 512 * (g + 1)],
                           psg[s][:], AF.Gelu, bias=bg1c[:, mc:mc + 1],
                           after=[tk, tbg])
                wg[s] = te
        psl = [p.ps([128, 8], F32, es=es_g) for _ in range(2)]
        prb = [p.sb([128, 8], F32, es=es_g) for _ in range(2)]
        aux = [p.sb([128, 12], F32, es=es_g) for _ in range(2)]
        junk8 = p.sb([128, 8], BF16, es=es_g)
        wpl = [None, None]
        gate_tok = [None] * NT
        for t in range(NT):
            s = t % 2
            tk = None
            for c in range(2):
                tk = p.mm(psl[s][:], g1fm[:, T * c + 128 * t: T * c + 128 * (t + 1)],
                          wg2b[:, 8 * c:8 * (c + 1)], start=(c == 0), stop=(c == 1),
                          after=[wg[0], wg[1], tw2, wpl[s]] if c == 0 else (),
                          inc=(c == 1))
            t1 = p.red(aux[s][:, 0:1], psl[s][:], OP.max, negate=True,
                       after=[tk, wpl[s]])
            t2 = p.act(junk8[:], psl[s][:], AF.Exp, bias=aux[s][:, 0:1],
                       accum_out=aux[s][:, 1:2], after=[t1])
            t3 = p.recip(aux[s][:, 2:3], aux[s][:, 1:2], after=[t2])
            t4 = p.act(prb[s][:], psl[s][:], AF.Exp, bias=aux[s][:, 0:1], after=[t2])
            t5 = p.ts(prb[s][:], prb[s][:], aux[s][:, 2:3], None, OP.mult,
                      after=[t3, t4])
            t6 = p.red(aux[s][:, 3:4], prb[s][:], OP.max, after=[t5])
            t7 = p.ts(aux[s][:, 4:12], prb[s][:], aux[s][:, 3:4], None, OP.subtract,
                      after=[t6])
            t8 = p.ts(aux[s][:, 4:12], aux[s][:, 4:12], 0.0, None, OP.is_ge,
                      after=[t7])
            t9 = p.stt(aux[s][:, 4:12], aux[s][:, 4:12], -1e30, prb[s][:],
                       OP.mult, OP.add, after=[t8])
            t10 = p.red(aux[s][:, 3:4], aux[s][:, 4:12], OP.max, after=[t9])
            t11 = p.ts(aux[s][:, 4:12], prb[s][:], aux[s][:, 3:4], None, OP.is_ge,
                       after=[t10])
            t12 = p.tt(gates[:, 8 * t:8 * (t + 1)], prb[s][:], aux[s][:, 4:12],
                       OP.mult, after=[t11])
            gate_tok[t] = t12
            wpl[s] = t12
        win_rdy, wout_rdy = load_expert(0, None, None)
        es_g.close()
        gates_done = [wg[0], wg[1], wpl[0], wpl[1]]
        psh = [p.ps([128, 512], F32, es=es_e) for _ in range(2)]
        pso = [p.ps([128, 1024], F32, es=es_e) for _ in range(2)]
        win_cons = None
        wout_cons = None
        wh = [None, None]
        wo_ = [None, None]
        first_mm = True
        for e in range(8):
            for tg in range(4):
                hlast = None
                for j in range(16):
                    s = j % 2
                    tk = None
                    for c in range(8):
                        tk = p.mm(psh[s][:],
                                  winb[:, 2048 * c + 128 * j: 2048 * c + 128 * (j + 1)],
                                  xnfm[:, c * T + 512 * tg: c * T + 512 * (tg + 1)],
                                  start=(c == 0), stop=(c == 7),
                                  after=([win_rdy[0], win_rdy[1], wh[s]]
                                         + (gates_done if first_mm else []))
                                  if c == 0 else (),
                                  inc=(c == 7))
                        first_mm = False
                    win_cons = tk
                    te = p.act(hfm[:, 512 * j:512 * (j + 1)], psh[s][:], AF.Gelu,
                               after=[tk])
                    wh[s] = te
                    hlast = te
                for sub in range(4):
                    t_ = 4 * tg + sub
                    s = sub % 2
                    tk = None
                    for j in range(16):
                        for n in range(2):
                            tk = p.mm(pso[s][:, 512 * n:512 * (n + 1)],
                                      hfm[:, 512 * j + 128 * sub: 512 * j + 128 * (sub + 1)],
                                      woutb[:, 1024 * j + 512 * n: 1024 * j + 512 * (n + 1)],
                                      start=(j == 0), stop=(j == 15),
                                      after=[hlast, wout_rdy[0], wout_rdy[1], wo_[s]]
                                      if (j == 0 and n == 0) else (),
                                      inc=(j == 15 and n == 1))
                    wout_cons = tk
                    tac = p.stt(y1[:, 1024 * t_:1024 * (t_ + 1)], pso[s][:],
                                gates[:, 8 * t_ + e: 8 * t_ + e + 1],
                                y1[:, 1024 * t_:1024 * (t_ + 1)], OP.mult, OP.add,
                                after=[tk, gate_tok[t_]] + (tmz if e == 0 else []))
                    wo_[s] = tac
            if e < 7:
                win_rdy, wout_rdy = load_expert(e + 1, win_cons, wout_cons)
        p.barrier()
        es_e.close()
        x1t = [p.sb([128, 1024], F32, es=es) for _ in range(2)]
        s1l = [p.sb([128, 1024], F32, es=es) for _ in range(2)]
        syt = [p.sb([128, 1024], F32, es=es) for _ in range(2)]
        ot = [p.sb([128, 1024], F32, es=es) for _ in range(2)]
        wx = [None, None]
        for t in range(NT):
            s = t % 2
            p.dma("sync", x1t[s][:], scr["x1"][t * 128:(t + 1) * 128, :],
                  after=[wx[s]], ch="fc%d" % s)
            tls = p.dma("sync", s1l[s][:], scr["s1"][t * 128:(t + 1) * 128, :],
                        after=[wx[s]], ch="fc%d" % s)
            tsy = p.act(syt[s][:], y1[:, 1024 * t:1024 * (t + 1)], AF.Silu,
                        after=[wx[s]])
            tm1 = p.tt(syt[s][:], syt[s][:], s1l[s][:], OP.mult, after=[tsy, tls])
            tx2 = p.tt(ot[s][:], syt[s][:], x1t[s][:], OP.add, after=[tm1, tls])
            td = p.dma("gpsimd", io["out"][t * 128:(t + 1) * 128, :], ot[s][:],
                       after=[tx2], ch="oo%d" % s)
            wx[s] = td
        p.barrier()


def emit_l1(p, io, scr, cst, xnfm):
    emit_ln(p, scr["x1"], xnfm, cst["cc"], cst["idb"], eps_col_idx=1,
            xn_tm_dram=scr["xn1"])
    emit_l1_attn(p, io, scr, cst, xnfm)
    emit_l1_ffn(p, io, scr, cst, xnfm)
